# version 10
# baseline (speedup 1.0000x reference)
import contextlib
import numpy as np
import concourse.bass as bass
import concourse.mybir as mybir
from concourse.bass_utils import run_bass_kernel_spmd
from concourse.bass_core_rust import IndirectOffsetOnAxis

F32 = mybir.dt.float32
BF16 = mybir.dt.bfloat16
I32 = mybir.dt.int32
AF = mybir.ActivationFunctionType
ALU = mybir.AluOpType
AX = mybir.AxisListType

B, D = 2, 1024
T = 8192
H, DK, DV, M = 4, 256, 256, 256
L = 256
NDC = D // 128
FH, DHD = 16, 64
EPS = 1e-5
GN = 16.0
BIS_ITERS = 48
TAPS = ()
STOP_AFTER = None
PARANOID = False
ENGS = ("te", "act", "dve", "pool", "sp")


class _Cut(Exception):
    pass


class PB:
    def __init__(self, nc, stack):
        self.nc = nc
        self.stack = stack
        self.q = {e: [] for e in ENGS}
        self.sems = {}
        self.cnt = {}
        self.hw = {e: {} for e in ENGS}

    def sem(self, name):
        if name not in self.sems:
            self.sems[name] = self.stack.enter_context(self.nc.semaphore(name))
            self.cnt[name] = 0
        return self.sems[name]

    def op(self, eng, fn, inc=(), after=(), dma=False):
        if eng in ("act", "dve", "pool") and not dma:
            after = tuple(after) + ((eng, None),)
            if not any(s == eng for s, _ in inc):
                inc = tuple(inc) + ((eng, 1),)
        hw = self.hw[eng]
        waits = []
        for s, v in after:
            vv = self.cnt[s] if v is None else v
            if vv <= 0 or vv <= hw.get(s, 0):
                continue
            hw[s] = vv
            waits.append((self.sems[s], vv))
        waits = tuple(waits)
        incs = tuple((self.sems[s], n) for s, n in inc)

        def play(h, waits=waits, incs=incs, fn=fn):
            for sh, v in waits:
                h.wait_ge(sh, v)
            ins = fn(h)
            for sh, n in incs:
                ins.then_inc(sh, n)

        self.q[eng].append(play)
        tok = []
        for s, n in inc:
            self.cnt[s] += n
            tok.append((s, self.cnt[s]))
        if PARANOID and eng != "_":
            self.barrier()
        return tuple(tok)

    def wait(self, eng, name, val=None):
        v = self.cnt[name] if val is None else val
        if v > 0 and v > self.hw[eng].get(name, 0):
            self.hw[eng][name] = v
            self.q[eng].append(lambda h, sh=self.sems[name], v=v: h.wait_ge(sh, v))

    def barrier(self):
        for e in ENGS:
            for s in self.sems:
                self.wait(e, s)

    def c(self, name):
        return self.cnt[name]


def build_program(taps=()):
    def cut(name):
        if STOP_AFTER == name:
            raise _Cut()
    NCH = T // L
    CSEL = T // 2
    QW = T // 8 + 128
    MYT = T // 4
    NKT = CSEL // 128

    nc = bass.Bass("TRN2", target_bir_lowering=False, debug=False)
    stack = contextlib.ExitStack()
    p = PB(nc, stack)
    for s in ("ld", "lda", "ldb", "lg", "lw", "te", "act", "dve", "pool", "st", "sg", "sga", "sgb", "sta", "stb", "out", "outa", "outb"):
        p.sem(s)

    din = {}
    for nm, shp, dt in (
        ("x_b", [T, D], F32), ("wqT", [D, D], F32), ("wkT", [D, D], F32),
        ("wvT", [D, D], F32), ("wfT", [D, D], F32), ("woT", [D, D], F32),
        ("fWq", [D, D], F32), ("fWk", [D, D], F32), ("fWv", [D, D], F32),
        ("fWo", [D, D], F32), ("wg", [1, D], F32), ("nw", [128, NDC], F32),
        ("gn", [128, NDC], F32), ("mcum", [L, L], F32), ("trim", [L, L], F32),
        ("ident", [128, 128], F32), ("lt64", [T // 128, T // 128], F32),
        ("lt128", [128, 128], F32), ("sel2", [2, 256], F32),
        ("myq16", [1, 1], I32),
    ):
        din[nm] = nc.dram_tensor(nm, shp, dt, kind="ExternalInput").ap()
    out_ext = nc.dram_tensor("out_mine", [MYT, D], F32, kind="ExternalOutput").ap()

    tap = {}
    for nm, shp in (("qT", [D, T]), ("kvf", [3, T, D]), ("o", [T, D]),
                    ("y1", [T, D]), ("logits", [128, T // 128]),
                    ("theta", [128, 1]), ("slot", [128, T // 128]),
                    ("attn_in", [CSEL, 1032]), ("o2", [D, QW]), ("dbg", [128, 64])):
        if nm in taps:
            tap[nm] = nc.dram_tensor("tap_" + nm, shp, F32, kind="ExternalOutput").ap()

    y1d = nc.dram_tensor("y1d", [T, 1032], BF16).ap()
    y1f32d = nc.dram_tensor("y1f32d", [T, D], F32).ap()
    atn = nc.dram_tensor("atn", [CSEL, 1032], BF16).ap()
    k2d = nc.dram_tensor("k2d", [NDC, 128, CSEL], BF16).ap()
    v2d = nc.dram_tensor("v2d", [CSEL, FH * 65], BF16).ap()
    o2d = nc.dram_tensor("o2d", [QW, D], BF16).ap()
    myrows = nc.dram_tensor("myrows", [QW, 1032], BF16).ap()
    srelD = nc.dram_tensor("srelD", [T // 128, 128], F32).ap()
    cpfD = nc.dram_tensor("cpfD", [T // 128, 1], F32).ap()

    sb = lambda name, shape, dt: stack.enter_context(nc.sbuf_tensor(name, shape, dt))

    try:

        id_b = sb("id_b", [128, 128], BF16)
        eps_c = sb("eps_c", [128, 1], F32)
        one_f = sb("one_f", [128, 1], F32)
        one_b = sb("one_b", [128, 1], BF16)
        onerow_b = sb("onerow_b", [1, 128], BF16)
        onesrow_f = sb("onesrow_f", [1, 128], F32)
        wgbc = sb("wgbc", [128, D], F32)
        myq_s = sb("myq_s", [1, 1], I32)
        junk = sb("junk", [128, D], BF16)
        lgT = sb("lgT", [128, T // 128], F32)
        colA = sb("colA", [128, 1], F32)
        colB = sb("colB", [128, 1], F32)
        colC = sb("colC", [128, 1], F32)
        ab_stack = contextlib.ExitStack()
        sba = lambda name, shape, dt: ab_stack.enter_context(nc.sbuf_tensor(name, shape, dt))
        w_q = sba("w_q", [128, NDC, D], BF16)
        w_k = sba("w_k", [128, NDC, D], BF16)
        w_v = sba("w_v", [128, NDC, D], BF16)
        w_f = sba("w_f", [128, NDC, D], BF16)
        w_o = sba("w_o", [128, NDC, D], BF16)
        nw_t = sba("nw_t", [128, NDC], F32)
        gn_t = sba("gn_t", [128, NDC], F32)
        mc_s = sba("mc_s", [128, 2, L], BF16)
        tri_s = sba("tri_s", [128, 2, L], F32)

        for (dst, src) in (
            (w_q, "wqT"), (w_k, "wkT"), (w_v, "wvT"), (w_f, "wfT"), (w_o, "woT")):
            for dc in range(NDC):
                p.op("pool", lambda g, d=dst, s=src, dc=dc: g.dma_start(
                    out=d[:, dc, :], in_=din[s][128 * dc:128 * (dc + 1), :]),
                    inc=(("lg", 16),), dma=True)
        p.op("pool", lambda g: g.dma_start(
            out=mc_s[:, :, :], in_=din["mcum"].rearrange("(tt pp) t -> pp tt t", pp=128)),
            inc=(("lg", 16),), dma=True)
        p.op("pool", lambda g: g.dma_start(out=id_b[:, :], in_=din["ident"][:, :]),
             inc=(("lg", 16),), dma=True)
        for nm, dst in (("nw", nw_t), ("gn", gn_t), ("myq16", myq_s)):
            p.op("sp", lambda g, nm=nm, dst=dst: g.dma_start(out=dst[:, :], in_=din[nm][:, :]),
                 inc=(("ld", 16),))
        p.op("sp", lambda g: g.dma_start(
            out=tri_s[:, :, :], in_=din["trim"].rearrange("(tt pp) t -> pp tt t", pp=128)),
            inc=(("ld", 16),))
        for f_, v_ in ((eps_c, EPS), (one_f, 1.0), (one_b, 1.0), (onerow_b, 1.0),
                       (onesrow_f, 1.0)):
            p.op("dve", lambda v, f_=f_, v_=v_: v.memset(f_[:, :], v_), inc=(("dve", 1),))
        p.barrier()
        for dc in range(NDC):
            p.op("dve", lambda v, dc=dc: v.tensor_scalar(
                out=w_o[:, dc, :], in0=w_o[:, dc, :], scalar1=gn_t[:, dc:dc + 1],
                scalar2=None, op0=ALU.mult), inc=(("dve", 1),))
        for wt in (w_q, w_k, w_v, w_f):
            for dc in range(NDC):
                p.op("dve", lambda v, wt=wt, dc=dc: v.tensor_scalar(
                    out=wt[:, dc, :], in0=wt[:, dc, :], scalar1=nw_t[:, dc:dc + 1],
                    scalar2=None, op0=ALU.mult), inc=(("dve", 1),))
        wg_stack = contextlib.ExitStack()
        wgrow = wg_stack.enter_context(nc.sbuf_tensor("wgrow", [1, D], F32))
        wgP = wg_stack.enter_context(nc.psum_tensor("wgP", [128, 512], F32))
        p.op("sp", lambda g: g.dma_start(out=wgrow[:, :], in_=din["wg"][:, :]),
             inc=(("ld", 16),))
        p.wait("te", "ld")
        for half in range(D // 512):
            p.op("te", lambda t, half=half: t.matmul(
                wgP[:, 0:512], onesrow_f[:, :], wgrow[:, 512 * half:512 * (half + 1)],
                start=True, stop=True), inc=(("te", 1),),
                after=(("dve", None), ("ld", None), ("lg", None)))
            p.op("dve", lambda v, half=half: v.tensor_copy(
                wgbc[:, 512 * half:512 * (half + 1)], wgP[:, 0:512]),
                inc=(("dve", 1),), after=(("te", None),))
        wg_stack.close()
        p.barrier()

        x_s = sba("x_s", [128, 3, 2, D], F32)
        hT_s = sba("hT_s", [128, NDC, L], BF16)
        qTw = sba("qTw", [128, NDC, 2, L], BF16)
        kTMw = sba("kTMw", [128, 2, 2, D], BF16)
        vTMw = sba("vTMw", [128, 2, 2, D], BF16)
        fTMw = sba("fTMw", [128, 2, 2, D], BF16)
        fabs_s = sba("fabs_s", [128, D], F32)
        stat = sba("stat", [128, 24], F32)
        LamB = sba("LamB", [128, 2, L], F32)
        EnG = sba("EnG", [128, 2, L], F32)
        Sp = sba("Sp", [128, 2, 2, L], BF16)
        SpMT = sba("SpMT", [128, 2, 2, L], BF16)
        kT_s = sba("kT_s", [128, 2, L], BF16)
        AmT = sba("AmT", [128, 2, L], BF16)
        WmT = sba("WmT", [128, 2, L], BF16)
        zS = sba("zS", [128, 2, L], F32)
        pEx = sba("pEx", [128, 2, L], BF16)
        PpTM = sba("PpTM", [128, 2, L], BF16)
        PpMT = sba("PpMT", [128, 2, 2, L], BF16)
        nmx = sba("nmx", [128, 2], F32)
        den = sba("den", [128, 2], F32)
        rden = sba("rden", [128, 2, 2], F32)
        lsC = sba("lsC", [128, 2], F32)
        lsR = sba("lsR", [1, L], BF16)
        tmpS = sba("tmpS", [128, 2, M], F32)
        Hm = sba("Hm", [128, H, 2, M], BF16)
        Hv = sba("Hv", [128, H, 2, DV], BF16)
        oB = sba("oB", [128, 2, D], F32)
        onb = sba("onb", [128, 2, D], BF16)
        xnb = onb
        onT = sba("onT", [128, NDC, L], BF16)
        aoT = sba("aoT", [128, NDC, L], BF16)
        y1b = sba("y1b", [128, 2, 1032], BF16)

        PSCE = ab_stack.enter_context(nc.psum_tensor("PSCE", [128, 512], F32))
        PSCO = ab_stack.enter_context(nc.psum_tensor("PSCO", [128, 512], F32))
        PAWA = ab_stack.enter_context(nc.psum_tensor("PAWA", [128, 512], F32))
        PAWB = ab_stack.enter_context(nc.psum_tensor("PAWB", [128, 512], F32))
        PKV0 = ab_stack.enter_context(nc.psum_tensor("PKV0", [128, 512], F32))
        PKV1 = ab_stack.enter_context(nc.psum_tensor("PKV1", [128, 512], F32))
        PQW = ab_stack.enter_context(nc.psum_tensor("PQW", [128, 512], F32))
        PTX = ab_stack.enter_context(nc.psum_tensor("PTX", [128, 1024], BF16))

        p.op("pool", lambda g: g.memset(Hm[:, :, :, :], 0.0), inc=(("pool", 1),))
        p.op("pool", lambda g: g.memset(Hv[:, :, :, :], 0.0), inc=(("pool", 1),))
        p.op("pool", lambda g: g.memset(y1b[:, :, :], 0.0), inc=(("pool", 1),))
        p.barrier()
        cut("init")

        DMASEMS = frozenset(("ld", "lda", "ldb", "lg", "lw", "st", "sta", "stb",
                             "sg", "sga", "sgb", "out", "outa", "outb"))

        def TK(*toks):
            d = {}
            for t in toks:
                for s, v in (t or ()):
                    if v and (s not in d or v > d[s]):
                        d[s] = v
            return tuple(d.items())

        def _promote(tk):
            return tuple((s, None if s in DMASEMS else v) for s, v in tk)

        RW = {}

        def rw(nm):
            e = RW.get(nm)
            if e is None:
                e = RW[nm] = [(), ()]
            return e

        def mark_w(nm, tok):
            e = rw(nm)
            e[0] = tok
            e[1] = ()

        def mark_r(nm, tok):
            e = rw(nm)
            e[1] = TK(e[1], tok)

        PSUMKEYS = frozenset(("scE", "scO", "awA", "awB", "kvA", "kvB", "qw", "pt"))

        def xop(eng, fn, reads=(), writes=(), inc=(), after=(), dma=False):
            deps = [after]
            for nm in reads:
                deps.append(rw(nm)[0])
                if nm in PSUMKEYS:
                    deps.append(rw(nm)[1])
            for nm in writes:
                e = rw(nm)
                deps.append(e[0])
                deps.append(e[1])
            for s, n in inc:
                if s in DMASEMS and p.cnt[s] > 0:
                    deps.append(((s, p.cnt[s]),))
            tok = p.op(eng, fn, inc=inc, after=_promote(TK(*deps)), dma=dma)
            for nm in writes:
                mark_w(nm, tok)
            for nm in reads:
                mark_r(nm, tok)
            return tok

        def mmg(rname, emits, reads=(), after=()):
            e = rw(rname)
            deps = [after, e[0], e[1]]
            for nm in reads:
                deps.append(rw(nm)[0])
            tok = ()
            n = len(emits)
            for i, f in enumerate(emits):
                tok = p.op("te", f, inc=(("te", 1),) if i == n - 1 else (),
                           after=_promote(TK(*deps)) if i == 0 else ())
            for nm in reads:
                mark_r(nm, tok)
            mark_w(rname, tok)
            return tok

        scb = lambda h: (PSCE, "scE") if h % 2 == 0 else (PSCO, "scO")
        KVB = ((PKV0, "kvA"), (PKV1, "kvB"))
        QRr = lambda i: PQW[:, 256 * (i % 2):256 * (i % 2 + 1)]
        kvstate = [0]

        def kvreg():
            i = kvstate[0] & 1
            kvstate[0] += 1
            reg, nm = KVB[i]
            return nm, reg[:, 0:512]

        fl = lambda ap: ap.rearrange("pp a b -> pp (a b)")

        def px(c):
            sx = c % 3
            for tt in range(2):
                r0 = L * c + 128 * tt
                xop("sp", lambda g, sx=sx, tt=tt, r0=r0: g.dma_start(
                    out=x_s[:, sx, tt, :], in_=din["x_b"][r0:r0 + 128, :]),
                    writes=(("x", sx, tt),), inc=(("lda" if tt == 0 else "ldb", 16),))

        def proj_A1(c):
            sx = c % 3
            cut("pA1")
            for tt in range(2):
                xop("act", lambda a, sx=sx, tt=tt: a.activation(
                    out=junk[:, :], in_=x_s[:, sx, tt, :], func=AF.Square,
                    accum_out=stat[:, 0 + tt:1 + tt]), reads=(("x", sx, tt),),
                    writes=(("st", 0 + tt),), inc=(("act", 1),))
            for tt in range(2):
                xop("act", lambda a, tt=tt: a.activation(
                    out=stat[:, 2 + tt:3 + tt], in_=stat[:, 0 + tt:1 + tt], func=AF.Sqrt,
                    scale=1.0 / D, bias=eps_c[:, 0:1]), reads=(("st", 0 + tt),),
                    writes=(("st", 2 + tt),), inc=(("act", 1),))
            for tt in range(2):
                xop("dve", lambda v, tt=tt: v.reciprocal(
                    stat[:, 4 + tt:5 + tt], stat[:, 2 + tt:3 + tt]),
                    reads=(("st", 2 + tt),), writes=(("st", 4 + tt),), inc=(("dve", 1),))
            for tt in range(2):
                xop("act", lambda a, sx=sx, tt=tt: a.activation(
                    out=xnb[:, tt, :], in_=x_s[:, sx, tt, :], func=AF.Copy,
                    scale=stat[:, 4 + tt:5 + tt], bias=0.0),
                    reads=(("x", sx, tt), ("st", 4 + tt)), writes=(("onb", tt),),
                    inc=(("act", 1),))
            cut("pA3")

        def proj_A2(c):
            for tt in range(2):
                mmg("pt", [lambda t, tt=tt, dc=dc: t.transpose(
                    PTX[:, 128 * dc:128 * (dc + 1)],
                    xnb[:, tt, 128 * dc:128 * (dc + 1)], id_b[:, :])
                    for dc in range(NDC)], reads=(("onb", tt),))
                xop("dve", lambda v, tt=tt: v.tensor_copy(
                    hT_s[:, :, 128 * tt:128 * (tt + 1)],
                    PTX[:, 0:1024].rearrange("pp (a b) -> pp a b", a=NDC)),
                    reads=("pt",), writes=(("hT", tt),), inc=(("dve", 1),))

        def proj_f(c, tt):
            w = c % 2
            regs2 = []
            for half in range(2):
                rname, reg = kvreg()
                regs2.append((rname, reg))
                mmg(rname, [lambda t, tt=tt, half=half, di2=di2, reg=reg: t.matmul(
                    reg, hT_s[:, di2, 128 * tt:128 * (tt + 1)],
                    w_f[:, di2, 512 * half:512 * (half + 1)],
                    start=(di2 == 0), stop=(di2 == NDC - 1)) for di2 in range(NDC)],
                    reads=(("hT", tt),))
            for half, (rname, reg) in enumerate(regs2):
                sl = slice(512 * half, 512 * (half + 1))
                xop("act", lambda a, reg=reg, sl=sl: a.activation(
                    out=fabs_s[:, sl], in_=reg, func=AF.Abs),
                    reads=(rname,), writes=(("fab", half),), inc=(("act", 1),))
            for half, (rname, reg) in enumerate(regs2):
                sl = slice(512 * half, 512 * (half + 1))
                xop("act", lambda a, sl=sl: a.activation(
                    out=fabs_s[:, sl], in_=fabs_s[:, sl], func=AF.Exp, scale=-1.0),
                    reads=(("fab", half),), writes=(("fab", half),), inc=(("act", 1),))
            for half, (rname, reg) in enumerate(regs2):
                sl = slice(512 * half, 512 * (half + 1))
                xop("act", lambda a, sl=sl: a.activation(
                    out=fabs_s[:, sl], in_=fabs_s[:, sl], func=AF.Ln, bias=one_f[:, 0:1]),
                    reads=(("fab", half),), writes=(("fab", half),), inc=(("act", 1),))
            for half, (rname, reg) in enumerate(regs2):
                sl = slice(512 * half, 512 * (half + 1))
                xop("dve", lambda v, reg=reg, w=w, tt=tt, sl=sl: v.scalar_tensor_tensor(
                    out=fTMw[:, w, tt, sl], in0=reg, scalar=0.0, in1=fabs_s[:, sl],
                    op0=ALU.min, op1=ALU.subtract),
                    reads=(rname, ("fab", half)), writes=(("f", w, tt, half),),
                    inc=(("dve", 1),))

        def proj_q(c, dcs):
            w = c % 2
            for pi in range(0, len(dcs), 2):
                pair = dcs[pi:pi + 2]
                for dc in pair:
                    mmg("qw", [lambda t, dc=dc, di2=di2: t.matmul(
                        QRr(dc), w_q[:, di2, 128 * dc:128 * (dc + 1)], hT_s[:, di2, :],
                        start=(di2 == 0), stop=(di2 == NDC - 1)) for di2 in range(NDC)],
                        reads=(("hT", 0), ("hT", 1)))
                for dc in pair:
                    xop("act", lambda a, dc=dc, w=w: a.activation(
                        out=qTw[:, dc, w, :], in_=QRr(dc), func=AF.Silu),
                        reads=("qw",), writes=(("q", w, dc),), inc=(("act", 1),))

        def proj_kv(c, wt, key, tt):
            w = c % 2
            dst = kTMw if key == "k" else vTMw
            for half in range(2):
                rname, reg = kvreg()
                mmg(rname, [lambda t, tt=tt, half=half, di2=di2, reg=reg, wt=wt: t.matmul(
                    reg, hT_s[:, di2, 128 * tt:128 * (tt + 1)],
                    wt[:, di2, 512 * half:512 * (half + 1)],
                    start=(di2 == 0), stop=(di2 == NDC - 1)) for di2 in range(NDC)],
                    reads=(("hT", tt),))
                xop("act", lambda a, w=w, tt=tt, half=half, dst=dst, reg=reg: a.activation(
                    out=dst[:, w, tt, 512 * half:512 * (half + 1)], in_=reg,
                    func=AF.Silu), reads=(rname,), writes=((key, w, tt, half),),
                    inc=(("act", 1),))

        def scan_p1(c, h, last):
            w = c % 2
            hp = h % 2
            hp2 = h // 2
            mcs = slice(M * h, M * (h + 1))
            dks = slice(DK * h, DK * (h + 1))
            SC, scn = scb(h)
            mmg("pt", [lambda t, tt2=tt2, dkc2=dkc2, w=w, dks=dks: t.transpose(
                PTX[:, 128 * (2 * dkc2 + tt2):128 * (2 * dkc2 + tt2 + 1)],
                kTMw[:, w, tt2, dks.start + 128 * dkc2:dks.start + 128 * (dkc2 + 1)],
                id_b[:, :]) for tt2 in range(2) for dkc2 in range(2)],
                reads=(("k", w, 0, hp2), ("k", w, 1, hp2)))
            xop("dve", lambda v: v.tensor_copy(
                fl(kT_s[:, :, :]), PTX[:, 0:512]),
                reads=("pt",), writes=(("kT",),), inc=(("dve", 1),))
            mmg(scn, [lambda t, tt2=tt2, tp2=tp2, w=w, mcs=mcs, SC=SC: t.matmul(
                SC[:, 256 * tt2:256 * tt2 + 256],
                mc_s[:, tp2, 128 * tt2:128 * (tt2 + 1)], fTMw[:, w, tp2, mcs],
                start=(tp2 == 0), stop=(tp2 == 1))
                for tt2 in range(2) for tp2 in range(2)],
                reads=(("f", w, 0, hp2), ("f", w, 1, hp2)))
            if h == 0:
                cut("s_g")
            xop("act", lambda a, SC=SC: a.activation(
                out=fl(LamB[:, :, :]), in_=SC[:, 0:512], func=AF.Exp, scale=1.0 / GN),
                reads=(scn,), writes=(("LamB",),), inc=(("act", 1),))
            xop("act", lambda a, SC=SC: a.activation(
                out=fl(EnG[:, :, :]), in_=SC[:, 0:512], func=AF.Exp, scale=-1.0 / GN),
                reads=(scn,), writes=(("EnG",),), inc=(("act", 1),))
            xop("dve", lambda v, w=w, mcs=mcs, SC=SC: v.tensor_sub(
                zS[:, :, :], fTMw[:, w, :, mcs],
                SC[:, 0:512].rearrange("pp (a b) -> pp a b", a=2)),
                reads=(scn, ("f", w, 0, hp2), ("f", w, 1, hp2)),
                writes=(("zS",),), inc=(("dve", 1),))
            xop("act", lambda a: a.activation(
                out=fl(zS[:, :, :]), in_=fl(zS[:, :, :]), func=AF.Exp, scale=1.0 / GN),
                reads=(("zS",),), writes=(("zS",),), inc=(("act", 1),))
            xop("dve", lambda v, hp=hp: v.tensor_sub(
                fl(Sp[:, hp, :, :]), fl(EnG[:, :, :]), fl(zS[:, :, :])),
                reads=(("EnG",), ("zS",)), writes=(("Sp", hp),), inc=(("dve", 1),))
            if h == 0:
                cut("s_kt")
            mmg("awA", [lambda t, taut=taut2, dkc2=dkc2, h=h, w=w: t.matmul(
                PAWA[:, 256 * taut:256 * (taut + 1)],
                kT_s[:, dkc2, 128 * taut:128 * (taut + 1)],
                qTw[:, 2 * h + dkc2, w, :], start=(dkc2 == 0), stop=(dkc2 == 1))
                for taut2 in range(2) for dkc2 in range(2)],
                reads=(("kT",), ("q", w, 2 * h), ("q", w, 2 * h + 1)))
            xop("dve", lambda v: v.tensor_mul(
                fl(AmT[:, :, :]), PAWA[:, 0:512], fl(tri_s[:, :, :])),
                reads=("awA",), writes=(("AmT",),), inc=(("dve", 1),))
            mmg("awB",
                [lambda t, tt2=tt2, i4=i4, h=h, w=w, hp=hp: (
                    t.matmul(PAWB[:, 256 * tt2:256 * tt2 + 256],
                             qTw[:, 2 * h + i4, w, 128 * tt2:128 * (tt2 + 1)],
                             Hm[:, h, i4, :], start=(i4 == 0), stop=False)
                    if i4 < 2 else
                    t.matmul(PAWB[:, 256 * tt2:256 * tt2 + 256],
                             AmT[:, i4 - 2, 128 * tt2:128 * (tt2 + 1)],
                             Sp[:, hp, i4 - 2, :], start=False, stop=(i4 == 3)))
                 for tt2 in range(2) for i4 in range(4)],
                reads=(("q", w, 2 * h), ("q", w, 2 * h + 1), ("Hm", h),
                       ("AmT",), ("Sp", hp)))
            xop("dve", lambda v: v.tensor_mul(
                fl(zS[:, :, :]), PAWB[:, 0:512], fl(LamB[:, :, :])),
                reads=("awB", ("LamB",)), writes=(("zS",),), inc=(("dve", 1),))
            for tt in range(2):
                xop("dve", lambda v, tt=tt: v.tensor_reduce(
                    out=nmx[:, tt:tt + 1], in_=zS[:, tt, :], axis=AX.X, op=ALU.max,
                    negate=True), reads=(("zS",),), writes=(("nmx", tt),),
                    inc=(("dve", 1),))
            for tt in range(2):
                xop("act", lambda a, tt=tt: a.activation(
                    out=pEx[:, tt, :], in_=zS[:, tt, :], func=AF.Exp,
                    bias=nmx[:, tt:tt + 1], accum_out=den[:, tt:tt + 1]),
                    reads=(("zS",), ("nmx", tt)), writes=(("pEx",), ("den", tt)),
                    inc=(("act", 1),))
            xop("dve", lambda v: v.tensor_mul(
                fl(PpTM[:, :, :]), fl(pEx[:, :, :]), fl(LamB[:, :, :])),
                reads=(("pEx",), ("LamB",)), writes=(("Pp",),), inc=(("dve", 1),))
            xop("dve", lambda v, hp=hp: v.reciprocal(rden[:, hp, 0:2], den[:, 0:2]),
                reads=(("den", 0), ("den", 1)), writes=(("rden", hp),), inc=(("dve", 1),))
            if h == 0:
                cut("s_z")
            mmg("pt", [lambda t, tt2=tt2, mcx2=mcx2, base=base, hp=hp: t.transpose(
                PTX[:, 128 * (base + 2 * mcx2 + tt2):128 * (base + 2 * mcx2 + tt2 + 1)],
                Sp[:, hp, tt2, 128 * mcx2:128 * (mcx2 + 1)] if base == 0
                else PpTM[:, tt2, 128 * mcx2:128 * (mcx2 + 1)],
                id_b[:, :])
                for base in (0, 4) for tt2 in range(2) for mcx2 in range(2)],
                reads=(("Sp", hp), ("Pp",)))
            xop("dve", lambda v, hp=hp: v.tensor_copy(fl(SpMT[:, hp, :, :]), PTX[:, 0:512]),
                reads=("pt",), writes=(("SpM", hp),), inc=(("dve", 1),))
            xop("dve", lambda v, hp=hp: v.tensor_copy(fl(PpMT[:, hp, :, :]), PTX[:, 512:1024]),
                reads=("pt",), writes=(("PpM", hp),), inc=(("dve", 1),))
            if h == 0:
                cut("s_t2")

        def scan_p2(c, h, last):
            w = c % 2
            wn = (c + 1) % 2
            hp = h % 2
            hp2 = h // 2
            mcs = slice(M * h, M * (h + 1))
            dks = slice(DK * h, DK * (h + 1))
            SC, scn = scb(h)
            mmg("awA", [lambda t, taut=taut2, mcx2=mcx2, hp=hp: t.matmul(
                PAWA[:, 256 * taut:256 * (taut + 1)],
                SpMT[:, hp, mcx2, 128 * taut:128 * (taut + 1)], PpMT[:, hp, mcx2, :],
                start=(mcx2 == 0), stop=(mcx2 == 1))
                for taut2 in range(2) for mcx2 in range(2)],
                reads=(("SpM", hp), ("PpM", hp)))
            xop("dve", lambda v: v.tensor_mul(
                fl(WmT[:, :, :]), PAWA[:, 0:512], fl(tri_s[:, :, :])),
                reads=("awA",), writes=(("WmT",),), inc=(("dve", 1),))
            col_emits = []
            for mcx in range(2):
                col_emits.append(lambda t, mcx=mcx, w=w, mcs=mcs, last=last, SC=SC: t.matmul(
                    SC[:, mcx:mcx + 1],
                    fTMw[:, w, 1, mcs.start + 128 * mcx:mcs.start + 128 * (mcx + 1)],
                    one_b[:, 0:1], start=True, stop=last))
                if not last:
                    col_emits.append(lambda t, mcx=mcx, wn=wn, mcs=mcs, SC=SC: t.matmul(
                        SC[:, mcx:mcx + 1],
                        fTMw[:, wn, 0, mcs.start + 128 * mcx:mcs.start + 128 * (mcx + 1)],
                        one_b[:, 0:1], start=False, stop=True))
            col_emits.append(lambda t, w=w, mcs=mcs, last=last, SC=SC: t.matmul(
                SC[0:1, 4:4 + 256], one_b[:, 0:1], fTMw[:, w, 1, mcs],
                start=True, stop=last))
            if not last:
                col_emits.append(lambda t, wn=wn, mcs=mcs, SC=SC: t.matmul(
                    SC[0:1, 4:4 + 256], one_b[:, 0:1], fTMw[:, wn, 0, mcs],
                    start=False, stop=True))
            lamreads = (("f", w, 1, hp2),) + ((("f", wn, 0, hp2),) if not last else ())
            mmg(scn, col_emits, reads=lamreads)
            xop("act", lambda a, SC=SC: a.activation(
                out=lsC[:, 0:2], in_=SC[:, 0:2], func=AF.Exp, scale=1.0 / GN),
                reads=(scn,), writes=(("lsC",),), inc=(("act", 1),))
            xop("act", lambda a, SC=SC: a.activation(
                out=lsR[0:1, :], in_=SC[0:1, 4:4 + 256], func=AF.Exp, scale=1.0 / GN),
                reads=(scn,), writes=(("lsR",),), inc=(("act", 1),))
            mmg(scn, [lambda t, SC=SC: t.matmul(
                SC[:, 256:512], onerow_b[0:1, :], lsR[0:1, 0:256], start=True,
                stop=True)], reads=(("lsR",),))
            if h == 0:
                cut("s_l")
            mmg("awB", [lambda t, dkc=dkc2, taut2=taut2, w=w, dks=dks, hp=hp: t.matmul(
                PAWB[:, 256 * dkc:256 * (dkc + 1)],
                kTMw[:, w, taut2, dks.start + 128 * dkc:dks.start + 128 * (dkc + 1)],
                Sp[:, hp, taut2, :], start=(taut2 == 0), stop=(taut2 == 1))
                for dkc2 in range(2) for taut2 in range(2)],
                reads=(("k", w, 0, hp2), ("k", w, 1, hp2), ("Sp", hp)))
            xop("dve", lambda v, h=h: v.tensor_add(
                fl(tmpS[:, :, :]), fl(Hm[:, h, :, :]), PAWB[:, 0:512]),
                reads=("awB", ("Hm", h)), writes=(("tmp",),), inc=(("dve", 1),))
            for dkc in range(2):
                xop("dve", lambda v, dkc=dkc, h=h, SC=SC: v.tensor_mul(
                    Hm[:, h, dkc, :], tmpS[:, dkc, :], SC[:, 256:512]),
                    reads=(("tmp",), scn), writes=(("Hm", h),), inc=(("dve", 1),))
            mmg(scn,
                [lambda t, tt2=tt2, i4=i4, h=h, w=w, SC=SC, hp=hp: (
                    t.matmul(SC[:, 256 * tt2:256 * tt2 + 256],
                             PpMT[:, hp, i4, 128 * tt2:128 * (tt2 + 1)], Hv[:, h, i4, :],
                             start=(i4 == 0), stop=False)
                    if i4 < 2 else
                    t.matmul(SC[:, 256 * tt2:256 * tt2 + 256],
                             WmT[:, i4 - 2, 128 * tt2:128 * (tt2 + 1)],
                             vTMw[:, w, i4 - 2, DV * h:DV * (h + 1)], start=False,
                             stop=(i4 == 3)))
                 for tt2 in range(2) for i4 in range(4)],
                reads=(("PpM", hp), ("Hv", h), ("WmT",), ("v", w, 0, hp2),
                       ("v", w, 1, hp2)))
            for tt in range(2):
                xop("act", lambda a, tt=tt, h=h, SC=SC, hp=hp: a.activation(
                    out=oB[:, tt, DV * h:DV * (h + 1)],
                    in_=SC[:, 256 * tt:256 * tt + 256], func=AF.Copy,
                    scale=rden[:, hp, tt:tt + 1], bias=0.0),
                    reads=(scn, ("rden", hp)), writes=(("oB", tt),),
                    inc=(("act", 1),))
            mmg("awB", [lambda t, mcx=mcx2, taut2=taut2, w=w, h=h, hp=hp: t.matmul(
                PAWB[:, 256 * mcx:256 * (mcx + 1)],
                Sp[:, hp, taut2, 128 * mcx:128 * (mcx + 1)],
                vTMw[:, w, taut2, DV * h:DV * (h + 1)],
                start=(taut2 == 0), stop=(taut2 == 1))
                for mcx2 in range(2) for taut2 in range(2)],
                reads=(("Sp", hp), ("v", w, 0, hp2), ("v", w, 1, hp2)))
            xop("dve", lambda v, h=h: v.tensor_add(
                fl(tmpS[:, :, :]), fl(Hv[:, h, :, :]), PAWB[:, 0:512]),
                reads=("awB", ("Hv", h)), writes=(("tmp",),), inc=(("dve", 1),))
            for mcx in range(2):
                xop("dve", lambda v, mcx=mcx, h=h: v.tensor_scalar(
                    out=Hv[:, h, mcx, :], in0=tmpS[:, mcx, :], scalar1=lsC[:, mcx:mcx + 1],
                    scalar2=None, op0=ALU.mult),
                    reads=(("tmp",), ("lsC",)), writes=(("Hv", h),),
                    inc=(("dve", 1),))
            if h == 0:
                cut("s_o")

        def epi_pre(c, tt):
            xop("act", lambda a, tt=tt: a.activation(
                out=oB[:, tt, :], in_=oB[:, tt, :], func=AF.Silu),
                reads=(("oB", tt),), writes=(("oB", tt),), inc=(("act", 1),))
            xop("act", lambda a, tt=tt: a.activation(
                out=junk[:, :], in_=oB[:, tt, :], func=AF.Square,
                accum_out=stat[:, 6 + tt:7 + tt]), reads=(("oB", tt),),
                writes=(("st", 6 + tt),), inc=(("act", 1),))
            xop("act", lambda a, tt=tt: a.activation(
                out=stat[:, 8 + tt:9 + tt], in_=stat[:, 6 + tt:7 + tt], func=AF.Sqrt,
                scale=1.0 / D, bias=eps_c[:, 0:1]), reads=(("st", 6 + tt),),
                writes=(("st", 8 + tt),), inc=(("act", 1),))
            xop("dve", lambda v, tt=tt: v.reciprocal(
                stat[:, 10 + tt:11 + tt], stat[:, 8 + tt:9 + tt]),
                reads=(("st", 8 + tt),), writes=(("st", 10 + tt),), inc=(("dve", 1),))
            xop("act", lambda a, tt=tt: a.activation(
                out=onb[:, tt, :], in_=oB[:, tt, :], func=AF.Copy,
                scale=stat[:, 10 + tt:11 + tt], bias=0.0),
                reads=(("oB", tt), ("st", 10 + tt)), writes=(("onb", tt),),
                inc=(("act", 1),))
            mmg("pt", [lambda t, tt=tt, dc=dc: t.transpose(
                PTX[:, 128 * dc:128 * (dc + 1)], onb[:, tt, 128 * dc:128 * (dc + 1)],
                id_b[:, :]) for dc in range(NDC)], reads=(("onb", tt),))
            xop("dve", lambda v, tt=tt: v.tensor_copy(
                onT[:, :, 128 * tt:128 * (tt + 1)],
                PTX[:, 0:1024].rearrange("pp (a b) -> pp a b", a=NDC)),
                reads=("pt",), writes=(("onT", tt),), inc=(("dve", 1),))

        def epi_wo(c):
            for pi in range(0, NDC, 2):
                for dc in (pi, pi + 1):
                    mmg("qw", [lambda t, dc=dc, di2=di2: t.matmul(
                        QRr(dc), w_o[:, di2, 128 * dc:128 * (dc + 1)], onT[:, di2, :],
                        start=(di2 == 0), stop=(di2 == NDC - 1)) for di2 in range(NDC)],
                        reads=(("onT", 0), ("onT", 1)))
                for dc in (pi, pi + 1):
                    xop("dve", lambda v, dc=dc: v.tensor_copy(aoT[:, dc, :], QRr(dc)),
                        reads=("qw",), writes=(("aoT", dc),), inc=(("dve", 1),))

        def epi_fin(c, tt):
            sx = c % 3
            mmg("pt", [lambda t, tt=tt, dc=dc: t.transpose(
                PTX[:, 128 * dc:128 * (dc + 1)], aoT[:, dc, 128 * tt:128 * (tt + 1)],
                id_b[:, :]) for dc in range(NDC)],
                reads=tuple(("aoT", dc) for dc in range(NDC)))
            for half in range(2):
                xop("dve", lambda v, tt=tt, sx=sx, half=half: v.tensor_add(
                    x_s[:, sx, tt, 512 * half:512 * (half + 1)],
                    x_s[:, sx, tt, 512 * half:512 * (half + 1)],
                    PTX[:, 512 * half:512 * (half + 1)]),
                    reads=(("x", sx, tt), "pt"), writes=(("x", sx, tt),),
                    inc=(("dve", 1),))
            m = 12 + 4 * tt
            xop("dve", lambda v, tt=tt, sx=sx, m=m: v.tensor_reduce(
                out=stat[:, m:m + 1], in_=x_s[:, sx, tt, :], axis=AX.X, op=ALU.add),
                reads=(("x", sx, tt),), writes=(("st", m),), inc=(("dve", 1),))
            xop("act", lambda a, tt=tt, sx=sx, m=m: a.activation(
                out=junk[:, :], in_=x_s[:, sx, tt, :], func=AF.Square,
                accum_out=stat[:, m + 1:m + 2]), reads=(("x", sx, tt),),
                writes=(("st", m + 1),), inc=(("act", 1),))
            xop("dve", lambda v, m=m: v.tensor_scalar(
                out=stat[:, m:m + 1], in0=stat[:, m:m + 1], scalar1=1.0 / D,
                scalar2=None, op0=ALU.mult),
                reads=(("st", m),), writes=(("st", m),), inc=(("dve", 1),))
            xop("dve", lambda v, m=m: v.tensor_mul(
                stat[:, m + 2:m + 3], stat[:, m:m + 1], stat[:, m:m + 1]),
                reads=(("st", m),), writes=(("st", m + 2),), inc=(("dve", 1),))
            xop("dve", lambda v, m=m: v.scalar_tensor_tensor(
                out=stat[:, m + 1:m + 2], in0=stat[:, m + 1:m + 2], scalar=1.0 / D,
                in1=stat[:, m + 2:m + 3], op0=ALU.mult, op1=ALU.subtract),
                reads=(("st", m + 1), ("st", m + 2)), writes=(("st", m + 1),),
                inc=(("dve", 1),))
            xop("act", lambda a, m=m: a.activation(
                out=stat[:, m + 1:m + 2], in_=stat[:, m + 1:m + 2], func=AF.Sqrt,
                bias=eps_c[:, 0:1]), reads=(("st", m + 1),), writes=(("st", m + 1),),
                inc=(("act", 1),))
            xop("dve", lambda v, m=m: v.reciprocal(
                stat[:, m + 3:m + 4], stat[:, m + 1:m + 2]),
                reads=(("st", m + 1),), writes=(("st", m + 3),), inc=(("dve", 1),))
            xop("dve", lambda v, m=m: v.scalar_tensor_tensor(
                out=stat[:, m + 2:m + 3], in0=stat[:, m:m + 1], scalar=-1.0,
                in1=stat[:, m + 3:m + 4], op0=ALU.mult, op1=ALU.mult),
                reads=(("st", m), ("st", m + 3)), writes=(("st", m + 2),),
                inc=(("dve", 1),))
            xop("act", lambda a, tt=tt, sx=sx, m=m: a.activation(
                out=x_s[:, sx, tt, :], in_=x_s[:, sx, tt, :], func=AF.Identity,
                scale=stat[:, m + 3:m + 4], bias=stat[:, m + 2:m + 3]),
                reads=(("x", sx, tt), ("st", m + 3), ("st", m + 2)),
                writes=(("x", sx, tt),), inc=(("act", 1),))
            xop("dve", lambda v, tt=tt, sx=sx: v.tensor_copy(
                y1b[:, tt, 0:D], x_s[:, sx, tt, :]),
                reads=(("x", sx, tt),), writes=(("y1b", tt),), inc=(("dve", 1),))
            r0 = L * c + 128 * tt
            xop("sp", lambda g, tt=tt, r0=r0, sx=sx: g.dma_start(
                out=y1f32d[r0:r0 + 128, :], in_=x_s[:, sx, tt, :]),
                reads=(("x", sx, tt),), inc=(("outa" if tt == 0 else "outb", 16),))
            if "y1" in taps:
                xop("sp", lambda g, tt=tt, r0=r0, sx=sx: g.dma_start(
                    out=tap["y1"][r0:r0 + 128, :], in_=x_s[:, sx, tt, :]),
                    reads=(("x", sx, tt),), inc=(("st", 16),))
            xop("dve", lambda v, tt=tt, sx=sx: v.tensor_mul(
                x_s[:, sx, tt, :], x_s[:, sx, tt, :], wgbc[:, :]),
                reads=(("x", sx, tt),), writes=(("x", sx, tt),), inc=(("dve", 1),))
            xop("dve", lambda v, tt=tt, sx=sx, c=c: v.tensor_reduce(
                out=lgT[:, 2 * c + tt:2 * c + tt + 1], in_=x_s[:, sx, tt, :], axis=AX.X,
                op=ALU.add), reads=(("x", sx, tt),), writes=(("lgT",),),
                inc=(("dve", 1),))
            xop("dve", lambda v, tt=tt, c=c: v.tensor_copy(
                y1b[:, tt, D:D + 1], lgT[:, 2 * c + tt:2 * c + tt + 1]),
                reads=(("lgT",),), writes=(("y1b", tt),), inc=(("dve", 1),))
            xop("sp", lambda g, tt=tt, r0=r0: g.dma_start(
                out=y1d[r0:r0 + 128, :], in_=y1b[:, tt, :]),
                reads=(("y1b", tt),), inc=(("sta" if tt == 0 else "stb", 16),))

        def ab_step(s):
            lastc = (s - 1 == NCH - 1)
            ep = s - 2
            doepi = 2 <= s <= NCH + 1
            if doepi:
                if "o" in taps:
                    p.barrier()
                    for tt in range(2):
                        p.op("sp", lambda g, ep=ep, tt=tt: g.dma_start(
                            out=tap["o"][L * ep + 128 * tt:L * ep + 128 * (tt + 1), :],
                            in_=oB[:, tt, :]), inc=(("st", 16),))
                    p.barrier()
                for tt in range(2):
                    epi_pre(ep, tt)
            if s == 0:
                px(0)
            if s < NCH:
                proj_A1(s)
            if doepi:
                epi_wo(ep)
            if s < NCH:
                proj_A2(s)
                cut("pA")
                proj_f(s, 0)
                cut("pF")
            sc = s >= 1 and s <= NCH
            if sc:
                scan_p1(s - 1, 0, lastc)
            if s < NCH:
                proj_q(s, (0, 1, 2, 3))
                cut("pQ")
            if sc:
                scan_p1(s - 1, 1, lastc)
            if s < NCH:
                proj_kv(s, w_k, "k", 0)
                proj_kv(s, w_k, "k", 1)
                cut("pK")
            if sc:
                scan_p2(s - 1, 0, lastc)
            if s < NCH:
                proj_kv(s, w_v, "v", 0)
            if sc:
                scan_p2(s - 1, 1, lastc)
            if s < NCH:
                proj_kv(s, w_v, "v", 1)
                proj_f(s, 1)
            if sc:
                scan_p1(s - 1, 2, lastc)
            if doepi:
                epi_fin(ep, 0)
            if sc:
                scan_p1(s - 1, 3, lastc)
            if s < NCH:
                proj_q(s, (4, 5, 6, 7))
            if sc:
                scan_p2(s - 1, 2, lastc)
            if doepi:
                epi_fin(ep, 1)
            if sc:
                scan_p2(s - 1, 3, lastc)
            if s + 1 < NCH:
                px(s + 1)
                if "qT" in taps:
                    p.barrier()
                    for dc in range(NDC):
                        p.op("pool", lambda g, s=s, dc=dc: g.dma_start(
                            out=tap["qT"][128 * dc:128 * (dc + 1), L * s:L * (s + 1)],
                            in_=qTw[:, dc, s % 2, :]), inc=(("lg", 16),), dma=True)
                if "kvf" in taps:
                    p.barrier()
                    for tt in range(2):
                        r0 = L * s + 128 * tt
                        for i, buf in ((0, kTMw), (1, vTMw), (2, fTMw)):
                            p.op("pool", lambda g, i=i, buf=buf, r0=r0, s=s, tt=tt: g.dma_start(
                                out=tap["kvf"][i, r0:r0 + 128, :], in_=buf[:, s % 2, tt, :]),
                                inc=(("lg", 16),), dma=True)
                if "qT" in taps or "kvf" in taps:
                    p.barrier()

        for step in range(NCH + 2):
            ab_step(step)
            if step < NCH:
                cut(("proj", step))
            if step >= 1 and step <= NCH:
                cut(("scan", step - 1))
        p.barrier()
        ab_stack.close()
        cut("ab")

        PAB = stack.enter_context(nc.psum_tensor("PAB", [128, 1024], F32))
        PCD = stack.enter_context(nc.psum_tensor("PCD", [128, 1024], F32))
        PE_ = stack.enter_context(nc.psum_tensor("PE_", [128, 512], F32))
        PF_ = stack.enter_context(nc.psum_tensor("PF_", [128, 512], F32))
        PT_ = stack.enter_context(nc.psum_tensor("PT_", [128, 1024], BF16))
        PS_ = stack.enter_context(nc.psum_tensor("PS_", [128, 512], F32))
        id_f = sb("id_f", [128, 128], F32)
        lt64_s = sb("lt64_s", [T // 128, T // 128], F32)
        lt128_s = sb("lt128_s", [128, 128], F32)
        sel2_s = sb("sel2_s", [2, 256], BF16)
        sqb = sb("sqb", [128, 512], BF16)
        nrm2 = sb("nrm2", [1, FH, (T // 2) // 512], F32)
        v2S = sb("v2S", [128, (T // 2) // 128, FH * 65], BF16)
        kmx2 = sb("kmx2", [1, FH], F32)
        qmx2 = sb("qmx2", [1, FH], F32)
        for nm, dst in (("ident", id_f), ("lt64", lt64_s), ("lt128", lt128_s)):
            p.op("sp", lambda g, nm=nm, dst=dst: g.dma_start(out=dst[:, :], in_=din[nm][:, :]),
                 inc=(("ld", 16),))
        p.op("pool", lambda g: g.dma_start(out=sel2_s[:, :], in_=din["sel2"][:, :]),
             inc=(("lg", 16),), dma=True)
        p.barrier()

        if "logits" in taps:
            p.op("sp", lambda g: g.dma_start(out=tap["logits"][:, :], in_=lgT[:, :]),
                 inc=(("st", 16),))
            p.barrier()

        NQT = T // 128

        lo = sb("lo", [128, 1], F32)
        hi = sb("hi", [128, 1], F32)
        mid = sb("mid", [128, 1], F32)
        sel = sb("sel", [128, 1], F32)
        selm1 = sb("selm1", [128, 1], F32)
        dtmp = sb("dtmp", [128, 1], F32)
        cntc = sb("cntc", [128, 1], F32)
        cnts = sb("cnts", [1, 1], F32)
        junkN = sb("junkN", [128, NQT], F32)
        p.op("dve", lambda v: v.memset(lo[:, :], -256.0), inc=(("dve", 1),))
        p.op("dve", lambda v: v.memset(hi[:, :], 256.0), inc=(("dve", 1),))
        p.op("dve", lambda v: v.memset(mid[:, :], 0.0), inc=(("dve", 1),))
        for it in range(BIS_ITERS):
            p.op("dve", lambda v: v.tensor_scalar(
                out=junkN[:, :], in0=lgT[:, :], scalar1=mid[:, 0:1], scalar2=0.0,
                op0=ALU.is_ge, op1=ALU.add, accum_out=cntc[:, 0:1]), inc=(("dve", 1),))
            p.op("te", lambda t: t.matmul(PS_[0:1, 0:1], cntc[:, 0:1], one_f[:, 0:1],
                                          start=True, stop=True), inc=(("te", 1),),
                 after=(("dve", None),))
            p.op("dve", lambda v: v.tensor_copy(cnts[0:1, 0:1], PS_[0:1, 0:1]),
                 inc=(("dve", 1),), after=(("te", None),))
            p.op("te", lambda t: t.matmul(PS_[:, 2:3], onesrow_f[0:1, :], cnts[0:1, 0:1],
                                          start=True, stop=True), inc=(("te", 1),),
                 after=(("dve", None),))
            p.op("dve", lambda v: v.tensor_scalar(
                out=sel[:, 0:1], in0=PS_[:, 2:3], scalar1=float(CSEL), scalar2=None,
                op0=ALU.is_ge), inc=(("dve", 1),), after=(("te", None),))
            p.op("dve", lambda v: v.tensor_scalar(
                out=selm1[:, 0:1], in0=sel[:, 0:1], scalar1=1.0, scalar2=None,
                op0=ALU.subtract), inc=(("dve", 1),))
            p.op("dve", lambda v: v.tensor_sub(dtmp[:, 0:1], mid[:, 0:1], lo[:, 0:1]),
                 inc=(("dve", 1),))
            p.op("dve", lambda v: v.scalar_tensor_tensor(
                out=lo[:, 0:1], in0=dtmp[:, 0:1], scalar=sel[:, 0:1], in1=lo[:, 0:1],
                op0=ALU.mult, op1=ALU.add), inc=(("dve", 1),))
            p.op("dve", lambda v: v.tensor_sub(dtmp[:, 0:1], hi[:, 0:1], mid[:, 0:1]),
                 inc=(("dve", 1),))
            p.op("dve", lambda v: v.scalar_tensor_tensor(
                out=hi[:, 0:1], in0=dtmp[:, 0:1], scalar=selm1[:, 0:1], in1=hi[:, 0:1],
                op0=ALU.mult, op1=ALU.add), inc=(("dve", 1),))
            p.op("dve", lambda v: v.tensor_add(dtmp[:, 0:1], lo[:, 0:1], hi[:, 0:1]),
                 inc=(("dve", 1),))
            p.op("dve", lambda v: v.tensor_scalar(
                out=mid[:, 0:1], in0=dtmp[:, 0:1], scalar1=0.5, scalar2=None, op0=ALU.mult),
                inc=(("dve", 1),))
        if "theta" in taps:
            p.op("sp", lambda g: g.dma_start(out=tap["theta"][:, :], in_=lo[:, :]),
                 inc=(("st", 16),), after=(("dve", None),))
        p.barrier()
        cut("bisect")

        maskA = sb("maskA", [128, NQT], F32)
        maskT = sb("maskT", [NQT, 128], F32)
        colcnt = sb("colcnt", [NQT, 1], F32)
        colpf = sb("colpf", [NQT, 1], F32)
        slotT = sb("slotT", [NQT, 128], F32)
        t1s = sb("t1s", [NQT, 128], F32)
        wsS = sb("wsS", [1, 1], F32)
        wsI = sb("wsI", [1, 1], I32)
        wsB = sb("wsB", [NQT, 1], F32)
        slotI = sb("slotI", [128, NQT], I32)
        srelW = sb("srelW", [128, MYT // 128], F32)

        p.op("dve", lambda v: v.tensor_scalar(
            out=maskA[:, :], in0=lgT[:, :], scalar1=lo[:, 0:1], scalar2=None,
            op0=ALU.is_ge), inc=(("dve", 1),))
        p.op("te", lambda t: t.transpose(PS_[0:NQT, 64:64 + 128], maskA[:, :], id_f[:, :]),
             inc=(("te", 1),), after=(("dve", None),))
        p.op("dve", lambda v: v.tensor_copy(maskT[:, :], PS_[0:NQT, 64:64 + 128]),
             inc=(("dve", 1),), after=(("te", None),))
        p.op("dve", lambda v: v.tensor_reduce(
            out=colcnt[:, 0:1], in_=maskT[:, :], axis=AX.X, op=ALU.add), inc=(("dve", 1),))
        p.op("te", lambda t: t.matmul(PS_[0:NQT, 0:1], lt64_s[:, :], colcnt[:, 0:1],
                                      start=True, stop=True), inc=(("te", 1),),
             after=(("dve", None),))
        p.op("dve", lambda v: v.tensor_copy(colpf[:, 0:1], PS_[0:NQT, 0:1]),
             inc=(("dve", 1),), after=(("te", None),))
        p.op("te", lambda t: t.matmul(PS_[0:NQT, 192:192 + 128], maskA[:, :], lt128_s[:, :],
                                      start=True, stop=True), inc=(("te", 1),),
             after=(("dve", None),))
        p.op("dve", lambda v: v.tensor_scalar(
            out=slotT[:, :], in0=PS_[0:NQT, 192:192 + 128], scalar1=colpf[:, 0:1],
            scalar2=None, op0=ALU.add), inc=(("dve", 1),), after=(("te", None),))
        p.op("dve", lambda v: v.tensor_scalar(
            out=t1s[:, :], in0=maskT[:, :], scalar1=-1.0e9, scalar2=1.0e9,
            op0=ALU.mult, op1=ALU.add), inc=(("dve", 1),))
        p.op("dve", lambda v: v.tensor_add(slotT[:, :], slotT[:, :], t1s[:, :]),
             inc=(("dve", 1),))
        p.op("sp", lambda g: g.dma_start(out=cpfD[:, :], in_=colpf[:, :]),
             inc=(("st", 16),), after=(("dve", None),))
        stv = p.c("st")

        def rt_read_cpf(g):
            r = g.value_load(myq_s[0:1, 0:1])
            return g.dma_start(out=wsS[0:1, 0:1], in_=cpfD[bass.ts(r, NQT // 4), :][0:1, :])
        p.op("pool", rt_read_cpf, dma=True, inc=(("lg", 16),), after=(("st", stv),))
        p.op("dve", lambda v: v.tensor_scalar(
            out=wsS[0:1, 0:1], in0=wsS[0:1, 0:1], scalar1=float(CSEL - QW), scalar2=None,
            op0=ALU.min), inc=(("dve", 1),), after=(("lg", None),))
        p.op("dve", lambda v: v.tensor_scalar(
            out=wsS[0:1, 0:1], in0=wsS[0:1, 0:1], scalar1=0.0, scalar2=None,
            op0=ALU.max), inc=(("dve", 1),))
        p.op("dve", lambda v: v.tensor_copy(wsI[0:1, 0:1], wsS[0:1, 0:1]), inc=(("dve", 1),))
        p.op("te", lambda t: t.matmul(PS_[0:NQT, 32:33], onesrow_f[0:1, 0:NQT],
                                      wsS[0:1, 0:1], start=True, stop=True),
             inc=(("te", 1),), after=(("dve", None),))
        p.op("dve", lambda v: v.tensor_copy(wsB[:, 0:1], PS_[0:NQT, 32:33]),
             inc=(("dve", 1),), after=(("te", None),))
        p.op("dve", lambda v: v.tensor_scalar(
            out=t1s[:, :], in0=slotT[:, :], scalar1=wsB[:, 0:1], scalar2=None,
            op0=ALU.subtract), inc=(("dve", 1),), after=(("pool", None),))
        p.op("sp", lambda g: g.dma_start(out=srelD[:, :], in_=t1s[:, :]),
             inc=(("st", 16),), after=(("dve", None),))
        p.op("te", lambda t: t.transpose(PS_[:, 320:320 + NQT], slotT[:, :],
                                         id_f[0:NQT, 0:NQT]), inc=(("te", 1),),
             after=(("dve", None),))
        p.op("dve", lambda v: v.tensor_copy(slotI[:, :], PS_[:, 320:320 + NQT]),
             inc=(("dve", 1),), after=(("te", None),))
        if "slot" in taps:
            p.op("sp", lambda g: g.dma_start(out=tap["slot"][:, :], in_=PS_[:, 320:320 + NQT]),
                 inc=(("st", 16),), after=(("te", None),))
        p.barrier()

        rowbuf = sb("rowbuf", [128, 2, 1032], BF16)
        regh = {}
        p.op("pool", lambda g: regh.setdefault("bc", g.to_reg(CSEL - 1)) and None, dma=True)
        scat_hist = []
        for i in range(NQT):
            bi = i % 2
            par = "lda" if bi == 0 else "ldb"
            spar = "sga" if bi == 0 else "sgb"
            p.op("sp", lambda g, i=i, bi=bi: g.dma_start(
                out=rowbuf[:, bi, :], in_=y1d[128 * i:128 * (i + 1), :]), inc=((par, 16),),
                after=((spar, scat_hist[i - 2]),) if i >= 2 else ())
            ldv = p.c(par)
            p.op("pool", lambda g, i=i, bi=bi: g.indirect_dma_start(
                out=atn[:, :], out_offset=IndirectOffsetOnAxis(slotI[:, i:i + 1], 0),
                in_=rowbuf[:, bi, :], in_offset=None, bounds_check=regh["bc"],
                oob_is_err=False), inc=((spar, 16),), after=((par, ldv),), dma=True)
            scat_hist.append(p.c(spar))
        p.barrier()
        cut("scatter")
        if "attn_in" in taps:
            atnf = sb("atnf", [128, 1032], F32)
            for i in range(CSEL // 128):
                p.op("pool", lambda g, i=i: g.dma_start(
                    out=atnf[:, :], in_=atn[128 * i:128 * (i + 1), :]), inc=(("lg", 16),),
                    after=(("st", None),), dma=True)
                p.op("sp", lambda g, i=i: g.dma_start(
                    out=tap["attn_in"][128 * i:128 * (i + 1), :], in_=atnf[:, :]),
                    inc=(("st", 16),), after=(("lg", None),))
                p.barrier()

        kv_stack = contextlib.ExitStack()
        sbk = lambda name, shape, dt: kv_stack.enter_context(nc.sbuf_tensor(name, shape, dt))
        fwk_s = sbk("fwk_s", [128, NDC, D], BF16)
        fwv_s = sbk("fwv_s", [128, NDC, D], BF16)
        for (dst, src) in ((fwk_s, "fWk"), (fwv_s, "fWv")):
            for dc in range(NDC):
                p.op("pool", lambda g, d=dst, s=src, dc=dc: g.dma_start(
                    out=d[:, dc, :], in_=din[s][128 * dc:128 * (dc + 1), :]),
                    inc=(("lw", 16),), dma=True)

        aT = sbk("aT", [128, NDC, CSEL], BF16)
        for j in range(NDC):
            p.op("sp", lambda g, j=j: g.dma_start(
                out=aT[:, j, :], in_=atn[:, 128 * j:128 * (j + 1)], transpose=True),
                inc=(("ld", 16),))
        p.barrier()
        cut("aT")
        p.wait("act", "lw")
        p.wait("dve", "lw")
        p.wait("te", "lw")

        kvbuf = sbk("kvbuf", [128, 2, FH * 65], BF16)
        k2st_hist = []
        for dc in range(NDC):
            for qt in range(CSEL // 512):
                it = dc * (CSEL // 512) + qt
                reg = PAB[:, 0:512] if qt % 2 == 0 else PAB[:, 512:1024]
                for di in range(NDC):
                    p.op("te", lambda t, dc=dc, qt=qt, di=di, reg=reg: t.matmul(
                        reg, fwk_s[:, di, 128 * dc:128 * (dc + 1)],
                        aT[:, di, 512 * qt:512 * (qt + 1)],
                        start=(di == 0), stop=(di == NDC - 1)),
                        inc=(("te", 1),) if di == NDC - 1 else (),
                        after=(("dve", None), ("act", None)) if di == 0 else ())
                bi = it % 2
                spar = "sta" if bi == 0 else "stb"
                p.op("dve", lambda v, bi=bi, reg=reg: v.tensor_copy(kvbuf[:, bi, 0:512], reg),
                     inc=(("dve", 1),),
                     after=(("te", None),) + (((spar, k2st_hist[it - 2]),) if it >= 2 else ()))
                p.op("sp", lambda g, dc=dc, qt=qt, bi=bi: g.dma_start(
                    out=k2d[dc, :, 512 * qt:512 * (qt + 1)], in_=kvbuf[:, bi, 0:512]),
                    inc=((spar, 16),), after=(("dve", None),))
                k2st_hist.append(p.c(spar))
                p.op("act", lambda a, bi=bi: a.activation(
                    out=sqb[:, 0:512], in_=kvbuf[:, bi, 0:512], func=AF.Square),
                    inc=(("act", 1),), after=(("dve", None),))
                for hh in range(2):
                    h = 2 * dc + hh
                    p.op("te", lambda t, hh=hh: t.matmul(
                        PS_[0:1, 0:512], one_b[0:64, 0:1] if hh == 0 else one_b[64:128, 0:1],
                        sqb[0:64, 0:512] if hh == 0 else sqb[64:128, 0:512],
                        start=True, stop=True), inc=(("te", 1),),
                        after=(("act", None), ("dve", None)))
                    p.op("dve", lambda v, h=h, qt=qt: v.tensor_reduce(
                        out=nrm2[0:1, h, qt:qt + 1], in_=PS_[0:1, 0:512], axis=AX.X,
                        op=ALU.max), inc=(("dve", 1),), after=(("te", None),))
        for h in range(FH):
            p.op("dve", lambda v, h=h: v.tensor_reduce(
                out=kmx2[0:1, h:h + 1], in_=nrm2[0:1, h, :], axis=AX.X, op=ALU.max),
                inc=(("dve", 1),))
        p.barrier()
        p.op("dve", lambda v: v.memset(v2S[:, :, :], 0.0), inc=(("dve", 1),))
        for h in range(FH):
            p.op("dve", lambda v, h=h: v.memset(
                v2S[:, :, 65 * h + 64:65 * h + 65], 1.0), inc=(("dve", 1),))
        for kt in range(NKT):
            for half in range(2):
                reg = PCD[:, 0:512] if half == 0 else PCD[:, 512:1024]
                for di in range(NDC):
                    p.op("te", lambda t, kt=kt, half=half, di=di, reg=reg: t.matmul(
                        reg, aT[:, di, 128 * kt:128 * (kt + 1)],
                        fwv_s[:, di, 512 * half:512 * (half + 1)],
                        start=(di == 0), stop=(di == NDC - 1)),
                        inc=(("te", 1),) if di == NDC - 1 else (),
                        after=(("dve", None),) if di == 0 else ())
                for hh in range(8):
                    h = 8 * half + hh
                    p.op("dve", lambda v, kt=kt, h=h, hh=hh, reg=reg: v.tensor_copy(
                        v2S[:, kt, 65 * h:65 * h + 64], reg[:, 64 * hh:64 * (hh + 1)]),
                        inc=(("dve", 1),),
                        after=((("te", None),) if hh == 0 else ()))
        p.barrier()
        cut("kv")

        kv_stack.close()
        d_stack = contextlib.ExitStack()
        sbd = lambda name, shape, dt: d_stack.enter_context(nc.sbuf_tensor(name, shape, dt))
        q2T = sbd("q2T", [128, NDC, QW], BF16)
        bmix = sbd("bmix", [1, FH], F32)
        bpair = sbd("bpair", [1, NDC], F32)
        bpb = sbd("bpb", [1, NDC], BF16)
        bcols = sbd("bcols", [128, NDC], F32)
        q_stack = contextlib.ExitStack()
        sbq = lambda name, shape, dt: q_stack.enter_context(nc.sbuf_tensor(name, shape, dt))
        fwq_s = sbq("fwq_s", [128, NDC, D], BF16)
        for dc in range(NDC):
            p.op("pool", lambda g, dc=dc: g.dma_start(
                out=fwq_s[:, dc, :], in_=din["fWq"][128 * dc:128 * (dc + 1), :]),
                inc=(("lw", 16),), dma=True)
        myaT = sbq("myaT", [128, NDC, QW], BF16)

        def ldwin(g):
            r = g.value_load(wsI[0:1, 0:1])
            return g.dma_start(out=myrows[:, :], in_=atn[bass.ds(r, QW), :])
        p.op("pool", ldwin, dma=True, inc=(("lg", 16),))
        p.wait("sp", "lg")
        for j in range(NDC):
            p.op("sp", lambda g, j=j: g.dma_start(
                out=myaT[:, j, :], in_=myrows[:, 128 * j:128 * (j + 1)], transpose=True),
                inc=(("ld", 16),))
        p.barrier()

        qt_sizes = [(i * 512, min(512, QW - i * 512)) for i in range((QW + 511) // 512)]
        for dc in range(NDC):
            for (q0, qn) in qt_sizes:
                reg = PAB[:, 0:512]
                for di in range(NDC):
                    p.op("te", lambda t, dc=dc, di=di, q0=q0, qn=qn, reg=reg: t.matmul(
                        reg[:, 0:qn], fwq_s[:, di, 128 * dc:128 * (dc + 1)],
                        myaT[:, di, q0:q0 + qn], start=(di == 0), stop=(di == NDC - 1)),
                        inc=(("te", 1),) if di == NDC - 1 else (),
                        after=(("dve", None), ("act", None)) if di == 0 else ())
                p.op("dve", lambda v, dc=dc, q0=q0, qn=qn, reg=reg: v.tensor_copy(
                    q2T[:, dc, q0:q0 + qn], reg[:, 0:qn]), inc=(("dve", 1),),
                    after=(("te", None),))
                p.op("act", lambda a, qn=qn, reg=reg: a.activation(
                    out=sqb[:, 0:qn], in_=reg[:, 0:qn], func=AF.Square),
                    inc=(("act", 1),), after=(("dve", None),))
                qi = q0 // 512
                for hh in range(2):
                    h = 2 * dc + hh
                    p.op("te", lambda t, hh=hh, qn=qn: t.matmul(
                        PS_[0:1, 0:qn], one_b[0:64, 0:1] if hh == 0 else one_b[64:128, 0:1],
                        sqb[0:64, 0:qn] if hh == 0 else sqb[64:128, 0:qn],
                        start=True, stop=True), inc=(("te", 1),),
                        after=(("act", None), ("dve", None)))
                    p.op("dve", lambda v, h=h, qi=qi, qn=qn: v.tensor_reduce(
                        out=nrm2[0:1, h, qi:qi + 1], in_=PS_[0:1, 0:qn], axis=AX.X,
                        op=ALU.max), inc=(("dve", 1),), after=(("te", None),))
        NQB = len(qt_sizes)
        for h in range(FH):
            p.op("dve", lambda v, h=h: v.tensor_reduce(
                out=qmx2[0:1, h:h + 1], in_=nrm2[0:1, h, 0:NQB], axis=AX.X, op=ALU.max),
                inc=(("dve", 1),))
        p.op("dve", lambda v: v.tensor_mul(bmix[0:1, :], qmx2[0:1, :], kmx2[0:1, :]),
             inc=(("dve", 1),))
        p.op("act", lambda a: a.activation(out=bmix[0:1, :], in_=bmix[0:1, :], func=AF.Sqrt,
                                           scale=1.0 / 64.0), inc=(("act", 1),),
             after=(("dve", None),))
        p.op("dve", lambda v: v.tensor_max(
            bpair[0:1, :], bmix.rearrange("o (dc hh) -> o dc hh", hh=2)[0:1, :, 0],
            bmix.rearrange("o (dc hh) -> o dc hh", hh=2)[0:1, :, 1]),
            inc=(("dve", 1),), after=(("act", None),))
        p.op("dve", lambda v: v.tensor_scalar(
            out=bpair[0:1, :], in0=bpair[0:1, :], scalar1=-1.0, scalar2=None, op0=ALU.mult),
            inc=(("dve", 1),))
        p.op("dve", lambda v: v.tensor_copy(bpb[0:1, :], bpair[0:1, :]), inc=(("dve", 1),))
        p.op("te", lambda t: t.matmul(PS_[:, 0:NDC], onerow_b[0:1, :], bpb[0:1, :],
                                      start=True, stop=True), inc=(("te", 1),),
             after=(("dve", None),))
        p.op("dve", lambda v: v.tensor_copy(bcols[:, :], PS_[:, 0:NDC]), inc=(("dve", 1),),
             after=(("te", None),))
        p.barrier()
        cut("q2")

        q_stack.close()
        o2hT = sbd("o2hT", [128, NDC, QW], BF16)
        sc_stack = contextlib.ExitStack()
        sbs = lambda name, shape, dt: sc_stack.enter_context(nc.sbuf_tensor(name, shape, dt))
        eS = sbs("eS", [128, 2, 1024], BF16)
        k2blk = sbs("k2blk", [128, 2, 128], BF16)
        drow = sbs("drow", [128, 2, 512], F32)
        dr2 = sbs("dr2", [2, 512], BF16)
        rcpbE = sbs("rcpbE", [128, 512], F32)
        rcpbO = sbs("rcpbO", [128, 512], F32)
        oddst = sbs("oddst", [128, 512], BF16)
        schist = {}
        expv = {}
        for hp in range(NDC):
            for (q0, qn) in qt_sizes:
                for kt in range(NKT + 1):
                    if kt < NKT:
                        bi = kt % 2
                        par = "lda" if bi == 0 else "ldb"
                        guard = (("te", schist[kt - 2]),) if kt >= 2 else ()
                        p.op("sp", lambda g, hp=hp, kt=kt, bi=bi: g.dma_start(
                            out=k2blk[:, bi, :], in_=k2d[hp, :, 128 * kt:128 * (kt + 1)]),
                            inc=((par, 16),), after=guard)
                        ldk = p.c(par)
                        pS = PAB if kt % 2 == 0 else PCD
                        sguard = ((("act", expv[kt - 2]),) if kt >= 2 else
                                  ((("act", None),) if (hp, q0) != (0, 0) else ()))
                        p.op("te", lambda t, bi=bi, hp=hp, q0=q0, qn=qn, pS=pS, par=par: t.matmul(
                            pS[:, 0:qn], k2blk[0:64, bi, :], q2T[0:64, hp, q0:q0 + qn],
                            start=True, stop=True, tile_position=(0, 0)),
                            inc=(("te", 1),), after=((par, ldk),) + sguard)
                        p.op("te", lambda t, bi=bi, hp=hp, q0=q0, qn=qn, pS=pS: t.matmul(
                            pS[:, 512:512 + qn], k2blk[64:128, bi, :],
                            q2T[64:128, hp, q0:q0 + qn],
                            start=True, stop=True, tile_position=(64, 0)), inc=(("te", 1),))
                        schist[kt] = p.c("te")
                        p.op("act", lambda a, hp=hp, qn=qn, pS=pS, bi=bi: a.activation(
                            out=eS[:, bi, 0:2 * qn],
                            in_=pS[:, 0:1024].rearrange("pp (a b) -> pp a b", a=2)[:, :, 0:qn],
                            func=AF.Exp, scale=0.125, bias=bcols[:, hp:hp + 1]),
                            inc=(("act", 1),), after=(("te", schist[kt]),))
                        expv[kt] = p.c("act")
                    if kt >= 1:
                        ka = kt - 1
                        for par in range(2):
                            av = PE_ if par == 0 else PF_
                            p.op("te", lambda t, hp=hp, ka=ka, par=par, av=av, qn=qn: t.matmul(
                                av[0:65, 0:qn],
                                v2S[:, ka, 65 * (2 * hp + par):65 * (2 * hp + par) + 65],
                                eS[:, ka % 2, qn * par:qn * (par + 1)],
                                start=(ka == 0), stop=(ka == NKT - 1)),
                                inc=(("te", 1),),
                                after=(("act", expv[ka]),) + ((("dve", None),) if ka == 0 else ()))
                tev = p.c("te")
                for par in range(2):
                    av = PE_ if par == 0 else PF_
                    p.op("dve", lambda v, par=par, av=av, qn=qn: v.tensor_copy(
                        drow[64:65, par, 0:qn], av[64:65, 0:qn]), inc=(("dve", 1),),
                        after=(("te", tev), ("lg", None)))
                    p.op("dve", lambda v, par=par, qn=qn: v.reciprocal(
                        drow[64:65, par, 0:qn], drow[64:65, par, 0:qn]), inc=(("dve", 1),))
                    p.op("pool", lambda g, par=par, qn=qn: g.dma_start(
                        out=dr2[par:par + 1, 0:qn], in_=drow[64:65, par, 0:qn]),
                        inc=(("lg", 16),), after=(("dve", None), ("te", None)), dma=True)
                ldd = p.c("lg")
                p.op("te", lambda t, qn=qn: t.matmul(
                    PS_[:, 0:qn], sel2_s[:, 0:128], dr2[:, 0:qn], start=True, stop=True),
                    inc=(("te", 1),), after=(("lg", ldd), ("dve", None)))
                p.op("dve", lambda v, qn=qn: v.tensor_copy(rcpbE[:, 0:qn], PS_[:, 0:qn]),
                     inc=(("dve", 1),), after=(("te", None),))
                p.op("te", lambda t, qn=qn: t.matmul(
                    PS_[:, 0:qn], sel2_s[:, 128:256], dr2[:, 0:qn], start=True, stop=True),
                    inc=(("te", 1),), after=(("dve", None),))
                p.op("dve", lambda v, qn=qn: v.tensor_copy(rcpbO[:, 0:qn], PS_[:, 0:qn]),
                     inc=(("dve", 1),), after=(("te", None),))
                p.op("dve", lambda v, hp=hp, q0=q0, qn=qn: v.tensor_mul(
                    o2hT[0:64, hp, q0:q0 + qn], PE_[0:64, 0:qn], rcpbE[0:64, 0:qn]),
                    inc=(("dve", 1),))
                p.op("dve", lambda v, qn=qn: v.tensor_mul(
                    oddst[0:64, 0:qn], PF_[0:64, 0:qn], rcpbO[0:64, 0:qn]),
                    inc=(("dve", 1),), after=(("st", None),))
                p.op("sp", lambda g, hp=hp, q0=q0, qn=qn: g.dma_start(
                    out=o2hT[64:128, hp, q0:q0 + qn], in_=oddst[0:64, 0:qn]),
                    inc=(("st", 16),), after=(("dve", None),))
                cut(("sc", hp, q0))
        cut("scores")
        if "o2" in taps:
            o2f = sbs("o2f", [128, QW], F32)
            for dc in range(NDC):
                p.op("dve", lambda v, dc=dc: v.tensor_copy(o2f[:, :], o2hT[:, dc, :]),
                     inc=(("dve", 1),))
                p.op("sp", lambda g, dc=dc: g.dma_start(
                    out=tap["o2"][128 * dc:128 * (dc + 1), :], in_=o2f[:, :]),
                    inc=(("st", 16),), after=(("dve", None),))
                p.barrier()

        sc_stack.close()
        wo_stack = contextlib.ExitStack()
        sbw = lambda name, shape, dt: wo_stack.enter_context(nc.sbuf_tensor(name, shape, dt))
        fwo_s = sbw("fwo_s", [128, NDC, D], BF16)
        for dc in range(NDC):
            p.op("pool", lambda g, dc=dc: g.dma_start(
                out=fwo_s[:, dc, :], in_=din["fWo"][128 * dc:128 * (dc + 1), :]),
                inc=(("lw", 16),), dma=True,
                after=(("lg", None), ("lda", None), ("ldb", None), ("st", None)))
        p.wait("te", "lw")
        ffnT = sbw("ffnT", [128, NDC, QW], BF16)
        fsl = sbw("fsl", [128, 2, D], BF16)
        for dc in range(NDC):
            for (q0, qn) in qt_sizes:
                reg = PAB[:, 0:512]
                for di in range(NDC):
                    p.op("te", lambda t, dc=dc, di=di, q0=q0, qn=qn, reg=reg: t.matmul(
                        reg[:, 0:qn], fwo_s[:, di, 128 * dc:128 * (dc + 1)],
                        o2hT[:, di, q0:q0 + qn], start=(di == 0), stop=(di == NDC - 1)),
                        inc=(("te", 1),) if di == NDC - 1 else (),
                        after=(("dve", None),) if di == 0 else ())
                p.op("dve", lambda v, dc=dc, q0=q0, qn=qn, reg=reg: v.tensor_copy(
                    ffnT[:, dc, q0:q0 + qn], reg[:, 0:qn]), inc=(("dve", 1),),
                    after=(("te", None),))
        p.barrier()
        fst_hist = []
        for tb in range(QW // 128):
            spar = "sta" if tb % 2 == 0 else "stb"
            for dc in range(NDC):
                p.op("te", lambda t, tb=tb, dc=dc: t.transpose(
                    PT_[:, 128 * dc:128 * (dc + 1)], ffnT[:, dc, 128 * tb:128 * (tb + 1)],
                    id_b[:, :]), inc=(("te", 1),),
                    after=(("dve", None),))
                guard = (("sta" if tb % 2 == 0 else "stb", fst_hist[tb - 2]),) if (tb >= 2 and dc == 0) else ()
                p.op("dve", lambda v, tb=tb, dc=dc: v.tensor_copy(
                    fsl[:, tb % 2, 128 * dc:128 * (dc + 1)], PT_[:, 128 * dc:128 * (dc + 1)]),
                    inc=(("dve", 1),), after=(("te", None),) + guard)
            p.op("sp", lambda g, tb=tb: g.dma_start(
                out=o2d[128 * tb:128 * (tb + 1), :], in_=fsl[:, tb % 2, :]),
                inc=((spar, 16),), after=(("dve", None),))
            fst_hist.append(p.c(spar))
        p.barrier()
        p.barrier()
        cut("wo")

        wo_stack.close()
        d_stack.close()
        fin_stack = contextlib.ExitStack()
        sbf = lambda name, shape, dt: fin_stack.enter_context(nc.sbuf_tensor(name, shape, dt))
        y1t = sbf("y1t", [128, MYT // 128, D], F32)
        gbuf = sbf("gbuf", [128, D], BF16)
        idxc = sbf("idxc", [128, 2], I32)
        srl16 = sbf("srl16", [MYT // 128, 128], F32)
        fcol = sbf("fcol", [128, 1], F32)
        ytmp = sbf("ytmp", [128, D], F32)
        yout = sbf("yout", [128, D], F32)
        NJT = MYT // 128

        def rt_load_srel(g):
            r = g.value_load(myq_s[0:1, 0:1])
            return g.dma_start(out=srl16[:, :], in_=srelD[bass.ts(r, NJT), :])
        p.op("pool", rt_load_srel, dma=True, inc=(("lg", 16),), after=(("st", None),))

        def rt_load_y1(g):
            r = g.value_load(myq_s[0:1, 0:1])
            return g.dma_start(
                out=y1t[:, :, :],
                in_=y1f32d[bass.ds(r * MYT, MYT), :].rearrange("(j pp) d -> pp j d", pp=128))
        p.op("pool", rt_load_y1, dma=True, inc=(("lg", 16),), after=(("st", None),))
        p.op("te", lambda t: t.transpose(PS_[:, 64:64 + NJT], srl16[:, :],
                                         id_f[0:NJT, 0:NJT]), inc=(("te", 1),),
             after=(("lg", None),))
        p.op("dve", lambda v: v.tensor_copy(srelW[:, :], PS_[:, 64:64 + NJT]),
             inc=(("dve", 1),), after=(("te", None),))
        if "dbg" in taps:
            p.op("dve", lambda v: v.tensor_copy(junkN[:, 0:MYT // 128], srelW[:, :]),
                 inc=(("dve", 1),))
            p.op("sp", lambda g: g.dma_start(out=tap["dbg"][:, 0:MYT // 128],
                                             in_=junkN[:, 0:MYT // 128]),
                 inc=(("st", 16),), after=(("dve", None),))
        p.barrier()

        for j in range(NJT):
            ldyv = p.c("lg")
            gguard = (("sg", 16 * (j - 1)),) if j >= 2 else ()
            p.op("dve", lambda v, j=j: v.tensor_copy(idxc[:, j % 2:j % 2 + 1], srelW[:, j:j + 1]),
                 inc=(("dve", 1),), after=gguard)
            p.op("dve", lambda v: v.memset(gbuf[:, :], 0.0), inc=(("dve", 1),))
            dvg = p.c("dve")
            if j == 0:
                p.op("pool", lambda g: regh.__setitem__("bc2", g.to_reg(QW - 1)), dma=True)
            p.op("pool", lambda g, j=j: g.indirect_dma_start(
                out=gbuf[:, :], out_offset=None, in_=o2d[:, :],
                in_offset=IndirectOffsetOnAxis(idxc[:, j % 2:j % 2 + 1], 0),
                bounds_check=regh["bc2"],
                oob_is_err=False), inc=(("sg", 16),), after=(("dve", dvg),), dma=True)
            sgv2 = p.c("sg")
            if j == 0 and "dbg" in taps:
                p.op("dve", lambda v: v.tensor_copy(junk[:, 0:1], idxc[:, 0:1]),
                     inc=(("dve", 1),))
                p.op("dve", lambda v: v.tensor_copy(junk[:, 1:41], gbuf[:, 0:40]),
                     inc=(("dve", 1),), after=(("sg", sgv2),))
                p.op("sp", lambda g: g.dma_start(out=tap["dbg"][:, 0:41], in_=junk[:, 0:41]),
                     inc=(("st", 16),), after=(("dve", None),))
            p.op("dve", lambda v, j=j: v.tensor_mul(junk[:, :], y1t[:, j, :], wgbc[:, :]),
                 inc=(("dve", 1),),
                 after=(("lg", ldyv),) + ((("st", None),) if (j == 0 and "dbg" in taps) else ()))
            p.op("dve", lambda v: v.tensor_reduce(
                out=colA[:, 0:1], in_=junk[:, :], axis=AX.X, op=ALU.add),
                inc=(("dve", 1),))
            p.op("act", lambda a: a.activation(out=fcol[:, 0:1], in_=colA[:, 0:1],
                                               func=AF.Sigmoid), inc=(("act", 1),),
                 after=(("dve", None),))
            p.op("dve", lambda v, j=j: v.scalar_tensor_tensor(
                out=ytmp[:, :], in0=gbuf[:, :], scalar=fcol[:, 0:1], in1=y1t[:, j, :],
                op0=ALU.mult, op1=ALU.add), inc=(("dve", 1),),
                after=(("sg", sgv2), ("act", None)))
            p.op("dve", lambda v: v.tensor_reduce(
                out=colA[:, 0:1], in_=ytmp[:, :], axis=AX.X, op=ALU.add), inc=(("dve", 1),))
            p.op("act", lambda a: a.activation(
                out=junk[:, :], in_=ytmp[:, :], func=AF.Square, accum_out=colB[:, 0:1]),
                inc=(("act", 1),), after=(("dve", None),))
            p.op("dve", lambda v: v.tensor_scalar(
                out=colA[:, 0:1], in0=colA[:, 0:1], scalar1=1.0 / D, scalar2=None,
                op0=ALU.mult), inc=(("dve", 1),), after=(("act", None),))
            p.op("dve", lambda v: v.tensor_mul(colC[:, 0:1], colA[:, 0:1], colA[:, 0:1]),
                 inc=(("dve", 1),))
            p.op("dve", lambda v: v.scalar_tensor_tensor(
                out=colB[:, 0:1], in0=colB[:, 0:1], scalar=1.0 / D, in1=colC[:, 0:1],
                op0=ALU.mult, op1=ALU.subtract), inc=(("dve", 1),))
            p.op("act", lambda a: a.activation(
                out=colB[:, 0:1], in_=colB[:, 0:1], func=AF.Sqrt, bias=eps_c[:, 0:1]),
                inc=(("act", 1),), after=(("dve", None),))
            p.op("dve", lambda v: v.reciprocal(colB[:, 0:1], colB[:, 0:1]),
                 inc=(("dve", 1),), after=(("act", None),))
            p.op("dve", lambda v: v.scalar_tensor_tensor(
                out=colC[:, 0:1], in0=colA[:, 0:1], scalar=-1.0, in1=colB[:, 0:1],
                op0=ALU.mult, op1=ALU.mult), inc=(("dve", 1),))
            if j == 0 and "dbg" in taps:
                p.op("dve", lambda v: v.tensor_copy(junkN[:, 0:4], ytmp[:, 0:4]),
                     inc=(("dve", 1),))
                p.op("dve", lambda v: v.tensor_copy(junkN[:, 4:5], fcol[:, 0:1]),
                     inc=(("dve", 1),))
                p.op("dve", lambda v: v.tensor_copy(junkN[:, 5:6], colB[:, 0:1]),
                     inc=(("dve", 1),))
                p.op("dve", lambda v: v.tensor_copy(junkN[:, 6:7], colC[:, 0:1]),
                     inc=(("dve", 1),))
            opar = "outa" if j % 2 == 0 else "outb"
            prevpar = "outb" if j % 2 == 0 else "outa"
            oguard = ((prevpar, p.c(prevpar)),) if j >= 1 else ()
            p.op("act", lambda a, j=j: a.activation(
                out=yout[:, :], in_=ytmp[:, :], func=AF.Identity, scale=colB[:, 0:1],
                bias=colC[:, 0:1]), inc=(("act", 1),), after=(("dve", None),) + oguard)
            p.op("sp", lambda g, j=j: g.dma_start(
                out=out_ext[128 * j:128 * (j + 1), :], in_=yout[:, :]),
                inc=((opar, 16),), after=(("act", None),))
        if "dbg" in taps:
            p.barrier()
            p.op("sp", lambda g: g.dma_start(out=tap["dbg"][:, 0:8], in_=junkN[:, 0:8]),
                 inc=(("st", 16),), after=(("dve", None),))
        for e in ENGS:
            p.wait(e, "outa")
            p.wait(e, "outb")
            p.wait(e, "st")
            p.wait(e, "sta")
            p.wait(e, "stb")
            p.wait(e, "sg")

    except _Cut:
        p.barrier()
    try:
        fin_stack.close()
    except Exception:
        pass
    with nc.Block() as block:
        @block.tensor
        def _(te):
            for f in p.q["te"]:
                f(te)

        @block.scalar
        def _(a):
            for f in p.q["act"]:
                f(a)

        @block.vector
        def _(v):
            for f in p.q["dve"]:
                f(v)

        @block.gpsimd
        def _(g):
            for f in p.q["pool"]:
                f(g)

        @block.sync
        def _(s):
            for f in p.q["sp"]:
                f(s)

    try:
        stack.close()
    except Exception:
        pass
    return nc, tap


def _host_inputs(inp):
    x = np.ascontiguousarray(np.asarray(inp["L0_attn_hidden_states"], np.float32))
    mk = {}
    mk["wqT"] = np.ascontiguousarray(np.asarray(inp["L0_attn_w_q"], np.float32).T)
    mk["wkT"] = np.ascontiguousarray(np.asarray(inp["L0_attn_w_k"], np.float32).T)
    mk["wvT"] = np.ascontiguousarray(np.asarray(inp["L0_attn_w_v"], np.float32).T)
    mk["wfT"] = np.ascontiguousarray(np.asarray(inp["L0_attn_w_f"], np.float32).T)
    mk["woT"] = np.ascontiguousarray(np.asarray(inp["L0_attn_w_o"], np.float32).T)
    mk["fWq"] = np.ascontiguousarray(np.asarray(inp["L0_ffn_Wq"], np.float32))
    mk["fWk"] = np.ascontiguousarray(np.asarray(inp["L0_ffn_Wk"], np.float32))
    mk["fWv"] = np.ascontiguousarray(np.asarray(inp["L0_ffn_Wv"], np.float32))
    mk["fWo"] = np.ascontiguousarray(np.asarray(inp["L0_ffn_Wo"], np.float32))
    mk["wg"] = np.ascontiguousarray(np.asarray(inp["L0_ffn_Wg"], np.float32).reshape(1, D))
    mk["nw"] = np.ascontiguousarray(
        np.asarray(inp["L0_attn_norm_weight"], np.float32).reshape(NDC, 128).T)
    mk["gn"] = np.ascontiguousarray(
        np.asarray(inp["L0_attn_gnorm_weight"], np.float32).reshape(NDC, 128).T)
    tpi = np.arange(L)[:, None]
    ti = np.arange(L)[None, :]
    mk["mcum"] = np.ascontiguousarray(
        (tpi <= ti).astype(np.float32) - (tpi <= L // 2 - 1).astype(np.float32))
    mk["trim"] = np.ascontiguousarray((tpi <= ti).astype(np.float32))
    mk["ident"] = np.eye(128, dtype=np.float32)
    c64 = np.arange(T // 128)
    mk["lt64"] = np.ascontiguousarray((c64[:, None] < c64[None, :]).astype(np.float32))
    c128 = np.arange(128)
    mk["lt128"] = np.ascontiguousarray((c128[:, None] < c128[None, :]).astype(np.float32))
    s2 = np.zeros((2, 256), np.float32)
    s2[0, 0:64] = 1.0
    s2[1, 128:192] = 1.0
    mk["sel2"] = s2
    in_maps = []
    for core in range(8):
        b, q = core // 4, core % 4
        m = dict(mk)
        m["x_b"] = np.ascontiguousarray(x[b])
        m["myq16"] = np.array([[q]], np.int32)
        in_maps.append(m)
    return in_maps


def kernel(**inputs):
    nc, _ = build_program(TAPS)
    in_maps = _host_inputs(inputs)
    res = run_bass_kernel_spmd(nc, in_maps, core_ids=list(range(8)))
    parts = [res.results[c]["out_mine"] for c in range(8)]
    out = np.concatenate(parts, axis=0).reshape(B, T, D)
    return np.ascontiguousarray(out.astype(np.float32))


# version 11
# speedup vs baseline: 1.0140x; 1.0140x over previous
import contextlib
import numpy as np
import concourse.bass as bass
import concourse.mybir as mybir
from concourse.bass_utils import run_bass_kernel_spmd
from concourse.bass_core_rust import IndirectOffsetOnAxis

F32 = mybir.dt.float32
BF16 = mybir.dt.bfloat16
I32 = mybir.dt.int32
AF = mybir.ActivationFunctionType
ALU = mybir.AluOpType
AX = mybir.AxisListType

B, D = 2, 1024
T = 8192
H, DK, DV, M = 4, 256, 256, 256
L = 256
NDC = D // 128
FH, DHD = 16, 64
EPS = 1e-5
GN = 16.0
BIS_ITERS = 48
TAPS = ()
STOP_AFTER = None
PARANOID = False
ENGS = ("te", "act", "dve", "pool", "sp")


class _Cut(Exception):
    pass


class PB:
    def __init__(self, nc, stack):
        self.nc = nc
        self.stack = stack
        self.q = {e: [] for e in ENGS}
        self.sems = {}
        self.cnt = {}
        self.hw = {e: {} for e in ENGS}

    def sem(self, name):
        if name not in self.sems:
            self.sems[name] = self.stack.enter_context(self.nc.semaphore(name))
            self.cnt[name] = 0
        return self.sems[name]

    def op(self, eng, fn, inc=(), after=(), dma=False):
        if eng in ("act", "dve", "pool") and not dma:
            after = tuple(after) + ((eng, None),)
            if not any(s == eng for s, _ in inc):
                inc = tuple(inc) + ((eng, 1),)
        hw = self.hw[eng]
        waits = []
        for s, v in after:
            vv = self.cnt[s] if v is None else v
            if vv <= 0 or vv <= hw.get(s, 0):
                continue
            hw[s] = vv
            waits.append((self.sems[s], vv))
        waits = tuple(waits)
        incs = tuple((self.sems[s], n) for s, n in inc)

        def play(h, waits=waits, incs=incs, fn=fn):
            for sh, v in waits:
                h.wait_ge(sh, v)
            ins = fn(h)
            for sh, n in incs:
                ins.then_inc(sh, n)

        self.q[eng].append(play)
        tok = []
        for s, n in inc:
            self.cnt[s] += n
            tok.append((s, self.cnt[s]))
        if PARANOID and eng != "_":
            self.barrier()
        return tuple(tok)

    def wait(self, eng, name, val=None):
        v = self.cnt[name] if val is None else val
        if v > 0 and v > self.hw[eng].get(name, 0):
            self.hw[eng][name] = v
            self.q[eng].append(lambda h, sh=self.sems[name], v=v: h.wait_ge(sh, v))

    def barrier(self):
        for e in ENGS:
            for s in self.sems:
                self.wait(e, s)

    def c(self, name):
        return self.cnt[name]


def build_program(taps=()):
    def cut(name):
        if STOP_AFTER == name:
            raise _Cut()
    NCH = T // L
    CSEL = T // 2
    QW = T // 8 + 128
    MYT = T // 4
    NKT = CSEL // 128

    nc = bass.Bass("TRN2", target_bir_lowering=False, debug=False)
    stack = contextlib.ExitStack()
    p = PB(nc, stack)
    for s in ("ld", "lda", "ldb", "lg", "lw", "te", "act", "dve", "pool", "st", "sg", "sga", "sgb", "sta", "stb", "out", "outa", "outb"):
        p.sem(s)

    din = {}
    for nm, shp, dt in (
        ("x_b", [T, D], F32), ("wqT", [D, D], F32), ("wkT", [D, D], F32),
        ("wvT", [D, D], F32), ("wfT", [D, D], F32), ("woT", [D, D], F32),
        ("fWq", [D, D], F32), ("fWk", [D, D], F32), ("fWv", [D, D], F32),
        ("fWo", [D, D], F32), ("wg", [1, D], F32), ("nw", [128, NDC], F32),
        ("gn", [128, NDC], F32), ("mcum", [L, L], F32), ("trim", [L, L], F32),
        ("ident", [128, 128], F32), ("lt64", [T // 128, T // 128], F32),
        ("lt128", [128, 128], F32), ("sel2", [2, 256], F32),
        ("myq16", [1, 1], I32),
    ):
        din[nm] = nc.dram_tensor(nm, shp, dt, kind="ExternalInput").ap()
    out_ext = nc.dram_tensor("out_mine", [MYT, D], F32, kind="ExternalOutput").ap()

    tap = {}
    for nm, shp in (("qT", [D, T]), ("kvf", [3, T, D]), ("o", [T, D]),
                    ("y1", [T, D]), ("logits", [128, T // 128]),
                    ("theta", [128, 1]), ("slot", [128, T // 128]),
                    ("attn_in", [CSEL, 1032]), ("o2", [D, QW]), ("dbg", [128, 64])):
        if nm in taps:
            tap[nm] = nc.dram_tensor("tap_" + nm, shp, F32, kind="ExternalOutput").ap()

    y1d = nc.dram_tensor("y1d", [T, 1032], BF16).ap()
    y1f32d = nc.dram_tensor("y1f32d", [T, D], F32).ap()
    atn = nc.dram_tensor("atn", [CSEL, 1032], BF16).ap()
    k2d = nc.dram_tensor("k2d", [NDC, 128, CSEL], BF16).ap()
    v2d = nc.dram_tensor("v2d", [CSEL, FH * 65], BF16).ap()
    o2d = nc.dram_tensor("o2d", [QW, D], BF16).ap()
    myrows = nc.dram_tensor("myrows", [QW, 1032], BF16).ap()
    srelD = nc.dram_tensor("srelD", [T // 128, 128], F32).ap()
    cpfD = nc.dram_tensor("cpfD", [T // 128, 1], F32).ap()

    sb = lambda name, shape, dt: stack.enter_context(nc.sbuf_tensor(name, shape, dt))

    try:

        id_b = sb("id_b", [128, 128], BF16)
        eps_c = sb("eps_c", [128, 1], F32)
        one_f = sb("one_f", [128, 1], F32)
        one_b = sb("one_b", [128, 1], BF16)
        onerow_b = sb("onerow_b", [1, 128], BF16)
        onesrow_f = sb("onesrow_f", [1, 128], F32)
        wgbc = sb("wgbc", [128, D], F32)
        myq_s = sb("myq_s", [1, 1], I32)
        junk = sb("junk", [128, D], BF16)
        lgT = sb("lgT", [128, T // 128], F32)
        colA = sb("colA", [128, 1], F32)
        colB = sb("colB", [128, 1], F32)
        colC = sb("colC", [128, 1], F32)
        ab_stack = contextlib.ExitStack()
        sba = lambda name, shape, dt: ab_stack.enter_context(nc.sbuf_tensor(name, shape, dt))
        w_q = sba("w_q", [128, NDC, D], BF16)
        w_k = sba("w_k", [128, NDC, D], BF16)
        w_v = sba("w_v", [128, NDC, D], BF16)
        w_f = sba("w_f", [128, NDC, D], BF16)
        w_o = sba("w_o", [128, NDC, D], BF16)
        nw_t = sba("nw_t", [128, NDC], F32)
        gn_t = sba("gn_t", [128, NDC], F32)
        mc_s = sba("mc_s", [128, 2, L], BF16)
        tri_s = sba("tri_s", [128, 2, L], F32)

        for (dst, src) in (
            (w_q, "wqT"), (w_k, "wkT"), (w_v, "wvT"), (w_f, "wfT"), (w_o, "woT")):
            for dc in range(NDC):
                p.op("pool", lambda g, d=dst, s=src, dc=dc: g.dma_start(
                    out=d[:, dc, :], in_=din[s][128 * dc:128 * (dc + 1), :]),
                    inc=(("lg", 16),), dma=True)
        p.op("pool", lambda g: g.dma_start(
            out=mc_s[:, :, :], in_=din["mcum"].rearrange("(tt pp) t -> pp tt t", pp=128)),
            inc=(("lg", 16),), dma=True)
        p.op("pool", lambda g: g.dma_start(out=id_b[:, :], in_=din["ident"][:, :]),
             inc=(("lg", 16),), dma=True)
        for nm, dst in (("nw", nw_t), ("gn", gn_t), ("myq16", myq_s)):
            p.op("sp", lambda g, nm=nm, dst=dst: g.dma_start(out=dst[:, :], in_=din[nm][:, :]),
                 inc=(("ld", 16),))
        p.op("sp", lambda g: g.dma_start(
            out=tri_s[:, :, :], in_=din["trim"].rearrange("(tt pp) t -> pp tt t", pp=128)),
            inc=(("ld", 16),))
        for f_, v_ in ((eps_c, EPS), (one_f, 1.0), (one_b, 1.0), (onerow_b, 1.0),
                       (onesrow_f, 1.0)):
            p.op("dve", lambda v, f_=f_, v_=v_: v.memset(f_[:, :], v_), inc=(("dve", 1),))
        p.barrier()
        for dc in range(NDC):
            p.op("dve", lambda v, dc=dc: v.tensor_scalar(
                out=w_o[:, dc, :], in0=w_o[:, dc, :], scalar1=gn_t[:, dc:dc + 1],
                scalar2=None, op0=ALU.mult), inc=(("dve", 1),))
        for wt in (w_q, w_k, w_v, w_f):
            for dc in range(NDC):
                p.op("dve", lambda v, wt=wt, dc=dc: v.tensor_scalar(
                    out=wt[:, dc, :], in0=wt[:, dc, :], scalar1=nw_t[:, dc:dc + 1],
                    scalar2=None, op0=ALU.mult), inc=(("dve", 1),))
        wg_stack = contextlib.ExitStack()
        wgrow = wg_stack.enter_context(nc.sbuf_tensor("wgrow", [1, D], F32))
        wgP = wg_stack.enter_context(nc.psum_tensor("wgP", [128, 512], F32))
        p.op("sp", lambda g: g.dma_start(out=wgrow[:, :], in_=din["wg"][:, :]),
             inc=(("ld", 16),))
        p.wait("te", "ld")
        for half in range(D // 512):
            p.op("te", lambda t, half=half: t.matmul(
                wgP[:, 0:512], onesrow_f[:, :], wgrow[:, 512 * half:512 * (half + 1)],
                start=True, stop=True), inc=(("te", 1),),
                after=(("dve", None), ("ld", None), ("lg", None)))
            p.op("dve", lambda v, half=half: v.tensor_copy(
                wgbc[:, 512 * half:512 * (half + 1)], wgP[:, 0:512]),
                inc=(("dve", 1),), after=(("te", None),))
        wg_stack.close()
        p.barrier()

        x_s = sba("x_s", [128, 3, 2, D], F32)
        hT_s = sba("hT_s", [128, NDC, L], BF16)
        qTw = sba("qTw", [128, NDC, 2, L], BF16)
        kTMw = sba("kTMw", [128, 2, 2, D], BF16)
        vTMw = sba("vTMw", [128, 2, 2, D], BF16)
        fTMw = sba("fTMw", [128, 2, 2, D], BF16)
        fabs_s = sba("fabs_s", [128, D], F32)
        stat = sba("stat", [128, 24], F32)
        LamB = sba("LamB", [128, 2, L], F32)
        EnG = sba("EnG", [128, 2, L], F32)
        Sp = sba("Sp", [128, 2, 2, L], BF16)
        SpMT = sba("SpMT", [128, 2, 2, L], BF16)
        kT_s = sba("kT_s", [128, 2, L], BF16)
        AmT = sba("AmT", [128, 2, L], BF16)
        WmT = sba("WmT", [128, 2, L], BF16)
        zS = sba("zS", [128, 2, L], F32)
        pEx = sba("pEx", [128, 2, L], BF16)
        PpTM = sba("PpTM", [128, 2, 2, L], BF16)
        PpMT = sba("PpMT", [128, 2, 2, L], BF16)
        nmx = sba("nmx", [128, 2], F32)
        den = sba("den", [128, 2], F32)
        rden = sba("rden", [128, 2, 2], F32)
        lsC = sba("lsC", [128, 2], F32)
        lsR = sba("lsR", [1, L], BF16)
        tmpS = sba("tmpS", [128, 2, M], F32)
        Hm = sba("Hm", [128, H, 2, M], BF16)
        Hv = sba("Hv", [128, H, 2, DV], BF16)
        oB = sba("oB", [128, 2, D], F32)
        onb = sba("onb", [128, 2, D], BF16)
        xnb = onb
        onT = sba("onT", [128, NDC, L], BF16)
        aoT = sba("aoT", [128, NDC, L], BF16)
        y1b = sba("y1b", [128, 2, 1032], BF16)

        PSCE = ab_stack.enter_context(nc.psum_tensor("PSCE", [128, 512], F32))
        PSCO = ab_stack.enter_context(nc.psum_tensor("PSCO", [128, 512], F32))
        PAWA = ab_stack.enter_context(nc.psum_tensor("PAWA", [128, 512], F32))
        PAWB = ab_stack.enter_context(nc.psum_tensor("PAWB", [128, 512], F32))
        PKV0 = ab_stack.enter_context(nc.psum_tensor("PKV0", [128, 512], F32))
        PKV1 = ab_stack.enter_context(nc.psum_tensor("PKV1", [128, 512], F32))
        PQW = ab_stack.enter_context(nc.psum_tensor("PQW", [128, 512], F32))
        PTX = ab_stack.enter_context(nc.psum_tensor("PTX", [128, 1024], BF16))

        p.op("pool", lambda g: g.memset(Hm[:, :, :, :], 0.0), inc=(("pool", 1),))
        p.op("pool", lambda g: g.memset(Hv[:, :, :, :], 0.0), inc=(("pool", 1),))
        p.op("pool", lambda g: g.memset(y1b[:, :, :], 0.0), inc=(("pool", 1),))
        p.barrier()
        cut("init")

        DMASEMS = frozenset(("ld", "lda", "ldb", "lg", "lw", "st", "sta", "stb",
                             "sg", "sga", "sgb", "out", "outa", "outb"))

        def TK(*toks):
            d = {}
            for t in toks:
                for s, v in (t or ()):
                    if v and (s not in d or v > d[s]):
                        d[s] = v
            return tuple(d.items())

        def _promote(tk):
            return tuple((s, None if s in DMASEMS else v) for s, v in tk)

        RW = {}

        def rw(nm):
            e = RW.get(nm)
            if e is None:
                e = RW[nm] = [(), ()]
            return e

        def mark_w(nm, tok):
            e = rw(nm)
            e[0] = tok
            e[1] = ()

        def mark_r(nm, tok):
            e = rw(nm)
            e[1] = TK(e[1], tok)

        PSUMKEYS = frozenset(("scE", "scO", "awA", "awB", "kvA", "kvB", "qw", "pt"))

        def xop(eng, fn, reads=(), writes=(), inc=(), after=(), dma=False):
            deps = [after]
            for nm in reads:
                deps.append(rw(nm)[0])
                if nm in PSUMKEYS:
                    deps.append(rw(nm)[1])
            for nm in writes:
                e = rw(nm)
                deps.append(e[0])
                deps.append(e[1])
            for s, n in inc:
                if s in DMASEMS and p.cnt[s] > 0:
                    deps.append(((s, p.cnt[s]),))
            tok = p.op(eng, fn, inc=inc, after=_promote(TK(*deps)), dma=dma)
            for nm in writes:
                mark_w(nm, tok)
            for nm in reads:
                mark_r(nm, tok)
            return tok

        def mmg(rname, emits, reads=(), after=()):
            e = rw(rname)
            deps = [after, e[0], e[1]]
            for nm in reads:
                deps.append(rw(nm)[0])
            tok = ()
            n = len(emits)
            for i, f in enumerate(emits):
                tok = p.op("te", f, inc=(("te", 1),) if i == n - 1 else (),
                           after=_promote(TK(*deps)) if i == 0 else ())
            for nm in reads:
                mark_r(nm, tok)
            mark_w(rname, tok)
            return tok

        scb = lambda h: (PSCE, "scE") if h % 2 == 0 else (PSCO, "scO")
        KVB = ((PKV0, "kvA"), (PKV1, "kvB"))
        QRr = lambda i: PQW[:, 256 * (i % 2):256 * (i % 2 + 1)]
        kvstate = [0]

        def kvreg():
            i = kvstate[0] & 1
            kvstate[0] += 1
            reg, nm = KVB[i]
            return nm, reg[:, 0:512]

        fl = lambda ap: ap.rearrange("pp a b -> pp (a b)")

        def px(c):
            sx = c % 3
            for tt in range(2):
                r0 = L * c + 128 * tt
                xop("sp", lambda g, sx=sx, tt=tt, r0=r0: g.dma_start(
                    out=x_s[:, sx, tt, :], in_=din["x_b"][r0:r0 + 128, :]),
                    writes=(("x", sx, tt),), inc=(("lda" if tt == 0 else "ldb", 16),))

        def proj_A1(c):
            sx = c % 3
            cut("pA1")
            for tt in range(2):
                xop("act", lambda a, sx=sx, tt=tt: a.activation(
                    out=junk[:, :], in_=x_s[:, sx, tt, :], func=AF.Square,
                    accum_out=stat[:, 0 + tt:1 + tt]), reads=(("x", sx, tt),),
                    writes=(("st", 0 + tt),), inc=(("act", 1),))
            for tt in range(2):
                xop("act", lambda a, tt=tt: a.activation(
                    out=stat[:, 2 + tt:3 + tt], in_=stat[:, 0 + tt:1 + tt], func=AF.Sqrt,
                    scale=1.0 / D, bias=eps_c[:, 0:1]), reads=(("st", 0 + tt),),
                    writes=(("st", 2 + tt),), inc=(("act", 1),))
            for tt in range(2):
                xop("dve", lambda v, tt=tt: v.reciprocal(
                    stat[:, 4 + tt:5 + tt], stat[:, 2 + tt:3 + tt]),
                    reads=(("st", 2 + tt),), writes=(("st", 4 + tt),), inc=(("dve", 1),))
            for tt in range(2):
                xop("act", lambda a, sx=sx, tt=tt: a.activation(
                    out=xnb[:, tt, :], in_=x_s[:, sx, tt, :], func=AF.Copy,
                    scale=stat[:, 4 + tt:5 + tt], bias=0.0),
                    reads=(("x", sx, tt), ("st", 4 + tt)), writes=(("onb", tt),),
                    inc=(("act", 1),))
            cut("pA3")

        def proj_A2(c):
            for tt in range(2):
                mmg("pt", [lambda t, tt=tt, dc=dc: t.transpose(
                    PTX[:, 128 * dc:128 * (dc + 1)],
                    xnb[:, tt, 128 * dc:128 * (dc + 1)], id_b[:, :])
                    for dc in range(NDC)], reads=(("onb", tt),))
                xop("dve", lambda v, tt=tt: v.tensor_copy(
                    hT_s[:, :, 128 * tt:128 * (tt + 1)],
                    PTX[:, 0:1024].rearrange("pp (a b) -> pp a b", a=NDC)),
                    reads=("pt",), writes=(("hT", tt),), inc=(("dve", 1),))

        def proj_f(c, tt):
            w = c % 2
            regs2 = []
            for half in range(2):
                rname, reg = kvreg()
                regs2.append((rname, reg))
                mmg(rname, [lambda t, tt=tt, half=half, di2=di2, reg=reg: t.matmul(
                    reg, hT_s[:, di2, 128 * tt:128 * (tt + 1)],
                    w_f[:, di2, 512 * half:512 * (half + 1)],
                    start=(di2 == 0), stop=(di2 == NDC - 1)) for di2 in range(NDC)],
                    reads=(("hT", tt),))
            for half, (rname, reg) in enumerate(regs2):
                sl = slice(512 * half, 512 * (half + 1))
                xop("act", lambda a, reg=reg, sl=sl: a.activation(
                    out=fabs_s[:, sl], in_=reg, func=AF.Abs),
                    reads=(rname,), writes=(("fab", half),), inc=(("act", 1),))
            for half, (rname, reg) in enumerate(regs2):
                sl = slice(512 * half, 512 * (half + 1))
                xop("act", lambda a, sl=sl: a.activation(
                    out=fabs_s[:, sl], in_=fabs_s[:, sl], func=AF.Exp, scale=-1.0),
                    reads=(("fab", half),), writes=(("fab", half),), inc=(("act", 1),))
            for half, (rname, reg) in enumerate(regs2):
                sl = slice(512 * half, 512 * (half + 1))
                xop("act", lambda a, sl=sl: a.activation(
                    out=fabs_s[:, sl], in_=fabs_s[:, sl], func=AF.Ln, bias=one_f[:, 0:1]),
                    reads=(("fab", half),), writes=(("fab", half),), inc=(("act", 1),))
            for half, (rname, reg) in enumerate(regs2):
                sl = slice(512 * half, 512 * (half + 1))
                xop("dve", lambda v, reg=reg, w=w, tt=tt, sl=sl: v.scalar_tensor_tensor(
                    out=fTMw[:, w, tt, sl], in0=reg, scalar=0.0, in1=fabs_s[:, sl],
                    op0=ALU.min, op1=ALU.subtract),
                    reads=(rname, ("fab", half)), writes=(("f", w, tt, half),),
                    inc=(("dve", 1),))

        def proj_q(c, dcs):
            w = c % 2
            for pi in range(0, len(dcs), 2):
                pair = dcs[pi:pi + 2]
                for dc in pair:
                    mmg("qw", [lambda t, dc=dc, di2=di2: t.matmul(
                        QRr(dc), w_q[:, di2, 128 * dc:128 * (dc + 1)], hT_s[:, di2, :],
                        start=(di2 == 0), stop=(di2 == NDC - 1)) for di2 in range(NDC)],
                        reads=(("hT", 0), ("hT", 1)))
                for dc in pair:
                    xop("act", lambda a, dc=dc, w=w: a.activation(
                        out=qTw[:, dc, w, :], in_=QRr(dc), func=AF.Silu),
                        reads=("qw",), writes=(("q", w, dc),), inc=(("act", 1),))

        def proj_kv(c, wt, key, tt):
            w = c % 2
            dst = kTMw if key == "k" else vTMw
            for half in range(2):
                rname, reg = kvreg()
                mmg(rname, [lambda t, tt=tt, half=half, di2=di2, reg=reg, wt=wt: t.matmul(
                    reg, hT_s[:, di2, 128 * tt:128 * (tt + 1)],
                    wt[:, di2, 512 * half:512 * (half + 1)],
                    start=(di2 == 0), stop=(di2 == NDC - 1)) for di2 in range(NDC)],
                    reads=(("hT", tt),))
                xop("act", lambda a, w=w, tt=tt, half=half, dst=dst, reg=reg: a.activation(
                    out=dst[:, w, tt, 512 * half:512 * (half + 1)], in_=reg,
                    func=AF.Silu), reads=(rname,), writes=((key, w, tt, half),),
                    inc=(("act", 1),))

        def scan_p1(c, h, last):
            w = c % 2
            hp = h % 2
            hp2 = h // 2
            mcs = slice(M * h, M * (h + 1))
            dks = slice(DK * h, DK * (h + 1))
            SC, scn = scb(h)
            mmg("pt", [lambda t, tt2=tt2, dkc2=dkc2, w=w, dks=dks: t.transpose(
                PTX[:, 128 * (2 * dkc2 + tt2):128 * (2 * dkc2 + tt2 + 1)],
                kTMw[:, w, tt2, dks.start + 128 * dkc2:dks.start + 128 * (dkc2 + 1)],
                id_b[:, :]) for tt2 in range(2) for dkc2 in range(2)],
                reads=(("k", w, 0, hp2), ("k", w, 1, hp2)))
            xop("dve", lambda v: v.tensor_copy(
                fl(kT_s[:, :, :]), PTX[:, 0:512]),
                reads=("pt",), writes=(("kT",),), inc=(("dve", 1),))
            mmg(scn, [lambda t, tt2=tt2, tp2=tp2, w=w, mcs=mcs, SC=SC: t.matmul(
                SC[:, 256 * tt2:256 * tt2 + 256],
                mc_s[:, tp2, 128 * tt2:128 * (tt2 + 1)], fTMw[:, w, tp2, mcs],
                start=(tp2 == 0), stop=(tp2 == 1))
                for tt2 in range(2) for tp2 in range(2)],
                reads=(("f", w, 0, hp2), ("f", w, 1, hp2)))
            if h == 0:
                cut("s_g")
            xop("act", lambda a, SC=SC: a.activation(
                out=fl(LamB[:, :, :]), in_=SC[:, 0:512], func=AF.Exp, scale=1.0 / GN),
                reads=(scn,), writes=(("LamB",),), inc=(("act", 1),))
            xop("act", lambda a, SC=SC: a.activation(
                out=fl(EnG[:, :, :]), in_=SC[:, 0:512], func=AF.Exp, scale=-1.0 / GN),
                reads=(scn,), writes=(("EnG",),), inc=(("act", 1),))
            xop("dve", lambda v, w=w, mcs=mcs, SC=SC: v.tensor_sub(
                zS[:, :, :], fTMw[:, w, :, mcs],
                SC[:, 0:512].rearrange("pp (a b) -> pp a b", a=2)),
                reads=(scn, ("f", w, 0, hp2), ("f", w, 1, hp2)),
                writes=(("zS",),), inc=(("dve", 1),))
            xop("act", lambda a: a.activation(
                out=fl(zS[:, :, :]), in_=fl(zS[:, :, :]), func=AF.Exp, scale=1.0 / GN),
                reads=(("zS",),), writes=(("zS",),), inc=(("act", 1),))
            xop("dve", lambda v, hp=hp: v.tensor_sub(
                fl(Sp[:, hp, :, :]), fl(EnG[:, :, :]), fl(zS[:, :, :])),
                reads=(("EnG",), ("zS",)), writes=(("Sp", hp),), inc=(("dve", 1),))
            if h == 0:
                cut("s_kt")
            mmg("awA", [lambda t, taut=taut2, dkc2=dkc2, h=h, w=w: t.matmul(
                PAWA[:, 256 * taut:256 * (taut + 1)],
                kT_s[:, dkc2, 128 * taut:128 * (taut + 1)],
                qTw[:, 2 * h + dkc2, w, :], start=(dkc2 == 0), stop=(dkc2 == 1))
                for taut2 in range(2) for dkc2 in range(2)],
                reads=(("kT",), ("q", w, 2 * h), ("q", w, 2 * h + 1)))
            xop("dve", lambda v: v.tensor_mul(
                fl(AmT[:, :, :]), PAWA[:, 0:512], fl(tri_s[:, :, :])),
                reads=("awA",), writes=(("AmT",),), inc=(("dve", 1),))
            mmg("awB",
                [lambda t, tt2=tt2, i4=i4, h=h, w=w, hp=hp: (
                    t.matmul(PAWB[:, 256 * tt2:256 * tt2 + 256],
                             qTw[:, 2 * h + i4, w, 128 * tt2:128 * (tt2 + 1)],
                             Hm[:, h, i4, :], start=(i4 == 0), stop=False)
                    if i4 < 2 else
                    t.matmul(PAWB[:, 256 * tt2:256 * tt2 + 256],
                             AmT[:, i4 - 2, 128 * tt2:128 * (tt2 + 1)],
                             Sp[:, hp, i4 - 2, :], start=False, stop=(i4 == 3)))
                 for tt2 in range(2) for i4 in range(4)],
                reads=(("q", w, 2 * h), ("q", w, 2 * h + 1), ("Hm", h),
                       ("AmT",), ("Sp", hp)))
            xop("dve", lambda v: v.tensor_mul(
                fl(zS[:, :, :]), PAWB[:, 0:512], fl(LamB[:, :, :])),
                reads=("awB", ("LamB",)), writes=(("zS",),), inc=(("dve", 1),))
            for tt in range(2):
                xop("dve", lambda v, tt=tt: v.tensor_reduce(
                    out=nmx[:, tt:tt + 1], in_=zS[:, tt, :], axis=AX.X, op=ALU.max,
                    negate=True), reads=(("zS",),), writes=(("nmx", tt),),
                    inc=(("dve", 1),))
            for tt in range(2):
                xop("act", lambda a, tt=tt: a.activation(
                    out=pEx[:, tt, :], in_=zS[:, tt, :], func=AF.Exp,
                    bias=nmx[:, tt:tt + 1], accum_out=den[:, tt:tt + 1]),
                    reads=(("zS",), ("nmx", tt)), writes=(("pEx",), ("den", tt)),
                    inc=(("act", 1),))
            xop("dve", lambda v, hp=hp: v.tensor_mul(
                fl(PpTM[:, hp, :, :]), fl(pEx[:, :, :]), fl(LamB[:, :, :])),
                reads=(("pEx",), ("LamB",)), writes=(("Pp", hp),), inc=(("dve", 1),))
            xop("dve", lambda v, hp=hp: v.reciprocal(rden[:, hp, 0:2], den[:, 0:2]),
                reads=(("den", 0), ("den", 1)), writes=(("rden", hp),), inc=(("dve", 1),))
            if h == 0:
                cut("s_z")

        def scan_p2(c, h, last):
            w = c % 2
            wn = (c + 1) % 2
            hp = h % 2
            hp2 = h // 2
            mcs = slice(M * h, M * (h + 1))
            dks = slice(DK * h, DK * (h + 1))
            SC, scn = scb(h)
            mmg("pt", [lambda t, tt2=tt2, mcx2=mcx2, base=base, hp=hp: t.transpose(
                PTX[:, 128 * (base + 2 * mcx2 + tt2):128 * (base + 2 * mcx2 + tt2 + 1)],
                Sp[:, hp, tt2, 128 * mcx2:128 * (mcx2 + 1)] if base == 0
                else PpTM[:, hp, tt2, 128 * mcx2:128 * (mcx2 + 1)],
                id_b[:, :])
                for base in (0, 4) for tt2 in range(2) for mcx2 in range(2)],
                reads=(("Sp", hp), ("Pp", hp)))
            xop("dve", lambda v, hp=hp: v.tensor_copy(fl(SpMT[:, hp, :, :]), PTX[:, 0:512]),
                reads=("pt",), writes=(("SpM", hp),), inc=(("dve", 1),))
            xop("dve", lambda v, hp=hp: v.tensor_copy(fl(PpMT[:, hp, :, :]), PTX[:, 512:1024]),
                reads=("pt",), writes=(("PpM", hp),), inc=(("dve", 1),))
            if h == 0:
                cut("s_t2")
            mmg("awA", [lambda t, taut=taut2, mcx2=mcx2, hp=hp: t.matmul(
                PAWA[:, 256 * taut:256 * (taut + 1)],
                SpMT[:, hp, mcx2, 128 * taut:128 * (taut + 1)], PpMT[:, hp, mcx2, :],
                start=(mcx2 == 0), stop=(mcx2 == 1))
                for taut2 in range(2) for mcx2 in range(2)],
                reads=(("SpM", hp), ("PpM", hp)))
            xop("dve", lambda v: v.tensor_mul(
                fl(WmT[:, :, :]), PAWA[:, 0:512], fl(tri_s[:, :, :])),
                reads=("awA",), writes=(("WmT",),), inc=(("dve", 1),))
            col_emits = []
            for mcx in range(2):
                col_emits.append(lambda t, mcx=mcx, w=w, mcs=mcs, last=last, SC=SC: t.matmul(
                    SC[:, mcx:mcx + 1],
                    fTMw[:, w, 1, mcs.start + 128 * mcx:mcs.start + 128 * (mcx + 1)],
                    one_b[:, 0:1], start=True, stop=last))
                if not last:
                    col_emits.append(lambda t, mcx=mcx, wn=wn, mcs=mcs, SC=SC: t.matmul(
                        SC[:, mcx:mcx + 1],
                        fTMw[:, wn, 0, mcs.start + 128 * mcx:mcs.start + 128 * (mcx + 1)],
                        one_b[:, 0:1], start=False, stop=True))
            col_emits.append(lambda t, w=w, mcs=mcs, last=last, SC=SC: t.matmul(
                SC[0:1, 4:4 + 256], one_b[:, 0:1], fTMw[:, w, 1, mcs],
                start=True, stop=last))
            if not last:
                col_emits.append(lambda t, wn=wn, mcs=mcs, SC=SC: t.matmul(
                    SC[0:1, 4:4 + 256], one_b[:, 0:1], fTMw[:, wn, 0, mcs],
                    start=False, stop=True))
            lamreads = (("f", w, 1, hp2),) + ((("f", wn, 0, hp2),) if not last else ())
            mmg(scn, col_emits, reads=lamreads)
            xop("act", lambda a, SC=SC: a.activation(
                out=lsC[:, 0:2], in_=SC[:, 0:2], func=AF.Exp, scale=1.0 / GN),
                reads=(scn,), writes=(("lsC",),), inc=(("act", 1),))
            xop("act", lambda a, SC=SC: a.activation(
                out=lsR[0:1, :], in_=SC[0:1, 4:4 + 256], func=AF.Exp, scale=1.0 / GN),
                reads=(scn,), writes=(("lsR",),), inc=(("act", 1),))
            mmg(scn, [lambda t, SC=SC: t.matmul(
                SC[:, 256:512], onerow_b[0:1, :], lsR[0:1, 0:256], start=True,
                stop=True)], reads=(("lsR",),))
            if h == 0:
                cut("s_l")
            mmg("awB", [lambda t, dkc=dkc2, taut2=taut2, w=w, dks=dks, hp=hp: t.matmul(
                PAWB[:, 256 * dkc:256 * (dkc + 1)],
                kTMw[:, w, taut2, dks.start + 128 * dkc:dks.start + 128 * (dkc + 1)],
                Sp[:, hp, taut2, :], start=(taut2 == 0), stop=(taut2 == 1))
                for dkc2 in range(2) for taut2 in range(2)],
                reads=(("k", w, 0, hp2), ("k", w, 1, hp2), ("Sp", hp)))
            xop("dve", lambda v, h=h: v.tensor_add(
                fl(tmpS[:, :, :]), fl(Hm[:, h, :, :]), PAWB[:, 0:512]),
                reads=("awB", ("Hm", h)), writes=(("tmp",),), inc=(("dve", 1),))
            for dkc in range(2):
                xop("dve", lambda v, dkc=dkc, h=h, SC=SC: v.tensor_mul(
                    Hm[:, h, dkc, :], tmpS[:, dkc, :], SC[:, 256:512]),
                    reads=(("tmp",), scn), writes=(("Hm", h),), inc=(("dve", 1),))
            mmg(scn,
                [lambda t, tt2=tt2, i4=i4, h=h, w=w, SC=SC, hp=hp: (
                    t.matmul(SC[:, 256 * tt2:256 * tt2 + 256],
                             PpMT[:, hp, i4, 128 * tt2:128 * (tt2 + 1)], Hv[:, h, i4, :],
                             start=(i4 == 0), stop=False)
                    if i4 < 2 else
                    t.matmul(SC[:, 256 * tt2:256 * tt2 + 256],
                             WmT[:, i4 - 2, 128 * tt2:128 * (tt2 + 1)],
                             vTMw[:, w, i4 - 2, DV * h:DV * (h + 1)], start=False,
                             stop=(i4 == 3)))
                 for tt2 in range(2) for i4 in range(4)],
                reads=(("PpM", hp), ("Hv", h), ("WmT",), ("v", w, 0, hp2),
                       ("v", w, 1, hp2)))
            for tt in range(2):
                xop("act", lambda a, tt=tt, h=h, SC=SC, hp=hp: a.activation(
                    out=oB[:, tt, DV * h:DV * (h + 1)],
                    in_=SC[:, 256 * tt:256 * tt + 256], func=AF.Copy,
                    scale=rden[:, hp, tt:tt + 1], bias=0.0),
                    reads=(scn, ("rden", hp)), writes=(("oB", tt),),
                    inc=(("act", 1),))
            mmg("awB", [lambda t, mcx=mcx2, taut2=taut2, w=w, h=h, hp=hp: t.matmul(
                PAWB[:, 256 * mcx:256 * (mcx + 1)],
                Sp[:, hp, taut2, 128 * mcx:128 * (mcx + 1)],
                vTMw[:, w, taut2, DV * h:DV * (h + 1)],
                start=(taut2 == 0), stop=(taut2 == 1))
                for mcx2 in range(2) for taut2 in range(2)],
                reads=(("Sp", hp), ("v", w, 0, hp2), ("v", w, 1, hp2)))
            xop("dve", lambda v, h=h: v.tensor_add(
                fl(tmpS[:, :, :]), fl(Hv[:, h, :, :]), PAWB[:, 0:512]),
                reads=("awB", ("Hv", h)), writes=(("tmp",),), inc=(("dve", 1),))
            for mcx in range(2):
                xop("dve", lambda v, mcx=mcx, h=h: v.tensor_scalar(
                    out=Hv[:, h, mcx, :], in0=tmpS[:, mcx, :], scalar1=lsC[:, mcx:mcx + 1],
                    scalar2=None, op0=ALU.mult),
                    reads=(("tmp",), ("lsC",)), writes=(("Hv", h),),
                    inc=(("dve", 1),))
            if h == 0:
                cut("s_o")

        def epi_pre(c, tt):
            xop("act", lambda a, tt=tt: a.activation(
                out=oB[:, tt, :], in_=oB[:, tt, :], func=AF.Silu),
                reads=(("oB", tt),), writes=(("oB", tt),), inc=(("act", 1),))
            xop("act", lambda a, tt=tt: a.activation(
                out=junk[:, :], in_=oB[:, tt, :], func=AF.Square,
                accum_out=stat[:, 6 + tt:7 + tt]), reads=(("oB", tt),),
                writes=(("st", 6 + tt),), inc=(("act", 1),))
            xop("act", lambda a, tt=tt: a.activation(
                out=stat[:, 8 + tt:9 + tt], in_=stat[:, 6 + tt:7 + tt], func=AF.Sqrt,
                scale=1.0 / D, bias=eps_c[:, 0:1]), reads=(("st", 6 + tt),),
                writes=(("st", 8 + tt),), inc=(("act", 1),))
            xop("dve", lambda v, tt=tt: v.reciprocal(
                stat[:, 10 + tt:11 + tt], stat[:, 8 + tt:9 + tt]),
                reads=(("st", 8 + tt),), writes=(("st", 10 + tt),), inc=(("dve", 1),))
            xop("act", lambda a, tt=tt: a.activation(
                out=onb[:, tt, :], in_=oB[:, tt, :], func=AF.Copy,
                scale=stat[:, 10 + tt:11 + tt], bias=0.0),
                reads=(("oB", tt), ("st", 10 + tt)), writes=(("onb", tt),),
                inc=(("act", 1),))
            mmg("pt", [lambda t, tt=tt, dc=dc: t.transpose(
                PTX[:, 128 * dc:128 * (dc + 1)], onb[:, tt, 128 * dc:128 * (dc + 1)],
                id_b[:, :]) for dc in range(NDC)], reads=(("onb", tt),))
            xop("dve", lambda v, tt=tt: v.tensor_copy(
                onT[:, :, 128 * tt:128 * (tt + 1)],
                PTX[:, 0:1024].rearrange("pp (a b) -> pp a b", a=NDC)),
                reads=("pt",), writes=(("onT", tt),), inc=(("dve", 1),))

        def epi_wo(c):
            for pi in range(0, NDC, 2):
                for dc in (pi, pi + 1):
                    mmg("qw", [lambda t, dc=dc, di2=di2: t.matmul(
                        QRr(dc), w_o[:, di2, 128 * dc:128 * (dc + 1)], onT[:, di2, :],
                        start=(di2 == 0), stop=(di2 == NDC - 1)) for di2 in range(NDC)],
                        reads=(("onT", 0), ("onT", 1)))
                for dc in (pi, pi + 1):
                    xop("dve", lambda v, dc=dc: v.tensor_copy(aoT[:, dc, :], QRr(dc)),
                        reads=("qw",), writes=(("aoT", dc),), inc=(("dve", 1),))

        def epi_fin(c, tt):
            sx = c % 3
            mmg("pt", [lambda t, tt=tt, dc=dc: t.transpose(
                PTX[:, 128 * dc:128 * (dc + 1)], aoT[:, dc, 128 * tt:128 * (tt + 1)],
                id_b[:, :]) for dc in range(NDC)],
                reads=tuple(("aoT", dc) for dc in range(NDC)))
            for half in range(2):
                xop("dve", lambda v, tt=tt, sx=sx, half=half: v.tensor_add(
                    x_s[:, sx, tt, 512 * half:512 * (half + 1)],
                    x_s[:, sx, tt, 512 * half:512 * (half + 1)],
                    PTX[:, 512 * half:512 * (half + 1)]),
                    reads=(("x", sx, tt), "pt"), writes=(("x", sx, tt),),
                    inc=(("dve", 1),))
            m = 12 + 4 * tt
            xop("dve", lambda v, tt=tt, sx=sx, m=m: v.tensor_reduce(
                out=stat[:, m:m + 1], in_=x_s[:, sx, tt, :], axis=AX.X, op=ALU.add),
                reads=(("x", sx, tt),), writes=(("st", m),), inc=(("dve", 1),))
            xop("act", lambda a, tt=tt, sx=sx, m=m: a.activation(
                out=junk[:, :], in_=x_s[:, sx, tt, :], func=AF.Square,
                accum_out=stat[:, m + 1:m + 2]), reads=(("x", sx, tt),),
                writes=(("st", m + 1),), inc=(("act", 1),))
            xop("dve", lambda v, m=m: v.tensor_scalar(
                out=stat[:, m:m + 1], in0=stat[:, m:m + 1], scalar1=1.0 / D,
                scalar2=None, op0=ALU.mult),
                reads=(("st", m),), writes=(("st", m),), inc=(("dve", 1),))
            xop("dve", lambda v, m=m: v.tensor_mul(
                stat[:, m + 2:m + 3], stat[:, m:m + 1], stat[:, m:m + 1]),
                reads=(("st", m),), writes=(("st", m + 2),), inc=(("dve", 1),))
            xop("dve", lambda v, m=m: v.scalar_tensor_tensor(
                out=stat[:, m + 1:m + 2], in0=stat[:, m + 1:m + 2], scalar=1.0 / D,
                in1=stat[:, m + 2:m + 3], op0=ALU.mult, op1=ALU.subtract),
                reads=(("st", m + 1), ("st", m + 2)), writes=(("st", m + 1),),
                inc=(("dve", 1),))
            xop("act", lambda a, m=m: a.activation(
                out=stat[:, m + 1:m + 2], in_=stat[:, m + 1:m + 2], func=AF.Sqrt,
                bias=eps_c[:, 0:1]), reads=(("st", m + 1),), writes=(("st", m + 1),),
                inc=(("act", 1),))
            xop("dve", lambda v, m=m: v.reciprocal(
                stat[:, m + 3:m + 4], stat[:, m + 1:m + 2]),
                reads=(("st", m + 1),), writes=(("st", m + 3),), inc=(("dve", 1),))
            xop("dve", lambda v, m=m: v.scalar_tensor_tensor(
                out=stat[:, m + 2:m + 3], in0=stat[:, m:m + 1], scalar=-1.0,
                in1=stat[:, m + 3:m + 4], op0=ALU.mult, op1=ALU.mult),
                reads=(("st", m), ("st", m + 3)), writes=(("st", m + 2),),
                inc=(("dve", 1),))
            xop("act", lambda a, tt=tt, sx=sx, m=m: a.activation(
                out=x_s[:, sx, tt, :], in_=x_s[:, sx, tt, :], func=AF.Identity,
                scale=stat[:, m + 3:m + 4], bias=stat[:, m + 2:m + 3]),
                reads=(("x", sx, tt), ("st", m + 3), ("st", m + 2)),
                writes=(("x", sx, tt),), inc=(("act", 1),))
            xop("dve", lambda v, tt=tt, sx=sx: v.tensor_copy(
                y1b[:, tt, 0:D], x_s[:, sx, tt, :]),
                reads=(("x", sx, tt),), writes=(("y1b", tt),), inc=(("dve", 1),))
            r0 = L * c + 128 * tt
            xop("sp", lambda g, tt=tt, r0=r0, sx=sx: g.dma_start(
                out=y1f32d[r0:r0 + 128, :], in_=x_s[:, sx, tt, :]),
                reads=(("x", sx, tt),), inc=(("outa" if tt == 0 else "outb", 16),))
            if "y1" in taps:
                xop("sp", lambda g, tt=tt, r0=r0, sx=sx: g.dma_start(
                    out=tap["y1"][r0:r0 + 128, :], in_=x_s[:, sx, tt, :]),
                    reads=(("x", sx, tt),), inc=(("st", 16),))
            xop("dve", lambda v, tt=tt, sx=sx: v.tensor_mul(
                x_s[:, sx, tt, :], x_s[:, sx, tt, :], wgbc[:, :]),
                reads=(("x", sx, tt),), writes=(("x", sx, tt),), inc=(("dve", 1),))
            xop("dve", lambda v, tt=tt, sx=sx, c=c: v.tensor_reduce(
                out=lgT[:, 2 * c + tt:2 * c + tt + 1], in_=x_s[:, sx, tt, :], axis=AX.X,
                op=ALU.add), reads=(("x", sx, tt),), writes=(("lgT",),),
                inc=(("dve", 1),))
            xop("dve", lambda v, tt=tt, c=c: v.tensor_copy(
                y1b[:, tt, D:D + 1], lgT[:, 2 * c + tt:2 * c + tt + 1]),
                reads=(("lgT",),), writes=(("y1b", tt),), inc=(("dve", 1),))
            xop("sp", lambda g, tt=tt, r0=r0: g.dma_start(
                out=y1d[r0:r0 + 128, :], in_=y1b[:, tt, :]),
                reads=(("y1b", tt),), inc=(("sta" if tt == 0 else "stb", 16),))

        def ab_step(s):
            lastc = (s - 1 == NCH - 1)
            ep = s - 2
            doepi = 2 <= s <= NCH + 1
            if doepi:
                if "o" in taps:
                    p.barrier()
                    for tt in range(2):
                        p.op("sp", lambda g, ep=ep, tt=tt: g.dma_start(
                            out=tap["o"][L * ep + 128 * tt:L * ep + 128 * (tt + 1), :],
                            in_=oB[:, tt, :]), inc=(("st", 16),))
                    p.barrier()
                for tt in range(2):
                    epi_pre(ep, tt)
            if s == 0:
                px(0)
            if s < NCH:
                proj_A1(s)
            if doepi:
                epi_wo(ep)
            if s < NCH:
                proj_A2(s)
                cut("pA")
                proj_f(s, 0)
                cut("pF")
            sc = s >= 1 and s <= NCH
            if sc:
                scan_p1(s - 1, 0, lastc)
            if s < NCH:
                proj_q(s, (0, 1, 2, 3))
                cut("pQ")
            if sc:
                scan_p1(s - 1, 1, lastc)
            if s < NCH:
                proj_kv(s, w_k, "k", 0)
                proj_kv(s, w_k, "k", 1)
                cut("pK")
            if sc:
                scan_p2(s - 1, 0, lastc)
            if s < NCH:
                proj_kv(s, w_v, "v", 0)
            if sc:
                scan_p2(s - 1, 1, lastc)
            if s < NCH:
                proj_kv(s, w_v, "v", 1)
                proj_f(s, 1)
            if sc:
                scan_p1(s - 1, 2, lastc)
            if doepi:
                epi_fin(ep, 0)
            if sc:
                scan_p1(s - 1, 3, lastc)
            if s < NCH:
                proj_q(s, (4, 5, 6, 7))
            if sc:
                scan_p2(s - 1, 2, lastc)
            if doepi:
                epi_fin(ep, 1)
            if sc:
                scan_p2(s - 1, 3, lastc)
            if s + 1 < NCH:
                px(s + 1)
                if "qT" in taps:
                    p.barrier()
                    for dc in range(NDC):
                        p.op("pool", lambda g, s=s, dc=dc: g.dma_start(
                            out=tap["qT"][128 * dc:128 * (dc + 1), L * s:L * (s + 1)],
                            in_=qTw[:, dc, s % 2, :]), inc=(("lg", 16),), dma=True)
                if "kvf" in taps:
                    p.barrier()
                    for tt in range(2):
                        r0 = L * s + 128 * tt
                        for i, buf in ((0, kTMw), (1, vTMw), (2, fTMw)):
                            p.op("pool", lambda g, i=i, buf=buf, r0=r0, s=s, tt=tt: g.dma_start(
                                out=tap["kvf"][i, r0:r0 + 128, :], in_=buf[:, s % 2, tt, :]),
                                inc=(("lg", 16),), dma=True)
                if "qT" in taps or "kvf" in taps:
                    p.barrier()

        for step in range(NCH + 2):
            ab_step(step)
            if step < NCH:
                cut(("proj", step))
            if step >= 1 and step <= NCH:
                cut(("scan", step - 1))
        p.barrier()
        ab_stack.close()
        cut("ab")

        PAB = stack.enter_context(nc.psum_tensor("PAB", [128, 1024], F32))
        PCD = stack.enter_context(nc.psum_tensor("PCD", [128, 1024], F32))
        PE_ = stack.enter_context(nc.psum_tensor("PE_", [128, 512], F32))
        PF_ = stack.enter_context(nc.psum_tensor("PF_", [128, 512], F32))
        PT_ = stack.enter_context(nc.psum_tensor("PT_", [128, 1024], BF16))
        PS_ = stack.enter_context(nc.psum_tensor("PS_", [128, 512], F32))
        id_f = sb("id_f", [128, 128], F32)
        lt64_s = sb("lt64_s", [T // 128, T // 128], F32)
        lt128_s = sb("lt128_s", [128, 128], F32)
        sel2_s = sb("sel2_s", [2, 256], BF16)
        sqb = sb("sqb", [128, 512], BF16)
        nrm2 = sb("nrm2", [1, FH, (T // 2) // 512], F32)
        v2S = sb("v2S", [128, (T // 2) // 128, FH * 65], BF16)
        kmx2 = sb("kmx2", [1, FH], F32)
        qmx2 = sb("qmx2", [1, FH], F32)
        for nm, dst in (("ident", id_f), ("lt64", lt64_s), ("lt128", lt128_s)):
            p.op("sp", lambda g, nm=nm, dst=dst: g.dma_start(out=dst[:, :], in_=din[nm][:, :]),
                 inc=(("ld", 16),))
        p.op("pool", lambda g: g.dma_start(out=sel2_s[:, :], in_=din["sel2"][:, :]),
             inc=(("lg", 16),), dma=True)
        p.barrier()

        if "logits" in taps:
            p.op("sp", lambda g: g.dma_start(out=tap["logits"][:, :], in_=lgT[:, :]),
                 inc=(("st", 16),))
            p.barrier()

        NQT = T // 128

        lo = sb("lo", [128, 1], F32)
        hi = sb("hi", [128, 1], F32)
        mid = sb("mid", [128, 1], F32)
        sel = sb("sel", [128, 1], F32)
        selm1 = sb("selm1", [128, 1], F32)
        dtmp = sb("dtmp", [128, 1], F32)
        cntc = sb("cntc", [128, 1], F32)
        cnts = sb("cnts", [1, 1], F32)
        junkN = sb("junkN", [128, NQT], F32)
        p.op("dve", lambda v: v.memset(lo[:, :], -256.0), inc=(("dve", 1),))
        p.op("dve", lambda v: v.memset(hi[:, :], 256.0), inc=(("dve", 1),))
        p.op("dve", lambda v: v.memset(mid[:, :], 0.0), inc=(("dve", 1),))
        for it in range(BIS_ITERS):
            p.op("dve", lambda v: v.tensor_scalar(
                out=junkN[:, :], in0=lgT[:, :], scalar1=mid[:, 0:1], scalar2=0.0,
                op0=ALU.is_ge, op1=ALU.add, accum_out=cntc[:, 0:1]), inc=(("dve", 1),))
            p.op("te", lambda t: t.matmul(PS_[0:1, 0:1], cntc[:, 0:1], one_f[:, 0:1],
                                          start=True, stop=True), inc=(("te", 1),),
                 after=(("dve", None),))
            p.op("dve", lambda v: v.tensor_copy(cnts[0:1, 0:1], PS_[0:1, 0:1]),
                 inc=(("dve", 1),), after=(("te", None),))
            p.op("te", lambda t: t.matmul(PS_[:, 2:3], onesrow_f[0:1, :], cnts[0:1, 0:1],
                                          start=True, stop=True), inc=(("te", 1),),
                 after=(("dve", None),))
            p.op("dve", lambda v: v.tensor_scalar(
                out=sel[:, 0:1], in0=PS_[:, 2:3], scalar1=float(CSEL), scalar2=None,
                op0=ALU.is_ge), inc=(("dve", 1),), after=(("te", None),))
            p.op("dve", lambda v: v.tensor_scalar(
                out=selm1[:, 0:1], in0=sel[:, 0:1], scalar1=1.0, scalar2=None,
                op0=ALU.subtract), inc=(("dve", 1),))
            p.op("dve", lambda v: v.tensor_sub(dtmp[:, 0:1], mid[:, 0:1], lo[:, 0:1]),
                 inc=(("dve", 1),))
            p.op("dve", lambda v: v.scalar_tensor_tensor(
                out=lo[:, 0:1], in0=dtmp[:, 0:1], scalar=sel[:, 0:1], in1=lo[:, 0:1],
                op0=ALU.mult, op1=ALU.add), inc=(("dve", 1),))
            p.op("dve", lambda v: v.tensor_sub(dtmp[:, 0:1], hi[:, 0:1], mid[:, 0:1]),
                 inc=(("dve", 1),))
            p.op("dve", lambda v: v.scalar_tensor_tensor(
                out=hi[:, 0:1], in0=dtmp[:, 0:1], scalar=selm1[:, 0:1], in1=hi[:, 0:1],
                op0=ALU.mult, op1=ALU.add), inc=(("dve", 1),))
            p.op("dve", lambda v: v.tensor_add(dtmp[:, 0:1], lo[:, 0:1], hi[:, 0:1]),
                 inc=(("dve", 1),))
            p.op("dve", lambda v: v.tensor_scalar(
                out=mid[:, 0:1], in0=dtmp[:, 0:1], scalar1=0.5, scalar2=None, op0=ALU.mult),
                inc=(("dve", 1),))
        if "theta" in taps:
            p.op("sp", lambda g: g.dma_start(out=tap["theta"][:, :], in_=lo[:, :]),
                 inc=(("st", 16),), after=(("dve", None),))
        p.barrier()
        cut("bisect")

        maskA = sb("maskA", [128, NQT], F32)
        maskT = sb("maskT", [NQT, 128], F32)
        colcnt = sb("colcnt", [NQT, 1], F32)
        colpf = sb("colpf", [NQT, 1], F32)
        slotT = sb("slotT", [NQT, 128], F32)
        t1s = sb("t1s", [NQT, 128], F32)
        wsS = sb("wsS", [1, 1], F32)
        wsI = sb("wsI", [1, 1], I32)
        wsB = sb("wsB", [NQT, 1], F32)
        slotI = sb("slotI", [128, NQT], I32)
        srelW = sb("srelW", [128, MYT // 128], F32)

        p.op("dve", lambda v: v.tensor_scalar(
            out=maskA[:, :], in0=lgT[:, :], scalar1=lo[:, 0:1], scalar2=None,
            op0=ALU.is_ge), inc=(("dve", 1),))
        p.op("te", lambda t: t.transpose(PS_[0:NQT, 64:64 + 128], maskA[:, :], id_f[:, :]),
             inc=(("te", 1),), after=(("dve", None),))
        p.op("dve", lambda v: v.tensor_copy(maskT[:, :], PS_[0:NQT, 64:64 + 128]),
             inc=(("dve", 1),), after=(("te", None),))
        p.op("dve", lambda v: v.tensor_reduce(
            out=colcnt[:, 0:1], in_=maskT[:, :], axis=AX.X, op=ALU.add), inc=(("dve", 1),))
        p.op("te", lambda t: t.matmul(PS_[0:NQT, 0:1], lt64_s[:, :], colcnt[:, 0:1],
                                      start=True, stop=True), inc=(("te", 1),),
             after=(("dve", None),))
        p.op("dve", lambda v: v.tensor_copy(colpf[:, 0:1], PS_[0:NQT, 0:1]),
             inc=(("dve", 1),), after=(("te", None),))
        p.op("te", lambda t: t.matmul(PS_[0:NQT, 192:192 + 128], maskA[:, :], lt128_s[:, :],
                                      start=True, stop=True), inc=(("te", 1),),
             after=(("dve", None),))
        p.op("dve", lambda v: v.tensor_scalar(
            out=slotT[:, :], in0=PS_[0:NQT, 192:192 + 128], scalar1=colpf[:, 0:1],
            scalar2=None, op0=ALU.add), inc=(("dve", 1),), after=(("te", None),))
        p.op("dve", lambda v: v.tensor_scalar(
            out=t1s[:, :], in0=maskT[:, :], scalar1=-1.0e9, scalar2=1.0e9,
            op0=ALU.mult, op1=ALU.add), inc=(("dve", 1),))
        p.op("dve", lambda v: v.tensor_add(slotT[:, :], slotT[:, :], t1s[:, :]),
             inc=(("dve", 1),))
        p.op("sp", lambda g: g.dma_start(out=cpfD[:, :], in_=colpf[:, :]),
             inc=(("st", 16),), after=(("dve", None),))
        stv = p.c("st")

        def rt_read_cpf(g):
            r = g.value_load(myq_s[0:1, 0:1])
            return g.dma_start(out=wsS[0:1, 0:1], in_=cpfD[bass.ts(r, NQT // 4), :][0:1, :])
        p.op("pool", rt_read_cpf, dma=True, inc=(("lg", 16),), after=(("st", stv),))
        p.op("dve", lambda v: v.tensor_scalar(
            out=wsS[0:1, 0:1], in0=wsS[0:1, 0:1], scalar1=float(CSEL - QW), scalar2=None,
            op0=ALU.min), inc=(("dve", 1),), after=(("lg", None),))
        p.op("dve", lambda v: v.tensor_scalar(
            out=wsS[0:1, 0:1], in0=wsS[0:1, 0:1], scalar1=0.0, scalar2=None,
            op0=ALU.max), inc=(("dve", 1),))
        p.op("dve", lambda v: v.tensor_copy(wsI[0:1, 0:1], wsS[0:1, 0:1]), inc=(("dve", 1),))
        p.op("te", lambda t: t.matmul(PS_[0:NQT, 32:33], onesrow_f[0:1, 0:NQT],
                                      wsS[0:1, 0:1], start=True, stop=True),
             inc=(("te", 1),), after=(("dve", None),))
        p.op("dve", lambda v: v.tensor_copy(wsB[:, 0:1], PS_[0:NQT, 32:33]),
             inc=(("dve", 1),), after=(("te", None),))
        p.op("dve", lambda v: v.tensor_scalar(
            out=t1s[:, :], in0=slotT[:, :], scalar1=wsB[:, 0:1], scalar2=None,
            op0=ALU.subtract), inc=(("dve", 1),), after=(("pool", None),))
        p.op("sp", lambda g: g.dma_start(out=srelD[:, :], in_=t1s[:, :]),
             inc=(("st", 16),), after=(("dve", None),))
        p.op("te", lambda t: t.transpose(PS_[:, 320:320 + NQT], slotT[:, :],
                                         id_f[0:NQT, 0:NQT]), inc=(("te", 1),),
             after=(("dve", None),))
        p.op("dve", lambda v: v.tensor_copy(slotI[:, :], PS_[:, 320:320 + NQT]),
             inc=(("dve", 1),), after=(("te", None),))
        if "slot" in taps:
            p.op("sp", lambda g: g.dma_start(out=tap["slot"][:, :], in_=PS_[:, 320:320 + NQT]),
                 inc=(("st", 16),), after=(("te", None),))
        p.barrier()

        rowbuf = sb("rowbuf", [128, 2, 1032], BF16)
        regh = {}
        p.op("pool", lambda g: regh.setdefault("bc", g.to_reg(CSEL - 1)) and None, dma=True)
        scat_hist = []
        for i in range(NQT):
            bi = i % 2
            par = "lda" if bi == 0 else "ldb"
            spar = "sga" if bi == 0 else "sgb"
            p.op("sp", lambda g, i=i, bi=bi: g.dma_start(
                out=rowbuf[:, bi, :], in_=y1d[128 * i:128 * (i + 1), :]), inc=((par, 16),),
                after=((spar, scat_hist[i - 2]),) if i >= 2 else ())
            ldv = p.c(par)
            p.op("pool", lambda g, i=i, bi=bi: g.indirect_dma_start(
                out=atn[:, :], out_offset=IndirectOffsetOnAxis(slotI[:, i:i + 1], 0),
                in_=rowbuf[:, bi, :], in_offset=None, bounds_check=regh["bc"],
                oob_is_err=False), inc=((spar, 16),), after=((par, ldv),), dma=True)
            scat_hist.append(p.c(spar))
        p.barrier()
        cut("scatter")
        if "attn_in" in taps:
            atnf = sb("atnf", [128, 1032], F32)
            for i in range(CSEL // 128):
                p.op("pool", lambda g, i=i: g.dma_start(
                    out=atnf[:, :], in_=atn[128 * i:128 * (i + 1), :]), inc=(("lg", 16),),
                    after=(("st", None),), dma=True)
                p.op("sp", lambda g, i=i: g.dma_start(
                    out=tap["attn_in"][128 * i:128 * (i + 1), :], in_=atnf[:, :]),
                    inc=(("st", 16),), after=(("lg", None),))
                p.barrier()

        kv_stack = contextlib.ExitStack()
        sbk = lambda name, shape, dt: kv_stack.enter_context(nc.sbuf_tensor(name, shape, dt))
        fwk_s = sbk("fwk_s", [128, NDC, D], BF16)
        fwv_s = sbk("fwv_s", [128, NDC, D], BF16)
        for (dst, src) in ((fwk_s, "fWk"), (fwv_s, "fWv")):
            for dc in range(NDC):
                p.op("pool", lambda g, d=dst, s=src, dc=dc: g.dma_start(
                    out=d[:, dc, :], in_=din[s][128 * dc:128 * (dc + 1), :]),
                    inc=(("lw", 16),), dma=True)

        aT = sbk("aT", [128, NDC, CSEL], BF16)
        for j in range(NDC):
            p.op("sp", lambda g, j=j: g.dma_start(
                out=aT[:, j, :], in_=atn[:, 128 * j:128 * (j + 1)], transpose=True),
                inc=(("ld", 16),))
        p.barrier()
        cut("aT")
        p.wait("act", "lw")
        p.wait("dve", "lw")
        p.wait("te", "lw")

        kvbuf = sbk("kvbuf", [128, 2, FH * 65], BF16)
        k2st_hist = []
        for dc in range(NDC):
            for qt in range(CSEL // 512):
                it = dc * (CSEL // 512) + qt
                reg = PAB[:, 0:512] if qt % 2 == 0 else PAB[:, 512:1024]
                for di in range(NDC):
                    p.op("te", lambda t, dc=dc, qt=qt, di=di, reg=reg: t.matmul(
                        reg, fwk_s[:, di, 128 * dc:128 * (dc + 1)],
                        aT[:, di, 512 * qt:512 * (qt + 1)],
                        start=(di == 0), stop=(di == NDC - 1)),
                        inc=(("te", 1),) if di == NDC - 1 else (),
                        after=(("dve", None), ("act", None)) if di == 0 else ())
                bi = it % 2
                spar = "sta" if bi == 0 else "stb"
                p.op("dve", lambda v, bi=bi, reg=reg: v.tensor_copy(kvbuf[:, bi, 0:512], reg),
                     inc=(("dve", 1),),
                     after=(("te", None),) + (((spar, k2st_hist[it - 2]),) if it >= 2 else ()))
                p.op("sp", lambda g, dc=dc, qt=qt, bi=bi: g.dma_start(
                    out=k2d[dc, :, 512 * qt:512 * (qt + 1)], in_=kvbuf[:, bi, 0:512]),
                    inc=((spar, 16),), after=(("dve", None),))
                k2st_hist.append(p.c(spar))
                p.op("act", lambda a, bi=bi: a.activation(
                    out=sqb[:, 0:512], in_=kvbuf[:, bi, 0:512], func=AF.Square),
                    inc=(("act", 1),), after=(("dve", None),))
                for hh in range(2):
                    h = 2 * dc + hh
                    p.op("te", lambda t, hh=hh: t.matmul(
                        PS_[0:1, 0:512], one_b[0:64, 0:1] if hh == 0 else one_b[64:128, 0:1],
                        sqb[0:64, 0:512] if hh == 0 else sqb[64:128, 0:512],
                        start=True, stop=True), inc=(("te", 1),),
                        after=(("act", None), ("dve", None)))
                    p.op("dve", lambda v, h=h, qt=qt: v.tensor_reduce(
                        out=nrm2[0:1, h, qt:qt + 1], in_=PS_[0:1, 0:512], axis=AX.X,
                        op=ALU.max), inc=(("dve", 1),), after=(("te", None),))
        for h in range(FH):
            p.op("dve", lambda v, h=h: v.tensor_reduce(
                out=kmx2[0:1, h:h + 1], in_=nrm2[0:1, h, :], axis=AX.X, op=ALU.max),
                inc=(("dve", 1),))
        p.barrier()
        p.op("dve", lambda v: v.memset(v2S[:, :, :], 0.0), inc=(("dve", 1),))
        for h in range(FH):
            p.op("dve", lambda v, h=h: v.memset(
                v2S[:, :, 65 * h + 64:65 * h + 65], 1.0), inc=(("dve", 1),))
        for kt in range(NKT):
            for half in range(2):
                reg = PCD[:, 0:512] if half == 0 else PCD[:, 512:1024]
                for di in range(NDC):
                    p.op("te", lambda t, kt=kt, half=half, di=di, reg=reg: t.matmul(
                        reg, aT[:, di, 128 * kt:128 * (kt + 1)],
                        fwv_s[:, di, 512 * half:512 * (half + 1)],
                        start=(di == 0), stop=(di == NDC - 1)),
                        inc=(("te", 1),) if di == NDC - 1 else (),
                        after=(("dve", None),) if di == 0 else ())
                for hh in range(8):
                    h = 8 * half + hh
                    p.op("dve", lambda v, kt=kt, h=h, hh=hh, reg=reg: v.tensor_copy(
                        v2S[:, kt, 65 * h:65 * h + 64], reg[:, 64 * hh:64 * (hh + 1)]),
                        inc=(("dve", 1),),
                        after=((("te", None),) if hh == 0 else ()))
        p.barrier()
        cut("kv")

        kv_stack.close()
        d_stack = contextlib.ExitStack()
        sbd = lambda name, shape, dt: d_stack.enter_context(nc.sbuf_tensor(name, shape, dt))
        q2T = sbd("q2T", [128, NDC, QW], BF16)
        bmix = sbd("bmix", [1, FH], F32)
        bpair = sbd("bpair", [1, NDC], F32)
        bpb = sbd("bpb", [1, NDC], BF16)
        bcols = sbd("bcols", [128, NDC], F32)
        q_stack = contextlib.ExitStack()
        sbq = lambda name, shape, dt: q_stack.enter_context(nc.sbuf_tensor(name, shape, dt))
        fwq_s = sbq("fwq_s", [128, NDC, D], BF16)
        for dc in range(NDC):
            p.op("pool", lambda g, dc=dc: g.dma_start(
                out=fwq_s[:, dc, :], in_=din["fWq"][128 * dc:128 * (dc + 1), :]),
                inc=(("lw", 16),), dma=True)
        myaT = sbq("myaT", [128, NDC, QW], BF16)

        def ldwin(g):
            r = g.value_load(wsI[0:1, 0:1])
            return g.dma_start(out=myrows[:, :], in_=atn[bass.ds(r, QW), :])
        p.op("pool", ldwin, dma=True, inc=(("lg", 16),))
        p.wait("sp", "lg")
        for j in range(NDC):
            p.op("sp", lambda g, j=j: g.dma_start(
                out=myaT[:, j, :], in_=myrows[:, 128 * j:128 * (j + 1)], transpose=True),
                inc=(("ld", 16),))
        p.barrier()

        qt_sizes = [(i * 512, min(512, QW - i * 512)) for i in range((QW + 511) // 512)]
        for dc in range(NDC):
            for (q0, qn) in qt_sizes:
                reg = PAB[:, 0:512]
                for di in range(NDC):
                    p.op("te", lambda t, dc=dc, di=di, q0=q0, qn=qn, reg=reg: t.matmul(
                        reg[:, 0:qn], fwq_s[:, di, 128 * dc:128 * (dc + 1)],
                        myaT[:, di, q0:q0 + qn], start=(di == 0), stop=(di == NDC - 1)),
                        inc=(("te", 1),) if di == NDC - 1 else (),
                        after=(("dve", None), ("act", None)) if di == 0 else ())
                p.op("dve", lambda v, dc=dc, q0=q0, qn=qn, reg=reg: v.tensor_copy(
                    q2T[:, dc, q0:q0 + qn], reg[:, 0:qn]), inc=(("dve", 1),),
                    after=(("te", None),))
                p.op("act", lambda a, qn=qn, reg=reg: a.activation(
                    out=sqb[:, 0:qn], in_=reg[:, 0:qn], func=AF.Square),
                    inc=(("act", 1),), after=(("dve", None),))
                qi = q0 // 512
                for hh in range(2):
                    h = 2 * dc + hh
                    p.op("te", lambda t, hh=hh, qn=qn: t.matmul(
                        PS_[0:1, 0:qn], one_b[0:64, 0:1] if hh == 0 else one_b[64:128, 0:1],
                        sqb[0:64, 0:qn] if hh == 0 else sqb[64:128, 0:qn],
                        start=True, stop=True), inc=(("te", 1),),
                        after=(("act", None), ("dve", None)))
                    p.op("dve", lambda v, h=h, qi=qi, qn=qn: v.tensor_reduce(
                        out=nrm2[0:1, h, qi:qi + 1], in_=PS_[0:1, 0:qn], axis=AX.X,
                        op=ALU.max), inc=(("dve", 1),), after=(("te", None),))
        NQB = len(qt_sizes)
        for h in range(FH):
            p.op("dve", lambda v, h=h: v.tensor_reduce(
                out=qmx2[0:1, h:h + 1], in_=nrm2[0:1, h, 0:NQB], axis=AX.X, op=ALU.max),
                inc=(("dve", 1),))
        p.op("dve", lambda v: v.tensor_mul(bmix[0:1, :], qmx2[0:1, :], kmx2[0:1, :]),
             inc=(("dve", 1),))
        p.op("act", lambda a: a.activation(out=bmix[0:1, :], in_=bmix[0:1, :], func=AF.Sqrt,
                                           scale=1.0 / 64.0), inc=(("act", 1),),
             after=(("dve", None),))
        p.op("dve", lambda v: v.tensor_max(
            bpair[0:1, :], bmix.rearrange("o (dc hh) -> o dc hh", hh=2)[0:1, :, 0],
            bmix.rearrange("o (dc hh) -> o dc hh", hh=2)[0:1, :, 1]),
            inc=(("dve", 1),), after=(("act", None),))
        p.op("dve", lambda v: v.tensor_scalar(
            out=bpair[0:1, :], in0=bpair[0:1, :], scalar1=-1.0, scalar2=None, op0=ALU.mult),
            inc=(("dve", 1),))
        p.op("dve", lambda v: v.tensor_copy(bpb[0:1, :], bpair[0:1, :]), inc=(("dve", 1),))
        p.op("te", lambda t: t.matmul(PS_[:, 0:NDC], onerow_b[0:1, :], bpb[0:1, :],
                                      start=True, stop=True), inc=(("te", 1),),
             after=(("dve", None),))
        p.op("dve", lambda v: v.tensor_copy(bcols[:, :], PS_[:, 0:NDC]), inc=(("dve", 1),),
             after=(("te", None),))
        p.barrier()
        cut("q2")

        q_stack.close()
        o2hT = sbd("o2hT", [128, NDC, QW], BF16)
        sc_stack = contextlib.ExitStack()
        sbs = lambda name, shape, dt: sc_stack.enter_context(nc.sbuf_tensor(name, shape, dt))
        eS = sbs("eS", [128, 2, 1024], BF16)
        k2blk = sbs("k2blk", [128, 2, 128], BF16)
        drow = sbs("drow", [128, 2, 512], F32)
        dr2 = sbs("dr2", [2, 512], BF16)
        rcpbE = sbs("rcpbE", [128, 512], F32)
        rcpbO = sbs("rcpbO", [128, 512], F32)
        oddst = sbs("oddst", [128, 512], BF16)
        schist = {}
        expv = {}
        for hp in range(NDC):
            for (q0, qn) in qt_sizes:
                for kt in range(NKT + 1):
                    if kt < NKT:
                        bi = kt % 2
                        par = "lda" if bi == 0 else "ldb"
                        guard = (("te", schist[kt - 2]),) if kt >= 2 else ()
                        p.op("sp", lambda g, hp=hp, kt=kt, bi=bi: g.dma_start(
                            out=k2blk[:, bi, :], in_=k2d[hp, :, 128 * kt:128 * (kt + 1)]),
                            inc=((par, 16),), after=guard)
                        ldk = p.c(par)
                        pS = PAB if kt % 2 == 0 else PCD
                        sguard = ((("act", expv[kt - 2]),) if kt >= 2 else
                                  ((("act", None),) if (hp, q0) != (0, 0) else ()))
                        p.op("te", lambda t, bi=bi, hp=hp, q0=q0, qn=qn, pS=pS, par=par: t.matmul(
                            pS[:, 0:qn], k2blk[0:64, bi, :], q2T[0:64, hp, q0:q0 + qn],
                            start=True, stop=True, tile_position=(0, 0)),
                            inc=(("te", 1),), after=((par, ldk),) + sguard)
                        p.op("te", lambda t, bi=bi, hp=hp, q0=q0, qn=qn, pS=pS: t.matmul(
                            pS[:, 512:512 + qn], k2blk[64:128, bi, :],
                            q2T[64:128, hp, q0:q0 + qn],
                            start=True, stop=True, tile_position=(64, 0)), inc=(("te", 1),))
                        schist[kt] = p.c("te")
                        p.op("act", lambda a, hp=hp, qn=qn, pS=pS, bi=bi: a.activation(
                            out=eS[:, bi, 0:2 * qn],
                            in_=pS[:, 0:1024].rearrange("pp (a b) -> pp a b", a=2)[:, :, 0:qn],
                            func=AF.Exp, scale=0.125, bias=bcols[:, hp:hp + 1]),
                            inc=(("act", 1),), after=(("te", schist[kt]),))
                        expv[kt] = p.c("act")
                    if kt >= 1:
                        ka = kt - 1
                        for par in range(2):
                            av = PE_ if par == 0 else PF_
                            p.op("te", lambda t, hp=hp, ka=ka, par=par, av=av, qn=qn: t.matmul(
                                av[0:65, 0:qn],
                                v2S[:, ka, 65 * (2 * hp + par):65 * (2 * hp + par) + 65],
                                eS[:, ka % 2, qn * par:qn * (par + 1)],
                                start=(ka == 0), stop=(ka == NKT - 1)),
                                inc=(("te", 1),),
                                after=(("act", expv[ka]),) + ((("dve", None),) if ka == 0 else ()))
                tev = p.c("te")
                for par in range(2):
                    av = PE_ if par == 0 else PF_
                    p.op("dve", lambda v, par=par, av=av, qn=qn: v.tensor_copy(
                        drow[64:65, par, 0:qn], av[64:65, 0:qn]), inc=(("dve", 1),),
                        after=(("te", tev), ("lg", None)))
                    p.op("dve", lambda v, par=par, qn=qn: v.reciprocal(
                        drow[64:65, par, 0:qn], drow[64:65, par, 0:qn]), inc=(("dve", 1),))
                    p.op("pool", lambda g, par=par, qn=qn: g.dma_start(
                        out=dr2[par:par + 1, 0:qn], in_=drow[64:65, par, 0:qn]),
                        inc=(("lg", 16),), after=(("dve", None), ("te", None)), dma=True)
                ldd = p.c("lg")
                p.op("te", lambda t, qn=qn: t.matmul(
                    PS_[:, 0:qn], sel2_s[:, 0:128], dr2[:, 0:qn], start=True, stop=True),
                    inc=(("te", 1),), after=(("lg", ldd), ("dve", None)))
                p.op("dve", lambda v, qn=qn: v.tensor_copy(rcpbE[:, 0:qn], PS_[:, 0:qn]),
                     inc=(("dve", 1),), after=(("te", None),))
                p.op("te", lambda t, qn=qn: t.matmul(
                    PS_[:, 0:qn], sel2_s[:, 128:256], dr2[:, 0:qn], start=True, stop=True),
                    inc=(("te", 1),), after=(("dve", None),))
                p.op("dve", lambda v, qn=qn: v.tensor_copy(rcpbO[:, 0:qn], PS_[:, 0:qn]),
                     inc=(("dve", 1),), after=(("te", None),))
                p.op("dve", lambda v, hp=hp, q0=q0, qn=qn: v.tensor_mul(
                    o2hT[0:64, hp, q0:q0 + qn], PE_[0:64, 0:qn], rcpbE[0:64, 0:qn]),
                    inc=(("dve", 1),))
                p.op("dve", lambda v, qn=qn: v.tensor_mul(
                    oddst[0:64, 0:qn], PF_[0:64, 0:qn], rcpbO[0:64, 0:qn]),
                    inc=(("dve", 1),), after=(("st", None),))
                p.op("sp", lambda g, hp=hp, q0=q0, qn=qn: g.dma_start(
                    out=o2hT[64:128, hp, q0:q0 + qn], in_=oddst[0:64, 0:qn]),
                    inc=(("st", 16),), after=(("dve", None),))
                cut(("sc", hp, q0))
        cut("scores")
        if "o2" in taps:
            o2f = sbs("o2f", [128, QW], F32)
            for dc in range(NDC):
                p.op("dve", lambda v, dc=dc: v.tensor_copy(o2f[:, :], o2hT[:, dc, :]),
                     inc=(("dve", 1),))
                p.op("sp", lambda g, dc=dc: g.dma_start(
                    out=tap["o2"][128 * dc:128 * (dc + 1), :], in_=o2f[:, :]),
                    inc=(("st", 16),), after=(("dve", None),))
                p.barrier()

        sc_stack.close()
        wo_stack = contextlib.ExitStack()
        sbw = lambda name, shape, dt: wo_stack.enter_context(nc.sbuf_tensor(name, shape, dt))
        fwo_s = sbw("fwo_s", [128, NDC, D], BF16)
        for dc in range(NDC):
            p.op("pool", lambda g, dc=dc: g.dma_start(
                out=fwo_s[:, dc, :], in_=din["fWo"][128 * dc:128 * (dc + 1), :]),
                inc=(("lw", 16),), dma=True,
                after=(("lg", None), ("lda", None), ("ldb", None), ("st", None)))
        p.wait("te", "lw")
        ffnT = sbw("ffnT", [128, NDC, QW], BF16)
        fsl = sbw("fsl", [128, 2, D], BF16)
        for dc in range(NDC):
            for (q0, qn) in qt_sizes:
                reg = PAB[:, 0:512]
                for di in range(NDC):
                    p.op("te", lambda t, dc=dc, di=di, q0=q0, qn=qn, reg=reg: t.matmul(
                        reg[:, 0:qn], fwo_s[:, di, 128 * dc:128 * (dc + 1)],
                        o2hT[:, di, q0:q0 + qn], start=(di == 0), stop=(di == NDC - 1)),
                        inc=(("te", 1),) if di == NDC - 1 else (),
                        after=(("dve", None),) if di == 0 else ())
                p.op("dve", lambda v, dc=dc, q0=q0, qn=qn, reg=reg: v.tensor_copy(
                    ffnT[:, dc, q0:q0 + qn], reg[:, 0:qn]), inc=(("dve", 1),),
                    after=(("te", None),))
        p.barrier()
        fst_hist = []
        for tb in range(QW // 128):
            spar = "sta" if tb % 2 == 0 else "stb"
            for dc in range(NDC):
                p.op("te", lambda t, tb=tb, dc=dc: t.transpose(
                    PT_[:, 128 * dc:128 * (dc + 1)], ffnT[:, dc, 128 * tb:128 * (tb + 1)],
                    id_b[:, :]), inc=(("te", 1),),
                    after=(("dve", None),))
                guard = (("sta" if tb % 2 == 0 else "stb", fst_hist[tb - 2]),) if (tb >= 2 and dc == 0) else ()
                p.op("dve", lambda v, tb=tb, dc=dc: v.tensor_copy(
                    fsl[:, tb % 2, 128 * dc:128 * (dc + 1)], PT_[:, 128 * dc:128 * (dc + 1)]),
                    inc=(("dve", 1),), after=(("te", None),) + guard)
            p.op("sp", lambda g, tb=tb: g.dma_start(
                out=o2d[128 * tb:128 * (tb + 1), :], in_=fsl[:, tb % 2, :]),
                inc=((spar, 16),), after=(("dve", None),))
            fst_hist.append(p.c(spar))
        p.barrier()
        p.barrier()
        cut("wo")

        wo_stack.close()
        d_stack.close()
        fin_stack = contextlib.ExitStack()
        sbf = lambda name, shape, dt: fin_stack.enter_context(nc.sbuf_tensor(name, shape, dt))
        y1t = sbf("y1t", [128, MYT // 128, D], F32)
        gbuf = sbf("gbuf", [128, D], BF16)
        idxc = sbf("idxc", [128, 2], I32)
        srl16 = sbf("srl16", [MYT // 128, 128], F32)
        fcol = sbf("fcol", [128, 1], F32)
        ytmp = sbf("ytmp", [128, D], F32)
        yout = sbf("yout", [128, D], F32)
        NJT = MYT // 128

        def rt_load_srel(g):
            r = g.value_load(myq_s[0:1, 0:1])
            return g.dma_start(out=srl16[:, :], in_=srelD[bass.ts(r, NJT), :])
        p.op("pool", rt_load_srel, dma=True, inc=(("lg", 16),), after=(("st", None),))

        def rt_load_y1(g):
            r = g.value_load(myq_s[0:1, 0:1])
            return g.dma_start(
                out=y1t[:, :, :],
                in_=y1f32d[bass.ds(r * MYT, MYT), :].rearrange("(j pp) d -> pp j d", pp=128))
        p.op("pool", rt_load_y1, dma=True, inc=(("lg", 16),), after=(("st", None),))
        p.op("te", lambda t: t.transpose(PS_[:, 64:64 + NJT], srl16[:, :],
                                         id_f[0:NJT, 0:NJT]), inc=(("te", 1),),
             after=(("lg", None),))
        p.op("dve", lambda v: v.tensor_copy(srelW[:, :], PS_[:, 64:64 + NJT]),
             inc=(("dve", 1),), after=(("te", None),))
        if "dbg" in taps:
            p.op("dve", lambda v: v.tensor_copy(junkN[:, 0:MYT // 128], srelW[:, :]),
                 inc=(("dve", 1),))
            p.op("sp", lambda g: g.dma_start(out=tap["dbg"][:, 0:MYT // 128],
                                             in_=junkN[:, 0:MYT // 128]),
                 inc=(("st", 16),), after=(("dve", None),))
        p.barrier()

        for j in range(NJT):
            ldyv = p.c("lg")
            gguard = (("sg", 16 * (j - 1)),) if j >= 2 else ()
            p.op("dve", lambda v, j=j: v.tensor_copy(idxc[:, j % 2:j % 2 + 1], srelW[:, j:j + 1]),
                 inc=(("dve", 1),), after=gguard)
            p.op("dve", lambda v: v.memset(gbuf[:, :], 0.0), inc=(("dve", 1),))
            dvg = p.c("dve")
            if j == 0:
                p.op("pool", lambda g: regh.__setitem__("bc2", g.to_reg(QW - 1)), dma=True)
            p.op("pool", lambda g, j=j: g.indirect_dma_start(
                out=gbuf[:, :], out_offset=None, in_=o2d[:, :],
                in_offset=IndirectOffsetOnAxis(idxc[:, j % 2:j % 2 + 1], 0),
                bounds_check=regh["bc2"],
                oob_is_err=False), inc=(("sg", 16),), after=(("dve", dvg),), dma=True)
            sgv2 = p.c("sg")
            if j == 0 and "dbg" in taps:
                p.op("dve", lambda v: v.tensor_copy(junk[:, 0:1], idxc[:, 0:1]),
                     inc=(("dve", 1),))
                p.op("dve", lambda v: v.tensor_copy(junk[:, 1:41], gbuf[:, 0:40]),
                     inc=(("dve", 1),), after=(("sg", sgv2),))
                p.op("sp", lambda g: g.dma_start(out=tap["dbg"][:, 0:41], in_=junk[:, 0:41]),
                     inc=(("st", 16),), after=(("dve", None),))
            p.op("dve", lambda v, j=j: v.tensor_mul(junk[:, :], y1t[:, j, :], wgbc[:, :]),
                 inc=(("dve", 1),),
                 after=(("lg", ldyv),) + ((("st", None),) if (j == 0 and "dbg" in taps) else ()))
            p.op("dve", lambda v: v.tensor_reduce(
                out=colA[:, 0:1], in_=junk[:, :], axis=AX.X, op=ALU.add),
                inc=(("dve", 1),))
            p.op("act", lambda a: a.activation(out=fcol[:, 0:1], in_=colA[:, 0:1],
                                               func=AF.Sigmoid), inc=(("act", 1),),
                 after=(("dve", None),))
            p.op("dve", lambda v, j=j: v.scalar_tensor_tensor(
                out=ytmp[:, :], in0=gbuf[:, :], scalar=fcol[:, 0:1], in1=y1t[:, j, :],
                op0=ALU.mult, op1=ALU.add), inc=(("dve", 1),),
                after=(("sg", sgv2), ("act", None)))
            p.op("dve", lambda v: v.tensor_reduce(
                out=colA[:, 0:1], in_=ytmp[:, :], axis=AX.X, op=ALU.add), inc=(("dve", 1),))
            p.op("act", lambda a: a.activation(
                out=junk[:, :], in_=ytmp[:, :], func=AF.Square, accum_out=colB[:, 0:1]),
                inc=(("act", 1),), after=(("dve", None),))
            p.op("dve", lambda v: v.tensor_scalar(
                out=colA[:, 0:1], in0=colA[:, 0:1], scalar1=1.0 / D, scalar2=None,
                op0=ALU.mult), inc=(("dve", 1),), after=(("act", None),))
            p.op("dve", lambda v: v.tensor_mul(colC[:, 0:1], colA[:, 0:1], colA[:, 0:1]),
                 inc=(("dve", 1),))
            p.op("dve", lambda v: v.scalar_tensor_tensor(
                out=colB[:, 0:1], in0=colB[:, 0:1], scalar=1.0 / D, in1=colC[:, 0:1],
                op0=ALU.mult, op1=ALU.subtract), inc=(("dve", 1),))
            p.op("act", lambda a: a.activation(
                out=colB[:, 0:1], in_=colB[:, 0:1], func=AF.Sqrt, bias=eps_c[:, 0:1]),
                inc=(("act", 1),), after=(("dve", None),))
            p.op("dve", lambda v: v.reciprocal(colB[:, 0:1], colB[:, 0:1]),
                 inc=(("dve", 1),), after=(("act", None),))
            p.op("dve", lambda v: v.scalar_tensor_tensor(
                out=colC[:, 0:1], in0=colA[:, 0:1], scalar=-1.0, in1=colB[:, 0:1],
                op0=ALU.mult, op1=ALU.mult), inc=(("dve", 1),))
            if j == 0 and "dbg" in taps:
                p.op("dve", lambda v: v.tensor_copy(junkN[:, 0:4], ytmp[:, 0:4]),
                     inc=(("dve", 1),))
                p.op("dve", lambda v: v.tensor_copy(junkN[:, 4:5], fcol[:, 0:1]),
                     inc=(("dve", 1),))
                p.op("dve", lambda v: v.tensor_copy(junkN[:, 5:6], colB[:, 0:1]),
                     inc=(("dve", 1),))
                p.op("dve", lambda v: v.tensor_copy(junkN[:, 6:7], colC[:, 0:1]),
                     inc=(("dve", 1),))
            opar = "outa" if j % 2 == 0 else "outb"
            prevpar = "outb" if j % 2 == 0 else "outa"
            oguard = ((prevpar, p.c(prevpar)),) if j >= 1 else ()
            p.op("act", lambda a, j=j: a.activation(
                out=yout[:, :], in_=ytmp[:, :], func=AF.Identity, scale=colB[:, 0:1],
                bias=colC[:, 0:1]), inc=(("act", 1),), after=(("dve", None),) + oguard)
            p.op("sp", lambda g, j=j: g.dma_start(
                out=out_ext[128 * j:128 * (j + 1), :], in_=yout[:, :]),
                inc=((opar, 16),), after=(("act", None),))
        if "dbg" in taps:
            p.barrier()
            p.op("sp", lambda g: g.dma_start(out=tap["dbg"][:, 0:8], in_=junkN[:, 0:8]),
                 inc=(("st", 16),), after=(("dve", None),))
        for e in ENGS:
            p.wait(e, "outa")
            p.wait(e, "outb")
            p.wait(e, "st")
            p.wait(e, "sta")
            p.wait(e, "stb")
            p.wait(e, "sg")

    except _Cut:
        p.barrier()
    try:
        fin_stack.close()
    except Exception:
        pass
    with nc.Block() as block:
        @block.tensor
        def _(te):
            for f in p.q["te"]:
                f(te)

        @block.scalar
        def _(a):
            for f in p.q["act"]:
                f(a)

        @block.vector
        def _(v):
            for f in p.q["dve"]:
                f(v)

        @block.gpsimd
        def _(g):
            for f in p.q["pool"]:
                f(g)

        @block.sync
        def _(s):
            for f in p.q["sp"]:
                f(s)

    try:
        stack.close()
    except Exception:
        pass
    return nc, tap


def _host_inputs(inp):
    x = np.ascontiguousarray(np.asarray(inp["L0_attn_hidden_states"], np.float32))
    mk = {}
    mk["wqT"] = np.ascontiguousarray(np.asarray(inp["L0_attn_w_q"], np.float32).T)
    mk["wkT"] = np.ascontiguousarray(np.asarray(inp["L0_attn_w_k"], np.float32).T)
    mk["wvT"] = np.ascontiguousarray(np.asarray(inp["L0_attn_w_v"], np.float32).T)
    mk["wfT"] = np.ascontiguousarray(np.asarray(inp["L0_attn_w_f"], np.float32).T)
    mk["woT"] = np.ascontiguousarray(np.asarray(inp["L0_attn_w_o"], np.float32).T)
    mk["fWq"] = np.ascontiguousarray(np.asarray(inp["L0_ffn_Wq"], np.float32))
    mk["fWk"] = np.ascontiguousarray(np.asarray(inp["L0_ffn_Wk"], np.float32))
    mk["fWv"] = np.ascontiguousarray(np.asarray(inp["L0_ffn_Wv"], np.float32))
    mk["fWo"] = np.ascontiguousarray(np.asarray(inp["L0_ffn_Wo"], np.float32))
    mk["wg"] = np.ascontiguousarray(np.asarray(inp["L0_ffn_Wg"], np.float32).reshape(1, D))
    mk["nw"] = np.ascontiguousarray(
        np.asarray(inp["L0_attn_norm_weight"], np.float32).reshape(NDC, 128).T)
    mk["gn"] = np.ascontiguousarray(
        np.asarray(inp["L0_attn_gnorm_weight"], np.float32).reshape(NDC, 128).T)
    tpi = np.arange(L)[:, None]
    ti = np.arange(L)[None, :]
    mk["mcum"] = np.ascontiguousarray(
        (tpi <= ti).astype(np.float32) - (tpi <= L // 2 - 1).astype(np.float32))
    mk["trim"] = np.ascontiguousarray((tpi <= ti).astype(np.float32))
    mk["ident"] = np.eye(128, dtype=np.float32)
    c64 = np.arange(T // 128)
    mk["lt64"] = np.ascontiguousarray((c64[:, None] < c64[None, :]).astype(np.float32))
    c128 = np.arange(128)
    mk["lt128"] = np.ascontiguousarray((c128[:, None] < c128[None, :]).astype(np.float32))
    s2 = np.zeros((2, 256), np.float32)
    s2[0, 0:64] = 1.0
    s2[1, 128:192] = 1.0
    mk["sel2"] = s2
    in_maps = []
    for core in range(8):
        b, q = core // 4, core % 4
        m = dict(mk)
        m["x_b"] = np.ascontiguousarray(x[b])
        m["myq16"] = np.array([[q]], np.int32)
        in_maps.append(m)
    return in_maps


def kernel(**inputs):
    nc, _ = build_program(TAPS)
    in_maps = _host_inputs(inputs)
    res = run_bass_kernel_spmd(nc, in_maps, core_ids=list(range(8)))
    parts = [res.results[c]["out_mine"] for c in range(8)]
    out = np.concatenate(parts, axis=0).reshape(B, T, D)
    return np.ascontiguousarray(out.astype(np.float32))


# version 12
# speedup vs baseline: 1.0474x; 1.0330x over previous
import contextlib
import numpy as np
import concourse.bass as bass
import concourse.mybir as mybir
from concourse.bass_utils import run_bass_kernel_spmd
from concourse.bass_core_rust import IndirectOffsetOnAxis

F32 = mybir.dt.float32
BF16 = mybir.dt.bfloat16
I32 = mybir.dt.int32
AF = mybir.ActivationFunctionType
ALU = mybir.AluOpType
AX = mybir.AxisListType

B, D = 2, 1024
T = 8192
H, DK, DV, M = 4, 256, 256, 256
L = 256
NDC = D // 128
FH, DHD = 16, 64
EPS = 1e-5
GN = 16.0
BIS_ITERS = 48
TAPS = ()
STOP_AFTER = None
PARANOID = False
ENGS = ("te", "act", "dve", "pool", "sp")


class _Cut(Exception):
    pass


class PB:
    def __init__(self, nc, stack):
        self.nc = nc
        self.stack = stack
        self.q = {e: [] for e in ENGS}
        self.sems = {}
        self.cnt = {}
        self.hw = {e: {} for e in ENGS}

    def sem(self, name):
        if name not in self.sems:
            self.sems[name] = self.stack.enter_context(self.nc.semaphore(name))
            self.cnt[name] = 0
        return self.sems[name]

    def op(self, eng, fn, inc=(), after=(), dma=False):
        if eng in ("act", "dve", "pool") and not dma:
            after = tuple(after) + ((eng, None),)
            if not any(s == eng for s, _ in inc):
                inc = tuple(inc) + ((eng, 1),)
        hw = self.hw[eng]
        waits = []
        for s, v in after:
            vv = self.cnt[s] if v is None else v
            if vv <= 0 or vv <= hw.get(s, 0):
                continue
            hw[s] = vv
            waits.append((self.sems[s], vv))
        waits = tuple(waits)
        incs = tuple((self.sems[s], n) for s, n in inc)

        def play(h, waits=waits, incs=incs, fn=fn):
            for sh, v in waits:
                h.wait_ge(sh, v)
            ins = fn(h)
            for sh, n in incs:
                ins.then_inc(sh, n)

        self.q[eng].append(play)
        tok = []
        for s, n in inc:
            self.cnt[s] += n
            tok.append((s, self.cnt[s]))
        if PARANOID and eng != "_":
            self.barrier()
        return tuple(tok)

    def wait(self, eng, name, val=None):
        v = self.cnt[name] if val is None else val
        if v > 0 and v > self.hw[eng].get(name, 0):
            self.hw[eng][name] = v
            self.q[eng].append(lambda h, sh=self.sems[name], v=v: h.wait_ge(sh, v))

    def barrier(self):
        for e in ENGS:
            for s in self.sems:
                self.wait(e, s)

    def c(self, name):
        return self.cnt[name]


def build_program(taps=()):
    def cut(name):
        if STOP_AFTER == name:
            raise _Cut()
    NCH = T // L
    CSEL = T // 2
    QW = T // 8 + 128
    MYT = T // 4
    NKT = CSEL // 128

    nc = bass.Bass("TRN2", target_bir_lowering=False, debug=False)
    stack = contextlib.ExitStack()
    p = PB(nc, stack)
    for s in ("ld", "lda", "ldb", "lg", "lw", "te", "act", "dve", "pool", "st", "sg", "sga", "sgb", "sta", "stb", "out", "outa", "outb"):
        p.sem(s)

    din = {}
    for nm, shp, dt in (
        ("x_b", [T, D], F32), ("wqT", [D, D], F32), ("wkT", [D, D], F32),
        ("wvT", [D, D], F32), ("wfT", [D, D], F32), ("woT", [D, D], F32),
        ("fWq", [D, D], F32), ("fWk", [D, D], F32), ("fWv", [D, D], F32),
        ("fWo", [D, D], F32), ("wg", [1, D], F32), ("nw", [128, NDC], F32),
        ("gn", [128, NDC], F32), ("mcum", [L, L], F32), ("trim", [L, L], F32),
        ("ident", [128, 128], F32), ("lt64", [T // 128, T // 128], F32),
        ("lt128", [128, 128], F32), ("sel2", [2, 256], F32),
        ("myq16", [1, 1], I32),
    ):
        din[nm] = nc.dram_tensor(nm, shp, dt, kind="ExternalInput").ap()
    out_ext = nc.dram_tensor("out_mine", [MYT, D], F32, kind="ExternalOutput").ap()

    tap = {}
    for nm, shp in (("qT", [D, T]), ("kvf", [3, T, D]), ("o", [T, D]),
                    ("y1", [T, D]), ("logits", [128, T // 128]),
                    ("theta", [128, 1]), ("slot", [128, T // 128]),
                    ("attn_in", [CSEL, 1032]), ("o2", [D, QW]), ("dbg", [128, 64])):
        if nm in taps:
            tap[nm] = nc.dram_tensor("tap_" + nm, shp, F32, kind="ExternalOutput").ap()

    y1d = nc.dram_tensor("y1d", [T, 1032], BF16).ap()
    y1f32d = nc.dram_tensor("y1f32d", [T, D], F32).ap()
    atn = nc.dram_tensor("atn", [CSEL, 1032], BF16).ap()
    k2d = nc.dram_tensor("k2d", [NDC, 128, CSEL], BF16).ap()
    v2d = nc.dram_tensor("v2d", [CSEL, FH * 65], BF16).ap()
    o2d = nc.dram_tensor("o2d", [QW, D], BF16).ap()
    myrows = nc.dram_tensor("myrows", [QW, 1032], BF16).ap()
    srelD = nc.dram_tensor("srelD", [T // 128, 128], F32).ap()
    cpfD = nc.dram_tensor("cpfD", [T // 128, 1], F32).ap()

    sb = lambda name, shape, dt: stack.enter_context(nc.sbuf_tensor(name, shape, dt))

    try:

        id_b = sb("id_b", [128, 128], BF16)
        eps_c = sb("eps_c", [128, 1], F32)
        one_f = sb("one_f", [128, 1], F32)
        one_b = sb("one_b", [128, 1], BF16)
        onerow_b = sb("onerow_b", [1, 128], BF16)
        onesrow_f = sb("onesrow_f", [1, 128], F32)
        wgbc = sb("wgbc", [128, D], F32)
        myq_s = sb("myq_s", [1, 1], I32)
        junk = sb("junk", [128, D], BF16)
        lgT = sb("lgT", [128, T // 128], F32)
        colA = sb("colA", [128, 1], F32)
        colB = sb("colB", [128, 1], F32)
        colC = sb("colC", [128, 1], F32)
        ab_stack = contextlib.ExitStack()
        sba = lambda name, shape, dt: ab_stack.enter_context(nc.sbuf_tensor(name, shape, dt))
        w_q = sba("w_q", [128, NDC, D], BF16)
        w_k = sba("w_k", [128, NDC, D], BF16)
        w_v = sba("w_v", [128, NDC, D], BF16)
        w_f = sba("w_f", [128, NDC, D], BF16)
        w_o = sba("w_o", [128, NDC, D], BF16)
        nw_t = sba("nw_t", [128, NDC], F32)
        gn_t = sba("gn_t", [128, NDC], F32)
        mc_s = sba("mc_s", [128, 2, L], BF16)
        tri_s = sba("tri_s", [128, 2, L], F32)

        for (dst, src) in (
            (w_q, "wqT"), (w_k, "wkT"), (w_v, "wvT"), (w_f, "wfT"), (w_o, "woT")):
            for dc in range(NDC):
                p.op("pool", lambda g, d=dst, s=src, dc=dc: g.dma_start(
                    out=d[:, dc, :], in_=din[s][128 * dc:128 * (dc + 1), :]),
                    inc=(("lg", 16),), dma=True)
        p.op("pool", lambda g: g.dma_start(
            out=mc_s[:, :, :], in_=din["mcum"].rearrange("(tt pp) t -> pp tt t", pp=128)),
            inc=(("lg", 16),), dma=True)
        p.op("pool", lambda g: g.dma_start(out=id_b[:, :], in_=din["ident"][:, :]),
             inc=(("lg", 16),), dma=True)
        for nm, dst in (("nw", nw_t), ("gn", gn_t), ("myq16", myq_s)):
            p.op("sp", lambda g, nm=nm, dst=dst: g.dma_start(out=dst[:, :], in_=din[nm][:, :]),
                 inc=(("ld", 16),))
        p.op("sp", lambda g: g.dma_start(
            out=tri_s[:, :, :], in_=din["trim"].rearrange("(tt pp) t -> pp tt t", pp=128)),
            inc=(("ld", 16),))
        for f_, v_ in ((eps_c, EPS), (one_f, 1.0), (one_b, 1.0), (onerow_b, 1.0),
                       (onesrow_f, 1.0)):
            p.op("dve", lambda v, f_=f_, v_=v_: v.memset(f_[:, :], v_), inc=(("dve", 1),))
        p.barrier()
        for dc in range(NDC):
            p.op("dve", lambda v, dc=dc: v.tensor_scalar(
                out=w_o[:, dc, :], in0=w_o[:, dc, :], scalar1=gn_t[:, dc:dc + 1],
                scalar2=None, op0=ALU.mult), inc=(("dve", 1),))
        for wt in (w_q, w_k, w_v, w_f):
            for dc in range(NDC):
                p.op("dve", lambda v, wt=wt, dc=dc: v.tensor_scalar(
                    out=wt[:, dc, :], in0=wt[:, dc, :], scalar1=nw_t[:, dc:dc + 1],
                    scalar2=None, op0=ALU.mult), inc=(("dve", 1),))
        wg_stack = contextlib.ExitStack()
        wgrow = wg_stack.enter_context(nc.sbuf_tensor("wgrow", [1, D], F32))
        wgP = wg_stack.enter_context(nc.psum_tensor("wgP", [128, 512], F32))
        p.op("sp", lambda g: g.dma_start(out=wgrow[:, :], in_=din["wg"][:, :]),
             inc=(("ld", 16),))
        p.wait("te", "ld")
        for half in range(D // 512):
            p.op("te", lambda t, half=half: t.matmul(
                wgP[:, 0:512], onesrow_f[:, :], wgrow[:, 512 * half:512 * (half + 1)],
                start=True, stop=True), inc=(("te", 1),),
                after=(("dve", None), ("ld", None), ("lg", None)))
            p.op("dve", lambda v, half=half: v.tensor_copy(
                wgbc[:, 512 * half:512 * (half + 1)], wgP[:, 0:512]),
                inc=(("dve", 1),), after=(("te", None),))
        wg_stack.close()
        p.barrier()

        x_s = sba("x_s", [128, 3, 2, D], F32)
        hT_s = sba("hT_s", [128, NDC, L], BF16)
        qTw = sba("qTw", [128, NDC, 2, L], BF16)
        kTMw = sba("kTMw", [128, 2, 2, D], BF16)
        vTMw = sba("vTMw", [128, 2, 2, D], BF16)
        fTMw = sba("fTMw", [128, 2, 2, D], BF16)
        fabs_s = sba("fabs_s", [128, D], F32)
        stat = sba("stat", [128, 24], F32)
        LamB = sba("LamB", [128, 2, L], F32)
        EnG = sba("EnG", [128, 2, L], F32)
        Sp = sba("Sp", [128, 2, 2, L], BF16)
        SpMT = sba("SpMT", [128, 2, 2, L], BF16)
        kT_s = sba("kT_s", [128, 2, L], BF16)
        AmT = sba("AmT", [128, 2, L], BF16)
        WmT = sba("WmT", [128, 2, L], BF16)
        zS = sba("zS", [128, 2, L], F32)
        pEx = sba("pEx", [128, 2, L], BF16)
        PpTM = sba("PpTM", [128, 2, 2, L], BF16)
        PpMT = sba("PpMT", [128, 2, 2, L], BF16)
        nmx = sba("nmx", [128, 2], F32)
        den = sba("den", [128, 2], F32)
        rden = sba("rden", [128, 2, 2], F32)
        lsC = sba("lsC", [128, 2], F32)
        lsR = sba("lsR", [1, L], BF16)
        tmpS = sba("tmpS", [128, 2, M], F32)
        Hm = sba("Hm", [128, H, 2, M], BF16)
        Hv = sba("Hv", [128, H, 2, DV], BF16)
        oB = sba("oB", [128, 2, D], F32)
        onb = sba("onb", [128, 2, D], BF16)
        xnb = onb
        onT = sba("onT", [128, NDC, L], BF16)
        aoT = sba("aoT", [128, NDC, L], BF16)
        y1b = sba("y1b", [128, 2, 1032], BF16)

        PSCE = ab_stack.enter_context(nc.psum_tensor("PSCE", [128, 512], F32))
        PSCO = ab_stack.enter_context(nc.psum_tensor("PSCO", [128, 512], F32))
        PAWA = ab_stack.enter_context(nc.psum_tensor("PAWA", [128, 512], F32))
        PAWB = ab_stack.enter_context(nc.psum_tensor("PAWB", [128, 512], F32))
        PKV0 = ab_stack.enter_context(nc.psum_tensor("PKV0", [128, 512], F32))
        PKV1 = ab_stack.enter_context(nc.psum_tensor("PKV1", [128, 512], F32))
        PQW = ab_stack.enter_context(nc.psum_tensor("PQW", [128, 512], F32))
        PTX = ab_stack.enter_context(nc.psum_tensor("PTX", [128, 1024], BF16))

        p.op("pool", lambda g: g.memset(Hm[:, :, :, :], 0.0), inc=(("pool", 1),))
        p.op("pool", lambda g: g.memset(Hv[:, :, :, :], 0.0), inc=(("pool", 1),))
        p.op("pool", lambda g: g.memset(y1b[:, :, :], 0.0), inc=(("pool", 1),))
        p.barrier()
        cut("init")

        DMASEMS = frozenset(("ld", "lda", "ldb", "lg", "lw", "st", "sta", "stb",
                             "sg", "sga", "sgb", "out", "outa", "outb"))

        def TK(*toks):
            d = {}
            for t in toks:
                for s, v in (t or ()):
                    if v and (s not in d or v > d[s]):
                        d[s] = v
            return tuple(d.items())

        def _promote(tk):
            return tuple((s, None if s in DMASEMS else v) for s, v in tk)

        RW = {}

        def rw(nm):
            e = RW.get(nm)
            if e is None:
                e = RW[nm] = [(), ()]
            return e

        def mark_w(nm, tok):
            e = rw(nm)
            e[0] = tok
            e[1] = ()

        def mark_r(nm, tok):
            e = rw(nm)
            e[1] = TK(e[1], tok)

        PSUMKEYS = frozenset(("scE", "scO", "awA", "awB", "kvA", "kvB", "qw", "pt"))

        def xop(eng, fn, reads=(), writes=(), inc=(), after=(), dma=False):
            deps = [after]
            for nm in reads:
                deps.append(rw(nm)[0])
                if nm in PSUMKEYS:
                    deps.append(rw(nm)[1])
            for nm in writes:
                e = rw(nm)
                deps.append(e[0])
                deps.append(e[1])
            for s, n in inc:
                if s in DMASEMS and p.cnt[s] > 0:
                    deps.append(((s, p.cnt[s]),))
            tok = p.op(eng, fn, inc=inc, after=_promote(TK(*deps)), dma=dma)
            for nm in writes:
                mark_w(nm, tok)
            for nm in reads:
                mark_r(nm, tok)
            return tok

        def mmg(rname, emits, reads=(), after=()):
            e = rw(rname)
            deps = [after, e[0], e[1]]
            for nm in reads:
                deps.append(rw(nm)[0])
            tok = ()
            n = len(emits)
            for i, f in enumerate(emits):
                tok = p.op("te", f, inc=(("te", 1),) if i == n - 1 else (),
                           after=_promote(TK(*deps)) if i == 0 else ())
            for nm in reads:
                mark_r(nm, tok)
            mark_w(rname, tok)
            return tok

        scb = lambda h: (PSCE, "scE") if h % 2 == 0 else (PSCO, "scO")
        KVB = ((PKV0, "kvA"), (PKV1, "kvB"))
        QRr = lambda i: PQW[:, 256 * (i % 2):256 * (i % 2 + 1)]
        kvstate = [0]

        def kvreg():
            i = kvstate[0] & 1
            kvstate[0] += 1
            reg, nm = KVB[i]
            return nm, reg[:, 0:512]

        fl = lambda ap: ap.rearrange("pp a b -> pp (a b)")

        def px(c):
            sx = c % 3
            for tt in range(2):
                r0 = L * c + 128 * tt
                xop("sp", lambda g, sx=sx, tt=tt, r0=r0: g.dma_start(
                    out=x_s[:, sx, tt, :], in_=din["x_b"][r0:r0 + 128, :]),
                    writes=(("x", sx, tt),), inc=(("lda" if tt == 0 else "ldb", 16),))

        def proj_A1(c):
            sx = c % 3
            cut("pA1")
            for tt in range(2):
                xop("act", lambda a, sx=sx, tt=tt: a.activation(
                    out=junk[:, :], in_=x_s[:, sx, tt, :], func=AF.Square,
                    accum_out=stat[:, 0 + tt:1 + tt]), reads=(("x", sx, tt),),
                    writes=(("st", 0 + tt),), inc=(("act", 1),))
            for tt in range(2):
                xop("act", lambda a, tt=tt: a.activation(
                    out=stat[:, 2 + tt:3 + tt], in_=stat[:, 0 + tt:1 + tt], func=AF.Sqrt,
                    scale=1.0 / D, bias=eps_c[:, 0:1]), reads=(("st", 0 + tt),),
                    writes=(("st", 2 + tt),), inc=(("act", 1),))
            for tt in range(2):
                xop("dve", lambda v, tt=tt: v.reciprocal(
                    stat[:, 4 + tt:5 + tt], stat[:, 2 + tt:3 + tt]),
                    reads=(("st", 2 + tt),), writes=(("st", 4 + tt),), inc=(("dve", 1),))
            for tt in range(2):
                xop("act", lambda a, sx=sx, tt=tt: a.activation(
                    out=xnb[:, tt, :], in_=x_s[:, sx, tt, :], func=AF.Copy,
                    scale=stat[:, 4 + tt:5 + tt], bias=0.0),
                    reads=(("x", sx, tt), ("st", 4 + tt)), writes=(("onb", tt),),
                    inc=(("act", 1),))
            cut("pA3")

        def proj_A2(c):
            for tt in range(2):
                mmg("pt", [lambda t, tt=tt, dc=dc: t.transpose(
                    PTX[:, 128 * dc:128 * (dc + 1)],
                    xnb[:, tt, 128 * dc:128 * (dc + 1)], id_b[:, :])
                    for dc in range(NDC)], reads=(("onb", tt),))
                xop("dve", lambda v, tt=tt: v.tensor_copy(
                    hT_s[:, :, 128 * tt:128 * (tt + 1)],
                    PTX[:, 0:1024].rearrange("pp (a b) -> pp a b", a=NDC)),
                    reads=("pt",), writes=(("hT", tt),), inc=(("dve", 1),))

        def proj_f(c, tt):
            w = c % 2
            regs2 = []
            for half in range(2):
                rname, reg = kvreg()
                regs2.append((rname, reg))
                mmg(rname, [lambda t, tt=tt, half=half, di2=di2, reg=reg: t.matmul(
                    reg, hT_s[:, di2, 128 * tt:128 * (tt + 1)],
                    w_f[:, di2, 512 * half:512 * (half + 1)],
                    start=(di2 == 0), stop=(di2 == NDC - 1)) for di2 in range(NDC)],
                    reads=(("hT", tt),))
            for half, (rname, reg) in enumerate(regs2):
                sl = slice(512 * half, 512 * (half + 1))
                xop("act", lambda a, reg=reg, sl=sl: a.activation(
                    out=fabs_s[:, sl], in_=reg, func=AF.Abs),
                    reads=(rname,), writes=(("fab", half),), inc=(("act", 1),))
            for half, (rname, reg) in enumerate(regs2):
                sl = slice(512 * half, 512 * (half + 1))
                xop("act", lambda a, sl=sl: a.activation(
                    out=fabs_s[:, sl], in_=fabs_s[:, sl], func=AF.Exp, scale=-1.0),
                    reads=(("fab", half),), writes=(("fab", half),), inc=(("act", 1),))
            for half, (rname, reg) in enumerate(regs2):
                sl = slice(512 * half, 512 * (half + 1))
                xop("act", lambda a, sl=sl: a.activation(
                    out=fabs_s[:, sl], in_=fabs_s[:, sl], func=AF.Ln, bias=one_f[:, 0:1]),
                    reads=(("fab", half),), writes=(("fab", half),), inc=(("act", 1),))
            for half, (rname, reg) in enumerate(regs2):
                sl = slice(512 * half, 512 * (half + 1))
                xop("dve", lambda v, reg=reg, w=w, tt=tt, sl=sl: v.scalar_tensor_tensor(
                    out=fTMw[:, w, tt, sl], in0=reg, scalar=0.0, in1=fabs_s[:, sl],
                    op0=ALU.min, op1=ALU.subtract),
                    reads=(rname, ("fab", half)), writes=(("f", w, tt, half),),
                    inc=(("dve", 1),))

        def proj_q(c, dcs):
            w = c % 2
            for pi in range(0, len(dcs), 2):
                pair = dcs[pi:pi + 2]
                for dc in pair:
                    mmg("qw", [lambda t, dc=dc, di2=di2: t.matmul(
                        QRr(dc), w_q[:, di2, 128 * dc:128 * (dc + 1)], hT_s[:, di2, :],
                        start=(di2 == 0), stop=(di2 == NDC - 1)) for di2 in range(NDC)],
                        reads=(("hT", 0), ("hT", 1)))
                for dc in pair:
                    xop("act", lambda a, dc=dc, w=w: a.activation(
                        out=qTw[:, dc, w, :], in_=QRr(dc), func=AF.Silu),
                        reads=("qw",), writes=(("q", w, dc),), inc=(("act", 1),))

        def proj_kv(c, wt, key, tt):
            w = c % 2
            dst = kTMw if key == "k" else vTMw
            for half in range(2):
                rname, reg = kvreg()
                mmg(rname, [lambda t, tt=tt, half=half, di2=di2, reg=reg, wt=wt: t.matmul(
                    reg, hT_s[:, di2, 128 * tt:128 * (tt + 1)],
                    wt[:, di2, 512 * half:512 * (half + 1)],
                    start=(di2 == 0), stop=(di2 == NDC - 1)) for di2 in range(NDC)],
                    reads=(("hT", tt),))
                xop("act", lambda a, w=w, tt=tt, half=half, dst=dst, reg=reg: a.activation(
                    out=dst[:, w, tt, 512 * half:512 * (half + 1)], in_=reg,
                    func=AF.Silu), reads=(rname,), writes=((key, w, tt, half),),
                    inc=(("act", 1),))

        def scan_p1(c, h, last):
            w = c % 2
            hp = h % 2
            hp2 = h // 2
            mcs = slice(M * h, M * (h + 1))
            dks = slice(DK * h, DK * (h + 1))
            SC, scn = scb(h)
            mmg("pt", [lambda t, tt2=tt2, dkc2=dkc2, w=w, dks=dks: t.transpose(
                PTX[:, 128 * (2 * dkc2 + tt2):128 * (2 * dkc2 + tt2 + 1)],
                kTMw[:, w, tt2, dks.start + 128 * dkc2:dks.start + 128 * (dkc2 + 1)],
                id_b[:, :]) for tt2 in range(2) for dkc2 in range(2)],
                reads=(("k", w, 0, hp2), ("k", w, 1, hp2)))
            xop("dve", lambda v: v.tensor_copy(
                fl(kT_s[:, :, :]), PTX[:, 0:512]),
                reads=("pt",), writes=(("kT",),), inc=(("dve", 1),))
            mmg(scn, [lambda t, tt2=tt2, tp2=tp2, w=w, mcs=mcs, SC=SC: t.matmul(
                SC[:, 256 * tt2:256 * tt2 + 256],
                mc_s[:, tp2, 128 * tt2:128 * (tt2 + 1)], fTMw[:, w, tp2, mcs],
                start=(tp2 == 0), stop=(tp2 == 1))
                for tt2 in range(2) for tp2 in range(2)],
                reads=(("f", w, 0, hp2), ("f", w, 1, hp2)))
            if h == 0:
                cut("s_g")
            xop("act", lambda a, SC=SC: a.activation(
                out=fl(LamB[:, :, :]), in_=SC[:, 0:512], func=AF.Exp, scale=1.0 / GN),
                reads=(scn,), writes=(("LamB",),), inc=(("act", 1),))
            xop("act", lambda a, SC=SC: a.activation(
                out=fl(EnG[:, :, :]), in_=SC[:, 0:512], func=AF.Exp, scale=-1.0 / GN),
                reads=(scn,), writes=(("EnG",),), inc=(("act", 1),))
            xop("dve", lambda v, w=w, mcs=mcs, SC=SC: v.tensor_sub(
                zS[:, :, :], fTMw[:, w, :, mcs],
                SC[:, 0:512].rearrange("pp (a b) -> pp a b", a=2)),
                reads=(scn, ("f", w, 0, hp2), ("f", w, 1, hp2)),
                writes=(("zS",),), inc=(("dve", 1),))
            xop("act", lambda a: a.activation(
                out=fl(zS[:, :, :]), in_=fl(zS[:, :, :]), func=AF.Exp, scale=1.0 / GN),
                reads=(("zS",),), writes=(("zS",),), inc=(("act", 1),))
            xop("dve", lambda v, hp=hp: v.tensor_sub(
                fl(Sp[:, hp, :, :]), fl(EnG[:, :, :]), fl(zS[:, :, :])),
                reads=(("EnG",), ("zS",)), writes=(("Sp", hp),), inc=(("dve", 1),))
            if h == 0:
                cut("s_kt")
            mmg("awA", [lambda t, taut=taut2, dkc2=dkc2, h=h, w=w: t.matmul(
                PAWA[:, 256 * taut:256 * (taut + 1)],
                kT_s[:, dkc2, 128 * taut:128 * (taut + 1)],
                qTw[:, 2 * h + dkc2, w, :], start=(dkc2 == 0), stop=(dkc2 == 1))
                for taut2 in range(2) for dkc2 in range(2)],
                reads=(("kT",), ("q", w, 2 * h), ("q", w, 2 * h + 1)))
            xop("dve", lambda v: v.tensor_mul(
                fl(AmT[:, :, :]), PAWA[:, 0:512], fl(tri_s[:, :, :])),
                reads=("awA",), writes=(("AmT",),), inc=(("dve", 1),))
            mmg("awB",
                [lambda t, tt2=tt2, i4=i4, h=h, w=w, hp=hp: (
                    t.matmul(PAWB[:, 256 * tt2:256 * tt2 + 256],
                             qTw[:, 2 * h + i4, w, 128 * tt2:128 * (tt2 + 1)],
                             Hm[:, h, i4, :], start=(i4 == 0), stop=False)
                    if i4 < 2 else
                    t.matmul(PAWB[:, 256 * tt2:256 * tt2 + 256],
                             AmT[:, i4 - 2, 128 * tt2:128 * (tt2 + 1)],
                             Sp[:, hp, i4 - 2, :], start=False, stop=(i4 == 3)))
                 for tt2 in range(2) for i4 in range(4)],
                reads=(("q", w, 2 * h), ("q", w, 2 * h + 1), ("Hm", h),
                       ("AmT",), ("Sp", hp)))
            xop("dve", lambda v: v.tensor_mul(
                fl(zS[:, :, :]), PAWB[:, 0:512], fl(LamB[:, :, :])),
                reads=("awB", ("LamB",)), writes=(("zS",),), inc=(("dve", 1),))
            for tt in range(2):
                xop("dve", lambda v, tt=tt: v.tensor_reduce(
                    out=nmx[:, tt:tt + 1], in_=zS[:, tt, :], axis=AX.X, op=ALU.max,
                    negate=True), reads=(("zS",),), writes=(("nmx", tt),),
                    inc=(("dve", 1),))
            for tt in range(2):
                xop("act", lambda a, tt=tt: a.activation(
                    out=pEx[:, tt, :], in_=zS[:, tt, :], func=AF.Exp,
                    bias=nmx[:, tt:tt + 1], accum_out=den[:, tt:tt + 1]),
                    reads=(("zS",), ("nmx", tt)), writes=(("pEx",), ("den", tt)),
                    inc=(("act", 1),))
            xop("dve", lambda v, hp=hp: v.tensor_mul(
                fl(PpTM[:, hp, :, :]), fl(pEx[:, :, :]), fl(LamB[:, :, :])),
                reads=(("pEx",), ("LamB",)), writes=(("Pp", hp),), inc=(("dve", 1),))
            xop("dve", lambda v, hp=hp: v.reciprocal(rden[:, hp, 0:2], den[:, 0:2]),
                reads=(("den", 0), ("den", 1)), writes=(("rden", hp),), inc=(("dve", 1),))
            if h == 0:
                cut("s_z")

        def scan_p2(c, h, last):
            w = c % 2
            wn = (c + 1) % 2
            hp = h % 2
            hp2 = h // 2
            mcs = slice(M * h, M * (h + 1))
            dks = slice(DK * h, DK * (h + 1))
            SC, scn = scb(h)
            mmg("pt", [lambda t, tt2=tt2, mcx2=mcx2, base=base, hp=hp: t.transpose(
                PTX[:, 128 * (base + 2 * mcx2 + tt2):128 * (base + 2 * mcx2 + tt2 + 1)],
                Sp[:, hp, tt2, 128 * mcx2:128 * (mcx2 + 1)] if base == 0
                else PpTM[:, hp, tt2, 128 * mcx2:128 * (mcx2 + 1)],
                id_b[:, :])
                for base in (0, 4) for tt2 in range(2) for mcx2 in range(2)],
                reads=(("Sp", hp), ("Pp", hp)))
            xop("dve", lambda v, hp=hp: v.tensor_copy(fl(SpMT[:, hp, :, :]), PTX[:, 0:512]),
                reads=("pt",), writes=(("SpM", hp),), inc=(("dve", 1),))
            xop("dve", lambda v, hp=hp: v.tensor_copy(fl(PpMT[:, hp, :, :]), PTX[:, 512:1024]),
                reads=("pt",), writes=(("PpM", hp),), inc=(("dve", 1),))
            if h == 0:
                cut("s_t2")
            mmg("awA", [lambda t, taut=taut2, mcx2=mcx2, hp=hp: t.matmul(
                PAWA[:, 256 * taut:256 * (taut + 1)],
                SpMT[:, hp, mcx2, 128 * taut:128 * (taut + 1)], PpMT[:, hp, mcx2, :],
                start=(mcx2 == 0), stop=(mcx2 == 1))
                for taut2 in range(2) for mcx2 in range(2)],
                reads=(("SpM", hp), ("PpM", hp)))
            xop("dve", lambda v: v.tensor_mul(
                fl(WmT[:, :, :]), PAWA[:, 0:512], fl(tri_s[:, :, :])),
                reads=("awA",), writes=(("WmT",),), inc=(("dve", 1),))
            col_emits = []
            for mcx in range(2):
                col_emits.append(lambda t, mcx=mcx, w=w, mcs=mcs, last=last, SC=SC: t.matmul(
                    SC[:, mcx:mcx + 1],
                    fTMw[:, w, 1, mcs.start + 128 * mcx:mcs.start + 128 * (mcx + 1)],
                    one_b[:, 0:1], start=True, stop=last))
                if not last:
                    col_emits.append(lambda t, mcx=mcx, wn=wn, mcs=mcs, SC=SC: t.matmul(
                        SC[:, mcx:mcx + 1],
                        fTMw[:, wn, 0, mcs.start + 128 * mcx:mcs.start + 128 * (mcx + 1)],
                        one_b[:, 0:1], start=False, stop=True))
            col_emits.append(lambda t, w=w, mcs=mcs, last=last, SC=SC: t.matmul(
                SC[0:1, 4:4 + 256], one_b[:, 0:1], fTMw[:, w, 1, mcs],
                start=True, stop=last))
            if not last:
                col_emits.append(lambda t, wn=wn, mcs=mcs, SC=SC: t.matmul(
                    SC[0:1, 4:4 + 256], one_b[:, 0:1], fTMw[:, wn, 0, mcs],
                    start=False, stop=True))
            lamreads = (("f", w, 1, hp2),) + ((("f", wn, 0, hp2),) if not last else ())
            mmg(scn, col_emits, reads=lamreads)
            xop("act", lambda a, SC=SC: a.activation(
                out=lsC[:, 0:2], in_=SC[:, 0:2], func=AF.Exp, scale=1.0 / GN),
                reads=(scn,), writes=(("lsC",),), inc=(("act", 1),))
            xop("act", lambda a, SC=SC: a.activation(
                out=lsR[0:1, :], in_=SC[0:1, 4:4 + 256], func=AF.Exp, scale=1.0 / GN),
                reads=(scn,), writes=(("lsR",),), inc=(("act", 1),))
            mmg(scn, [lambda t, SC=SC: t.matmul(
                SC[:, 256:512], onerow_b[0:1, :], lsR[0:1, 0:256], start=True,
                stop=True)], reads=(("lsR",),))
            if h == 0:
                cut("s_l")
            mmg("awB", [lambda t, dkc=dkc2, taut2=taut2, w=w, dks=dks, hp=hp: t.matmul(
                PAWB[:, 256 * dkc:256 * (dkc + 1)],
                kTMw[:, w, taut2, dks.start + 128 * dkc:dks.start + 128 * (dkc + 1)],
                Sp[:, hp, taut2, :], start=(taut2 == 0), stop=(taut2 == 1))
                for dkc2 in range(2) for taut2 in range(2)],
                reads=(("k", w, 0, hp2), ("k", w, 1, hp2), ("Sp", hp)))
            xop("dve", lambda v, h=h: v.tensor_add(
                fl(tmpS[:, :, :]), fl(Hm[:, h, :, :]), PAWB[:, 0:512]),
                reads=("awB", ("Hm", h)), writes=(("tmp",),), inc=(("dve", 1),))
            for dkc in range(2):
                xop("dve", lambda v, dkc=dkc, h=h, SC=SC: v.tensor_mul(
                    Hm[:, h, dkc, :], tmpS[:, dkc, :], SC[:, 256:512]),
                    reads=(("tmp",), scn), writes=(("Hm", h),), inc=(("dve", 1),))
            mmg(scn,
                [lambda t, tt2=tt2, i4=i4, h=h, w=w, SC=SC, hp=hp: (
                    t.matmul(SC[:, 256 * tt2:256 * tt2 + 256],
                             PpMT[:, hp, i4, 128 * tt2:128 * (tt2 + 1)], Hv[:, h, i4, :],
                             start=(i4 == 0), stop=False)
                    if i4 < 2 else
                    t.matmul(SC[:, 256 * tt2:256 * tt2 + 256],
                             WmT[:, i4 - 2, 128 * tt2:128 * (tt2 + 1)],
                             vTMw[:, w, i4 - 2, DV * h:DV * (h + 1)], start=False,
                             stop=(i4 == 3)))
                 for tt2 in range(2) for i4 in range(4)],
                reads=(("PpM", hp), ("Hv", h), ("WmT",), ("v", w, 0, hp2),
                       ("v", w, 1, hp2)))
            for tt in range(2):
                xop("act", lambda a, tt=tt, h=h, SC=SC, hp=hp: a.activation(
                    out=oB[:, tt, DV * h:DV * (h + 1)],
                    in_=SC[:, 256 * tt:256 * tt + 256], func=AF.Copy,
                    scale=rden[:, hp, tt:tt + 1], bias=0.0),
                    reads=(scn, ("rden", hp)), writes=(("oB", tt),),
                    inc=(("act", 1),))
            mmg("awB", [lambda t, mcx=mcx2, taut2=taut2, w=w, h=h, hp=hp: t.matmul(
                PAWB[:, 256 * mcx:256 * (mcx + 1)],
                Sp[:, hp, taut2, 128 * mcx:128 * (mcx + 1)],
                vTMw[:, w, taut2, DV * h:DV * (h + 1)],
                start=(taut2 == 0), stop=(taut2 == 1))
                for mcx2 in range(2) for taut2 in range(2)],
                reads=(("Sp", hp), ("v", w, 0, hp2), ("v", w, 1, hp2)))
            xop("dve", lambda v, h=h: v.tensor_add(
                fl(tmpS[:, :, :]), fl(Hv[:, h, :, :]), PAWB[:, 0:512]),
                reads=("awB", ("Hv", h)), writes=(("tmp",),), inc=(("dve", 1),))
            for mcx in range(2):
                xop("dve", lambda v, mcx=mcx, h=h: v.tensor_scalar(
                    out=Hv[:, h, mcx, :], in0=tmpS[:, mcx, :], scalar1=lsC[:, mcx:mcx + 1],
                    scalar2=None, op0=ALU.mult),
                    reads=(("tmp",), ("lsC",)), writes=(("Hv", h),),
                    inc=(("dve", 1),))
            if h == 0:
                cut("s_o")

        def epi_pre(c, tt):
            xop("act", lambda a, tt=tt: a.activation(
                out=oB[:, tt, :], in_=oB[:, tt, :], func=AF.Silu),
                reads=(("oB", tt),), writes=(("oB", tt),), inc=(("act", 1),))
            xop("act", lambda a, tt=tt: a.activation(
                out=junk[:, :], in_=oB[:, tt, :], func=AF.Square,
                accum_out=stat[:, 6 + tt:7 + tt]), reads=(("oB", tt),),
                writes=(("st", 6 + tt),), inc=(("act", 1),))
            xop("act", lambda a, tt=tt: a.activation(
                out=stat[:, 8 + tt:9 + tt], in_=stat[:, 6 + tt:7 + tt], func=AF.Sqrt,
                scale=1.0 / D, bias=eps_c[:, 0:1]), reads=(("st", 6 + tt),),
                writes=(("st", 8 + tt),), inc=(("act", 1),))
            xop("dve", lambda v, tt=tt: v.reciprocal(
                stat[:, 10 + tt:11 + tt], stat[:, 8 + tt:9 + tt]),
                reads=(("st", 8 + tt),), writes=(("st", 10 + tt),), inc=(("dve", 1),))
            xop("act", lambda a, tt=tt: a.activation(
                out=onb[:, tt, :], in_=oB[:, tt, :], func=AF.Copy,
                scale=stat[:, 10 + tt:11 + tt], bias=0.0),
                reads=(("oB", tt), ("st", 10 + tt)), writes=(("onb", tt),),
                inc=(("act", 1),))
            mmg("pt", [lambda t, tt=tt, dc=dc: t.transpose(
                PTX[:, 128 * dc:128 * (dc + 1)], onb[:, tt, 128 * dc:128 * (dc + 1)],
                id_b[:, :]) for dc in range(NDC)], reads=(("onb", tt),))
            xop("dve", lambda v, tt=tt: v.tensor_copy(
                onT[:, :, 128 * tt:128 * (tt + 1)],
                PTX[:, 0:1024].rearrange("pp (a b) -> pp a b", a=NDC)),
                reads=("pt",), writes=(("onT", tt),), inc=(("dve", 1),))

        def epi_wo(c):
            for pi in range(0, NDC, 2):
                for dc in (pi, pi + 1):
                    mmg("qw", [lambda t, dc=dc, di2=di2: t.matmul(
                        QRr(dc), w_o[:, di2, 128 * dc:128 * (dc + 1)], onT[:, di2, :],
                        start=(di2 == 0), stop=(di2 == NDC - 1)) for di2 in range(NDC)],
                        reads=(("onT", 0), ("onT", 1)))
                for dc in (pi, pi + 1):
                    xop("dve", lambda v, dc=dc: v.tensor_copy(aoT[:, dc, :], QRr(dc)),
                        reads=("qw",), writes=(("aoT", dc),), inc=(("dve", 1),))

        def epi_fin(c, tt):
            sx = c % 3
            mmg("pt", [lambda t, tt=tt, dc=dc: t.transpose(
                PTX[:, 128 * dc:128 * (dc + 1)], aoT[:, dc, 128 * tt:128 * (tt + 1)],
                id_b[:, :]) for dc in range(NDC)],
                reads=tuple(("aoT", dc) for dc in range(NDC)))
            for half in range(2):
                xop("dve", lambda v, tt=tt, sx=sx, half=half: v.tensor_add(
                    x_s[:, sx, tt, 512 * half:512 * (half + 1)],
                    x_s[:, sx, tt, 512 * half:512 * (half + 1)],
                    PTX[:, 512 * half:512 * (half + 1)]),
                    reads=(("x", sx, tt), "pt"), writes=(("x", sx, tt),),
                    inc=(("dve", 1),))
            m = 12 + 4 * tt
            xop("dve", lambda v, tt=tt, sx=sx, m=m: v.tensor_reduce(
                out=stat[:, m:m + 1], in_=x_s[:, sx, tt, :], axis=AX.X, op=ALU.add),
                reads=(("x", sx, tt),), writes=(("st", m),), inc=(("dve", 1),))
            xop("act", lambda a, tt=tt, sx=sx, m=m: a.activation(
                out=junk[:, :], in_=x_s[:, sx, tt, :], func=AF.Square,
                accum_out=stat[:, m + 1:m + 2]), reads=(("x", sx, tt),),
                writes=(("st", m + 1),), inc=(("act", 1),))
            xop("dve", lambda v, m=m: v.tensor_scalar(
                out=stat[:, m:m + 1], in0=stat[:, m:m + 1], scalar1=1.0 / D,
                scalar2=None, op0=ALU.mult),
                reads=(("st", m),), writes=(("st", m),), inc=(("dve", 1),))
            xop("dve", lambda v, m=m: v.tensor_mul(
                stat[:, m + 2:m + 3], stat[:, m:m + 1], stat[:, m:m + 1]),
                reads=(("st", m),), writes=(("st", m + 2),), inc=(("dve", 1),))
            xop("dve", lambda v, m=m: v.scalar_tensor_tensor(
                out=stat[:, m + 1:m + 2], in0=stat[:, m + 1:m + 2], scalar=1.0 / D,
                in1=stat[:, m + 2:m + 3], op0=ALU.mult, op1=ALU.subtract),
                reads=(("st", m + 1), ("st", m + 2)), writes=(("st", m + 1),),
                inc=(("dve", 1),))
            xop("act", lambda a, m=m: a.activation(
                out=stat[:, m + 1:m + 2], in_=stat[:, m + 1:m + 2], func=AF.Sqrt,
                bias=eps_c[:, 0:1]), reads=(("st", m + 1),), writes=(("st", m + 1),),
                inc=(("act", 1),))
            xop("dve", lambda v, m=m: v.reciprocal(
                stat[:, m + 3:m + 4], stat[:, m + 1:m + 2]),
                reads=(("st", m + 1),), writes=(("st", m + 3),), inc=(("dve", 1),))
            xop("dve", lambda v, m=m: v.scalar_tensor_tensor(
                out=stat[:, m + 2:m + 3], in0=stat[:, m:m + 1], scalar=-1.0,
                in1=stat[:, m + 3:m + 4], op0=ALU.mult, op1=ALU.mult),
                reads=(("st", m), ("st", m + 3)), writes=(("st", m + 2),),
                inc=(("dve", 1),))
            xop("act", lambda a, tt=tt, sx=sx, m=m: a.activation(
                out=x_s[:, sx, tt, :], in_=x_s[:, sx, tt, :], func=AF.Identity,
                scale=stat[:, m + 3:m + 4], bias=stat[:, m + 2:m + 3]),
                reads=(("x", sx, tt), ("st", m + 3), ("st", m + 2)),
                writes=(("x", sx, tt),), inc=(("act", 1),))
            xop("dve", lambda v, tt=tt, sx=sx: v.tensor_copy(
                y1b[:, tt, 0:D], x_s[:, sx, tt, :]),
                reads=(("x", sx, tt),), writes=(("y1b", tt),), inc=(("dve", 1),))
            r0 = L * c + 128 * tt
            xop("sp", lambda g, tt=tt, r0=r0, sx=sx: g.dma_start(
                out=y1f32d[r0:r0 + 128, :], in_=x_s[:, sx, tt, :]),
                reads=(("x", sx, tt),), inc=(("outa" if tt == 0 else "outb", 16),))
            if "y1" in taps:
                xop("sp", lambda g, tt=tt, r0=r0, sx=sx: g.dma_start(
                    out=tap["y1"][r0:r0 + 128, :], in_=x_s[:, sx, tt, :]),
                    reads=(("x", sx, tt),), inc=(("st", 16),))
            xop("dve", lambda v, tt=tt, sx=sx: v.tensor_mul(
                x_s[:, sx, tt, :], x_s[:, sx, tt, :], wgbc[:, :]),
                reads=(("x", sx, tt),), writes=(("x", sx, tt),), inc=(("dve", 1),))
            xop("dve", lambda v, tt=tt, sx=sx, c=c: v.tensor_reduce(
                out=lgT[:, 2 * c + tt:2 * c + tt + 1], in_=x_s[:, sx, tt, :], axis=AX.X,
                op=ALU.add), reads=(("x", sx, tt),), writes=(("lgT",),),
                inc=(("dve", 1),))
            xop("dve", lambda v, tt=tt, c=c: v.tensor_copy(
                y1b[:, tt, D:D + 1], lgT[:, 2 * c + tt:2 * c + tt + 1]),
                reads=(("lgT",),), writes=(("y1b", tt),), inc=(("dve", 1),))
            xop("sp", lambda g, tt=tt, r0=r0: g.dma_start(
                out=y1d[r0:r0 + 128, :], in_=y1b[:, tt, :]),
                reads=(("y1b", tt),), inc=(("sta" if tt == 0 else "stb", 16),))

        def ab_step(s):
            lastc = (s - 1 == NCH - 1)
            ep = s - 2
            doepi = 2 <= s <= NCH + 1
            if doepi:
                if "o" in taps:
                    p.barrier()
                    for tt in range(2):
                        p.op("sp", lambda g, ep=ep, tt=tt: g.dma_start(
                            out=tap["o"][L * ep + 128 * tt:L * ep + 128 * (tt + 1), :],
                            in_=oB[:, tt, :]), inc=(("st", 16),))
                    p.barrier()
                for tt in range(2):
                    epi_pre(ep, tt)
            if s == 0:
                px(0)
            if s < NCH:
                proj_A1(s)
            if doepi:
                epi_wo(ep)
            if s < NCH:
                proj_A2(s)
                cut("pA")
                proj_f(s, 0)
                cut("pF")
            sc = s >= 1 and s <= NCH
            if sc:
                scan_p1(s - 1, 0, lastc)
            if s < NCH:
                proj_q(s, (0, 1, 2, 3))
                cut("pQ")
            if sc:
                scan_p1(s - 1, 1, lastc)
            if s < NCH:
                proj_kv(s, w_k, "k", 0)
                proj_kv(s, w_k, "k", 1)
                cut("pK")
            if sc:
                scan_p2(s - 1, 0, lastc)
            if s < NCH:
                proj_kv(s, w_v, "v", 0)
            if sc:
                scan_p2(s - 1, 1, lastc)
            if s < NCH:
                proj_kv(s, w_v, "v", 1)
                proj_f(s, 1)
            if sc:
                scan_p1(s - 1, 2, lastc)
            if doepi:
                epi_fin(ep, 0)
            if sc:
                scan_p1(s - 1, 3, lastc)
            if s < NCH:
                proj_q(s, (4, 5, 6, 7))
            if sc:
                scan_p2(s - 1, 2, lastc)
            if doepi:
                epi_fin(ep, 1)
            if sc:
                scan_p2(s - 1, 3, lastc)
            if s + 1 < NCH:
                px(s + 1)
                if "qT" in taps:
                    p.barrier()
                    for dc in range(NDC):
                        p.op("pool", lambda g, s=s, dc=dc: g.dma_start(
                            out=tap["qT"][128 * dc:128 * (dc + 1), L * s:L * (s + 1)],
                            in_=qTw[:, dc, s % 2, :]), inc=(("lg", 16),), dma=True)
                if "kvf" in taps:
                    p.barrier()
                    for tt in range(2):
                        r0 = L * s + 128 * tt
                        for i, buf in ((0, kTMw), (1, vTMw), (2, fTMw)):
                            p.op("pool", lambda g, i=i, buf=buf, r0=r0, s=s, tt=tt: g.dma_start(
                                out=tap["kvf"][i, r0:r0 + 128, :], in_=buf[:, s % 2, tt, :]),
                                inc=(("lg", 16),), dma=True)
                if "qT" in taps or "kvf" in taps:
                    p.barrier()

        for step in range(NCH + 2):
            ab_step(step)
            if step < NCH:
                cut(("proj", step))
            if step >= 1 and step <= NCH:
                cut(("scan", step - 1))
        p.barrier()
        ab_stack.close()
        cut("ab")

        PAB = stack.enter_context(nc.psum_tensor("PAB", [128, 1024], F32))
        PCD = stack.enter_context(nc.psum_tensor("PCD", [128, 1024], F32))
        PE_ = stack.enter_context(nc.psum_tensor("PE_", [128, 512], F32))
        PF_ = stack.enter_context(nc.psum_tensor("PF_", [128, 512], F32))
        PT_ = stack.enter_context(nc.psum_tensor("PT_", [128, 1024], BF16))
        PS_ = stack.enter_context(nc.psum_tensor("PS_", [128, 512], F32))
        id_f = sb("id_f", [128, 128], F32)
        lt64_s = sb("lt64_s", [T // 128, T // 128], F32)
        lt128_s = sb("lt128_s", [128, 128], F32)
        sel2_s = sb("sel2_s", [2, 256], BF16)
        sqb = sb("sqb", [128, 512], BF16)
        nrm2 = sb("nrm2", [1, FH, (T // 2) // 512], F32)
        v2S = sb("v2S", [128, (T // 2) // 128, FH * 65], BF16)
        kmx2 = sb("kmx2", [1, FH], F32)
        qmx2 = sb("qmx2", [1, FH], F32)
        for nm, dst in (("ident", id_f), ("lt64", lt64_s), ("lt128", lt128_s)):
            p.op("sp", lambda g, nm=nm, dst=dst: g.dma_start(out=dst[:, :], in_=din[nm][:, :]),
                 inc=(("ld", 16),))
        p.op("pool", lambda g: g.dma_start(out=sel2_s[:, :], in_=din["sel2"][:, :]),
             inc=(("lg", 16),), dma=True)
        p.barrier()

        if "logits" in taps:
            p.op("sp", lambda g: g.dma_start(out=tap["logits"][:, :], in_=lgT[:, :]),
                 inc=(("st", 16),))
            p.barrier()

        NQT = T // 128

        lo = sb("lo", [128, 1], F32)
        hi = sb("hi", [128, 1], F32)
        mid = sb("mid", [128, 1], F32)
        sel = sb("sel", [128, 1], F32)
        selm1 = sb("selm1", [128, 1], F32)
        dtmp = sb("dtmp", [128, 1], F32)
        cntc = sb("cntc", [128, 1], F32)
        cnts = sb("cnts", [1, 1], F32)
        junkN = sb("junkN", [128, NQT], F32)
        p.op("dve", lambda v: v.memset(lo[:, :], -256.0), inc=(("dve", 1),))
        p.op("dve", lambda v: v.memset(hi[:, :], 256.0), inc=(("dve", 1),))
        p.op("dve", lambda v: v.memset(mid[:, :], 0.0), inc=(("dve", 1),))
        for it in range(BIS_ITERS):
            p.op("dve", lambda v: v.tensor_scalar(
                out=junkN[:, :], in0=lgT[:, :], scalar1=mid[:, 0:1], scalar2=0.0,
                op0=ALU.is_ge, op1=ALU.add, accum_out=cntc[:, 0:1]), inc=(("dve", 1),))
            p.op("te", lambda t: t.matmul(PS_[0:1, 0:1], cntc[:, 0:1], one_f[:, 0:1],
                                          start=True, stop=True), inc=(("te", 1),),
                 after=(("dve", None),))
            p.op("dve", lambda v: v.tensor_copy(cnts[0:1, 0:1], PS_[0:1, 0:1]),
                 inc=(("dve", 1),), after=(("te", None),))
            p.op("te", lambda t: t.matmul(PS_[:, 2:3], onesrow_f[0:1, :], cnts[0:1, 0:1],
                                          start=True, stop=True), inc=(("te", 1),),
                 after=(("dve", None),))
            p.op("dve", lambda v: v.tensor_scalar(
                out=sel[:, 0:1], in0=PS_[:, 2:3], scalar1=float(CSEL), scalar2=None,
                op0=ALU.is_ge), inc=(("dve", 1),), after=(("te", None),))
            p.op("dve", lambda v: v.tensor_scalar(
                out=selm1[:, 0:1], in0=sel[:, 0:1], scalar1=1.0, scalar2=None,
                op0=ALU.subtract), inc=(("dve", 1),))
            p.op("dve", lambda v: v.tensor_sub(dtmp[:, 0:1], mid[:, 0:1], lo[:, 0:1]),
                 inc=(("dve", 1),))
            p.op("dve", lambda v: v.scalar_tensor_tensor(
                out=lo[:, 0:1], in0=dtmp[:, 0:1], scalar=sel[:, 0:1], in1=lo[:, 0:1],
                op0=ALU.mult, op1=ALU.add), inc=(("dve", 1),))
            p.op("dve", lambda v: v.tensor_sub(dtmp[:, 0:1], hi[:, 0:1], mid[:, 0:1]),
                 inc=(("dve", 1),))
            p.op("dve", lambda v: v.scalar_tensor_tensor(
                out=hi[:, 0:1], in0=dtmp[:, 0:1], scalar=selm1[:, 0:1], in1=hi[:, 0:1],
                op0=ALU.mult, op1=ALU.add), inc=(("dve", 1),))
            p.op("dve", lambda v: v.tensor_add(dtmp[:, 0:1], lo[:, 0:1], hi[:, 0:1]),
                 inc=(("dve", 1),))
            p.op("dve", lambda v: v.tensor_scalar(
                out=mid[:, 0:1], in0=dtmp[:, 0:1], scalar1=0.5, scalar2=None, op0=ALU.mult),
                inc=(("dve", 1),))
        if "theta" in taps:
            p.op("sp", lambda g: g.dma_start(out=tap["theta"][:, :], in_=lo[:, :]),
                 inc=(("st", 16),), after=(("dve", None),))
        p.barrier()
        cut("bisect")

        maskA = sb("maskA", [128, NQT], F32)
        maskT = sb("maskT", [NQT, 128], F32)
        colcnt = sb("colcnt", [NQT, 1], F32)
        colpf = sb("colpf", [NQT, 1], F32)
        slotT = sb("slotT", [NQT, 128], F32)
        t1s = sb("t1s", [NQT, 128], F32)
        wsS = sb("wsS", [1, 1], F32)
        wsI = sb("wsI", [1, 1], I32)
        wsB = sb("wsB", [NQT, 1], F32)
        slotI = sb("slotI", [128, NQT], I32)
        srelW = sb("srelW", [128, MYT // 128], F32)

        p.op("dve", lambda v: v.tensor_scalar(
            out=maskA[:, :], in0=lgT[:, :], scalar1=lo[:, 0:1], scalar2=None,
            op0=ALU.is_ge), inc=(("dve", 1),))
        p.op("te", lambda t: t.transpose(PS_[0:NQT, 64:64 + 128], maskA[:, :], id_f[:, :]),
             inc=(("te", 1),), after=(("dve", None),))
        p.op("dve", lambda v: v.tensor_copy(maskT[:, :], PS_[0:NQT, 64:64 + 128]),
             inc=(("dve", 1),), after=(("te", None),))
        p.op("dve", lambda v: v.tensor_reduce(
            out=colcnt[:, 0:1], in_=maskT[:, :], axis=AX.X, op=ALU.add), inc=(("dve", 1),))
        p.op("te", lambda t: t.matmul(PS_[0:NQT, 0:1], lt64_s[:, :], colcnt[:, 0:1],
                                      start=True, stop=True), inc=(("te", 1),),
             after=(("dve", None),))
        p.op("dve", lambda v: v.tensor_copy(colpf[:, 0:1], PS_[0:NQT, 0:1]),
             inc=(("dve", 1),), after=(("te", None),))
        p.op("te", lambda t: t.matmul(PS_[0:NQT, 192:192 + 128], maskA[:, :], lt128_s[:, :],
                                      start=True, stop=True), inc=(("te", 1),),
             after=(("dve", None),))
        p.op("dve", lambda v: v.tensor_scalar(
            out=slotT[:, :], in0=PS_[0:NQT, 192:192 + 128], scalar1=colpf[:, 0:1],
            scalar2=None, op0=ALU.add), inc=(("dve", 1),), after=(("te", None),))
        p.op("dve", lambda v: v.tensor_scalar(
            out=t1s[:, :], in0=maskT[:, :], scalar1=-1.0e9, scalar2=1.0e9,
            op0=ALU.mult, op1=ALU.add), inc=(("dve", 1),))
        p.op("dve", lambda v: v.tensor_add(slotT[:, :], slotT[:, :], t1s[:, :]),
             inc=(("dve", 1),))
        p.op("sp", lambda g: g.dma_start(out=cpfD[:, :], in_=colpf[:, :]),
             inc=(("st", 16),), after=(("dve", None),))
        stv = p.c("st")

        def rt_read_cpf(g):
            r = g.value_load(myq_s[0:1, 0:1])
            return g.dma_start(out=wsS[0:1, 0:1], in_=cpfD[bass.ts(r, NQT // 4), :][0:1, :])
        p.op("pool", rt_read_cpf, dma=True, inc=(("lg", 16),), after=(("st", stv),))
        p.op("dve", lambda v: v.tensor_scalar(
            out=wsS[0:1, 0:1], in0=wsS[0:1, 0:1], scalar1=float(CSEL - QW), scalar2=None,
            op0=ALU.min), inc=(("dve", 1),), after=(("lg", None),))
        p.op("dve", lambda v: v.tensor_scalar(
            out=wsS[0:1, 0:1], in0=wsS[0:1, 0:1], scalar1=0.0, scalar2=None,
            op0=ALU.max), inc=(("dve", 1),))
        p.op("dve", lambda v: v.tensor_copy(wsI[0:1, 0:1], wsS[0:1, 0:1]), inc=(("dve", 1),))
        p.op("te", lambda t: t.matmul(PS_[0:NQT, 32:33], onesrow_f[0:1, 0:NQT],
                                      wsS[0:1, 0:1], start=True, stop=True),
             inc=(("te", 1),), after=(("dve", None),))
        p.op("dve", lambda v: v.tensor_copy(wsB[:, 0:1], PS_[0:NQT, 32:33]),
             inc=(("dve", 1),), after=(("te", None),))
        p.op("dve", lambda v: v.tensor_scalar(
            out=t1s[:, :], in0=slotT[:, :], scalar1=wsB[:, 0:1], scalar2=None,
            op0=ALU.subtract), inc=(("dve", 1),), after=(("pool", None),))
        p.op("sp", lambda g: g.dma_start(out=srelD[:, :], in_=t1s[:, :]),
             inc=(("st", 16),), after=(("dve", None),))
        p.op("te", lambda t: t.transpose(PS_[:, 320:320 + NQT], slotT[:, :],
                                         id_f[0:NQT, 0:NQT]), inc=(("te", 1),),
             after=(("dve", None),))
        p.op("dve", lambda v: v.tensor_copy(slotI[:, :], PS_[:, 320:320 + NQT]),
             inc=(("dve", 1),), after=(("te", None),))
        if "slot" in taps:
            p.op("sp", lambda g: g.dma_start(out=tap["slot"][:, :], in_=PS_[:, 320:320 + NQT]),
                 inc=(("st", 16),), after=(("te", None),))
        p.barrier()

        rowbuf = sb("rowbuf", [128, 2, 1032], BF16)
        regh = {}
        p.op("pool", lambda g: regh.setdefault("bc", g.to_reg(CSEL - 1)) and None, dma=True)
        scat_hist = []
        for i in range(NQT):
            bi = i % 2
            par = "lda" if bi == 0 else "ldb"
            spar = "sga" if bi == 0 else "sgb"
            p.op("sp", lambda g, i=i, bi=bi: g.dma_start(
                out=rowbuf[:, bi, :], in_=y1d[128 * i:128 * (i + 1), :]), inc=((par, 16),),
                after=((spar, scat_hist[i - 2]),) if i >= 2 else ())
            ldv = p.c(par)
            p.op("pool", lambda g, i=i, bi=bi: g.indirect_dma_start(
                out=atn[:, :], out_offset=IndirectOffsetOnAxis(slotI[:, i:i + 1], 0),
                in_=rowbuf[:, bi, :], in_offset=None, bounds_check=regh["bc"],
                oob_is_err=False), inc=((spar, 16),), after=((par, ldv),), dma=True)
            scat_hist.append(p.c(spar))
        p.barrier()
        cut("scatter")
        if "attn_in" in taps:
            atnf = sb("atnf", [128, 1032], F32)
            for i in range(CSEL // 128):
                p.op("pool", lambda g, i=i: g.dma_start(
                    out=atnf[:, :], in_=atn[128 * i:128 * (i + 1), :]), inc=(("lg", 16),),
                    after=(("st", None),), dma=True)
                p.op("sp", lambda g, i=i: g.dma_start(
                    out=tap["attn_in"][128 * i:128 * (i + 1), :], in_=atnf[:, :]),
                    inc=(("st", 16),), after=(("lg", None),))
                p.barrier()

        kv_stack = contextlib.ExitStack()
        sbk = lambda name, shape, dt: kv_stack.enter_context(nc.sbuf_tensor(name, shape, dt))
        fwk_s = sbk("fwk_s", [128, NDC, D], BF16)
        fwv_s = sbk("fwv_s", [128, NDC, D], BF16)
        for (dst, src) in ((fwk_s, "fWk"), (fwv_s, "fWv")):
            for dc in range(NDC):
                p.op("pool", lambda g, d=dst, s=src, dc=dc: g.dma_start(
                    out=d[:, dc, :], in_=din[s][128 * dc:128 * (dc + 1), :]),
                    inc=(("lw", 16),), dma=True)

        aT = sbk("aT", [128, NDC, CSEL], BF16)
        for j in range(NDC):
            p.op("sp", lambda g, j=j: g.dma_start(
                out=aT[:, j, :], in_=atn[:, 128 * j:128 * (j + 1)], transpose=True),
                inc=(("ld", 16),))
        p.barrier()
        cut("aT")
        p.wait("act", "lw")
        p.wait("dve", "lw")
        p.wait("te", "lw")

        kvbuf = sbk("kvbuf", [128, 2, FH * 65], BF16)
        k2st_hist = []
        cpv = {}
        sqv = {}
        nrv = {}
        for dc in range(NDC):
            for qt in range(CSEL // 512):
                it = dc * (CSEL // 512) + qt
                reg = PAB[:, 0:512] if qt % 2 == 0 else PAB[:, 512:1024]
                for di in range(NDC):
                    p.op("te", lambda t, dc=dc, qt=qt, di=di, reg=reg: t.matmul(
                        reg, fwk_s[:, di, 128 * dc:128 * (dc + 1)],
                        aT[:, di, 512 * qt:512 * (qt + 1)],
                        start=(di == 0), stop=(di == NDC - 1)),
                        inc=(("te", 1),) if di == NDC - 1 else (),
                        after=((("dve", cpv.get(it - 2, 0)),) if di == 0 else ()))
                grp = p.c("te")
                bi = it % 2
                spar = "sta" if bi == 0 else "stb"
                p.op("dve", lambda v, bi=bi, reg=reg: v.tensor_copy(kvbuf[:, bi, 0:512], reg),
                     inc=(("dve", 1),),
                     after=(("te", grp),) + (((spar, k2st_hist[it - 2]),) if it >= 2 else ()))
                cpv[it] = p.c("dve")
                p.op("sp", lambda g, dc=dc, qt=qt, bi=bi: g.dma_start(
                    out=k2d[dc, :, 512 * qt:512 * (qt + 1)], in_=kvbuf[:, bi, 0:512]),
                    inc=((spar, 16),), after=(("dve", cpv[it]),))
                k2st_hist.append(p.c(spar))
                p.op("act", lambda a, bi=bi: a.activation(
                    out=sqb[:, 0:512], in_=kvbuf[:, bi, 0:512], func=AF.Square),
                    inc=(("act", 1),), after=(("dve", cpv[it]),))
                sqv[it] = p.c("act")
                for hh in range(2):
                    h = 2 * dc + hh
                    p.op("te", lambda t, hh=hh: t.matmul(
                        PS_[0:1, 0:512], one_b[0:64, 0:1] if hh == 0 else one_b[64:128, 0:1],
                        sqb[0:64, 0:512] if hh == 0 else sqb[64:128, 0:512],
                        start=True, stop=True), inc=(("te", 1),),
                        after=(("act", sqv[it]),) +
                        ((("dve", nrv[it - 1]),) if (hh == 0 and it >= 1) else
                         ((("dve", None),) if hh == 1 else ())))
                    p.op("dve", lambda v, h=h, qt=qt: v.tensor_reduce(
                        out=nrm2[0:1, h, qt:qt + 1], in_=PS_[0:1, 0:512], axis=AX.X,
                        op=ALU.max), inc=(("dve", 1),), after=(("te", None),))
                nrv[it] = p.c("dve")
        for h in range(FH):
            p.op("dve", lambda v, h=h: v.tensor_reduce(
                out=kmx2[0:1, h:h + 1], in_=nrm2[0:1, h, :], axis=AX.X, op=ALU.max),
                inc=(("dve", 1),))
        p.barrier()
        p.op("dve", lambda v: v.memset(v2S[:, :, :], 0.0), inc=(("dve", 1),))
        for h in range(FH):
            p.op("dve", lambda v, h=h: v.memset(
                v2S[:, :, 65 * h + 64:65 * h + 65], 1.0), inc=(("dve", 1),))
        vcp = {}
        for kt in range(NKT):
            for half in range(2):
                ih = 2 * kt + half
                reg = PCD[:, 0:512] if half == 0 else PCD[:, 512:1024]
                for di in range(NDC):
                    p.op("te", lambda t, kt=kt, half=half, di=di, reg=reg: t.matmul(
                        reg, aT[:, di, 128 * kt:128 * (kt + 1)],
                        fwv_s[:, di, 512 * half:512 * (half + 1)],
                        start=(di == 0), stop=(di == NDC - 1)),
                        inc=(("te", 1),) if di == NDC - 1 else (),
                        after=((("dve", vcp.get(ih - 2, 0)),) if di == 0 else ()))
                vgrp = p.c("te")
                for hh in range(8):
                    h = 8 * half + hh
                    p.op("dve", lambda v, kt=kt, h=h, hh=hh, reg=reg: v.tensor_copy(
                        v2S[:, kt, 65 * h:65 * h + 64], reg[:, 64 * hh:64 * (hh + 1)]),
                        inc=(("dve", 1),),
                        after=((("te", vgrp),) if hh == 0 else ()))
                vcp[ih] = p.c("dve")
        p.barrier()
        cut("kv")

        kv_stack.close()
        d_stack = contextlib.ExitStack()
        sbd = lambda name, shape, dt: d_stack.enter_context(nc.sbuf_tensor(name, shape, dt))
        q2T = sbd("q2T", [128, NDC, QW], BF16)
        bmix = sbd("bmix", [1, FH], F32)
        bpair = sbd("bpair", [1, NDC], F32)
        bpb = sbd("bpb", [1, NDC], BF16)
        bcols = sbd("bcols", [128, NDC], F32)
        q_stack = contextlib.ExitStack()
        sbq = lambda name, shape, dt: q_stack.enter_context(nc.sbuf_tensor(name, shape, dt))
        fwq_s = sbq("fwq_s", [128, NDC, D], BF16)
        for dc in range(NDC):
            p.op("pool", lambda g, dc=dc: g.dma_start(
                out=fwq_s[:, dc, :], in_=din["fWq"][128 * dc:128 * (dc + 1), :]),
                inc=(("lw", 16),), dma=True)
        myaT = sbq("myaT", [128, NDC, QW], BF16)

        def ldwin(g):
            r = g.value_load(wsI[0:1, 0:1])
            return g.dma_start(out=myrows[:, :], in_=atn[bass.ds(r, QW), :])
        p.op("pool", ldwin, dma=True, inc=(("lg", 16),))
        p.wait("sp", "lg")
        for j in range(NDC):
            p.op("sp", lambda g, j=j: g.dma_start(
                out=myaT[:, j, :], in_=myrows[:, 128 * j:128 * (j + 1)], transpose=True),
                inc=(("ld", 16),))
        p.barrier()

        qt_sizes = [(i * 512, min(512, QW - i * 512)) for i in range((QW + 511) // 512)]
        for dc in range(NDC):
            for (q0, qn) in qt_sizes:
                reg = PAB[:, 0:512]
                for di in range(NDC):
                    p.op("te", lambda t, dc=dc, di=di, q0=q0, qn=qn, reg=reg: t.matmul(
                        reg[:, 0:qn], fwq_s[:, di, 128 * dc:128 * (dc + 1)],
                        myaT[:, di, q0:q0 + qn], start=(di == 0), stop=(di == NDC - 1)),
                        inc=(("te", 1),) if di == NDC - 1 else (),
                        after=(("dve", None), ("act", None)) if di == 0 else ())
                p.op("dve", lambda v, dc=dc, q0=q0, qn=qn, reg=reg: v.tensor_copy(
                    q2T[:, dc, q0:q0 + qn], reg[:, 0:qn]), inc=(("dve", 1),),
                    after=(("te", None),))
                p.op("act", lambda a, qn=qn, reg=reg: a.activation(
                    out=sqb[:, 0:qn], in_=reg[:, 0:qn], func=AF.Square),
                    inc=(("act", 1),), after=(("dve", None),))
                qi = q0 // 512
                for hh in range(2):
                    h = 2 * dc + hh
                    p.op("te", lambda t, hh=hh, qn=qn: t.matmul(
                        PS_[0:1, 0:qn], one_b[0:64, 0:1] if hh == 0 else one_b[64:128, 0:1],
                        sqb[0:64, 0:qn] if hh == 0 else sqb[64:128, 0:qn],
                        start=True, stop=True), inc=(("te", 1),),
                        after=(("act", None), ("dve", None)))
                    p.op("dve", lambda v, h=h, qi=qi, qn=qn: v.tensor_reduce(
                        out=nrm2[0:1, h, qi:qi + 1], in_=PS_[0:1, 0:qn], axis=AX.X,
                        op=ALU.max), inc=(("dve", 1),), after=(("te", None),))
        NQB = len(qt_sizes)
        for h in range(FH):
            p.op("dve", lambda v, h=h: v.tensor_reduce(
                out=qmx2[0:1, h:h + 1], in_=nrm2[0:1, h, 0:NQB], axis=AX.X, op=ALU.max),
                inc=(("dve", 1),))
        p.op("dve", lambda v: v.tensor_mul(bmix[0:1, :], qmx2[0:1, :], kmx2[0:1, :]),
             inc=(("dve", 1),))
        p.op("act", lambda a: a.activation(out=bmix[0:1, :], in_=bmix[0:1, :], func=AF.Sqrt,
                                           scale=1.0 / 64.0), inc=(("act", 1),),
             after=(("dve", None),))
        p.op("dve", lambda v: v.tensor_max(
            bpair[0:1, :], bmix.rearrange("o (dc hh) -> o dc hh", hh=2)[0:1, :, 0],
            bmix.rearrange("o (dc hh) -> o dc hh", hh=2)[0:1, :, 1]),
            inc=(("dve", 1),), after=(("act", None),))
        p.op("dve", lambda v: v.tensor_scalar(
            out=bpair[0:1, :], in0=bpair[0:1, :], scalar1=-1.0, scalar2=None, op0=ALU.mult),
            inc=(("dve", 1),))
        p.op("dve", lambda v: v.tensor_copy(bpb[0:1, :], bpair[0:1, :]), inc=(("dve", 1),))
        p.op("te", lambda t: t.matmul(PS_[:, 0:NDC], onerow_b[0:1, :], bpb[0:1, :],
                                      start=True, stop=True), inc=(("te", 1),),
             after=(("dve", None),))
        p.op("dve", lambda v: v.tensor_copy(bcols[:, :], PS_[:, 0:NDC]), inc=(("dve", 1),),
             after=(("te", None),))
        p.barrier()
        cut("q2")

        q_stack.close()
        o2hT = sbd("o2hT", [128, NDC, QW], BF16)
        sc_stack = contextlib.ExitStack()
        sbs = lambda name, shape, dt: sc_stack.enter_context(nc.sbuf_tensor(name, shape, dt))
        eS = sbs("eS", [128, 2, 1024], BF16)
        k2blk = sbs("k2blk", [128, 2, 128], BF16)
        drow = sbs("drow", [128, 2, 512], F32)
        dr2 = sbs("dr2", [2, 512], BF16)
        rcpbE = sbs("rcpbE", [128, 512], F32)
        rcpbO = sbs("rcpbO", [128, 512], F32)
        oddst = sbs("oddst", [128, 512], BF16)
        schist = {}
        expv = {}
        for hp in range(NDC):
            for (q0, qn) in qt_sizes:
                for kt in range(NKT + 1):
                    if kt < NKT:
                        bi = kt % 2
                        par = "lda" if bi == 0 else "ldb"
                        guard = (("te", schist[kt - 2]),) if kt >= 2 else ()
                        p.op("sp", lambda g, hp=hp, kt=kt, bi=bi: g.dma_start(
                            out=k2blk[:, bi, :], in_=k2d[hp, :, 128 * kt:128 * (kt + 1)]),
                            inc=((par, 16),), after=guard)
                        ldk = p.c(par)
                        pS = PAB if kt % 2 == 0 else PCD
                        sguard = ((("act", expv[kt - 2]),) if kt >= 2 else
                                  ((("act", None),) if (hp, q0) != (0, 0) else ()))
                        p.op("te", lambda t, bi=bi, hp=hp, q0=q0, qn=qn, pS=pS, par=par: t.matmul(
                            pS[:, 0:qn], k2blk[0:64, bi, :], q2T[0:64, hp, q0:q0 + qn],
                            start=True, stop=True, tile_position=(0, 0)),
                            inc=(("te", 1),), after=((par, ldk),) + sguard)
                        p.op("te", lambda t, bi=bi, hp=hp, q0=q0, qn=qn, pS=pS: t.matmul(
                            pS[:, 512:512 + qn], k2blk[64:128, bi, :],
                            q2T[64:128, hp, q0:q0 + qn],
                            start=True, stop=True, tile_position=(64, 0)), inc=(("te", 1),))
                        schist[kt] = p.c("te")
                        p.op("act", lambda a, hp=hp, qn=qn, pS=pS, bi=bi: a.activation(
                            out=eS[:, bi, 0:2 * qn],
                            in_=pS[:, 0:1024].rearrange("pp (a b) -> pp a b", a=2)[:, :, 0:qn],
                            func=AF.Exp, scale=0.125, bias=bcols[:, hp:hp + 1]),
                            inc=(("act", 1),), after=(("te", schist[kt]),))
                        expv[kt] = p.c("act")
                    if kt >= 1:
                        ka = kt - 1
                        for par in range(2):
                            av = PE_ if par == 0 else PF_
                            p.op("te", lambda t, hp=hp, ka=ka, par=par, av=av, qn=qn: t.matmul(
                                av[0:65, 0:qn],
                                v2S[:, ka, 65 * (2 * hp + par):65 * (2 * hp + par) + 65],
                                eS[:, ka % 2, qn * par:qn * (par + 1)],
                                start=(ka == 0), stop=(ka == NKT - 1)),
                                inc=(("te", 1),),
                                after=(("act", expv[ka]),) + ((("dve", None),) if ka == 0 else ()))
                tev = p.c("te")
                for par in range(2):
                    av = PE_ if par == 0 else PF_
                    p.op("dve", lambda v, par=par, av=av, qn=qn: v.tensor_copy(
                        drow[64:65, par, 0:qn], av[64:65, 0:qn]), inc=(("dve", 1),),
                        after=(("te", tev), ("lg", None)))
                    p.op("dve", lambda v, par=par, qn=qn: v.reciprocal(
                        drow[64:65, par, 0:qn], drow[64:65, par, 0:qn]), inc=(("dve", 1),))
                    p.op("pool", lambda g, par=par, qn=qn: g.dma_start(
                        out=dr2[par:par + 1, 0:qn], in_=drow[64:65, par, 0:qn]),
                        inc=(("lg", 16),), after=(("dve", None), ("te", None)), dma=True)
                ldd = p.c("lg")
                p.op("te", lambda t, qn=qn: t.matmul(
                    PS_[:, 0:qn], sel2_s[:, 0:128], dr2[:, 0:qn], start=True, stop=True),
                    inc=(("te", 1),), after=(("lg", ldd), ("dve", None)))
                p.op("dve", lambda v, qn=qn: v.tensor_copy(rcpbE[:, 0:qn], PS_[:, 0:qn]),
                     inc=(("dve", 1),), after=(("te", None),))
                p.op("te", lambda t, qn=qn: t.matmul(
                    PS_[:, 0:qn], sel2_s[:, 128:256], dr2[:, 0:qn], start=True, stop=True),
                    inc=(("te", 1),), after=(("dve", None),))
                p.op("dve", lambda v, qn=qn: v.tensor_copy(rcpbO[:, 0:qn], PS_[:, 0:qn]),
                     inc=(("dve", 1),), after=(("te", None),))
                p.op("dve", lambda v, hp=hp, q0=q0, qn=qn: v.tensor_mul(
                    o2hT[0:64, hp, q0:q0 + qn], PE_[0:64, 0:qn], rcpbE[0:64, 0:qn]),
                    inc=(("dve", 1),))
                p.op("dve", lambda v, qn=qn: v.tensor_mul(
                    oddst[0:64, 0:qn], PF_[0:64, 0:qn], rcpbO[0:64, 0:qn]),
                    inc=(("dve", 1),), after=(("st", None),))
                p.op("sp", lambda g, hp=hp, q0=q0, qn=qn: g.dma_start(
                    out=o2hT[64:128, hp, q0:q0 + qn], in_=oddst[0:64, 0:qn]),
                    inc=(("st", 16),), after=(("dve", None),))
                cut(("sc", hp, q0))
        cut("scores")
        if "o2" in taps:
            o2f = sbs("o2f", [128, QW], F32)
            for dc in range(NDC):
                p.op("dve", lambda v, dc=dc: v.tensor_copy(o2f[:, :], o2hT[:, dc, :]),
                     inc=(("dve", 1),))
                p.op("sp", lambda g, dc=dc: g.dma_start(
                    out=tap["o2"][128 * dc:128 * (dc + 1), :], in_=o2f[:, :]),
                    inc=(("st", 16),), after=(("dve", None),))
                p.barrier()

        sc_stack.close()
        wo_stack = contextlib.ExitStack()
        sbw = lambda name, shape, dt: wo_stack.enter_context(nc.sbuf_tensor(name, shape, dt))
        fwo_s = sbw("fwo_s", [128, NDC, D], BF16)
        for dc in range(NDC):
            p.op("pool", lambda g, dc=dc: g.dma_start(
                out=fwo_s[:, dc, :], in_=din["fWo"][128 * dc:128 * (dc + 1), :]),
                inc=(("lw", 16),), dma=True,
                after=(("lg", None), ("lda", None), ("ldb", None), ("st", None)))
        p.wait("te", "lw")
        ffnT = sbw("ffnT", [128, NDC, QW], BF16)
        fsl = sbw("fsl", [128, 2, D], BF16)
        for dc in range(NDC):
            for (q0, qn) in qt_sizes:
                reg = PAB[:, 0:512]
                for di in range(NDC):
                    p.op("te", lambda t, dc=dc, di=di, q0=q0, qn=qn, reg=reg: t.matmul(
                        reg[:, 0:qn], fwo_s[:, di, 128 * dc:128 * (dc + 1)],
                        o2hT[:, di, q0:q0 + qn], start=(di == 0), stop=(di == NDC - 1)),
                        inc=(("te", 1),) if di == NDC - 1 else (),
                        after=(("dve", None),) if di == 0 else ())
                p.op("dve", lambda v, dc=dc, q0=q0, qn=qn, reg=reg: v.tensor_copy(
                    ffnT[:, dc, q0:q0 + qn], reg[:, 0:qn]), inc=(("dve", 1),),
                    after=(("te", None),))
        p.barrier()
        fst_hist = []
        for tb in range(QW // 128):
            spar = "sta" if tb % 2 == 0 else "stb"
            for dc in range(NDC):
                p.op("te", lambda t, tb=tb, dc=dc: t.transpose(
                    PT_[:, 128 * dc:128 * (dc + 1)], ffnT[:, dc, 128 * tb:128 * (tb + 1)],
                    id_b[:, :]), inc=(("te", 1),),
                    after=(("dve", None),))
                guard = (("sta" if tb % 2 == 0 else "stb", fst_hist[tb - 2]),) if (tb >= 2 and dc == 0) else ()
                p.op("dve", lambda v, tb=tb, dc=dc: v.tensor_copy(
                    fsl[:, tb % 2, 128 * dc:128 * (dc + 1)], PT_[:, 128 * dc:128 * (dc + 1)]),
                    inc=(("dve", 1),), after=(("te", None),) + guard)
            p.op("sp", lambda g, tb=tb: g.dma_start(
                out=o2d[128 * tb:128 * (tb + 1), :], in_=fsl[:, tb % 2, :]),
                inc=((spar, 16),), after=(("dve", None),))
            fst_hist.append(p.c(spar))
        p.barrier()
        p.barrier()
        cut("wo")

        wo_stack.close()
        d_stack.close()
        fin_stack = contextlib.ExitStack()
        sbf = lambda name, shape, dt: fin_stack.enter_context(nc.sbuf_tensor(name, shape, dt))
        y1t = sbf("y1t", [128, MYT // 128, D], F32)
        gbuf = sbf("gbuf", [128, D], BF16)
        idxc = sbf("idxc", [128, 2], I32)
        srl16 = sbf("srl16", [MYT // 128, 128], F32)
        fcol = sbf("fcol", [128, 1], F32)
        ytmp = sbf("ytmp", [128, D], F32)
        yout = sbf("yout", [128, D], F32)
        NJT = MYT // 128

        def rt_load_srel(g):
            r = g.value_load(myq_s[0:1, 0:1])
            return g.dma_start(out=srl16[:, :], in_=srelD[bass.ts(r, NJT), :])
        p.op("pool", rt_load_srel, dma=True, inc=(("lg", 16),), after=(("st", None),))

        def rt_load_y1(g):
            r = g.value_load(myq_s[0:1, 0:1])
            return g.dma_start(
                out=y1t[:, :, :],
                in_=y1f32d[bass.ds(r * MYT, MYT), :].rearrange("(j pp) d -> pp j d", pp=128))
        p.op("pool", rt_load_y1, dma=True, inc=(("lg", 16),), after=(("st", None),))
        p.op("te", lambda t: t.transpose(PS_[:, 64:64 + NJT], srl16[:, :],
                                         id_f[0:NJT, 0:NJT]), inc=(("te", 1),),
             after=(("lg", None),))
        p.op("dve", lambda v: v.tensor_copy(srelW[:, :], PS_[:, 64:64 + NJT]),
             inc=(("dve", 1),), after=(("te", None),))
        if "dbg" in taps:
            p.op("dve", lambda v: v.tensor_copy(junkN[:, 0:MYT // 128], srelW[:, :]),
                 inc=(("dve", 1),))
            p.op("sp", lambda g: g.dma_start(out=tap["dbg"][:, 0:MYT // 128],
                                             in_=junkN[:, 0:MYT // 128]),
                 inc=(("st", 16),), after=(("dve", None),))
        p.barrier()

        for j in range(NJT):
            ldyv = p.c("lg")
            gguard = (("sg", 16 * (j - 1)),) if j >= 2 else ()
            p.op("dve", lambda v, j=j: v.tensor_copy(idxc[:, j % 2:j % 2 + 1], srelW[:, j:j + 1]),
                 inc=(("dve", 1),), after=gguard)
            p.op("dve", lambda v: v.memset(gbuf[:, :], 0.0), inc=(("dve", 1),))
            dvg = p.c("dve")
            if j == 0:
                p.op("pool", lambda g: regh.__setitem__("bc2", g.to_reg(QW - 1)), dma=True)
            p.op("pool", lambda g, j=j: g.indirect_dma_start(
                out=gbuf[:, :], out_offset=None, in_=o2d[:, :],
                in_offset=IndirectOffsetOnAxis(idxc[:, j % 2:j % 2 + 1], 0),
                bounds_check=regh["bc2"],
                oob_is_err=False), inc=(("sg", 16),), after=(("dve", dvg),), dma=True)
            sgv2 = p.c("sg")
            if j == 0 and "dbg" in taps:
                p.op("dve", lambda v: v.tensor_copy(junk[:, 0:1], idxc[:, 0:1]),
                     inc=(("dve", 1),))
                p.op("dve", lambda v: v.tensor_copy(junk[:, 1:41], gbuf[:, 0:40]),
                     inc=(("dve", 1),), after=(("sg", sgv2),))
                p.op("sp", lambda g: g.dma_start(out=tap["dbg"][:, 0:41], in_=junk[:, 0:41]),
                     inc=(("st", 16),), after=(("dve", None),))
            p.op("dve", lambda v, j=j: v.tensor_mul(junk[:, :], y1t[:, j, :], wgbc[:, :]),
                 inc=(("dve", 1),),
                 after=(("lg", ldyv),) + ((("st", None),) if (j == 0 and "dbg" in taps) else ()))
            p.op("dve", lambda v: v.tensor_reduce(
                out=colA[:, 0:1], in_=junk[:, :], axis=AX.X, op=ALU.add),
                inc=(("dve", 1),))
            p.op("act", lambda a: a.activation(out=fcol[:, 0:1], in_=colA[:, 0:1],
                                               func=AF.Sigmoid), inc=(("act", 1),),
                 after=(("dve", None),))
            p.op("dve", lambda v, j=j: v.scalar_tensor_tensor(
                out=ytmp[:, :], in0=gbuf[:, :], scalar=fcol[:, 0:1], in1=y1t[:, j, :],
                op0=ALU.mult, op1=ALU.add), inc=(("dve", 1),),
                after=(("sg", sgv2), ("act", None)))
            p.op("dve", lambda v: v.tensor_reduce(
                out=colA[:, 0:1], in_=ytmp[:, :], axis=AX.X, op=ALU.add), inc=(("dve", 1),))
            p.op("act", lambda a: a.activation(
                out=junk[:, :], in_=ytmp[:, :], func=AF.Square, accum_out=colB[:, 0:1]),
                inc=(("act", 1),), after=(("dve", None),))
            p.op("dve", lambda v: v.tensor_scalar(
                out=colA[:, 0:1], in0=colA[:, 0:1], scalar1=1.0 / D, scalar2=None,
                op0=ALU.mult), inc=(("dve", 1),), after=(("act", None),))
            p.op("dve", lambda v: v.tensor_mul(colC[:, 0:1], colA[:, 0:1], colA[:, 0:1]),
                 inc=(("dve", 1),))
            p.op("dve", lambda v: v.scalar_tensor_tensor(
                out=colB[:, 0:1], in0=colB[:, 0:1], scalar=1.0 / D, in1=colC[:, 0:1],
                op0=ALU.mult, op1=ALU.subtract), inc=(("dve", 1),))
            p.op("act", lambda a: a.activation(
                out=colB[:, 0:1], in_=colB[:, 0:1], func=AF.Sqrt, bias=eps_c[:, 0:1]),
                inc=(("act", 1),), after=(("dve", None),))
            p.op("dve", lambda v: v.reciprocal(colB[:, 0:1], colB[:, 0:1]),
                 inc=(("dve", 1),), after=(("act", None),))
            p.op("dve", lambda v: v.scalar_tensor_tensor(
                out=colC[:, 0:1], in0=colA[:, 0:1], scalar=-1.0, in1=colB[:, 0:1],
                op0=ALU.mult, op1=ALU.mult), inc=(("dve", 1),))
            if j == 0 and "dbg" in taps:
                p.op("dve", lambda v: v.tensor_copy(junkN[:, 0:4], ytmp[:, 0:4]),
                     inc=(("dve", 1),))
                p.op("dve", lambda v: v.tensor_copy(junkN[:, 4:5], fcol[:, 0:1]),
                     inc=(("dve", 1),))
                p.op("dve", lambda v: v.tensor_copy(junkN[:, 5:6], colB[:, 0:1]),
                     inc=(("dve", 1),))
                p.op("dve", lambda v: v.tensor_copy(junkN[:, 6:7], colC[:, 0:1]),
                     inc=(("dve", 1),))
            opar = "outa" if j % 2 == 0 else "outb"
            prevpar = "outb" if j % 2 == 0 else "outa"
            oguard = ((prevpar, p.c(prevpar)),) if j >= 1 else ()
            p.op("act", lambda a, j=j: a.activation(
                out=yout[:, :], in_=ytmp[:, :], func=AF.Identity, scale=colB[:, 0:1],
                bias=colC[:, 0:1]), inc=(("act", 1),), after=(("dve", None),) + oguard)
            p.op("sp", lambda g, j=j: g.dma_start(
                out=out_ext[128 * j:128 * (j + 1), :], in_=yout[:, :]),
                inc=((opar, 16),), after=(("act", None),))
        if "dbg" in taps:
            p.barrier()
            p.op("sp", lambda g: g.dma_start(out=tap["dbg"][:, 0:8], in_=junkN[:, 0:8]),
                 inc=(("st", 16),), after=(("dve", None),))
        for e in ENGS:
            p.wait(e, "outa")
            p.wait(e, "outb")
            p.wait(e, "st")
            p.wait(e, "sta")
            p.wait(e, "stb")
            p.wait(e, "sg")

    except _Cut:
        p.barrier()
    try:
        fin_stack.close()
    except Exception:
        pass
    with nc.Block() as block:
        @block.tensor
        def _(te):
            for f in p.q["te"]:
                f(te)

        @block.scalar
        def _(a):
            for f in p.q["act"]:
                f(a)

        @block.vector
        def _(v):
            for f in p.q["dve"]:
                f(v)

        @block.gpsimd
        def _(g):
            for f in p.q["pool"]:
                f(g)

        @block.sync
        def _(s):
            for f in p.q["sp"]:
                f(s)

    try:
        stack.close()
    except Exception:
        pass
    return nc, tap


def _host_inputs(inp):
    x = np.ascontiguousarray(np.asarray(inp["L0_attn_hidden_states"], np.float32))
    mk = {}
    mk["wqT"] = np.ascontiguousarray(np.asarray(inp["L0_attn_w_q"], np.float32).T)
    mk["wkT"] = np.ascontiguousarray(np.asarray(inp["L0_attn_w_k"], np.float32).T)
    mk["wvT"] = np.ascontiguousarray(np.asarray(inp["L0_attn_w_v"], np.float32).T)
    mk["wfT"] = np.ascontiguousarray(np.asarray(inp["L0_attn_w_f"], np.float32).T)
    mk["woT"] = np.ascontiguousarray(np.asarray(inp["L0_attn_w_o"], np.float32).T)
    mk["fWq"] = np.ascontiguousarray(np.asarray(inp["L0_ffn_Wq"], np.float32))
    mk["fWk"] = np.ascontiguousarray(np.asarray(inp["L0_ffn_Wk"], np.float32))
    mk["fWv"] = np.ascontiguousarray(np.asarray(inp["L0_ffn_Wv"], np.float32))
    mk["fWo"] = np.ascontiguousarray(np.asarray(inp["L0_ffn_Wo"], np.float32))
    mk["wg"] = np.ascontiguousarray(np.asarray(inp["L0_ffn_Wg"], np.float32).reshape(1, D))
    mk["nw"] = np.ascontiguousarray(
        np.asarray(inp["L0_attn_norm_weight"], np.float32).reshape(NDC, 128).T)
    mk["gn"] = np.ascontiguousarray(
        np.asarray(inp["L0_attn_gnorm_weight"], np.float32).reshape(NDC, 128).T)
    tpi = np.arange(L)[:, None]
    ti = np.arange(L)[None, :]
    mk["mcum"] = np.ascontiguousarray(
        (tpi <= ti).astype(np.float32) - (tpi <= L // 2 - 1).astype(np.float32))
    mk["trim"] = np.ascontiguousarray((tpi <= ti).astype(np.float32))
    mk["ident"] = np.eye(128, dtype=np.float32)
    c64 = np.arange(T // 128)
    mk["lt64"] = np.ascontiguousarray((c64[:, None] < c64[None, :]).astype(np.float32))
    c128 = np.arange(128)
    mk["lt128"] = np.ascontiguousarray((c128[:, None] < c128[None, :]).astype(np.float32))
    s2 = np.zeros((2, 256), np.float32)
    s2[0, 0:64] = 1.0
    s2[1, 128:192] = 1.0
    mk["sel2"] = s2
    in_maps = []
    for core in range(8):
        b, q = core // 4, core % 4
        m = dict(mk)
        m["x_b"] = np.ascontiguousarray(x[b])
        m["myq16"] = np.array([[q]], np.int32)
        in_maps.append(m)
    return in_maps


def kernel(**inputs):
    nc, _ = build_program(TAPS)
    in_maps = _host_inputs(inputs)
    res = run_bass_kernel_spmd(nc, in_maps, core_ids=list(range(8)))
    parts = [res.results[c]["out_mine"] for c in range(8)]
    out = np.concatenate(parts, axis=0).reshape(B, T, D)
    return np.ascontiguousarray(out.astype(np.float32))


# version 13
# speedup vs baseline: 1.0538x; 1.0062x over previous
import contextlib
import numpy as np
import concourse.bass as bass
import concourse.mybir as mybir
from concourse.bass_utils import run_bass_kernel_spmd
from concourse.bass_core_rust import IndirectOffsetOnAxis

F32 = mybir.dt.float32
BF16 = mybir.dt.bfloat16
I32 = mybir.dt.int32
AF = mybir.ActivationFunctionType
ALU = mybir.AluOpType
AX = mybir.AxisListType

B, D = 2, 1024
T = 8192
H, DK, DV, M = 4, 256, 256, 256
L = 256
NDC = D // 128
FH, DHD = 16, 64
EPS = 1e-5
GN = 16.0
BIS_ITERS = 48
TAPS = ()
STOP_AFTER = None
PARANOID = False
ENGS = ("te", "act", "dve", "pool", "sp")


class _Cut(Exception):
    pass


class PB:
    def __init__(self, nc, stack):
        self.nc = nc
        self.stack = stack
        self.q = {e: [] for e in ENGS}
        self.sems = {}
        self.cnt = {}
        self.hw = {e: {} for e in ENGS}

    def sem(self, name):
        if name not in self.sems:
            self.sems[name] = self.stack.enter_context(self.nc.semaphore(name))
            self.cnt[name] = 0
        return self.sems[name]

    def op(self, eng, fn, inc=(), after=(), dma=False):
        if eng in ("act", "dve", "pool") and not dma:
            after = tuple(after) + ((eng, None),)
            if not any(s == eng for s, _ in inc):
                inc = tuple(inc) + ((eng, 1),)
        hw = self.hw[eng]
        waits = []
        for s, v in after:
            vv = self.cnt[s] if v is None else v
            if vv <= 0 or vv <= hw.get(s, 0):
                continue
            hw[s] = vv
            waits.append((self.sems[s], vv))
        waits = tuple(waits)
        incs = tuple((self.sems[s], n) for s, n in inc)

        def play(h, waits=waits, incs=incs, fn=fn):
            for sh, v in waits:
                h.wait_ge(sh, v)
            ins = fn(h)
            for sh, n in incs:
                ins.then_inc(sh, n)

        self.q[eng].append(play)
        tok = []
        for s, n in inc:
            self.cnt[s] += n
            tok.append((s, self.cnt[s]))
        if PARANOID and eng != "_":
            self.barrier()
        return tuple(tok)

    def wait(self, eng, name, val=None):
        v = self.cnt[name] if val is None else val
        if v > 0 and v > self.hw[eng].get(name, 0):
            self.hw[eng][name] = v
            self.q[eng].append(lambda h, sh=self.sems[name], v=v: h.wait_ge(sh, v))

    def barrier(self):
        for e in ENGS:
            for s in self.sems:
                self.wait(e, s)

    def c(self, name):
        return self.cnt[name]


def build_program(taps=()):
    def cut(name):
        if STOP_AFTER == name:
            raise _Cut()
    NCH = T // L
    CSEL = T // 2
    QW = T // 8 + 128
    MYT = T // 4
    NKT = CSEL // 128

    nc = bass.Bass("TRN2", target_bir_lowering=False, debug=False)
    stack = contextlib.ExitStack()
    p = PB(nc, stack)
    for s in ("ld", "lda", "ldb", "lg", "lw", "te", "act", "dve", "pool", "st", "sg", "sga", "sgb", "sta", "stb", "out", "outa", "outb"):
        p.sem(s)

    din = {}
    for nm, shp, dt in (
        ("x_b", [T, D], F32), ("wqT", [D, D], F32), ("wkT", [D, D], F32),
        ("wvT", [D, D], F32), ("wfT", [D, D], F32), ("woT", [D, D], F32),
        ("fWq", [D, D], F32), ("fWk", [D, D], F32), ("fWv", [D, D], F32),
        ("fWo", [D, D], F32), ("wg", [1, D], F32), ("nw", [128, NDC], F32),
        ("gn", [128, NDC], F32), ("mcum", [L, L], F32), ("trim", [L, L], F32),
        ("ident", [128, 128], F32), ("lt64", [T // 128, T // 128], F32),
        ("lt128", [128, 128], F32), ("sel2", [2, 256], F32),
        ("myq16", [1, 1], I32),
    ):
        din[nm] = nc.dram_tensor(nm, shp, dt, kind="ExternalInput").ap()
    out_ext = nc.dram_tensor("out_mine", [MYT, D], F32, kind="ExternalOutput").ap()

    tap = {}
    for nm, shp in (("qT", [D, T]), ("kvf", [3, T, D]), ("o", [T, D]),
                    ("y1", [T, D]), ("logits", [128, T // 128]),
                    ("theta", [128, 1]), ("slot", [128, T // 128]),
                    ("attn_in", [CSEL, 1032]), ("o2", [D, QW]), ("dbg", [128, 64])):
        if nm in taps:
            tap[nm] = nc.dram_tensor("tap_" + nm, shp, F32, kind="ExternalOutput").ap()

    y1d = nc.dram_tensor("y1d", [T, 1032], BF16).ap()
    y1f32d = nc.dram_tensor("y1f32d", [T, D], F32).ap()
    atn = nc.dram_tensor("atn", [CSEL, 1032], BF16).ap()
    k2d = nc.dram_tensor("k2d", [NDC, 128, CSEL], BF16).ap()
    v2d = nc.dram_tensor("v2d", [CSEL, FH * 65], BF16).ap()
    o2d = nc.dram_tensor("o2d", [QW, D], BF16).ap()
    myrows = nc.dram_tensor("myrows", [QW, 1032], BF16).ap()
    srelD = nc.dram_tensor("srelD", [T // 128, 128], F32).ap()
    cpfD = nc.dram_tensor("cpfD", [T // 128, 1], F32).ap()

    sb = lambda name, shape, dt: stack.enter_context(nc.sbuf_tensor(name, shape, dt))

    try:

        id_b = sb("id_b", [128, 128], BF16)
        eps_c = sb("eps_c", [128, 1], F32)
        one_f = sb("one_f", [128, 1], F32)
        one_b = sb("one_b", [128, 1], BF16)
        onerow_b = sb("onerow_b", [1, 128], BF16)
        onesrow_f = sb("onesrow_f", [1, 128], F32)
        wgbc = sb("wgbc", [128, D], F32)
        myq_s = sb("myq_s", [1, 1], I32)
        junk = sb("junk", [128, D], BF16)
        lgT = sb("lgT", [128, T // 128], F32)
        colA = sb("colA", [128, 1], F32)
        colB = sb("colB", [128, 1], F32)
        colC = sb("colC", [128, 1], F32)
        ab_stack = contextlib.ExitStack()
        sba = lambda name, shape, dt: ab_stack.enter_context(nc.sbuf_tensor(name, shape, dt))
        w_q = sba("w_q", [128, NDC, D], BF16)
        w_k = sba("w_k", [128, NDC, D], BF16)
        w_v = sba("w_v", [128, NDC, D], BF16)
        w_f = sba("w_f", [128, NDC, D], BF16)
        w_o = sba("w_o", [128, NDC, D], BF16)
        nw_t = sba("nw_t", [128, NDC], F32)
        gn_t = sba("gn_t", [128, NDC], F32)
        mc_s = sba("mc_s", [128, 2, L], BF16)
        tri_s = sba("tri_s", [128, 2, L], F32)

        for (dst, src) in (
            (w_q, "wqT"), (w_k, "wkT"), (w_v, "wvT"), (w_f, "wfT"), (w_o, "woT")):
            for dc in range(NDC):
                p.op("pool", lambda g, d=dst, s=src, dc=dc: g.dma_start(
                    out=d[:, dc, :], in_=din[s][128 * dc:128 * (dc + 1), :]),
                    inc=(("lg", 16),), dma=True)
        p.op("pool", lambda g: g.dma_start(
            out=mc_s[:, :, :], in_=din["mcum"].rearrange("(tt pp) t -> pp tt t", pp=128)),
            inc=(("lg", 16),), dma=True)
        p.op("pool", lambda g: g.dma_start(out=id_b[:, :], in_=din["ident"][:, :]),
             inc=(("lg", 16),), dma=True)
        for nm, dst in (("nw", nw_t), ("gn", gn_t), ("myq16", myq_s)):
            p.op("sp", lambda g, nm=nm, dst=dst: g.dma_start(out=dst[:, :], in_=din[nm][:, :]),
                 inc=(("ld", 16),))
        p.op("sp", lambda g: g.dma_start(
            out=tri_s[:, :, :], in_=din["trim"].rearrange("(tt pp) t -> pp tt t", pp=128)),
            inc=(("ld", 16),))
        for f_, v_ in ((eps_c, EPS), (one_f, 1.0), (one_b, 1.0), (onerow_b, 1.0),
                       (onesrow_f, 1.0)):
            p.op("dve", lambda v, f_=f_, v_=v_: v.memset(f_[:, :], v_), inc=(("dve", 1),))
        p.barrier()
        for dc in range(NDC):
            p.op("dve", lambda v, dc=dc: v.tensor_scalar(
                out=w_o[:, dc, :], in0=w_o[:, dc, :], scalar1=gn_t[:, dc:dc + 1],
                scalar2=None, op0=ALU.mult), inc=(("dve", 1),))
        for wt in (w_q, w_k, w_v, w_f):
            for dc in range(NDC):
                p.op("dve", lambda v, wt=wt, dc=dc: v.tensor_scalar(
                    out=wt[:, dc, :], in0=wt[:, dc, :], scalar1=nw_t[:, dc:dc + 1],
                    scalar2=None, op0=ALU.mult), inc=(("dve", 1),))
        wg_stack = contextlib.ExitStack()
        wgrow = wg_stack.enter_context(nc.sbuf_tensor("wgrow", [1, D], F32))
        wgP = wg_stack.enter_context(nc.psum_tensor("wgP", [128, 512], F32))
        p.op("sp", lambda g: g.dma_start(out=wgrow[:, :], in_=din["wg"][:, :]),
             inc=(("ld", 16),))
        p.wait("te", "ld")
        for half in range(D // 512):
            p.op("te", lambda t, half=half: t.matmul(
                wgP[:, 0:512], onesrow_f[:, :], wgrow[:, 512 * half:512 * (half + 1)],
                start=True, stop=True), inc=(("te", 1),),
                after=(("dve", None), ("ld", None), ("lg", None)))
            p.op("dve", lambda v, half=half: v.tensor_copy(
                wgbc[:, 512 * half:512 * (half + 1)], wgP[:, 0:512]),
                inc=(("dve", 1),), after=(("te", None),))
        wg_stack.close()
        p.barrier()

        x_s = sba("x_s", [128, 3, 2, D], F32)
        hT_s = sba("hT_s", [128, NDC, L], BF16)
        qTw = sba("qTw", [128, NDC, 2, L], BF16)
        kTMw = sba("kTMw", [128, 2, 2, D], BF16)
        vTMw = sba("vTMw", [128, 2, 2, D], BF16)
        fTMw = sba("fTMw", [128, 2, 2, D], BF16)
        fabs_s = sba("fabs_s", [128, D], F32)
        stat = sba("stat", [128, 24], F32)
        LamB = sba("LamB", [128, 2, L], F32)
        EnG = sba("EnG", [128, 2, L], F32)
        Sp = sba("Sp", [128, 2, 2, L], BF16)
        SpMT = sba("SpMT", [128, 2, 2, L], BF16)
        kT_s = sba("kT_s", [128, 2, L], BF16)
        AmT = sba("AmT", [128, 2, L], BF16)
        WmT = sba("WmT", [128, 2, L], BF16)
        zS = sba("zS", [128, 2, L], F32)
        pEx = sba("pEx", [128, 2, L], BF16)
        PpTM = sba("PpTM", [128, 2, 2, L], BF16)
        PpMT = sba("PpMT", [128, 2, 2, L], BF16)
        nmx = sba("nmx", [128, 2], F32)
        den = sba("den", [128, 2], F32)
        rden = sba("rden", [128, 2, 2], F32)
        lsC = sba("lsC", [128, 2], F32)
        lsR = sba("lsR", [1, L], BF16)
        tmpS = sba("tmpS", [128, 2, M], F32)
        Hm = sba("Hm", [128, H, 2, M], BF16)
        Hv = sba("Hv", [128, H, 2, DV], BF16)
        oB = sba("oB", [128, 2, D], F32)
        onb = sba("onb", [128, 2, D], BF16)
        xnb = onb
        onT = sba("onT", [128, NDC, L], BF16)
        aoT = sba("aoT", [128, NDC, L], BF16)
        y1b = sba("y1b", [128, 2, 1032], BF16)

        PSCE = ab_stack.enter_context(nc.psum_tensor("PSCE", [128, 512], F32))
        PSCO = ab_stack.enter_context(nc.psum_tensor("PSCO", [128, 512], F32))
        PAWA = ab_stack.enter_context(nc.psum_tensor("PAWA", [128, 512], F32))
        PAWB = ab_stack.enter_context(nc.psum_tensor("PAWB", [128, 512], F32))
        PKV0 = ab_stack.enter_context(nc.psum_tensor("PKV0", [128, 512], F32))
        PKV1 = ab_stack.enter_context(nc.psum_tensor("PKV1", [128, 512], F32))
        PQW = ab_stack.enter_context(nc.psum_tensor("PQW", [128, 512], F32))
        PTX = ab_stack.enter_context(nc.psum_tensor("PTX", [128, 1024], BF16))

        p.op("pool", lambda g: g.memset(Hm[:, :, :, :], 0.0), inc=(("pool", 1),))
        p.op("pool", lambda g: g.memset(Hv[:, :, :, :], 0.0), inc=(("pool", 1),))
        p.op("pool", lambda g: g.memset(y1b[:, :, :], 0.0), inc=(("pool", 1),))
        p.barrier()
        cut("init")

        DMASEMS = frozenset(("ld", "lda", "ldb", "lg", "lw", "st", "sta", "stb",
                             "sg", "sga", "sgb", "out", "outa", "outb"))

        def TK(*toks):
            d = {}
            for t in toks:
                for s, v in (t or ()):
                    if v and (s not in d or v > d[s]):
                        d[s] = v
            return tuple(d.items())

        def _promote(tk):
            return tuple((s, None if s in DMASEMS else v) for s, v in tk)

        RW = {}

        def rw(nm):
            e = RW.get(nm)
            if e is None:
                e = RW[nm] = [(), ()]
            return e

        def mark_w(nm, tok):
            e = rw(nm)
            e[0] = tok
            e[1] = ()

        def mark_r(nm, tok):
            e = rw(nm)
            e[1] = TK(e[1], tok)

        PSUMKEYS = frozenset(("scE", "scO", "awA", "awB", "kvA", "kvB", "qw", "pt"))

        def xop(eng, fn, reads=(), writes=(), inc=(), after=(), dma=False):
            deps = [after]
            for nm in reads:
                deps.append(rw(nm)[0])
                if nm in PSUMKEYS:
                    deps.append(rw(nm)[1])
            for nm in writes:
                e = rw(nm)
                deps.append(e[0])
                deps.append(e[1])
            for s, n in inc:
                if s in DMASEMS and p.cnt[s] > 0:
                    deps.append(((s, p.cnt[s]),))
            tok = p.op(eng, fn, inc=inc, after=_promote(TK(*deps)), dma=dma)
            for nm in writes:
                mark_w(nm, tok)
            for nm in reads:
                mark_r(nm, tok)
            return tok

        def mmg(rname, emits, reads=(), after=()):
            e = rw(rname)
            deps = [after, e[0], e[1]]
            for nm in reads:
                deps.append(rw(nm)[0])
            tok = ()
            n = len(emits)
            for i, f in enumerate(emits):
                tok = p.op("te", f, inc=(("te", 1),) if i == n - 1 else (),
                           after=_promote(TK(*deps)) if i == 0 else ())
            for nm in reads:
                mark_r(nm, tok)
            mark_w(rname, tok)
            return tok

        scb = lambda h: (PSCE, "scE") if h % 2 == 0 else (PSCO, "scO")
        KVB = ((PKV0, "kvA"), (PKV1, "kvB"))
        QRr = lambda i: PQW[:, 256 * (i % 2):256 * (i % 2 + 1)]
        kvstate = [0]

        def kvreg():
            i = kvstate[0] & 1
            kvstate[0] += 1
            reg, nm = KVB[i]
            return nm, reg[:, 0:512]

        fl = lambda ap: ap.rearrange("pp a b -> pp (a b)")

        def px(c):
            sx = c % 3
            for tt in range(2):
                r0 = L * c + 128 * tt
                xop("sp", lambda g, sx=sx, tt=tt, r0=r0: g.dma_start(
                    out=x_s[:, sx, tt, :], in_=din["x_b"][r0:r0 + 128, :]),
                    writes=(("x", sx, tt),), inc=(("lda" if tt == 0 else "ldb", 16),))

        def proj_A1(c):
            sx = c % 3
            cut("pA1")
            for tt in range(2):
                xop("act", lambda a, sx=sx, tt=tt: a.activation(
                    out=junk[:, :], in_=x_s[:, sx, tt, :], func=AF.Square,
                    accum_out=stat[:, 0 + tt:1 + tt]), reads=(("x", sx, tt),),
                    writes=(("st", 0 + tt),), inc=(("act", 1),))
            for tt in range(2):
                xop("act", lambda a, tt=tt: a.activation(
                    out=stat[:, 2 + tt:3 + tt], in_=stat[:, 0 + tt:1 + tt], func=AF.Sqrt,
                    scale=1.0 / D, bias=eps_c[:, 0:1]), reads=(("st", 0 + tt),),
                    writes=(("st", 2 + tt),), inc=(("act", 1),))
            for tt in range(2):
                xop("dve", lambda v, tt=tt: v.reciprocal(
                    stat[:, 4 + tt:5 + tt], stat[:, 2 + tt:3 + tt]),
                    reads=(("st", 2 + tt),), writes=(("st", 4 + tt),), inc=(("dve", 1),))
            for tt in range(2):
                xop("act", lambda a, sx=sx, tt=tt: a.activation(
                    out=xnb[:, tt, :], in_=x_s[:, sx, tt, :], func=AF.Copy,
                    scale=stat[:, 4 + tt:5 + tt], bias=0.0),
                    reads=(("x", sx, tt), ("st", 4 + tt)), writes=(("onb", tt),),
                    inc=(("act", 1),))
            cut("pA3")

        def proj_A2(c):
            for tt in range(2):
                mmg("pt", [lambda t, tt=tt, dc=dc: t.transpose(
                    PTX[:, 128 * dc:128 * (dc + 1)],
                    xnb[:, tt, 128 * dc:128 * (dc + 1)], id_b[:, :])
                    for dc in range(NDC)], reads=(("onb", tt),))
                xop("dve", lambda v, tt=tt: v.tensor_copy(
                    hT_s[:, :, 128 * tt:128 * (tt + 1)],
                    PTX[:, 0:1024].rearrange("pp (a b) -> pp a b", a=NDC)),
                    reads=("pt",), writes=(("hT", tt),), inc=(("dve", 1),))

        def proj_f(c, tt):
            w = c % 2
            regs2 = []
            for half in range(2):
                rname, reg = kvreg()
                regs2.append((rname, reg))
                mmg(rname, [lambda t, tt=tt, half=half, di2=di2, reg=reg: t.matmul(
                    reg, hT_s[:, di2, 128 * tt:128 * (tt + 1)],
                    w_f[:, di2, 512 * half:512 * (half + 1)],
                    start=(di2 == 0), stop=(di2 == NDC - 1)) for di2 in range(NDC)],
                    reads=(("hT", tt),))
            for half, (rname, reg) in enumerate(regs2):
                sl = slice(512 * half, 512 * (half + 1))
                xop("act", lambda a, reg=reg, sl=sl: a.activation(
                    out=fabs_s[:, sl], in_=reg, func=AF.Abs),
                    reads=(rname,), writes=(("fab", half),), inc=(("act", 1),))
            for half, (rname, reg) in enumerate(regs2):
                sl = slice(512 * half, 512 * (half + 1))
                xop("act", lambda a, sl=sl: a.activation(
                    out=fabs_s[:, sl], in_=fabs_s[:, sl], func=AF.Exp, scale=-1.0),
                    reads=(("fab", half),), writes=(("fab", half),), inc=(("act", 1),))
            for half, (rname, reg) in enumerate(regs2):
                sl = slice(512 * half, 512 * (half + 1))
                xop("act", lambda a, sl=sl: a.activation(
                    out=fabs_s[:, sl], in_=fabs_s[:, sl], func=AF.Ln, bias=one_f[:, 0:1]),
                    reads=(("fab", half),), writes=(("fab", half),), inc=(("act", 1),))
            for half, (rname, reg) in enumerate(regs2):
                sl = slice(512 * half, 512 * (half + 1))
                xop("dve", lambda v, reg=reg, w=w, tt=tt, sl=sl: v.scalar_tensor_tensor(
                    out=fTMw[:, w, tt, sl], in0=reg, scalar=0.0, in1=fabs_s[:, sl],
                    op0=ALU.min, op1=ALU.subtract),
                    reads=(rname, ("fab", half)), writes=(("f", w, tt, half),),
                    inc=(("dve", 1),))

        def proj_q(c, dcs):
            w = c % 2
            for pi in range(0, len(dcs), 2):
                pair = dcs[pi:pi + 2]
                for dc in pair:
                    mmg("qw", [lambda t, dc=dc, di2=di2: t.matmul(
                        QRr(dc), w_q[:, di2, 128 * dc:128 * (dc + 1)], hT_s[:, di2, :],
                        start=(di2 == 0), stop=(di2 == NDC - 1)) for di2 in range(NDC)],
                        reads=(("hT", 0), ("hT", 1)))
                for dc in pair:
                    xop("act", lambda a, dc=dc, w=w: a.activation(
                        out=qTw[:, dc, w, :], in_=QRr(dc), func=AF.Silu),
                        reads=("qw",), writes=(("q", w, dc),), inc=(("act", 1),))

        def proj_kv(c, wt, key, tt):
            w = c % 2
            dst = kTMw if key == "k" else vTMw
            for half in range(2):
                rname, reg = kvreg()
                mmg(rname, [lambda t, tt=tt, half=half, di2=di2, reg=reg, wt=wt: t.matmul(
                    reg, hT_s[:, di2, 128 * tt:128 * (tt + 1)],
                    wt[:, di2, 512 * half:512 * (half + 1)],
                    start=(di2 == 0), stop=(di2 == NDC - 1)) for di2 in range(NDC)],
                    reads=(("hT", tt),))
                xop("act", lambda a, w=w, tt=tt, half=half, dst=dst, reg=reg: a.activation(
                    out=dst[:, w, tt, 512 * half:512 * (half + 1)], in_=reg,
                    func=AF.Silu), reads=(rname,), writes=((key, w, tt, half),),
                    inc=(("act", 1),))

        def scan_p1(c, h, last):
            w = c % 2
            hp = h % 2
            hp2 = h // 2
            mcs = slice(M * h, M * (h + 1))
            dks = slice(DK * h, DK * (h + 1))
            SC, scn = scb(h)
            mmg("pt", [lambda t, tt2=tt2, dkc2=dkc2, w=w, dks=dks: t.transpose(
                PTX[:, 128 * (2 * dkc2 + tt2):128 * (2 * dkc2 + tt2 + 1)],
                kTMw[:, w, tt2, dks.start + 128 * dkc2:dks.start + 128 * (dkc2 + 1)],
                id_b[:, :]) for tt2 in range(2) for dkc2 in range(2)],
                reads=(("k", w, 0, hp2), ("k", w, 1, hp2)))
            xop("dve", lambda v: v.tensor_copy(
                fl(kT_s[:, :, :]), PTX[:, 0:512]),
                reads=("pt",), writes=(("kT",),), inc=(("dve", 1),))
            mmg(scn, [lambda t, tt2=tt2, tp2=tp2, w=w, mcs=mcs, SC=SC: t.matmul(
                SC[:, 256 * tt2:256 * tt2 + 256],
                mc_s[:, tp2, 128 * tt2:128 * (tt2 + 1)], fTMw[:, w, tp2, mcs],
                start=(tp2 == 0), stop=(tp2 == 1))
                for tt2 in range(2) for tp2 in range(2)],
                reads=(("f", w, 0, hp2), ("f", w, 1, hp2)))
            if h == 0:
                cut("s_g")
            xop("act", lambda a, SC=SC: a.activation(
                out=fl(LamB[:, :, :]), in_=SC[:, 0:512], func=AF.Exp, scale=1.0 / GN),
                reads=(scn,), writes=(("LamB",),), inc=(("act", 1),))
            xop("act", lambda a, SC=SC: a.activation(
                out=fl(EnG[:, :, :]), in_=SC[:, 0:512], func=AF.Exp, scale=-1.0 / GN),
                reads=(scn,), writes=(("EnG",),), inc=(("act", 1),))
            xop("dve", lambda v, w=w, mcs=mcs, SC=SC: v.tensor_sub(
                zS[:, :, :], fTMw[:, w, :, mcs],
                SC[:, 0:512].rearrange("pp (a b) -> pp a b", a=2)),
                reads=(scn, ("f", w, 0, hp2), ("f", w, 1, hp2)),
                writes=(("zS",),), inc=(("dve", 1),))
            xop("act", lambda a: a.activation(
                out=fl(zS[:, :, :]), in_=fl(zS[:, :, :]), func=AF.Exp, scale=1.0 / GN),
                reads=(("zS",),), writes=(("zS",),), inc=(("act", 1),))
            xop("dve", lambda v, hp=hp: v.tensor_sub(
                fl(Sp[:, hp, :, :]), fl(EnG[:, :, :]), fl(zS[:, :, :])),
                reads=(("EnG",), ("zS",)), writes=(("Sp", hp),), inc=(("dve", 1),))
            if h == 0:
                cut("s_kt")
            mmg("awA", [lambda t, taut=taut2, dkc2=dkc2, h=h, w=w: t.matmul(
                PAWA[:, 256 * taut:256 * (taut + 1)],
                kT_s[:, dkc2, 128 * taut:128 * (taut + 1)],
                qTw[:, 2 * h + dkc2, w, :], start=(dkc2 == 0), stop=(dkc2 == 1))
                for taut2 in range(2) for dkc2 in range(2)],
                reads=(("kT",), ("q", w, 2 * h), ("q", w, 2 * h + 1)))
            xop("dve", lambda v: v.tensor_mul(
                fl(AmT[:, :, :]), PAWA[:, 0:512], fl(tri_s[:, :, :])),
                reads=("awA",), writes=(("AmT",),), inc=(("dve", 1),))
            mmg("awB",
                [lambda t, tt2=tt2, i4=i4, h=h, w=w, hp=hp: (
                    t.matmul(PAWB[:, 256 * tt2:256 * tt2 + 256],
                             qTw[:, 2 * h + i4, w, 128 * tt2:128 * (tt2 + 1)],
                             Hm[:, h, i4, :], start=(i4 == 0), stop=False)
                    if i4 < 2 else
                    t.matmul(PAWB[:, 256 * tt2:256 * tt2 + 256],
                             AmT[:, i4 - 2, 128 * tt2:128 * (tt2 + 1)],
                             Sp[:, hp, i4 - 2, :], start=False, stop=(i4 == 3)))
                 for tt2 in range(2) for i4 in range(4)],
                reads=(("q", w, 2 * h), ("q", w, 2 * h + 1), ("Hm", h),
                       ("AmT",), ("Sp", hp)))
            xop("dve", lambda v: v.tensor_mul(
                fl(zS[:, :, :]), PAWB[:, 0:512], fl(LamB[:, :, :])),
                reads=("awB", ("LamB",)), writes=(("zS",),), inc=(("dve", 1),))
            for tt in range(2):
                xop("dve", lambda v, tt=tt: v.tensor_reduce(
                    out=nmx[:, tt:tt + 1], in_=zS[:, tt, :], axis=AX.X, op=ALU.max,
                    negate=True), reads=(("zS",),), writes=(("nmx", tt),),
                    inc=(("dve", 1),))
            for tt in range(2):
                xop("act", lambda a, tt=tt: a.activation(
                    out=pEx[:, tt, :], in_=zS[:, tt, :], func=AF.Exp,
                    bias=nmx[:, tt:tt + 1], accum_out=den[:, tt:tt + 1]),
                    reads=(("zS",), ("nmx", tt)), writes=(("pEx",), ("den", tt)),
                    inc=(("act", 1),))
            xop("dve", lambda v, hp=hp: v.tensor_mul(
                fl(PpTM[:, hp, :, :]), fl(pEx[:, :, :]), fl(LamB[:, :, :])),
                reads=(("pEx",), ("LamB",)), writes=(("Pp", hp),), inc=(("dve", 1),))
            xop("dve", lambda v, hp=hp: v.reciprocal(rden[:, hp, 0:2], den[:, 0:2]),
                reads=(("den", 0), ("den", 1)), writes=(("rden", hp),), inc=(("dve", 1),))
            if h == 0:
                cut("s_z")

        def scan_p2(c, h, last):
            w = c % 2
            wn = (c + 1) % 2
            hp = h % 2
            hp2 = h // 2
            mcs = slice(M * h, M * (h + 1))
            dks = slice(DK * h, DK * (h + 1))
            SC, scn = scb(h)
            mmg("pt", [lambda t, tt2=tt2, mcx2=mcx2, base=base, hp=hp: t.transpose(
                PTX[:, 128 * (base + 2 * mcx2 + tt2):128 * (base + 2 * mcx2 + tt2 + 1)],
                Sp[:, hp, tt2, 128 * mcx2:128 * (mcx2 + 1)] if base == 0
                else PpTM[:, hp, tt2, 128 * mcx2:128 * (mcx2 + 1)],
                id_b[:, :])
                for base in (0, 4) for tt2 in range(2) for mcx2 in range(2)],
                reads=(("Sp", hp), ("Pp", hp)))
            xop("dve", lambda v, hp=hp: v.tensor_copy(fl(SpMT[:, hp, :, :]), PTX[:, 0:512]),
                reads=("pt",), writes=(("SpM", hp),), inc=(("dve", 1),))
            xop("dve", lambda v, hp=hp: v.tensor_copy(fl(PpMT[:, hp, :, :]), PTX[:, 512:1024]),
                reads=("pt",), writes=(("PpM", hp),), inc=(("dve", 1),))
            if h == 0:
                cut("s_t2")
            mmg("awA", [lambda t, taut=taut2, mcx2=mcx2, hp=hp: t.matmul(
                PAWA[:, 256 * taut:256 * (taut + 1)],
                SpMT[:, hp, mcx2, 128 * taut:128 * (taut + 1)], PpMT[:, hp, mcx2, :],
                start=(mcx2 == 0), stop=(mcx2 == 1))
                for taut2 in range(2) for mcx2 in range(2)],
                reads=(("SpM", hp), ("PpM", hp)))
            xop("dve", lambda v: v.tensor_mul(
                fl(WmT[:, :, :]), PAWA[:, 0:512], fl(tri_s[:, :, :])),
                reads=("awA",), writes=(("WmT",),), inc=(("dve", 1),))
            col_emits = []
            for mcx in range(2):
                col_emits.append(lambda t, mcx=mcx, w=w, mcs=mcs, last=last, SC=SC: t.matmul(
                    SC[:, mcx:mcx + 1],
                    fTMw[:, w, 1, mcs.start + 128 * mcx:mcs.start + 128 * (mcx + 1)],
                    one_b[:, 0:1], start=True, stop=last))
                if not last:
                    col_emits.append(lambda t, mcx=mcx, wn=wn, mcs=mcs, SC=SC: t.matmul(
                        SC[:, mcx:mcx + 1],
                        fTMw[:, wn, 0, mcs.start + 128 * mcx:mcs.start + 128 * (mcx + 1)],
                        one_b[:, 0:1], start=False, stop=True))
            col_emits.append(lambda t, w=w, mcs=mcs, last=last, SC=SC: t.matmul(
                SC[0:1, 4:4 + 256], one_b[:, 0:1], fTMw[:, w, 1, mcs],
                start=True, stop=last))
            if not last:
                col_emits.append(lambda t, wn=wn, mcs=mcs, SC=SC: t.matmul(
                    SC[0:1, 4:4 + 256], one_b[:, 0:1], fTMw[:, wn, 0, mcs],
                    start=False, stop=True))
            lamreads = (("f", w, 1, hp2),) + ((("f", wn, 0, hp2),) if not last else ())
            mmg(scn, col_emits, reads=lamreads)
            xop("act", lambda a, SC=SC: a.activation(
                out=lsC[:, 0:2], in_=SC[:, 0:2], func=AF.Exp, scale=1.0 / GN),
                reads=(scn,), writes=(("lsC",),), inc=(("act", 1),))
            xop("act", lambda a, SC=SC: a.activation(
                out=lsR[0:1, :], in_=SC[0:1, 4:4 + 256], func=AF.Exp, scale=1.0 / GN),
                reads=(scn,), writes=(("lsR",),), inc=(("act", 1),))
            mmg(scn, [lambda t, SC=SC: t.matmul(
                SC[:, 256:512], onerow_b[0:1, :], lsR[0:1, 0:256], start=True,
                stop=True)], reads=(("lsR",),))
            if h == 0:
                cut("s_l")
            mmg("awB", [lambda t, dkc=dkc2, taut2=taut2, w=w, dks=dks, hp=hp: t.matmul(
                PAWB[:, 256 * dkc:256 * (dkc + 1)],
                kTMw[:, w, taut2, dks.start + 128 * dkc:dks.start + 128 * (dkc + 1)],
                Sp[:, hp, taut2, :], start=(taut2 == 0), stop=(taut2 == 1))
                for dkc2 in range(2) for taut2 in range(2)],
                reads=(("k", w, 0, hp2), ("k", w, 1, hp2), ("Sp", hp)))
            xop("dve", lambda v, h=h: v.tensor_add(
                fl(tmpS[:, :, :]), fl(Hm[:, h, :, :]), PAWB[:, 0:512]),
                reads=("awB", ("Hm", h)), writes=(("tmp",),), inc=(("dve", 1),))
            for dkc in range(2):
                xop("dve", lambda v, dkc=dkc, h=h, SC=SC: v.tensor_mul(
                    Hm[:, h, dkc, :], tmpS[:, dkc, :], SC[:, 256:512]),
                    reads=(("tmp",), scn), writes=(("Hm", h),), inc=(("dve", 1),))
            mmg(scn,
                [lambda t, tt2=tt2, i4=i4, h=h, w=w, SC=SC, hp=hp: (
                    t.matmul(SC[:, 256 * tt2:256 * tt2 + 256],
                             PpMT[:, hp, i4, 128 * tt2:128 * (tt2 + 1)], Hv[:, h, i4, :],
                             start=(i4 == 0), stop=False)
                    if i4 < 2 else
                    t.matmul(SC[:, 256 * tt2:256 * tt2 + 256],
                             WmT[:, i4 - 2, 128 * tt2:128 * (tt2 + 1)],
                             vTMw[:, w, i4 - 2, DV * h:DV * (h + 1)], start=False,
                             stop=(i4 == 3)))
                 for tt2 in range(2) for i4 in range(4)],
                reads=(("PpM", hp), ("Hv", h), ("WmT",), ("v", w, 0, hp2),
                       ("v", w, 1, hp2)))
            for tt in range(2):
                xop("dve", lambda v, tt=tt, h=h, SC=SC, hp=hp: v.tensor_scalar(
                    out=oB[:, tt, DV * h:DV * (h + 1)],
                    in0=SC[:, 256 * tt:256 * tt + 256],
                    scalar1=rden[:, hp, tt:tt + 1], scalar2=None, op0=ALU.mult),
                    reads=(scn, ("rden", hp)), writes=(("oB", tt),),
                    inc=(("dve", 1),))
            mmg("awB", [lambda t, mcx=mcx2, taut2=taut2, w=w, h=h, hp=hp: t.matmul(
                PAWB[:, 256 * mcx:256 * (mcx + 1)],
                Sp[:, hp, taut2, 128 * mcx:128 * (mcx + 1)],
                vTMw[:, w, taut2, DV * h:DV * (h + 1)],
                start=(taut2 == 0), stop=(taut2 == 1))
                for mcx2 in range(2) for taut2 in range(2)],
                reads=(("Sp", hp), ("v", w, 0, hp2), ("v", w, 1, hp2)))
            xop("dve", lambda v, h=h: v.tensor_add(
                fl(tmpS[:, :, :]), fl(Hv[:, h, :, :]), PAWB[:, 0:512]),
                reads=("awB", ("Hv", h)), writes=(("tmp",),), inc=(("dve", 1),))
            for mcx in range(2):
                xop("dve", lambda v, mcx=mcx, h=h: v.tensor_scalar(
                    out=Hv[:, h, mcx, :], in0=tmpS[:, mcx, :], scalar1=lsC[:, mcx:mcx + 1],
                    scalar2=None, op0=ALU.mult),
                    reads=(("tmp",), ("lsC",)), writes=(("Hv", h),),
                    inc=(("dve", 1),))
            if h == 0:
                cut("s_o")

        def epi_pre(c):
            for tt in range(2):
                xop("act", lambda a, tt=tt: a.activation(
                    out=oB[:, tt, :], in_=oB[:, tt, :], func=AF.Silu),
                    reads=(("oB", tt),), writes=(("oB", tt),), inc=(("act", 1),))
            for tt in range(2):
                xop("act", lambda a, tt=tt: a.activation(
                    out=junk[:, :], in_=oB[:, tt, :], func=AF.Square,
                    accum_out=stat[:, 6 + tt:7 + tt]), reads=(("oB", tt),),
                    writes=(("st", 6 + tt),), inc=(("act", 1),))
            for tt in range(2):
                xop("act", lambda a, tt=tt: a.activation(
                    out=stat[:, 8 + tt:9 + tt], in_=stat[:, 6 + tt:7 + tt], func=AF.Sqrt,
                    scale=1.0 / D, bias=eps_c[:, 0:1]), reads=(("st", 6 + tt),),
                    writes=(("st", 8 + tt),), inc=(("act", 1),))
            for tt in range(2):
                xop("dve", lambda v, tt=tt: v.reciprocal(
                    stat[:, 10 + tt:11 + tt], stat[:, 8 + tt:9 + tt]),
                    reads=(("st", 8 + tt),), writes=(("st", 10 + tt),), inc=(("dve", 1),))
            for tt in range(2):
                xop("act", lambda a, tt=tt: a.activation(
                    out=onb[:, tt, :], in_=oB[:, tt, :], func=AF.Copy,
                    scale=stat[:, 10 + tt:11 + tt], bias=0.0),
                    reads=(("oB", tt), ("st", 10 + tt)), writes=(("onb", tt),),
                    inc=(("act", 1),))
            for tt in range(2):
                mmg("pt", [lambda t, tt=tt, dc=dc: t.transpose(
                    PTX[:, 128 * dc:128 * (dc + 1)], onb[:, tt, 128 * dc:128 * (dc + 1)],
                    id_b[:, :]) for dc in range(NDC)], reads=(("onb", tt),))
                xop("dve", lambda v, tt=tt: v.tensor_copy(
                    onT[:, :, 128 * tt:128 * (tt + 1)],
                    PTX[:, 0:1024].rearrange("pp (a b) -> pp a b", a=NDC)),
                    reads=("pt",), writes=(("onT", tt),), inc=(("dve", 1),))

        def epi_wo(c):
            for pi in range(0, NDC, 2):
                for dc in (pi, pi + 1):
                    mmg("qw", [lambda t, dc=dc, di2=di2: t.matmul(
                        QRr(dc), w_o[:, di2, 128 * dc:128 * (dc + 1)], onT[:, di2, :],
                        start=(di2 == 0), stop=(di2 == NDC - 1)) for di2 in range(NDC)],
                        reads=(("onT", 0), ("onT", 1)))
                for dc in (pi, pi + 1):
                    xop("dve", lambda v, dc=dc: v.tensor_copy(aoT[:, dc, :], QRr(dc)),
                        reads=("qw",), writes=(("aoT", dc),), inc=(("dve", 1),))

        def epi_fin(c, tt):
            sx = c % 3
            mmg("pt", [lambda t, tt=tt, dc=dc: t.transpose(
                PTX[:, 128 * dc:128 * (dc + 1)], aoT[:, dc, 128 * tt:128 * (tt + 1)],
                id_b[:, :]) for dc in range(NDC)],
                reads=tuple(("aoT", dc) for dc in range(NDC)))
            for half in range(2):
                xop("dve", lambda v, tt=tt, sx=sx, half=half: v.tensor_add(
                    x_s[:, sx, tt, 512 * half:512 * (half + 1)],
                    x_s[:, sx, tt, 512 * half:512 * (half + 1)],
                    PTX[:, 512 * half:512 * (half + 1)]),
                    reads=(("x", sx, tt), "pt"), writes=(("x", sx, tt),),
                    inc=(("dve", 1),))
            m = 12 + 4 * tt
            xop("dve", lambda v, tt=tt, sx=sx, m=m: v.tensor_reduce(
                out=stat[:, m:m + 1], in_=x_s[:, sx, tt, :], axis=AX.X, op=ALU.add),
                reads=(("x", sx, tt),), writes=(("st", m),), inc=(("dve", 1),))
            xop("act", lambda a, tt=tt, sx=sx, m=m: a.activation(
                out=junk[:, :], in_=x_s[:, sx, tt, :], func=AF.Square,
                accum_out=stat[:, m + 1:m + 2]), reads=(("x", sx, tt),),
                writes=(("st", m + 1),), inc=(("act", 1),))
            xop("dve", lambda v, m=m: v.tensor_scalar(
                out=stat[:, m:m + 1], in0=stat[:, m:m + 1], scalar1=1.0 / D,
                scalar2=None, op0=ALU.mult),
                reads=(("st", m),), writes=(("st", m),), inc=(("dve", 1),))
            xop("dve", lambda v, m=m: v.tensor_mul(
                stat[:, m + 2:m + 3], stat[:, m:m + 1], stat[:, m:m + 1]),
                reads=(("st", m),), writes=(("st", m + 2),), inc=(("dve", 1),))
            xop("dve", lambda v, m=m: v.scalar_tensor_tensor(
                out=stat[:, m + 1:m + 2], in0=stat[:, m + 1:m + 2], scalar=1.0 / D,
                in1=stat[:, m + 2:m + 3], op0=ALU.mult, op1=ALU.subtract),
                reads=(("st", m + 1), ("st", m + 2)), writes=(("st", m + 1),),
                inc=(("dve", 1),))
            xop("act", lambda a, m=m: a.activation(
                out=stat[:, m + 1:m + 2], in_=stat[:, m + 1:m + 2], func=AF.Sqrt,
                bias=eps_c[:, 0:1]), reads=(("st", m + 1),), writes=(("st", m + 1),),
                inc=(("act", 1),))
            xop("dve", lambda v, m=m: v.reciprocal(
                stat[:, m + 3:m + 4], stat[:, m + 1:m + 2]),
                reads=(("st", m + 1),), writes=(("st", m + 3),), inc=(("dve", 1),))
            xop("dve", lambda v, m=m: v.scalar_tensor_tensor(
                out=stat[:, m + 2:m + 3], in0=stat[:, m:m + 1], scalar=-1.0,
                in1=stat[:, m + 3:m + 4], op0=ALU.mult, op1=ALU.mult),
                reads=(("st", m), ("st", m + 3)), writes=(("st", m + 2),),
                inc=(("dve", 1),))
            xop("act", lambda a, tt=tt, sx=sx, m=m: a.activation(
                out=x_s[:, sx, tt, :], in_=x_s[:, sx, tt, :], func=AF.Identity,
                scale=stat[:, m + 3:m + 4], bias=stat[:, m + 2:m + 3]),
                reads=(("x", sx, tt), ("st", m + 3), ("st", m + 2)),
                writes=(("x", sx, tt),), inc=(("act", 1),))
            xop("dve", lambda v, tt=tt, sx=sx: v.tensor_copy(
                y1b[:, tt, 0:D], x_s[:, sx, tt, :]),
                reads=(("x", sx, tt),), writes=(("y1b", tt),), inc=(("dve", 1),))
            r0 = L * c + 128 * tt
            xop("sp", lambda g, tt=tt, r0=r0, sx=sx: g.dma_start(
                out=y1f32d[r0:r0 + 128, :], in_=x_s[:, sx, tt, :]),
                reads=(("x", sx, tt),), inc=(("outa" if tt == 0 else "outb", 16),))
            if "y1" in taps:
                xop("sp", lambda g, tt=tt, r0=r0, sx=sx: g.dma_start(
                    out=tap["y1"][r0:r0 + 128, :], in_=x_s[:, sx, tt, :]),
                    reads=(("x", sx, tt),), inc=(("st", 16),))
            xop("dve", lambda v, tt=tt, sx=sx: v.tensor_mul(
                x_s[:, sx, tt, :], x_s[:, sx, tt, :], wgbc[:, :]),
                reads=(("x", sx, tt),), writes=(("x", sx, tt),), inc=(("dve", 1),))
            xop("dve", lambda v, tt=tt, sx=sx, c=c: v.tensor_reduce(
                out=lgT[:, 2 * c + tt:2 * c + tt + 1], in_=x_s[:, sx, tt, :], axis=AX.X,
                op=ALU.add), reads=(("x", sx, tt),), writes=(("lgT",),),
                inc=(("dve", 1),))
            xop("dve", lambda v, tt=tt, c=c: v.tensor_copy(
                y1b[:, tt, D:D + 1], lgT[:, 2 * c + tt:2 * c + tt + 1]),
                reads=(("lgT",),), writes=(("y1b", tt),), inc=(("dve", 1),))
            xop("sp", lambda g, tt=tt, r0=r0: g.dma_start(
                out=y1d[r0:r0 + 128, :], in_=y1b[:, tt, :]),
                reads=(("y1b", tt),), inc=(("sta" if tt == 0 else "stb", 16),))

        def ab_step(s):
            lastc = (s - 1 == NCH - 1)
            ep = s - 2
            doepi = 2 <= s <= NCH + 1
            if doepi:
                if "o" in taps:
                    p.barrier()
                    for tt in range(2):
                        p.op("sp", lambda g, ep=ep, tt=tt: g.dma_start(
                            out=tap["o"][L * ep + 128 * tt:L * ep + 128 * (tt + 1), :],
                            in_=oB[:, tt, :]), inc=(("st", 16),))
                    p.barrier()
                epi_pre(ep)
            if s == 0:
                px(0)
            if s < NCH:
                proj_A1(s)
            if doepi:
                epi_wo(ep)
            if s < NCH:
                proj_A2(s)
                cut("pA")
                proj_f(s, 0)
                cut("pF")
            sc = s >= 1 and s <= NCH
            if sc:
                scan_p1(s - 1, 0, lastc)
            if s < NCH:
                proj_q(s, (0, 1, 2, 3))
                cut("pQ")
            if sc:
                scan_p1(s - 1, 1, lastc)
            if s < NCH:
                proj_kv(s, w_k, "k", 0)
                proj_kv(s, w_k, "k", 1)
                cut("pK")
            if sc:
                scan_p2(s - 1, 0, lastc)
            if s < NCH:
                proj_kv(s, w_v, "v", 0)
            if sc:
                scan_p2(s - 1, 1, lastc)
            if s < NCH:
                proj_kv(s, w_v, "v", 1)
                proj_f(s, 1)
            if sc:
                scan_p1(s - 1, 2, lastc)
            if doepi:
                epi_fin(ep, 0)
            if sc:
                scan_p1(s - 1, 3, lastc)
            if s < NCH:
                proj_q(s, (4, 5, 6, 7))
            if sc:
                scan_p2(s - 1, 2, lastc)
            if doepi:
                epi_fin(ep, 1)
            if sc:
                scan_p2(s - 1, 3, lastc)
            if s + 1 < NCH:
                px(s + 1)
                if "qT" in taps:
                    p.barrier()
                    for dc in range(NDC):
                        p.op("pool", lambda g, s=s, dc=dc: g.dma_start(
                            out=tap["qT"][128 * dc:128 * (dc + 1), L * s:L * (s + 1)],
                            in_=qTw[:, dc, s % 2, :]), inc=(("lg", 16),), dma=True)
                if "kvf" in taps:
                    p.barrier()
                    for tt in range(2):
                        r0 = L * s + 128 * tt
                        for i, buf in ((0, kTMw), (1, vTMw), (2, fTMw)):
                            p.op("pool", lambda g, i=i, buf=buf, r0=r0, s=s, tt=tt: g.dma_start(
                                out=tap["kvf"][i, r0:r0 + 128, :], in_=buf[:, s % 2, tt, :]),
                                inc=(("lg", 16),), dma=True)
                if "qT" in taps or "kvf" in taps:
                    p.barrier()

        for step in range(NCH + 2):
            ab_step(step)
            if step < NCH:
                cut(("proj", step))
            if step >= 1 and step <= NCH:
                cut(("scan", step - 1))
        p.barrier()
        ab_stack.close()
        cut("ab")

        PAB = stack.enter_context(nc.psum_tensor("PAB", [128, 1024], F32))
        PCD = stack.enter_context(nc.psum_tensor("PCD", [128, 1024], F32))
        PE_ = stack.enter_context(nc.psum_tensor("PE_", [128, 512], F32))
        PF_ = stack.enter_context(nc.psum_tensor("PF_", [128, 512], F32))
        PT_ = stack.enter_context(nc.psum_tensor("PT_", [128, 1024], BF16))
        PS_ = stack.enter_context(nc.psum_tensor("PS_", [128, 512], F32))
        id_f = sb("id_f", [128, 128], F32)
        lt64_s = sb("lt64_s", [T // 128, T // 128], F32)
        lt128_s = sb("lt128_s", [128, 128], F32)
        sel2_s = sb("sel2_s", [2, 256], BF16)
        sqb = sb("sqb", [128, 512], BF16)
        nrm2 = sb("nrm2", [1, FH, (T // 2) // 512], F32)
        v2S = sb("v2S", [128, (T // 2) // 128, FH * 65], BF16)
        kmx2 = sb("kmx2", [1, FH], F32)
        qmx2 = sb("qmx2", [1, FH], F32)
        for nm, dst in (("ident", id_f), ("lt64", lt64_s), ("lt128", lt128_s)):
            p.op("sp", lambda g, nm=nm, dst=dst: g.dma_start(out=dst[:, :], in_=din[nm][:, :]),
                 inc=(("ld", 16),))
        p.op("pool", lambda g: g.dma_start(out=sel2_s[:, :], in_=din["sel2"][:, :]),
             inc=(("lg", 16),), dma=True)
        p.barrier()

        if "logits" in taps:
            p.op("sp", lambda g: g.dma_start(out=tap["logits"][:, :], in_=lgT[:, :]),
                 inc=(("st", 16),))
            p.barrier()

        NQT = T // 128

        lo = sb("lo", [128, 1], F32)
        hi = sb("hi", [128, 1], F32)
        mid = sb("mid", [128, 1], F32)
        sel = sb("sel", [128, 1], F32)
        selm1 = sb("selm1", [128, 1], F32)
        dtmp = sb("dtmp", [128, 1], F32)
        cntc = sb("cntc", [128, 1], F32)
        cnts = sb("cnts", [1, 1], F32)
        junkN = sb("junkN", [128, NQT], F32)
        p.op("dve", lambda v: v.memset(lo[:, :], -256.0), inc=(("dve", 1),))
        p.op("dve", lambda v: v.memset(hi[:, :], 256.0), inc=(("dve", 1),))
        p.op("dve", lambda v: v.memset(mid[:, :], 0.0), inc=(("dve", 1),))
        for it in range(BIS_ITERS):
            p.op("dve", lambda v: v.tensor_scalar(
                out=junkN[:, :], in0=lgT[:, :], scalar1=mid[:, 0:1], scalar2=0.0,
                op0=ALU.is_ge, op1=ALU.add, accum_out=cntc[:, 0:1]), inc=(("dve", 1),))
            p.op("te", lambda t: t.matmul(PS_[0:1, 0:1], cntc[:, 0:1], one_f[:, 0:1],
                                          start=True, stop=True), inc=(("te", 1),),
                 after=(("dve", None),))
            p.op("dve", lambda v: v.tensor_copy(cnts[0:1, 0:1], PS_[0:1, 0:1]),
                 inc=(("dve", 1),), after=(("te", None),))
            p.op("te", lambda t: t.matmul(PS_[:, 2:3], onesrow_f[0:1, :], cnts[0:1, 0:1],
                                          start=True, stop=True), inc=(("te", 1),),
                 after=(("dve", None),))
            p.op("dve", lambda v: v.tensor_scalar(
                out=sel[:, 0:1], in0=PS_[:, 2:3], scalar1=float(CSEL), scalar2=None,
                op0=ALU.is_ge), inc=(("dve", 1),), after=(("te", None),))
            p.op("dve", lambda v: v.tensor_scalar(
                out=selm1[:, 0:1], in0=sel[:, 0:1], scalar1=1.0, scalar2=None,
                op0=ALU.subtract), inc=(("dve", 1),))
            p.op("dve", lambda v: v.tensor_sub(dtmp[:, 0:1], mid[:, 0:1], lo[:, 0:1]),
                 inc=(("dve", 1),))
            p.op("dve", lambda v: v.scalar_tensor_tensor(
                out=lo[:, 0:1], in0=dtmp[:, 0:1], scalar=sel[:, 0:1], in1=lo[:, 0:1],
                op0=ALU.mult, op1=ALU.add), inc=(("dve", 1),))
            p.op("dve", lambda v: v.tensor_sub(dtmp[:, 0:1], hi[:, 0:1], mid[:, 0:1]),
                 inc=(("dve", 1),))
            p.op("dve", lambda v: v.scalar_tensor_tensor(
                out=hi[:, 0:1], in0=dtmp[:, 0:1], scalar=selm1[:, 0:1], in1=hi[:, 0:1],
                op0=ALU.mult, op1=ALU.add), inc=(("dve", 1),))
            p.op("dve", lambda v: v.tensor_add(dtmp[:, 0:1], lo[:, 0:1], hi[:, 0:1]),
                 inc=(("dve", 1),))
            p.op("dve", lambda v: v.tensor_scalar(
                out=mid[:, 0:1], in0=dtmp[:, 0:1], scalar1=0.5, scalar2=None, op0=ALU.mult),
                inc=(("dve", 1),))
        if "theta" in taps:
            p.op("sp", lambda g: g.dma_start(out=tap["theta"][:, :], in_=lo[:, :]),
                 inc=(("st", 16),), after=(("dve", None),))
        p.barrier()
        cut("bisect")

        maskA = sb("maskA", [128, NQT], F32)
        maskT = sb("maskT", [NQT, 128], F32)
        colcnt = sb("colcnt", [NQT, 1], F32)
        colpf = sb("colpf", [NQT, 1], F32)
        slotT = sb("slotT", [NQT, 128], F32)
        t1s = sb("t1s", [NQT, 128], F32)
        wsS = sb("wsS", [1, 1], F32)
        wsI = sb("wsI", [1, 1], I32)
        wsB = sb("wsB", [NQT, 1], F32)
        slotI = sb("slotI", [128, NQT], I32)
        srelW = sb("srelW", [128, MYT // 128], F32)

        p.op("dve", lambda v: v.tensor_scalar(
            out=maskA[:, :], in0=lgT[:, :], scalar1=lo[:, 0:1], scalar2=None,
            op0=ALU.is_ge), inc=(("dve", 1),))
        p.op("te", lambda t: t.transpose(PS_[0:NQT, 64:64 + 128], maskA[:, :], id_f[:, :]),
             inc=(("te", 1),), after=(("dve", None),))
        p.op("dve", lambda v: v.tensor_copy(maskT[:, :], PS_[0:NQT, 64:64 + 128]),
             inc=(("dve", 1),), after=(("te", None),))
        p.op("dve", lambda v: v.tensor_reduce(
            out=colcnt[:, 0:1], in_=maskT[:, :], axis=AX.X, op=ALU.add), inc=(("dve", 1),))
        p.op("te", lambda t: t.matmul(PS_[0:NQT, 0:1], lt64_s[:, :], colcnt[:, 0:1],
                                      start=True, stop=True), inc=(("te", 1),),
             after=(("dve", None),))
        p.op("dve", lambda v: v.tensor_copy(colpf[:, 0:1], PS_[0:NQT, 0:1]),
             inc=(("dve", 1),), after=(("te", None),))
        p.op("te", lambda t: t.matmul(PS_[0:NQT, 192:192 + 128], maskA[:, :], lt128_s[:, :],
                                      start=True, stop=True), inc=(("te", 1),),
             after=(("dve", None),))
        p.op("dve", lambda v: v.tensor_scalar(
            out=slotT[:, :], in0=PS_[0:NQT, 192:192 + 128], scalar1=colpf[:, 0:1],
            scalar2=None, op0=ALU.add), inc=(("dve", 1),), after=(("te", None),))
        p.op("dve", lambda v: v.tensor_scalar(
            out=t1s[:, :], in0=maskT[:, :], scalar1=-1.0e9, scalar2=1.0e9,
            op0=ALU.mult, op1=ALU.add), inc=(("dve", 1),))
        p.op("dve", lambda v: v.tensor_add(slotT[:, :], slotT[:, :], t1s[:, :]),
             inc=(("dve", 1),))
        p.op("sp", lambda g: g.dma_start(out=cpfD[:, :], in_=colpf[:, :]),
             inc=(("st", 16),), after=(("dve", None),))
        stv = p.c("st")

        def rt_read_cpf(g):
            r = g.value_load(myq_s[0:1, 0:1])
            return g.dma_start(out=wsS[0:1, 0:1], in_=cpfD[bass.ts(r, NQT // 4), :][0:1, :])
        p.op("pool", rt_read_cpf, dma=True, inc=(("lg", 16),), after=(("st", stv),))
        p.op("dve", lambda v: v.tensor_scalar(
            out=wsS[0:1, 0:1], in0=wsS[0:1, 0:1], scalar1=float(CSEL - QW), scalar2=None,
            op0=ALU.min), inc=(("dve", 1),), after=(("lg", None),))
        p.op("dve", lambda v: v.tensor_scalar(
            out=wsS[0:1, 0:1], in0=wsS[0:1, 0:1], scalar1=0.0, scalar2=None,
            op0=ALU.max), inc=(("dve", 1),))
        p.op("dve", lambda v: v.tensor_copy(wsI[0:1, 0:1], wsS[0:1, 0:1]), inc=(("dve", 1),))
        p.op("te", lambda t: t.matmul(PS_[0:NQT, 32:33], onesrow_f[0:1, 0:NQT],
                                      wsS[0:1, 0:1], start=True, stop=True),
             inc=(("te", 1),), after=(("dve", None),))
        p.op("dve", lambda v: v.tensor_copy(wsB[:, 0:1], PS_[0:NQT, 32:33]),
             inc=(("dve", 1),), after=(("te", None),))
        p.op("dve", lambda v: v.tensor_scalar(
            out=t1s[:, :], in0=slotT[:, :], scalar1=wsB[:, 0:1], scalar2=None,
            op0=ALU.subtract), inc=(("dve", 1),), after=(("pool", None),))
        p.op("sp", lambda g: g.dma_start(out=srelD[:, :], in_=t1s[:, :]),
             inc=(("st", 16),), after=(("dve", None),))
        p.op("te", lambda t: t.transpose(PS_[:, 320:320 + NQT], slotT[:, :],
                                         id_f[0:NQT, 0:NQT]), inc=(("te", 1),),
             after=(("dve", None),))
        p.op("dve", lambda v: v.tensor_copy(slotI[:, :], PS_[:, 320:320 + NQT]),
             inc=(("dve", 1),), after=(("te", None),))
        if "slot" in taps:
            p.op("sp", lambda g: g.dma_start(out=tap["slot"][:, :], in_=PS_[:, 320:320 + NQT]),
                 inc=(("st", 16),), after=(("te", None),))
        p.barrier()

        rowbuf = sb("rowbuf", [128, 2, 1032], BF16)
        regh = {}
        p.op("pool", lambda g: regh.setdefault("bc", g.to_reg(CSEL - 1)) and None, dma=True)
        scat_hist = []
        for i in range(NQT):
            bi = i % 2
            par = "lda" if bi == 0 else "ldb"
            spar = "sga" if bi == 0 else "sgb"
            p.op("sp", lambda g, i=i, bi=bi: g.dma_start(
                out=rowbuf[:, bi, :], in_=y1d[128 * i:128 * (i + 1), :]), inc=((par, 16),),
                after=((spar, scat_hist[i - 2]),) if i >= 2 else ())
            ldv = p.c(par)
            p.op("pool", lambda g, i=i, bi=bi: g.indirect_dma_start(
                out=atn[:, :], out_offset=IndirectOffsetOnAxis(slotI[:, i:i + 1], 0),
                in_=rowbuf[:, bi, :], in_offset=None, bounds_check=regh["bc"],
                oob_is_err=False), inc=((spar, 16),), after=((par, ldv),), dma=True)
            scat_hist.append(p.c(spar))
        p.barrier()
        cut("scatter")
        if "attn_in" in taps:
            atnf = sb("atnf", [128, 1032], F32)
            for i in range(CSEL // 128):
                p.op("pool", lambda g, i=i: g.dma_start(
                    out=atnf[:, :], in_=atn[128 * i:128 * (i + 1), :]), inc=(("lg", 16),),
                    after=(("st", None),), dma=True)
                p.op("sp", lambda g, i=i: g.dma_start(
                    out=tap["attn_in"][128 * i:128 * (i + 1), :], in_=atnf[:, :]),
                    inc=(("st", 16),), after=(("lg", None),))
                p.barrier()

        kv_stack = contextlib.ExitStack()
        sbk = lambda name, shape, dt: kv_stack.enter_context(nc.sbuf_tensor(name, shape, dt))
        fwk_s = sbk("fwk_s", [128, NDC, D], BF16)
        fwv_s = sbk("fwv_s", [128, NDC, D], BF16)
        for (dst, src) in ((fwk_s, "fWk"), (fwv_s, "fWv")):
            for dc in range(NDC):
                p.op("pool", lambda g, d=dst, s=src, dc=dc: g.dma_start(
                    out=d[:, dc, :], in_=din[s][128 * dc:128 * (dc + 1), :]),
                    inc=(("lw", 16),), dma=True)

        aT = sbk("aT", [128, NDC, CSEL], BF16)
        for j in range(NDC):
            p.op("sp", lambda g, j=j: g.dma_start(
                out=aT[:, j, :], in_=atn[:, 128 * j:128 * (j + 1)], transpose=True),
                inc=(("ld", 16),))
        p.barrier()
        cut("aT")
        p.wait("act", "lw")
        p.wait("dve", "lw")
        p.wait("te", "lw")

        kvbuf = sbk("kvbuf", [128, 2, FH * 65], BF16)
        k2st_hist = []
        cpv = {}
        sqv = {}
        nrv = {}
        for dc in range(NDC):
            for qt in range(CSEL // 512):
                it = dc * (CSEL // 512) + qt
                reg = PAB[:, 0:512] if qt % 2 == 0 else PAB[:, 512:1024]
                for di in range(NDC):
                    p.op("te", lambda t, dc=dc, qt=qt, di=di, reg=reg: t.matmul(
                        reg, fwk_s[:, di, 128 * dc:128 * (dc + 1)],
                        aT[:, di, 512 * qt:512 * (qt + 1)],
                        start=(di == 0), stop=(di == NDC - 1)),
                        inc=(("te", 1),) if di == NDC - 1 else (),
                        after=((("dve", cpv.get(it - 2, 0)),) if di == 0 else ()))
                grp = p.c("te")
                bi = it % 2
                spar = "sta" if bi == 0 else "stb"
                p.op("dve", lambda v, bi=bi, reg=reg: v.tensor_copy(kvbuf[:, bi, 0:512], reg),
                     inc=(("dve", 1),),
                     after=(("te", grp),) + (((spar, k2st_hist[it - 2]),) if it >= 2 else ()))
                cpv[it] = p.c("dve")
                p.op("sp", lambda g, dc=dc, qt=qt, bi=bi: g.dma_start(
                    out=k2d[dc, :, 512 * qt:512 * (qt + 1)], in_=kvbuf[:, bi, 0:512]),
                    inc=((spar, 16),), after=(("dve", cpv[it]),))
                k2st_hist.append(p.c(spar))
                p.op("act", lambda a, bi=bi: a.activation(
                    out=sqb[:, 0:512], in_=kvbuf[:, bi, 0:512], func=AF.Square),
                    inc=(("act", 1),), after=(("dve", cpv[it]),))
                sqv[it] = p.c("act")
                for hh in range(2):
                    h = 2 * dc + hh
                    p.op("te", lambda t, hh=hh: t.matmul(
                        PS_[0:1, 0:512], one_b[0:64, 0:1] if hh == 0 else one_b[64:128, 0:1],
                        sqb[0:64, 0:512] if hh == 0 else sqb[64:128, 0:512],
                        start=True, stop=True), inc=(("te", 1),),
                        after=(("act", sqv[it]),) +
                        ((("dve", nrv[it - 1]),) if (hh == 0 and it >= 1) else
                         ((("dve", None),) if hh == 1 else ())))
                    p.op("dve", lambda v, h=h, qt=qt: v.tensor_reduce(
                        out=nrm2[0:1, h, qt:qt + 1], in_=PS_[0:1, 0:512], axis=AX.X,
                        op=ALU.max), inc=(("dve", 1),), after=(("te", None),))
                nrv[it] = p.c("dve")
        for h in range(FH):
            p.op("dve", lambda v, h=h: v.tensor_reduce(
                out=kmx2[0:1, h:h + 1], in_=nrm2[0:1, h, :], axis=AX.X, op=ALU.max),
                inc=(("dve", 1),))
        p.barrier()
        p.op("dve", lambda v: v.memset(v2S[:, :, :], 0.0), inc=(("dve", 1),))
        for h in range(FH):
            p.op("dve", lambda v, h=h: v.memset(
                v2S[:, :, 65 * h + 64:65 * h + 65], 1.0), inc=(("dve", 1),))
        vcp = {}
        for kt in range(NKT):
            for half in range(2):
                ih = 2 * kt + half
                reg = PCD[:, 0:512] if half == 0 else PCD[:, 512:1024]
                for di in range(NDC):
                    p.op("te", lambda t, kt=kt, half=half, di=di, reg=reg: t.matmul(
                        reg, aT[:, di, 128 * kt:128 * (kt + 1)],
                        fwv_s[:, di, 512 * half:512 * (half + 1)],
                        start=(di == 0), stop=(di == NDC - 1)),
                        inc=(("te", 1),) if di == NDC - 1 else (),
                        after=((("dve", vcp.get(ih - 2, 0)),) if di == 0 else ()))
                vgrp = p.c("te")
                for hh in range(8):
                    h = 8 * half + hh
                    p.op("dve", lambda v, kt=kt, h=h, hh=hh, reg=reg: v.tensor_copy(
                        v2S[:, kt, 65 * h:65 * h + 64], reg[:, 64 * hh:64 * (hh + 1)]),
                        inc=(("dve", 1),),
                        after=((("te", vgrp),) if hh == 0 else ()))
                vcp[ih] = p.c("dve")
        p.barrier()
        cut("kv")

        kv_stack.close()
        d_stack = contextlib.ExitStack()
        sbd = lambda name, shape, dt: d_stack.enter_context(nc.sbuf_tensor(name, shape, dt))
        q2T = sbd("q2T", [128, NDC, QW], BF16)
        bmix = sbd("bmix", [1, FH], F32)
        bpair = sbd("bpair", [1, NDC], F32)
        bpb = sbd("bpb", [1, NDC], BF16)
        bcols = sbd("bcols", [128, NDC], F32)
        q_stack = contextlib.ExitStack()
        sbq = lambda name, shape, dt: q_stack.enter_context(nc.sbuf_tensor(name, shape, dt))
        fwq_s = sbq("fwq_s", [128, NDC, D], BF16)
        for dc in range(NDC):
            p.op("pool", lambda g, dc=dc: g.dma_start(
                out=fwq_s[:, dc, :], in_=din["fWq"][128 * dc:128 * (dc + 1), :]),
                inc=(("lw", 16),), dma=True)
        myaT = sbq("myaT", [128, NDC, QW], BF16)

        def ldwin(g):
            r = g.value_load(wsI[0:1, 0:1])
            return g.dma_start(out=myrows[:, :], in_=atn[bass.ds(r, QW), :])
        p.op("pool", ldwin, dma=True, inc=(("lg", 16),))
        p.wait("sp", "lg")
        for j in range(NDC):
            p.op("sp", lambda g, j=j: g.dma_start(
                out=myaT[:, j, :], in_=myrows[:, 128 * j:128 * (j + 1)], transpose=True),
                inc=(("ld", 16),))
        p.barrier()

        qt_sizes = [(i * 512, min(512, QW - i * 512)) for i in range((QW + 511) // 512)]
        for dc in range(NDC):
            for (q0, qn) in qt_sizes:
                reg = PAB[:, 0:512]
                for di in range(NDC):
                    p.op("te", lambda t, dc=dc, di=di, q0=q0, qn=qn, reg=reg: t.matmul(
                        reg[:, 0:qn], fwq_s[:, di, 128 * dc:128 * (dc + 1)],
                        myaT[:, di, q0:q0 + qn], start=(di == 0), stop=(di == NDC - 1)),
                        inc=(("te", 1),) if di == NDC - 1 else (),
                        after=(("dve", None), ("act", None)) if di == 0 else ())
                p.op("dve", lambda v, dc=dc, q0=q0, qn=qn, reg=reg: v.tensor_copy(
                    q2T[:, dc, q0:q0 + qn], reg[:, 0:qn]), inc=(("dve", 1),),
                    after=(("te", None),))
                p.op("act", lambda a, qn=qn, reg=reg: a.activation(
                    out=sqb[:, 0:qn], in_=reg[:, 0:qn], func=AF.Square),
                    inc=(("act", 1),), after=(("dve", None),))
                qi = q0 // 512
                for hh in range(2):
                    h = 2 * dc + hh
                    p.op("te", lambda t, hh=hh, qn=qn: t.matmul(
                        PS_[0:1, 0:qn], one_b[0:64, 0:1] if hh == 0 else one_b[64:128, 0:1],
                        sqb[0:64, 0:qn] if hh == 0 else sqb[64:128, 0:qn],
                        start=True, stop=True), inc=(("te", 1),),
                        after=(("act", None), ("dve", None)))
                    p.op("dve", lambda v, h=h, qi=qi, qn=qn: v.tensor_reduce(
                        out=nrm2[0:1, h, qi:qi + 1], in_=PS_[0:1, 0:qn], axis=AX.X,
                        op=ALU.max), inc=(("dve", 1),), after=(("te", None),))
        NQB = len(qt_sizes)
        for h in range(FH):
            p.op("dve", lambda v, h=h: v.tensor_reduce(
                out=qmx2[0:1, h:h + 1], in_=nrm2[0:1, h, 0:NQB], axis=AX.X, op=ALU.max),
                inc=(("dve", 1),))
        p.op("dve", lambda v: v.tensor_mul(bmix[0:1, :], qmx2[0:1, :], kmx2[0:1, :]),
             inc=(("dve", 1),))
        p.op("act", lambda a: a.activation(out=bmix[0:1, :], in_=bmix[0:1, :], func=AF.Sqrt,
                                           scale=1.0 / 64.0), inc=(("act", 1),),
             after=(("dve", None),))
        p.op("dve", lambda v: v.tensor_max(
            bpair[0:1, :], bmix.rearrange("o (dc hh) -> o dc hh", hh=2)[0:1, :, 0],
            bmix.rearrange("o (dc hh) -> o dc hh", hh=2)[0:1, :, 1]),
            inc=(("dve", 1),), after=(("act", None),))
        p.op("dve", lambda v: v.tensor_scalar(
            out=bpair[0:1, :], in0=bpair[0:1, :], scalar1=-1.0, scalar2=None, op0=ALU.mult),
            inc=(("dve", 1),))
        p.op("dve", lambda v: v.tensor_copy(bpb[0:1, :], bpair[0:1, :]), inc=(("dve", 1),))
        p.op("te", lambda t: t.matmul(PS_[:, 0:NDC], onerow_b[0:1, :], bpb[0:1, :],
                                      start=True, stop=True), inc=(("te", 1),),
             after=(("dve", None),))
        p.op("dve", lambda v: v.tensor_copy(bcols[:, :], PS_[:, 0:NDC]), inc=(("dve", 1),),
             after=(("te", None),))
        p.barrier()
        cut("q2")

        q_stack.close()
        o2hT = sbd("o2hT", [128, NDC, QW], BF16)
        sc_stack = contextlib.ExitStack()
        sbs = lambda name, shape, dt: sc_stack.enter_context(nc.sbuf_tensor(name, shape, dt))
        eS = sbs("eS", [128, 2, 1024], BF16)
        k2blk = sbs("k2blk", [128, 2, 128], BF16)
        drow = sbs("drow", [128, 2, 512], F32)
        dr2 = sbs("dr2", [2, 512], BF16)
        rcpbE = sbs("rcpbE", [128, 512], F32)
        rcpbO = sbs("rcpbO", [128, 512], F32)
        oddst = sbs("oddst", [128, 512], BF16)
        schist = {}
        expv = {}
        for hp in range(NDC):
            for (q0, qn) in qt_sizes:
                for kt in range(NKT + 1):
                    if kt < NKT:
                        bi = kt % 2
                        par = "lda" if bi == 0 else "ldb"
                        guard = (("te", schist[kt - 2]),) if kt >= 2 else ()
                        p.op("sp", lambda g, hp=hp, kt=kt, bi=bi: g.dma_start(
                            out=k2blk[:, bi, :], in_=k2d[hp, :, 128 * kt:128 * (kt + 1)]),
                            inc=((par, 16),), after=guard)
                        ldk = p.c(par)
                        pS = PAB if kt % 2 == 0 else PCD
                        sguard = ((("act", expv[kt - 2]),) if kt >= 2 else
                                  ((("act", None),) if (hp, q0) != (0, 0) else ()))
                        p.op("te", lambda t, bi=bi, hp=hp, q0=q0, qn=qn, pS=pS, par=par: t.matmul(
                            pS[:, 0:qn], k2blk[0:64, bi, :], q2T[0:64, hp, q0:q0 + qn],
                            start=True, stop=True, tile_position=(0, 0)),
                            inc=(("te", 1),), after=((par, ldk),) + sguard)
                        p.op("te", lambda t, bi=bi, hp=hp, q0=q0, qn=qn, pS=pS: t.matmul(
                            pS[:, 512:512 + qn], k2blk[64:128, bi, :],
                            q2T[64:128, hp, q0:q0 + qn],
                            start=True, stop=True, tile_position=(64, 0)), inc=(("te", 1),))
                        schist[kt] = p.c("te")
                        p.op("act", lambda a, hp=hp, qn=qn, pS=pS, bi=bi: a.activation(
                            out=eS[:, bi, 0:2 * qn],
                            in_=pS[:, 0:1024].rearrange("pp (a b) -> pp a b", a=2)[:, :, 0:qn],
                            func=AF.Exp, scale=0.125, bias=bcols[:, hp:hp + 1]),
                            inc=(("act", 1),), after=(("te", schist[kt]),))
                        expv[kt] = p.c("act")
                    if kt >= 1:
                        ka = kt - 1
                        for par in range(2):
                            av = PE_ if par == 0 else PF_
                            p.op("te", lambda t, hp=hp, ka=ka, par=par, av=av, qn=qn: t.matmul(
                                av[0:65, 0:qn],
                                v2S[:, ka, 65 * (2 * hp + par):65 * (2 * hp + par) + 65],
                                eS[:, ka % 2, qn * par:qn * (par + 1)],
                                start=(ka == 0), stop=(ka == NKT - 1)),
                                inc=(("te", 1),),
                                after=(("act", expv[ka]),) + ((("dve", None),) if ka == 0 else ()))
                tev = p.c("te")
                for par in range(2):
                    av = PE_ if par == 0 else PF_
                    p.op("dve", lambda v, par=par, av=av, qn=qn: v.tensor_copy(
                        drow[64:65, par, 0:qn], av[64:65, 0:qn]), inc=(("dve", 1),),
                        after=(("te", tev), ("lg", None)))
                    p.op("dve", lambda v, par=par, qn=qn: v.reciprocal(
                        drow[64:65, par, 0:qn], drow[64:65, par, 0:qn]), inc=(("dve", 1),))
                    p.op("pool", lambda g, par=par, qn=qn: g.dma_start(
                        out=dr2[par:par + 1, 0:qn], in_=drow[64:65, par, 0:qn]),
                        inc=(("lg", 16),), after=(("dve", None), ("te", None)), dma=True)
                ldd = p.c("lg")
                p.op("te", lambda t, qn=qn: t.matmul(
                    PS_[:, 0:qn], sel2_s[:, 0:128], dr2[:, 0:qn], start=True, stop=True),
                    inc=(("te", 1),), after=(("lg", ldd), ("dve", None)))
                p.op("dve", lambda v, qn=qn: v.tensor_copy(rcpbE[:, 0:qn], PS_[:, 0:qn]),
                     inc=(("dve", 1),), after=(("te", None),))
                p.op("te", lambda t, qn=qn: t.matmul(
                    PS_[:, 0:qn], sel2_s[:, 128:256], dr2[:, 0:qn], start=True, stop=True),
                    inc=(("te", 1),), after=(("dve", None),))
                p.op("dve", lambda v, qn=qn: v.tensor_copy(rcpbO[:, 0:qn], PS_[:, 0:qn]),
                     inc=(("dve", 1),), after=(("te", None),))
                p.op("dve", lambda v, hp=hp, q0=q0, qn=qn: v.tensor_mul(
                    o2hT[0:64, hp, q0:q0 + qn], PE_[0:64, 0:qn], rcpbE[0:64, 0:qn]),
                    inc=(("dve", 1),))
                p.op("dve", lambda v, qn=qn: v.tensor_mul(
                    oddst[0:64, 0:qn], PF_[0:64, 0:qn], rcpbO[0:64, 0:qn]),
                    inc=(("dve", 1),), after=(("st", None),))
                p.op("sp", lambda g, hp=hp, q0=q0, qn=qn: g.dma_start(
                    out=o2hT[64:128, hp, q0:q0 + qn], in_=oddst[0:64, 0:qn]),
                    inc=(("st", 16),), after=(("dve", None),))
                cut(("sc", hp, q0))
        cut("scores")
        if "o2" in taps:
            o2f = sbs("o2f", [128, QW], F32)
            for dc in range(NDC):
                p.op("dve", lambda v, dc=dc: v.tensor_copy(o2f[:, :], o2hT[:, dc, :]),
                     inc=(("dve", 1),))
                p.op("sp", lambda g, dc=dc: g.dma_start(
                    out=tap["o2"][128 * dc:128 * (dc + 1), :], in_=o2f[:, :]),
                    inc=(("st", 16),), after=(("dve", None),))
                p.barrier()

        sc_stack.close()
        wo_stack = contextlib.ExitStack()
        sbw = lambda name, shape, dt: wo_stack.enter_context(nc.sbuf_tensor(name, shape, dt))
        fwo_s = sbw("fwo_s", [128, NDC, D], BF16)
        for dc in range(NDC):
            p.op("pool", lambda g, dc=dc: g.dma_start(
                out=fwo_s[:, dc, :], in_=din["fWo"][128 * dc:128 * (dc + 1), :]),
                inc=(("lw", 16),), dma=True,
                after=(("lg", None), ("lda", None), ("ldb", None), ("st", None)))
        p.wait("te", "lw")
        ffnT = sbw("ffnT", [128, NDC, QW], BF16)
        fsl = sbw("fsl", [128, 2, D], BF16)
        for dc in range(NDC):
            for (q0, qn) in qt_sizes:
                reg = PAB[:, 0:512]
                for di in range(NDC):
                    p.op("te", lambda t, dc=dc, di=di, q0=q0, qn=qn, reg=reg: t.matmul(
                        reg[:, 0:qn], fwo_s[:, di, 128 * dc:128 * (dc + 1)],
                        o2hT[:, di, q0:q0 + qn], start=(di == 0), stop=(di == NDC - 1)),
                        inc=(("te", 1),) if di == NDC - 1 else (),
                        after=(("dve", None),) if di == 0 else ())
                p.op("dve", lambda v, dc=dc, q0=q0, qn=qn, reg=reg: v.tensor_copy(
                    ffnT[:, dc, q0:q0 + qn], reg[:, 0:qn]), inc=(("dve", 1),),
                    after=(("te", None),))
        p.barrier()
        fst_hist = []
        for tb in range(QW // 128):
            spar = "sta" if tb % 2 == 0 else "stb"
            for dc in range(NDC):
                p.op("te", lambda t, tb=tb, dc=dc: t.transpose(
                    PT_[:, 128 * dc:128 * (dc + 1)], ffnT[:, dc, 128 * tb:128 * (tb + 1)],
                    id_b[:, :]), inc=(("te", 1),),
                    after=(("dve", None),))
                guard = (("sta" if tb % 2 == 0 else "stb", fst_hist[tb - 2]),) if (tb >= 2 and dc == 0) else ()
                p.op("dve", lambda v, tb=tb, dc=dc: v.tensor_copy(
                    fsl[:, tb % 2, 128 * dc:128 * (dc + 1)], PT_[:, 128 * dc:128 * (dc + 1)]),
                    inc=(("dve", 1),), after=(("te", None),) + guard)
            p.op("sp", lambda g, tb=tb: g.dma_start(
                out=o2d[128 * tb:128 * (tb + 1), :], in_=fsl[:, tb % 2, :]),
                inc=((spar, 16),), after=(("dve", None),))
            fst_hist.append(p.c(spar))
        p.barrier()
        p.barrier()
        cut("wo")

        wo_stack.close()
        d_stack.close()
        fin_stack = contextlib.ExitStack()
        sbf = lambda name, shape, dt: fin_stack.enter_context(nc.sbuf_tensor(name, shape, dt))
        y1t = sbf("y1t", [128, MYT // 128, D], F32)
        gbuf = sbf("gbuf", [128, D], BF16)
        idxc = sbf("idxc", [128, 2], I32)
        srl16 = sbf("srl16", [MYT // 128, 128], F32)
        fcol = sbf("fcol", [128, 1], F32)
        ytmp = sbf("ytmp", [128, D], F32)
        yout = sbf("yout", [128, D], F32)
        NJT = MYT // 128

        def rt_load_srel(g):
            r = g.value_load(myq_s[0:1, 0:1])
            return g.dma_start(out=srl16[:, :], in_=srelD[bass.ts(r, NJT), :])
        p.op("pool", rt_load_srel, dma=True, inc=(("lg", 16),), after=(("st", None),))

        def rt_load_y1(g):
            r = g.value_load(myq_s[0:1, 0:1])
            return g.dma_start(
                out=y1t[:, :, :],
                in_=y1f32d[bass.ds(r * MYT, MYT), :].rearrange("(j pp) d -> pp j d", pp=128))
        p.op("pool", rt_load_y1, dma=True, inc=(("lg", 16),), after=(("st", None),))
        p.op("te", lambda t: t.transpose(PS_[:, 64:64 + NJT], srl16[:, :],
                                         id_f[0:NJT, 0:NJT]), inc=(("te", 1),),
             after=(("lg", None),))
        p.op("dve", lambda v: v.tensor_copy(srelW[:, :], PS_[:, 64:64 + NJT]),
             inc=(("dve", 1),), after=(("te", None),))
        if "dbg" in taps:
            p.op("dve", lambda v: v.tensor_copy(junkN[:, 0:MYT // 128], srelW[:, :]),
                 inc=(("dve", 1),))
            p.op("sp", lambda g: g.dma_start(out=tap["dbg"][:, 0:MYT // 128],
                                             in_=junkN[:, 0:MYT // 128]),
                 inc=(("st", 16),), after=(("dve", None),))
        p.barrier()

        for j in range(NJT):
            ldyv = p.c("lg")
            gguard = (("sg", 16 * (j - 1)),) if j >= 2 else ()
            p.op("dve", lambda v, j=j: v.tensor_copy(idxc[:, j % 2:j % 2 + 1], srelW[:, j:j + 1]),
                 inc=(("dve", 1),), after=gguard)
            p.op("dve", lambda v: v.memset(gbuf[:, :], 0.0), inc=(("dve", 1),))
            dvg = p.c("dve")
            if j == 0:
                p.op("pool", lambda g: regh.__setitem__("bc2", g.to_reg(QW - 1)), dma=True)
            p.op("pool", lambda g, j=j: g.indirect_dma_start(
                out=gbuf[:, :], out_offset=None, in_=o2d[:, :],
                in_offset=IndirectOffsetOnAxis(idxc[:, j % 2:j % 2 + 1], 0),
                bounds_check=regh["bc2"],
                oob_is_err=False), inc=(("sg", 16),), after=(("dve", dvg),), dma=True)
            sgv2 = p.c("sg")
            if j == 0 and "dbg" in taps:
                p.op("dve", lambda v: v.tensor_copy(junk[:, 0:1], idxc[:, 0:1]),
                     inc=(("dve", 1),))
                p.op("dve", lambda v: v.tensor_copy(junk[:, 1:41], gbuf[:, 0:40]),
                     inc=(("dve", 1),), after=(("sg", sgv2),))
                p.op("sp", lambda g: g.dma_start(out=tap["dbg"][:, 0:41], in_=junk[:, 0:41]),
                     inc=(("st", 16),), after=(("dve", None),))
            p.op("dve", lambda v, j=j: v.tensor_mul(junk[:, :], y1t[:, j, :], wgbc[:, :]),
                 inc=(("dve", 1),),
                 after=(("lg", ldyv),) + ((("st", None),) if (j == 0 and "dbg" in taps) else ()))
            p.op("dve", lambda v: v.tensor_reduce(
                out=colA[:, 0:1], in_=junk[:, :], axis=AX.X, op=ALU.add),
                inc=(("dve", 1),))
            p.op("act", lambda a: a.activation(out=fcol[:, 0:1], in_=colA[:, 0:1],
                                               func=AF.Sigmoid), inc=(("act", 1),),
                 after=(("dve", None),))
            p.op("dve", lambda v, j=j: v.scalar_tensor_tensor(
                out=ytmp[:, :], in0=gbuf[:, :], scalar=fcol[:, 0:1], in1=y1t[:, j, :],
                op0=ALU.mult, op1=ALU.add), inc=(("dve", 1),),
                after=(("sg", sgv2), ("act", None)))
            p.op("dve", lambda v: v.tensor_reduce(
                out=colA[:, 0:1], in_=ytmp[:, :], axis=AX.X, op=ALU.add), inc=(("dve", 1),))
            p.op("act", lambda a: a.activation(
                out=junk[:, :], in_=ytmp[:, :], func=AF.Square, accum_out=colB[:, 0:1]),
                inc=(("act", 1),), after=(("dve", None),))
            p.op("dve", lambda v: v.tensor_scalar(
                out=colA[:, 0:1], in0=colA[:, 0:1], scalar1=1.0 / D, scalar2=None,
                op0=ALU.mult), inc=(("dve", 1),), after=(("act", None),))
            p.op("dve", lambda v: v.tensor_mul(colC[:, 0:1], colA[:, 0:1], colA[:, 0:1]),
                 inc=(("dve", 1),))
            p.op("dve", lambda v: v.scalar_tensor_tensor(
                out=colB[:, 0:1], in0=colB[:, 0:1], scalar=1.0 / D, in1=colC[:, 0:1],
                op0=ALU.mult, op1=ALU.subtract), inc=(("dve", 1),))
            p.op("act", lambda a: a.activation(
                out=colB[:, 0:1], in_=colB[:, 0:1], func=AF.Sqrt, bias=eps_c[:, 0:1]),
                inc=(("act", 1),), after=(("dve", None),))
            p.op("dve", lambda v: v.reciprocal(colB[:, 0:1], colB[:, 0:1]),
                 inc=(("dve", 1),), after=(("act", None),))
            p.op("dve", lambda v: v.scalar_tensor_tensor(
                out=colC[:, 0:1], in0=colA[:, 0:1], scalar=-1.0, in1=colB[:, 0:1],
                op0=ALU.mult, op1=ALU.mult), inc=(("dve", 1),))
            if j == 0 and "dbg" in taps:
                p.op("dve", lambda v: v.tensor_copy(junkN[:, 0:4], ytmp[:, 0:4]),
                     inc=(("dve", 1),))
                p.op("dve", lambda v: v.tensor_copy(junkN[:, 4:5], fcol[:, 0:1]),
                     inc=(("dve", 1),))
                p.op("dve", lambda v: v.tensor_copy(junkN[:, 5:6], colB[:, 0:1]),
                     inc=(("dve", 1),))
                p.op("dve", lambda v: v.tensor_copy(junkN[:, 6:7], colC[:, 0:1]),
                     inc=(("dve", 1),))
            opar = "outa" if j % 2 == 0 else "outb"
            prevpar = "outb" if j % 2 == 0 else "outa"
            oguard = ((prevpar, p.c(prevpar)),) if j >= 1 else ()
            p.op("act", lambda a, j=j: a.activation(
                out=yout[:, :], in_=ytmp[:, :], func=AF.Identity, scale=colB[:, 0:1],
                bias=colC[:, 0:1]), inc=(("act", 1),), after=(("dve", None),) + oguard)
            p.op("sp", lambda g, j=j: g.dma_start(
                out=out_ext[128 * j:128 * (j + 1), :], in_=yout[:, :]),
                inc=((opar, 16),), after=(("act", None),))
        if "dbg" in taps:
            p.barrier()
            p.op("sp", lambda g: g.dma_start(out=tap["dbg"][:, 0:8], in_=junkN[:, 0:8]),
                 inc=(("st", 16),), after=(("dve", None),))
        for e in ENGS:
            p.wait(e, "outa")
            p.wait(e, "outb")
            p.wait(e, "st")
            p.wait(e, "sta")
            p.wait(e, "stb")
            p.wait(e, "sg")

    except _Cut:
        p.barrier()
    try:
        fin_stack.close()
    except Exception:
        pass
    with nc.Block() as block:
        @block.tensor
        def _(te):
            for f in p.q["te"]:
                f(te)

        @block.scalar
        def _(a):
            for f in p.q["act"]:
                f(a)

        @block.vector
        def _(v):
            for f in p.q["dve"]:
                f(v)

        @block.gpsimd
        def _(g):
            for f in p.q["pool"]:
                f(g)

        @block.sync
        def _(s):
            for f in p.q["sp"]:
                f(s)

    try:
        stack.close()
    except Exception:
        pass
    return nc, tap


def _host_inputs(inp):
    x = np.ascontiguousarray(np.asarray(inp["L0_attn_hidden_states"], np.float32))
    mk = {}
    mk["wqT"] = np.ascontiguousarray(np.asarray(inp["L0_attn_w_q"], np.float32).T)
    mk["wkT"] = np.ascontiguousarray(np.asarray(inp["L0_attn_w_k"], np.float32).T)
    mk["wvT"] = np.ascontiguousarray(np.asarray(inp["L0_attn_w_v"], np.float32).T)
    mk["wfT"] = np.ascontiguousarray(np.asarray(inp["L0_attn_w_f"], np.float32).T)
    mk["woT"] = np.ascontiguousarray(np.asarray(inp["L0_attn_w_o"], np.float32).T)
    mk["fWq"] = np.ascontiguousarray(np.asarray(inp["L0_ffn_Wq"], np.float32))
    mk["fWk"] = np.ascontiguousarray(np.asarray(inp["L0_ffn_Wk"], np.float32))
    mk["fWv"] = np.ascontiguousarray(np.asarray(inp["L0_ffn_Wv"], np.float32))
    mk["fWo"] = np.ascontiguousarray(np.asarray(inp["L0_ffn_Wo"], np.float32))
    mk["wg"] = np.ascontiguousarray(np.asarray(inp["L0_ffn_Wg"], np.float32).reshape(1, D))
    mk["nw"] = np.ascontiguousarray(
        np.asarray(inp["L0_attn_norm_weight"], np.float32).reshape(NDC, 128).T)
    mk["gn"] = np.ascontiguousarray(
        np.asarray(inp["L0_attn_gnorm_weight"], np.float32).reshape(NDC, 128).T)
    tpi = np.arange(L)[:, None]
    ti = np.arange(L)[None, :]
    mk["mcum"] = np.ascontiguousarray(
        (tpi <= ti).astype(np.float32) - (tpi <= L // 2 - 1).astype(np.float32))
    mk["trim"] = np.ascontiguousarray((tpi <= ti).astype(np.float32))
    mk["ident"] = np.eye(128, dtype=np.float32)
    c64 = np.arange(T // 128)
    mk["lt64"] = np.ascontiguousarray((c64[:, None] < c64[None, :]).astype(np.float32))
    c128 = np.arange(128)
    mk["lt128"] = np.ascontiguousarray((c128[:, None] < c128[None, :]).astype(np.float32))
    s2 = np.zeros((2, 256), np.float32)
    s2[0, 0:64] = 1.0
    s2[1, 128:192] = 1.0
    mk["sel2"] = s2
    in_maps = []
    for core in range(8):
        b, q = core // 4, core % 4
        m = dict(mk)
        m["x_b"] = np.ascontiguousarray(x[b])
        m["myq16"] = np.array([[q]], np.int32)
        in_maps.append(m)
    return in_maps


def kernel(**inputs):
    nc, _ = build_program(TAPS)
    in_maps = _host_inputs(inputs)
    res = run_bass_kernel_spmd(nc, in_maps, core_ids=list(range(8)))
    parts = [res.results[c]["out_mine"] for c in range(8)]
    out = np.concatenate(parts, axis=0).reshape(B, T, D)
    return np.ascontiguousarray(out.astype(np.float32))


# version 15
# speedup vs baseline: 1.0614x; 1.0072x over previous
import contextlib
import numpy as np
import concourse.bass as bass
import concourse.mybir as mybir
from concourse.bass_utils import run_bass_kernel_spmd
from concourse.bass_core_rust import IndirectOffsetOnAxis

F32 = mybir.dt.float32
BF16 = mybir.dt.bfloat16
I32 = mybir.dt.int32
AF = mybir.ActivationFunctionType
ALU = mybir.AluOpType
AX = mybir.AxisListType

B, D = 2, 1024
T = 8192
H, DK, DV, M = 4, 256, 256, 256
L = 256
NDC = D // 128
FH, DHD = 16, 64
EPS = 1e-5
GN = 16.0
BIS_ITERS = 48
TAPS = ()
STOP_AFTER = None
PARANOID = False
ENGS = ("te", "act", "dve", "pool", "sp")


class _Cut(Exception):
    pass


class PB:
    def __init__(self, nc, stack):
        self.nc = nc
        self.stack = stack
        self.q = {e: [] for e in ENGS}
        self.sems = {}
        self.cnt = {}
        self.hw = {e: {} for e in ENGS}

    def sem(self, name):
        if name not in self.sems:
            self.sems[name] = self.stack.enter_context(self.nc.semaphore(name))
            self.cnt[name] = 0
        return self.sems[name]

    def op(self, eng, fn, inc=(), after=(), dma=False):
        if eng in ("act", "dve", "pool") and not dma:
            after = tuple(after) + ((eng, None),)
            if not any(s == eng for s, _ in inc):
                inc = tuple(inc) + ((eng, 1),)
        hw = self.hw[eng]
        waits = []
        for s, v in after:
            vv = self.cnt[s] if v is None else v
            if vv <= 0 or vv <= hw.get(s, 0):
                continue
            hw[s] = vv
            waits.append((self.sems[s], vv))
        waits = tuple(waits)
        incs = tuple((self.sems[s], n) for s, n in inc)

        def play(h, waits=waits, incs=incs, fn=fn):
            for sh, v in waits:
                h.wait_ge(sh, v)
            ins = fn(h)
            for sh, n in incs:
                ins.then_inc(sh, n)

        self.q[eng].append(play)
        tok = []
        for s, n in inc:
            self.cnt[s] += n
            tok.append((s, self.cnt[s]))
        if PARANOID and eng != "_":
            self.barrier()
        return tuple(tok)

    def wait(self, eng, name, val=None):
        v = self.cnt[name] if val is None else val
        if v > 0 and v > self.hw[eng].get(name, 0):
            self.hw[eng][name] = v
            self.q[eng].append(lambda h, sh=self.sems[name], v=v: h.wait_ge(sh, v))

    def barrier(self):
        for e in ENGS:
            for s in self.sems:
                self.wait(e, s)

    def c(self, name):
        return self.cnt[name]


def build_program(taps=()):
    def cut(name):
        if STOP_AFTER == name:
            raise _Cut()
    NCH = T // L
    CSEL = T // 2
    QW = T // 8 + 128
    MYT = T // 4
    NKT = CSEL // 128

    nc = bass.Bass("TRN2", target_bir_lowering=False, debug=False)
    stack = contextlib.ExitStack()
    p = PB(nc, stack)
    for s in ("ld", "lda", "ldb", "lg", "lw", "te", "act", "dve", "pool", "st", "sg", "sga", "sgb", "sta", "stb", "out", "outa", "outb"):
        p.sem(s)

    din = {}
    for nm, shp, dt in (
        ("x_b", [T, D], F32), ("wqT", [D, D], F32), ("wkT", [D, D], F32),
        ("wvT", [D, D], F32), ("wfT", [D, D], F32), ("woT", [D, D], F32),
        ("fWq", [D, D], F32), ("fWk", [D, D], F32), ("fWv", [D, D], F32),
        ("fWo", [D, D], F32), ("wg", [1, D], F32), ("nw", [128, NDC], F32),
        ("gn", [128, NDC], F32), ("mcum", [L, L], F32), ("trim", [L, L], F32),
        ("ident", [128, 128], F32), ("lt64", [T // 128, T // 128], F32),
        ("lt128", [128, 128], F32), ("sel2", [2, 256], F32),
        ("myq16", [1, 1], I32),
    ):
        din[nm] = nc.dram_tensor(nm, shp, dt, kind="ExternalInput").ap()
    out_ext = nc.dram_tensor("out_mine", [MYT, D], F32, kind="ExternalOutput").ap()

    tap = {}
    for nm, shp in (("qT", [D, T]), ("kvf", [3, T, D]), ("o", [T, D]),
                    ("y1", [T, D]), ("logits", [128, T // 128]),
                    ("theta", [128, 1]), ("slot", [128, T // 128]),
                    ("attn_in", [CSEL, 1032]), ("o2", [D, QW]), ("dbg", [128, 64])):
        if nm in taps:
            tap[nm] = nc.dram_tensor("tap_" + nm, shp, F32, kind="ExternalOutput").ap()

    y1d = nc.dram_tensor("y1d", [T, 1032], BF16).ap()
    y1f32d = nc.dram_tensor("y1f32d", [T, D], F32).ap()
    atn = nc.dram_tensor("atn", [CSEL, 1032], BF16).ap()
    k2d = nc.dram_tensor("k2d", [NDC, 128, CSEL], BF16).ap()
    v2d = nc.dram_tensor("v2d", [CSEL, FH * 65], BF16).ap()
    o2d = nc.dram_tensor("o2d", [QW, D], BF16).ap()
    myrows = nc.dram_tensor("myrows", [QW, 1032], BF16).ap()
    srelD = nc.dram_tensor("srelD", [T // 128, 128], F32).ap()
    cpfD = nc.dram_tensor("cpfD", [T // 128, 1], F32).ap()

    sb = lambda name, shape, dt: stack.enter_context(nc.sbuf_tensor(name, shape, dt))

    try:

        id_b = sb("id_b", [128, 128], BF16)
        eps_c = sb("eps_c", [128, 1], F32)
        one_f = sb("one_f", [128, 1], F32)
        one_b = sb("one_b", [128, 1], BF16)
        onerow_b = sb("onerow_b", [1, 128], BF16)
        onesrow_f = sb("onesrow_f", [1, 128], F32)
        wgbc = sb("wgbc", [128, D], F32)
        myq_s = sb("myq_s", [1, 1], I32)
        junk = sb("junk", [128, D], BF16)
        lgT = sb("lgT", [128, T // 128], F32)
        colA = sb("colA", [128, 1], F32)
        colB = sb("colB", [128, 1], F32)
        colC = sb("colC", [128, 1], F32)
        ab_stack = contextlib.ExitStack()
        sba = lambda name, shape, dt: ab_stack.enter_context(nc.sbuf_tensor(name, shape, dt))
        w_q = sba("w_q", [128, NDC, D], BF16)
        w_k = sba("w_k", [128, NDC, D], BF16)
        w_v = sba("w_v", [128, NDC, D], BF16)
        w_f = sba("w_f", [128, NDC, D], BF16)
        w_o = sba("w_o", [128, NDC, D], BF16)
        nw_t = sba("nw_t", [128, NDC], F32)
        gn_t = sba("gn_t", [128, NDC], F32)
        mc_s = sba("mc_s", [128, 2, L], BF16)
        tri_s = sba("tri_s", [128, 2, L], F32)

        for (dst, src) in (
            (w_q, "wqT"), (w_k, "wkT"), (w_v, "wvT"), (w_f, "wfT"), (w_o, "woT")):
            for dc in range(NDC):
                p.op("pool", lambda g, d=dst, s=src, dc=dc: g.dma_start(
                    out=d[:, dc, :], in_=din[s][128 * dc:128 * (dc + 1), :]),
                    inc=(("lg", 16),), dma=True)
        p.op("pool", lambda g: g.dma_start(
            out=mc_s[:, :, :], in_=din["mcum"].rearrange("(tt pp) t -> pp tt t", pp=128)),
            inc=(("lg", 16),), dma=True)
        p.op("pool", lambda g: g.dma_start(out=id_b[:, :], in_=din["ident"][:, :]),
             inc=(("lg", 16),), dma=True)
        for nm, dst in (("nw", nw_t), ("gn", gn_t), ("myq16", myq_s)):
            p.op("sp", lambda g, nm=nm, dst=dst: g.dma_start(out=dst[:, :], in_=din[nm][:, :]),
                 inc=(("ld", 16),))
        p.op("sp", lambda g: g.dma_start(
            out=tri_s[:, :, :], in_=din["trim"].rearrange("(tt pp) t -> pp tt t", pp=128)),
            inc=(("ld", 16),))
        for f_, v_ in ((eps_c, EPS), (one_f, 1.0), (one_b, 1.0), (onerow_b, 1.0),
                       (onesrow_f, 1.0)):
            p.op("dve", lambda v, f_=f_, v_=v_: v.memset(f_[:, :], v_), inc=(("dve", 1),))
        p.barrier()
        for dc in range(NDC):
            p.op("dve", lambda v, dc=dc: v.tensor_scalar(
                out=w_o[:, dc, :], in0=w_o[:, dc, :], scalar1=gn_t[:, dc:dc + 1],
                scalar2=None, op0=ALU.mult), inc=(("dve", 1),))
        for wt in (w_q, w_k, w_v, w_f):
            for dc in range(NDC):
                p.op("dve", lambda v, wt=wt, dc=dc: v.tensor_scalar(
                    out=wt[:, dc, :], in0=wt[:, dc, :], scalar1=nw_t[:, dc:dc + 1],
                    scalar2=None, op0=ALU.mult), inc=(("dve", 1),))
        wg_stack = contextlib.ExitStack()
        wgrow = wg_stack.enter_context(nc.sbuf_tensor("wgrow", [1, D], F32))
        wgP = wg_stack.enter_context(nc.psum_tensor("wgP", [128, 512], F32))
        p.op("sp", lambda g: g.dma_start(out=wgrow[:, :], in_=din["wg"][:, :]),
             inc=(("ld", 16),))
        p.wait("te", "ld")
        for half in range(D // 512):
            p.op("te", lambda t, half=half: t.matmul(
                wgP[:, 0:512], onesrow_f[:, :], wgrow[:, 512 * half:512 * (half + 1)],
                start=True, stop=True), inc=(("te", 1),),
                after=(("dve", None), ("ld", None), ("lg", None)))
            p.op("dve", lambda v, half=half: v.tensor_copy(
                wgbc[:, 512 * half:512 * (half + 1)], wgP[:, 0:512]),
                inc=(("dve", 1),), after=(("te", None),))
        wg_stack.close()
        p.barrier()

        x_s = sba("x_s", [128, 3, 2, D], F32)
        hT_s = sba("hT_s", [128, NDC, L], BF16)
        qTw = sba("qTw", [128, NDC, 2, L], BF16)
        kTMw = sba("kTMw", [128, 2, 2, D], BF16)
        vTMw = sba("vTMw", [128, 2, 2, D], BF16)
        fTMw = sba("fTMw", [128, 2, 2, D], BF16)
        fabs_s = sba("fabs_s", [128, D], F32)
        stat = sba("stat", [128, 24], F32)
        LamB = sba("LamB", [128, 2, L], F32)
        EnG = sba("EnG", [128, 2, L], F32)
        Sp = sba("Sp", [128, 2, 2, L], BF16)
        SpMT = sba("SpMT", [128, 2, 2, L], BF16)
        kT_s = sba("kT_s", [128, 2, L], BF16)
        AmT = sba("AmT", [128, 2, L], BF16)
        WmT = sba("WmT", [128, 2, L], BF16)
        zS = sba("zS", [128, 2, L], F32)
        pEx = sba("pEx", [128, 2, L], BF16)
        PpTM = sba("PpTM", [128, 2, 2, L], BF16)
        PpMT = sba("PpMT", [128, 2, 2, L], BF16)
        nmx = sba("nmx", [128, 2], F32)
        den = sba("den", [128, 2], F32)
        rden = sba("rden", [128, 2, 2], F32)
        lsC = sba("lsC", [128, 2], F32)
        lsR = sba("lsR", [1, L], BF16)
        tmpS = sba("tmpS", [128, 2, M], F32)
        Hm = sba("Hm", [128, H, 2, M], BF16)
        Hv = sba("Hv", [128, H, 2, DV], BF16)
        oB = sba("oB", [128, 2, D], F32)
        onb = sba("onb", [128, 2, D], BF16)
        xnb = onb
        onT = sba("onT", [128, NDC, L], BF16)
        aoT = sba("aoT", [128, NDC, L], BF16)
        y1b = sba("y1b", [128, 2, 1032], BF16)

        PSCE = ab_stack.enter_context(nc.psum_tensor("PSCE", [128, 512], F32))
        PSCO = ab_stack.enter_context(nc.psum_tensor("PSCO", [128, 512], F32))
        PAWA = ab_stack.enter_context(nc.psum_tensor("PAWA", [128, 512], F32))
        PAWB = ab_stack.enter_context(nc.psum_tensor("PAWB", [128, 512], F32))
        PKV0 = ab_stack.enter_context(nc.psum_tensor("PKV0", [128, 512], F32))
        PKV1 = ab_stack.enter_context(nc.psum_tensor("PKV1", [128, 512], F32))
        PQW = ab_stack.enter_context(nc.psum_tensor("PQW", [128, 512], F32))
        PTX = ab_stack.enter_context(nc.psum_tensor("PTX", [128, 1024], BF16))

        p.op("pool", lambda g: g.memset(Hm[:, :, :, :], 0.0), inc=(("pool", 1),))
        p.op("pool", lambda g: g.memset(Hv[:, :, :, :], 0.0), inc=(("pool", 1),))
        p.op("pool", lambda g: g.memset(y1b[:, :, :], 0.0), inc=(("pool", 1),))
        p.barrier()
        cut("init")

        DMASEMS = frozenset(("ld", "lda", "ldb", "lg", "lw", "st", "sta", "stb",
                             "sg", "sga", "sgb", "out", "outa", "outb"))

        def TK(*toks):
            d = {}
            for t in toks:
                for s, v in (t or ()):
                    if v and (s not in d or v > d[s]):
                        d[s] = v
            return tuple(d.items())

        def _promote(tk):
            return tuple((s, None if s in DMASEMS else v) for s, v in tk)

        RW = {}

        def rw(nm):
            e = RW.get(nm)
            if e is None:
                e = RW[nm] = [(), ()]
            return e

        def mark_w(nm, tok):
            e = rw(nm)
            e[0] = tok
            e[1] = ()

        def mark_r(nm, tok):
            e = rw(nm)
            e[1] = TK(e[1], tok)

        PSUMKEYS = frozenset(("scE", "scO", "awA", "awB", "kvA", "kvB", "qw", "pt"))

        def xop(eng, fn, reads=(), writes=(), inc=(), after=(), dma=False):
            deps = [after]
            for nm in reads:
                deps.append(rw(nm)[0])
                if nm in PSUMKEYS:
                    deps.append(rw(nm)[1])
            for nm in writes:
                e = rw(nm)
                deps.append(e[0])
                deps.append(e[1])
            for s, n in inc:
                if s in DMASEMS and p.cnt[s] > 0:
                    deps.append(((s, p.cnt[s]),))
            tok = p.op(eng, fn, inc=inc, after=_promote(TK(*deps)), dma=dma)
            for nm in writes:
                mark_w(nm, tok)
            for nm in reads:
                mark_r(nm, tok)
            return tok

        def mmg(rname, emits, reads=(), after=()):
            e = rw(rname)
            deps = [after, e[0], e[1]]
            for nm in reads:
                deps.append(rw(nm)[0])
            tok = ()
            n = len(emits)
            for i, f in enumerate(emits):
                tok = p.op("te", f, inc=(("te", 1),) if i == n - 1 else (),
                           after=_promote(TK(*deps)) if i == 0 else ())
            for nm in reads:
                mark_r(nm, tok)
            mark_w(rname, tok)
            return tok

        scb = lambda h: (PSCE, "scE") if h % 2 == 0 else (PSCO, "scO")
        KVB = ((PKV0, "kvA"), (PKV1, "kvB"))
        QRr = lambda i: PQW[:, 256 * (i % 2):256 * (i % 2 + 1)]
        kvstate = [0]

        def kvreg():
            i = kvstate[0] & 1
            kvstate[0] += 1
            reg, nm = KVB[i]
            return nm, reg[:, 0:512]

        fl = lambda ap: ap.rearrange("pp a b -> pp (a b)")

        def px(c):
            sx = c % 3
            for tt in range(2):
                r0 = L * c + 128 * tt
                xop("sp", lambda g, sx=sx, tt=tt, r0=r0: g.dma_start(
                    out=x_s[:, sx, tt, :], in_=din["x_b"][r0:r0 + 128, :]),
                    writes=(("x", sx, tt),), inc=(("lda" if tt == 0 else "ldb", 16),))

        def proj_A1(c):
            sx = c % 3
            cut("pA1")
            for tt in range(2):
                xop("act", lambda a, sx=sx, tt=tt: a.activation(
                    out=junk[:, :], in_=x_s[:, sx, tt, :], func=AF.Square,
                    accum_out=stat[:, 0 + tt:1 + tt]), reads=(("x", sx, tt),),
                    writes=(("st", 0 + tt),), inc=(("act", 1),))
            for tt in range(2):
                xop("act", lambda a, tt=tt: a.activation(
                    out=stat[:, 2 + tt:3 + tt], in_=stat[:, 0 + tt:1 + tt], func=AF.Sqrt,
                    scale=1.0 / D, bias=eps_c[:, 0:1]), reads=(("st", 0 + tt),),
                    writes=(("st", 2 + tt),), inc=(("act", 1),))
            for tt in range(2):
                xop("dve", lambda v, tt=tt: v.reciprocal(
                    stat[:, 4 + tt:5 + tt], stat[:, 2 + tt:3 + tt]),
                    reads=(("st", 2 + tt),), writes=(("st", 4 + tt),), inc=(("dve", 1),))
            for tt in range(2):
                xop("act", lambda a, sx=sx, tt=tt: a.activation(
                    out=xnb[:, tt, :], in_=x_s[:, sx, tt, :], func=AF.Copy,
                    scale=stat[:, 4 + tt:5 + tt], bias=0.0),
                    reads=(("x", sx, tt), ("st", 4 + tt)), writes=(("onb", tt),),
                    inc=(("act", 1),))
            cut("pA3")

        def proj_A2(c):
            for tt in range(2):
                mmg("pt", [lambda t, tt=tt, dc=dc: t.transpose(
                    PTX[:, 128 * dc:128 * (dc + 1)],
                    xnb[:, tt, 128 * dc:128 * (dc + 1)], id_b[:, :])
                    for dc in range(NDC)], reads=(("onb", tt),))
                xop("dve", lambda v, tt=tt: v.tensor_copy(
                    hT_s[:, :, 128 * tt:128 * (tt + 1)],
                    PTX[:, 0:1024].rearrange("pp (a b) -> pp a b", a=NDC)),
                    reads=("pt",), writes=(("hT", tt),), inc=(("dve", 1),))

        def proj_f(c, tt):
            w = c % 2
            regs2 = []
            for half in range(2):
                rname, reg = kvreg()
                regs2.append((rname, reg))
                mmg(rname, [lambda t, tt=tt, half=half, di2=di2, reg=reg: t.matmul(
                    reg, hT_s[:, di2, 128 * tt:128 * (tt + 1)],
                    w_f[:, di2, 512 * half:512 * (half + 1)],
                    start=(di2 == 0), stop=(di2 == NDC - 1)) for di2 in range(NDC)],
                    reads=(("hT", tt),))
            for half, (rname, reg) in enumerate(regs2):
                sl = slice(512 * half, 512 * (half + 1))
                xop("act", lambda a, reg=reg, sl=sl: a.activation(
                    out=fabs_s[:, sl], in_=reg, func=AF.Abs),
                    reads=(rname,), writes=(("fab", half),), inc=(("act", 1),))
            for half, (rname, reg) in enumerate(regs2):
                sl = slice(512 * half, 512 * (half + 1))
                xop("act", lambda a, sl=sl: a.activation(
                    out=fabs_s[:, sl], in_=fabs_s[:, sl], func=AF.Exp, scale=-1.0),
                    reads=(("fab", half),), writes=(("fab", half),), inc=(("act", 1),))
            for half, (rname, reg) in enumerate(regs2):
                sl = slice(512 * half, 512 * (half + 1))
                xop("act", lambda a, sl=sl: a.activation(
                    out=fabs_s[:, sl], in_=fabs_s[:, sl], func=AF.Ln, bias=one_f[:, 0:1]),
                    reads=(("fab", half),), writes=(("fab", half),), inc=(("act", 1),))
            for half, (rname, reg) in enumerate(regs2):
                sl = slice(512 * half, 512 * (half + 1))
                xop("dve", lambda v, reg=reg, w=w, tt=tt, sl=sl: v.scalar_tensor_tensor(
                    out=fTMw[:, w, tt, sl], in0=reg, scalar=0.0, in1=fabs_s[:, sl],
                    op0=ALU.min, op1=ALU.subtract),
                    reads=(rname, ("fab", half)), writes=(("f", w, tt, half),),
                    inc=(("dve", 1),))

        def proj_q(c, dcs):
            w = c % 2
            for pi in range(0, len(dcs), 2):
                pair = dcs[pi:pi + 2]
                for dc in pair:
                    mmg("qw", [lambda t, dc=dc, di2=di2: t.matmul(
                        QRr(dc), w_q[:, di2, 128 * dc:128 * (dc + 1)], hT_s[:, di2, :],
                        start=(di2 == 0), stop=(di2 == NDC - 1)) for di2 in range(NDC)],
                        reads=(("hT", 0), ("hT", 1)))
                for dc in pair:
                    xop("act", lambda a, dc=dc, w=w: a.activation(
                        out=qTw[:, dc, w, :], in_=QRr(dc), func=AF.Silu),
                        reads=("qw",), writes=(("q", w, dc),), inc=(("act", 1),))

        def proj_kv(c, wt, key, tt):
            w = c % 2
            dst = kTMw if key == "k" else vTMw
            for half in range(2):
                rname, reg = kvreg()
                mmg(rname, [lambda t, tt=tt, half=half, di2=di2, reg=reg, wt=wt: t.matmul(
                    reg, hT_s[:, di2, 128 * tt:128 * (tt + 1)],
                    wt[:, di2, 512 * half:512 * (half + 1)],
                    start=(di2 == 0), stop=(di2 == NDC - 1)) for di2 in range(NDC)],
                    reads=(("hT", tt),))
                xop("act", lambda a, w=w, tt=tt, half=half, dst=dst, reg=reg: a.activation(
                    out=dst[:, w, tt, 512 * half:512 * (half + 1)], in_=reg,
                    func=AF.Silu), reads=(rname,), writes=((key, w, tt, half),),
                    inc=(("act", 1),))

        def scan_p1(c, h, last):
            w = c % 2
            hp = h % 2
            hp2 = h // 2
            mcs = slice(M * h, M * (h + 1))
            dks = slice(DK * h, DK * (h + 1))
            SC, scn = scb(h)
            mmg("pt", [lambda t, tt2=tt2, dkc2=dkc2, w=w, dks=dks: t.transpose(
                PTX[:, 128 * (2 * dkc2 + tt2):128 * (2 * dkc2 + tt2 + 1)],
                kTMw[:, w, tt2, dks.start + 128 * dkc2:dks.start + 128 * (dkc2 + 1)],
                id_b[:, :]) for tt2 in range(2) for dkc2 in range(2)],
                reads=(("k", w, 0, hp2), ("k", w, 1, hp2)))
            xop("dve", lambda v: v.tensor_copy(
                fl(kT_s[:, :, :]), PTX[:, 0:512]),
                reads=("pt",), writes=(("kT",),), inc=(("dve", 1),))
            mmg(scn, [lambda t, tt2=tt2, tp2=tp2, w=w, mcs=mcs, SC=SC: t.matmul(
                SC[:, 256 * tt2:256 * tt2 + 256],
                mc_s[:, tp2, 128 * tt2:128 * (tt2 + 1)], fTMw[:, w, tp2, mcs],
                start=(tp2 == 0), stop=(tp2 == 1))
                for tt2 in range(2) for tp2 in range(2)],
                reads=(("f", w, 0, hp2), ("f", w, 1, hp2)))
            if h == 0:
                cut("s_g")
            xop("act", lambda a, SC=SC: a.activation(
                out=fl(LamB[:, :, :]), in_=SC[:, 0:512], func=AF.Exp, scale=1.0 / GN),
                reads=(scn,), writes=(("LamB",),), inc=(("act", 1),))
            xop("act", lambda a, SC=SC: a.activation(
                out=fl(EnG[:, :, :]), in_=SC[:, 0:512], func=AF.Exp, scale=-1.0 / GN),
                reads=(scn,), writes=(("EnG",),), inc=(("act", 1),))
            xop("dve", lambda v, w=w, mcs=mcs, SC=SC: v.tensor_sub(
                zS[:, :, :], fTMw[:, w, :, mcs],
                SC[:, 0:512].rearrange("pp (a b) -> pp a b", a=2)),
                reads=(scn, ("f", w, 0, hp2), ("f", w, 1, hp2)),
                writes=(("zS",),), inc=(("dve", 1),))
            xop("act", lambda a: a.activation(
                out=fl(zS[:, :, :]), in_=fl(zS[:, :, :]), func=AF.Exp, scale=1.0 / GN),
                reads=(("zS",),), writes=(("zS",),), inc=(("act", 1),))
            xop("dve", lambda v, hp=hp: v.tensor_sub(
                fl(Sp[:, hp, :, :]), fl(EnG[:, :, :]), fl(zS[:, :, :])),
                reads=(("EnG",), ("zS",)), writes=(("Sp", hp),), inc=(("dve", 1),))
            if h == 0:
                cut("s_kt")
            mmg("awA", [lambda t, taut=taut2, dkc2=dkc2, h=h, w=w: t.matmul(
                PAWA[:, 256 * taut:256 * (taut + 1)],
                kT_s[:, dkc2, 128 * taut:128 * (taut + 1)],
                qTw[:, 2 * h + dkc2, w, :], start=(dkc2 == 0), stop=(dkc2 == 1))
                for taut2 in range(2) for dkc2 in range(2)],
                reads=(("kT",), ("q", w, 2 * h), ("q", w, 2 * h + 1)))
            xop("dve", lambda v: v.tensor_mul(
                fl(AmT[:, :, :]), PAWA[:, 0:512], fl(tri_s[:, :, :])),
                reads=("awA",), writes=(("AmT",),), inc=(("dve", 1),))
            mmg("awB",
                [lambda t, tt2=tt2, i4=i4, h=h, w=w, hp=hp: (
                    t.matmul(PAWB[:, 256 * tt2:256 * tt2 + 256],
                             qTw[:, 2 * h + i4, w, 128 * tt2:128 * (tt2 + 1)],
                             Hm[:, h, i4, :], start=(i4 == 0), stop=False)
                    if i4 < 2 else
                    t.matmul(PAWB[:, 256 * tt2:256 * tt2 + 256],
                             AmT[:, i4 - 2, 128 * tt2:128 * (tt2 + 1)],
                             Sp[:, hp, i4 - 2, :], start=False, stop=(i4 == 3)))
                 for tt2 in range(2) for i4 in range(4)],
                reads=(("q", w, 2 * h), ("q", w, 2 * h + 1), ("Hm", h),
                       ("AmT",), ("Sp", hp)))
            xop("dve", lambda v: v.tensor_mul(
                fl(zS[:, :, :]), PAWB[:, 0:512], fl(LamB[:, :, :])),
                reads=("awB", ("LamB",)), writes=(("zS",),), inc=(("dve", 1),))
            for tt in range(2):
                xop("dve", lambda v, tt=tt: v.tensor_reduce(
                    out=nmx[:, tt:tt + 1], in_=zS[:, tt, :], axis=AX.X, op=ALU.max,
                    negate=True), reads=(("zS",),), writes=(("nmx", tt),),
                    inc=(("dve", 1),))
            for tt in range(2):
                xop("act", lambda a, tt=tt: a.activation(
                    out=pEx[:, tt, :], in_=zS[:, tt, :], func=AF.Exp,
                    bias=nmx[:, tt:tt + 1], accum_out=den[:, tt:tt + 1]),
                    reads=(("zS",), ("nmx", tt)), writes=(("pEx",), ("den", tt)),
                    inc=(("act", 1),))
            xop("dve", lambda v, hp=hp: v.tensor_mul(
                fl(PpTM[:, hp, :, :]), fl(pEx[:, :, :]), fl(LamB[:, :, :])),
                reads=(("pEx",), ("LamB",)), writes=(("Pp", hp),), inc=(("dve", 1),))
            xop("dve", lambda v, hp=hp: v.reciprocal(rden[:, hp, 0:2], den[:, 0:2]),
                reads=(("den", 0), ("den", 1)), writes=(("rden", hp),), inc=(("dve", 1),))
            if h == 0:
                cut("s_z")

        def scan_p2(c, h, last):
            w = c % 2
            wn = (c + 1) % 2
            hp = h % 2
            hp2 = h // 2
            mcs = slice(M * h, M * (h + 1))
            dks = slice(DK * h, DK * (h + 1))
            SC, scn = scb(h)
            mmg("pt", [lambda t, tt2=tt2, mcx2=mcx2, base=base, hp=hp: t.transpose(
                PTX[:, 128 * (base + 2 * mcx2 + tt2):128 * (base + 2 * mcx2 + tt2 + 1)],
                Sp[:, hp, tt2, 128 * mcx2:128 * (mcx2 + 1)] if base == 0
                else PpTM[:, hp, tt2, 128 * mcx2:128 * (mcx2 + 1)],
                id_b[:, :])
                for base in (0, 4) for tt2 in range(2) for mcx2 in range(2)],
                reads=(("Sp", hp), ("Pp", hp)))
            xop("dve", lambda v, hp=hp: v.tensor_copy(fl(SpMT[:, hp, :, :]), PTX[:, 0:512]),
                reads=("pt",), writes=(("SpM", hp),), inc=(("dve", 1),))
            xop("dve", lambda v, hp=hp: v.tensor_copy(fl(PpMT[:, hp, :, :]), PTX[:, 512:1024]),
                reads=("pt",), writes=(("PpM", hp),), inc=(("dve", 1),))
            if h == 0:
                cut("s_t2")
            mmg("awA", [lambda t, taut=taut2, mcx2=mcx2, hp=hp: t.matmul(
                PAWA[:, 256 * taut:256 * (taut + 1)],
                SpMT[:, hp, mcx2, 128 * taut:128 * (taut + 1)], PpMT[:, hp, mcx2, :],
                start=(mcx2 == 0), stop=(mcx2 == 1))
                for taut2 in range(2) for mcx2 in range(2)],
                reads=(("SpM", hp), ("PpM", hp)))
            xop("dve", lambda v: v.tensor_mul(
                fl(WmT[:, :, :]), PAWA[:, 0:512], fl(tri_s[:, :, :])),
                reads=("awA",), writes=(("WmT",),), inc=(("dve", 1),))
            col_emits = []
            for mcx in range(2):
                col_emits.append(lambda t, mcx=mcx, w=w, mcs=mcs, last=last, SC=SC: t.matmul(
                    SC[:, mcx:mcx + 1],
                    fTMw[:, w, 1, mcs.start + 128 * mcx:mcs.start + 128 * (mcx + 1)],
                    one_b[:, 0:1], start=True, stop=last))
                if not last:
                    col_emits.append(lambda t, mcx=mcx, wn=wn, mcs=mcs, SC=SC: t.matmul(
                        SC[:, mcx:mcx + 1],
                        fTMw[:, wn, 0, mcs.start + 128 * mcx:mcs.start + 128 * (mcx + 1)],
                        one_b[:, 0:1], start=False, stop=True))
            col_emits.append(lambda t, w=w, mcs=mcs, last=last, SC=SC: t.matmul(
                SC[0:1, 4:4 + 256], one_b[:, 0:1], fTMw[:, w, 1, mcs],
                start=True, stop=last))
            if not last:
                col_emits.append(lambda t, wn=wn, mcs=mcs, SC=SC: t.matmul(
                    SC[0:1, 4:4 + 256], one_b[:, 0:1], fTMw[:, wn, 0, mcs],
                    start=False, stop=True))
            lamreads = (("f", w, 1, hp2),) + ((("f", wn, 0, hp2),) if not last else ())
            mmg(scn, col_emits, reads=lamreads)
            xop("act", lambda a, SC=SC: a.activation(
                out=lsC[:, 0:2], in_=SC[:, 0:2], func=AF.Exp, scale=1.0 / GN),
                reads=(scn,), writes=(("lsC",),), inc=(("act", 1),))
            xop("act", lambda a, SC=SC: a.activation(
                out=lsR[0:1, :], in_=SC[0:1, 4:4 + 256], func=AF.Exp, scale=1.0 / GN),
                reads=(scn,), writes=(("lsR",),), inc=(("act", 1),))
            mmg(scn, [lambda t, SC=SC: t.matmul(
                SC[:, 256:512], onerow_b[0:1, :], lsR[0:1, 0:256], start=True,
                stop=True)], reads=(("lsR",),))
            if h == 0:
                cut("s_l")
            mmg("awB", [lambda t, dkc=dkc2, taut2=taut2, w=w, dks=dks, hp=hp: t.matmul(
                PAWB[:, 256 * dkc:256 * (dkc + 1)],
                kTMw[:, w, taut2, dks.start + 128 * dkc:dks.start + 128 * (dkc + 1)],
                Sp[:, hp, taut2, :], start=(taut2 == 0), stop=(taut2 == 1))
                for dkc2 in range(2) for taut2 in range(2)],
                reads=(("k", w, 0, hp2), ("k", w, 1, hp2), ("Sp", hp)))
            xop("dve", lambda v, h=h: v.tensor_add(
                fl(tmpS[:, :, :]), fl(Hm[:, h, :, :]), PAWB[:, 0:512]),
                reads=("awB", ("Hm", h)), writes=(("tmp",),), inc=(("dve", 1),))
            for dkc in range(2):
                xop("dve", lambda v, dkc=dkc, h=h, SC=SC: v.tensor_mul(
                    Hm[:, h, dkc, :], tmpS[:, dkc, :], SC[:, 256:512]),
                    reads=(("tmp",), scn), writes=(("Hm", h),), inc=(("dve", 1),))
            mmg(scn,
                [lambda t, tt2=tt2, i4=i4, h=h, w=w, SC=SC, hp=hp: (
                    t.matmul(SC[:, 256 * tt2:256 * tt2 + 256],
                             PpMT[:, hp, i4, 128 * tt2:128 * (tt2 + 1)], Hv[:, h, i4, :],
                             start=(i4 == 0), stop=False)
                    if i4 < 2 else
                    t.matmul(SC[:, 256 * tt2:256 * tt2 + 256],
                             WmT[:, i4 - 2, 128 * tt2:128 * (tt2 + 1)],
                             vTMw[:, w, i4 - 2, DV * h:DV * (h + 1)], start=False,
                             stop=(i4 == 3)))
                 for tt2 in range(2) for i4 in range(4)],
                reads=(("PpM", hp), ("Hv", h), ("WmT",), ("v", w, 0, hp2),
                       ("v", w, 1, hp2)))
            for tt in range(2):
                xop("dve", lambda v, tt=tt, h=h, SC=SC, hp=hp: v.tensor_scalar(
                    out=oB[:, tt, DV * h:DV * (h + 1)],
                    in0=SC[:, 256 * tt:256 * tt + 256],
                    scalar1=rden[:, hp, tt:tt + 1], scalar2=None, op0=ALU.mult),
                    reads=(scn, ("rden", hp)), writes=(("oB", tt),),
                    inc=(("dve", 1),))
            mmg("awB", [lambda t, mcx=mcx2, taut2=taut2, w=w, h=h, hp=hp: t.matmul(
                PAWB[:, 256 * mcx:256 * (mcx + 1)],
                Sp[:, hp, taut2, 128 * mcx:128 * (mcx + 1)],
                vTMw[:, w, taut2, DV * h:DV * (h + 1)],
                start=(taut2 == 0), stop=(taut2 == 1))
                for mcx2 in range(2) for taut2 in range(2)],
                reads=(("Sp", hp), ("v", w, 0, hp2), ("v", w, 1, hp2)))
            xop("dve", lambda v, h=h: v.tensor_add(
                fl(tmpS[:, :, :]), fl(Hv[:, h, :, :]), PAWB[:, 0:512]),
                reads=("awB", ("Hv", h)), writes=(("tmp",),), inc=(("dve", 1),))
            for mcx in range(2):
                xop("dve", lambda v, mcx=mcx, h=h: v.tensor_scalar(
                    out=Hv[:, h, mcx, :], in0=tmpS[:, mcx, :], scalar1=lsC[:, mcx:mcx + 1],
                    scalar2=None, op0=ALU.mult),
                    reads=(("tmp",), ("lsC",)), writes=(("Hv", h),),
                    inc=(("dve", 1),))
            if h == 0:
                cut("s_o")

        def epi_pre(c):
            for tt in range(2):
                xop("act", lambda a, tt=tt: a.activation(
                    out=oB[:, tt, :], in_=oB[:, tt, :], func=AF.Silu),
                    reads=(("oB", tt),), writes=(("oB", tt),), inc=(("act", 1),))
            for tt in range(2):
                xop("act", lambda a, tt=tt: a.activation(
                    out=junk[:, :], in_=oB[:, tt, :], func=AF.Square,
                    accum_out=stat[:, 6 + tt:7 + tt]), reads=(("oB", tt),),
                    writes=(("st", 6 + tt),), inc=(("act", 1),))
            for tt in range(2):
                xop("act", lambda a, tt=tt: a.activation(
                    out=stat[:, 8 + tt:9 + tt], in_=stat[:, 6 + tt:7 + tt], func=AF.Sqrt,
                    scale=1.0 / D, bias=eps_c[:, 0:1]), reads=(("st", 6 + tt),),
                    writes=(("st", 8 + tt),), inc=(("act", 1),))
            for tt in range(2):
                xop("dve", lambda v, tt=tt: v.reciprocal(
                    stat[:, 10 + tt:11 + tt], stat[:, 8 + tt:9 + tt]),
                    reads=(("st", 8 + tt),), writes=(("st", 10 + tt),), inc=(("dve", 1),))
            for tt in range(2):
                xop("act", lambda a, tt=tt: a.activation(
                    out=onb[:, tt, :], in_=oB[:, tt, :], func=AF.Copy,
                    scale=stat[:, 10 + tt:11 + tt], bias=0.0),
                    reads=(("oB", tt), ("st", 10 + tt)), writes=(("onb", tt),),
                    inc=(("act", 1),))
            for tt in range(2):
                mmg("pt", [lambda t, tt=tt, dc=dc: t.transpose(
                    PTX[:, 128 * dc:128 * (dc + 1)], onb[:, tt, 128 * dc:128 * (dc + 1)],
                    id_b[:, :]) for dc in range(NDC)], reads=(("onb", tt),))
                xop("dve", lambda v, tt=tt: v.tensor_copy(
                    onT[:, :, 128 * tt:128 * (tt + 1)],
                    PTX[:, 0:1024].rearrange("pp (a b) -> pp a b", a=NDC)),
                    reads=("pt",), writes=(("onT", tt),), inc=(("dve", 1),))

        def epi_wo(c):
            for pi in range(0, NDC, 2):
                for dc in (pi, pi + 1):
                    mmg("qw", [lambda t, dc=dc, di2=di2: t.matmul(
                        QRr(dc), w_o[:, di2, 128 * dc:128 * (dc + 1)], onT[:, di2, :],
                        start=(di2 == 0), stop=(di2 == NDC - 1)) for di2 in range(NDC)],
                        reads=(("onT", 0), ("onT", 1)))
                for dc in (pi, pi + 1):
                    xop("dve", lambda v, dc=dc: v.tensor_copy(aoT[:, dc, :], QRr(dc)),
                        reads=("qw",), writes=(("aoT", dc),), inc=(("dve", 1),))

        def epi_fin(c, tt):
            sx = c % 3
            mmg("pt", [lambda t, tt=tt, dc=dc: t.transpose(
                PTX[:, 128 * dc:128 * (dc + 1)], aoT[:, dc, 128 * tt:128 * (tt + 1)],
                id_b[:, :]) for dc in range(NDC)],
                reads=tuple(("aoT", dc) for dc in range(NDC)))
            for half in range(2):
                xop("dve", lambda v, tt=tt, sx=sx, half=half: v.tensor_add(
                    x_s[:, sx, tt, 512 * half:512 * (half + 1)],
                    x_s[:, sx, tt, 512 * half:512 * (half + 1)],
                    PTX[:, 512 * half:512 * (half + 1)]),
                    reads=(("x", sx, tt), "pt"), writes=(("x", sx, tt),),
                    inc=(("dve", 1),))
            m = 12 + 4 * tt
            xop("dve", lambda v, tt=tt, sx=sx, m=m: v.tensor_reduce(
                out=stat[:, m:m + 1], in_=x_s[:, sx, tt, :], axis=AX.X, op=ALU.add),
                reads=(("x", sx, tt),), writes=(("st", m),), inc=(("dve", 1),))
            xop("act", lambda a, tt=tt, sx=sx, m=m: a.activation(
                out=junk[:, :], in_=x_s[:, sx, tt, :], func=AF.Square,
                accum_out=stat[:, m + 1:m + 2]), reads=(("x", sx, tt),),
                writes=(("st", m + 1),), inc=(("act", 1),))
            xop("dve", lambda v, m=m: v.tensor_scalar(
                out=stat[:, m:m + 1], in0=stat[:, m:m + 1], scalar1=1.0 / D,
                scalar2=None, op0=ALU.mult),
                reads=(("st", m),), writes=(("st", m),), inc=(("dve", 1),))
            xop("dve", lambda v, m=m: v.tensor_mul(
                stat[:, m + 2:m + 3], stat[:, m:m + 1], stat[:, m:m + 1]),
                reads=(("st", m),), writes=(("st", m + 2),), inc=(("dve", 1),))
            xop("dve", lambda v, m=m: v.scalar_tensor_tensor(
                out=stat[:, m + 1:m + 2], in0=stat[:, m + 1:m + 2], scalar=1.0 / D,
                in1=stat[:, m + 2:m + 3], op0=ALU.mult, op1=ALU.subtract),
                reads=(("st", m + 1), ("st", m + 2)), writes=(("st", m + 1),),
                inc=(("dve", 1),))
            xop("act", lambda a, m=m: a.activation(
                out=stat[:, m + 1:m + 2], in_=stat[:, m + 1:m + 2], func=AF.Sqrt,
                bias=eps_c[:, 0:1]), reads=(("st", m + 1),), writes=(("st", m + 1),),
                inc=(("act", 1),))
            xop("dve", lambda v, m=m: v.reciprocal(
                stat[:, m + 3:m + 4], stat[:, m + 1:m + 2]),
                reads=(("st", m + 1),), writes=(("st", m + 3),), inc=(("dve", 1),))
            xop("dve", lambda v, m=m: v.scalar_tensor_tensor(
                out=stat[:, m + 2:m + 3], in0=stat[:, m:m + 1], scalar=-1.0,
                in1=stat[:, m + 3:m + 4], op0=ALU.mult, op1=ALU.mult),
                reads=(("st", m), ("st", m + 3)), writes=(("st", m + 2),),
                inc=(("dve", 1),))
            xop("act", lambda a, tt=tt, sx=sx, m=m: a.activation(
                out=x_s[:, sx, tt, :], in_=x_s[:, sx, tt, :], func=AF.Identity,
                scale=stat[:, m + 3:m + 4], bias=stat[:, m + 2:m + 3]),
                reads=(("x", sx, tt), ("st", m + 3), ("st", m + 2)),
                writes=(("x", sx, tt),), inc=(("act", 1),))
            xop("dve", lambda v, tt=tt, sx=sx: v.tensor_copy(
                y1b[:, tt, 0:D], x_s[:, sx, tt, :]),
                reads=(("x", sx, tt),), writes=(("y1b", tt),), inc=(("dve", 1),))
            r0 = L * c + 128 * tt
            xop("sp", lambda g, tt=tt, r0=r0, sx=sx: g.dma_start(
                out=y1f32d[r0:r0 + 128, :], in_=x_s[:, sx, tt, :]),
                reads=(("x", sx, tt),), inc=(("outa" if tt == 0 else "outb", 16),))
            if "y1" in taps:
                xop("sp", lambda g, tt=tt, r0=r0, sx=sx: g.dma_start(
                    out=tap["y1"][r0:r0 + 128, :], in_=x_s[:, sx, tt, :]),
                    reads=(("x", sx, tt),), inc=(("st", 16),))
            xop("dve", lambda v, tt=tt, sx=sx: v.tensor_mul(
                x_s[:, sx, tt, :], x_s[:, sx, tt, :], wgbc[:, :]),
                reads=(("x", sx, tt),), writes=(("x", sx, tt),), inc=(("dve", 1),))
            xop("dve", lambda v, tt=tt, sx=sx, c=c: v.tensor_reduce(
                out=lgT[:, 2 * c + tt:2 * c + tt + 1], in_=x_s[:, sx, tt, :], axis=AX.X,
                op=ALU.add), reads=(("x", sx, tt),), writes=(("lgT",),),
                inc=(("dve", 1),))
            xop("dve", lambda v, tt=tt, c=c: v.tensor_copy(
                y1b[:, tt, D:D + 1], lgT[:, 2 * c + tt:2 * c + tt + 1]),
                reads=(("lgT",),), writes=(("y1b", tt),), inc=(("dve", 1),))
            xop("sp", lambda g, tt=tt, r0=r0: g.dma_start(
                out=y1d[r0:r0 + 128, :], in_=y1b[:, tt, :]),
                reads=(("y1b", tt),), inc=(("sta" if tt == 0 else "stb", 16),))

        def ab_step(s):
            lastc = (s - 1 == NCH - 1)
            ep = s - 2
            doepi = 2 <= s <= NCH + 1
            if doepi:
                if "o" in taps:
                    p.barrier()
                    for tt in range(2):
                        p.op("sp", lambda g, ep=ep, tt=tt: g.dma_start(
                            out=tap["o"][L * ep + 128 * tt:L * ep + 128 * (tt + 1), :],
                            in_=oB[:, tt, :]), inc=(("st", 16),))
                    p.barrier()
                epi_pre(ep)
            if s == 0:
                px(0)
            if s < NCH:
                proj_A1(s)
            if doepi:
                epi_wo(ep)
            if s < NCH:
                proj_A2(s)
                cut("pA")
                proj_f(s, 0)
                cut("pF")
            sc = s >= 1 and s <= NCH
            if sc:
                scan_p1(s - 1, 0, lastc)
            if s < NCH:
                proj_q(s, (0, 1, 2, 3))
                cut("pQ")
            if sc:
                scan_p1(s - 1, 1, lastc)
            if s < NCH:
                proj_kv(s, w_k, "k", 0)
                proj_kv(s, w_k, "k", 1)
                cut("pK")
            if sc:
                scan_p2(s - 1, 0, lastc)
            if s < NCH:
                proj_kv(s, w_v, "v", 0)
            if sc:
                scan_p2(s - 1, 1, lastc)
            if s < NCH:
                proj_kv(s, w_v, "v", 1)
                proj_f(s, 1)
            if sc:
                scan_p1(s - 1, 2, lastc)
            if doepi:
                epi_fin(ep, 0)
            if sc:
                scan_p1(s - 1, 3, lastc)
            if s < NCH:
                proj_q(s, (4, 5, 6, 7))
            if sc:
                scan_p2(s - 1, 2, lastc)
            if doepi:
                epi_fin(ep, 1)
            if sc:
                scan_p2(s - 1, 3, lastc)
            if s + 1 < NCH:
                px(s + 1)
                if "qT" in taps:
                    p.barrier()
                    for dc in range(NDC):
                        p.op("pool", lambda g, s=s, dc=dc: g.dma_start(
                            out=tap["qT"][128 * dc:128 * (dc + 1), L * s:L * (s + 1)],
                            in_=qTw[:, dc, s % 2, :]), inc=(("lg", 16),), dma=True)
                if "kvf" in taps:
                    p.barrier()
                    for tt in range(2):
                        r0 = L * s + 128 * tt
                        for i, buf in ((0, kTMw), (1, vTMw), (2, fTMw)):
                            p.op("pool", lambda g, i=i, buf=buf, r0=r0, s=s, tt=tt: g.dma_start(
                                out=tap["kvf"][i, r0:r0 + 128, :], in_=buf[:, s % 2, tt, :]),
                                inc=(("lg", 16),), dma=True)
                if "qT" in taps or "kvf" in taps:
                    p.barrier()

        for step in range(NCH + 2):
            ab_step(step)
            if step < NCH:
                cut(("proj", step))
            if step >= 1 and step <= NCH:
                cut(("scan", step - 1))
        p.barrier()
        ab_stack.close()
        cut("ab")

        PAB = stack.enter_context(nc.psum_tensor("PAB", [128, 1024], F32))
        PCD = stack.enter_context(nc.psum_tensor("PCD", [128, 1024], F32))
        PE_ = stack.enter_context(nc.psum_tensor("PE_", [128, 512], F32))
        PF_ = stack.enter_context(nc.psum_tensor("PF_", [128, 512], F32))
        PT_ = stack.enter_context(nc.psum_tensor("PT_", [128, 1024], BF16))
        PS_ = stack.enter_context(nc.psum_tensor("PS_", [128, 512], F32))
        id_f = sb("id_f", [128, 128], F32)
        lt64_s = sb("lt64_s", [T // 128, T // 128], F32)
        lt128_s = sb("lt128_s", [128, 128], F32)
        sel2_s = sb("sel2_s", [2, 256], BF16)
        sqb = sb("sqb", [128, 512], BF16)
        nrm2 = sb("nrm2", [1, FH, (T // 2) // 512], F32)
        v2S = sb("v2S", [128, (T // 2) // 128, FH * 65], BF16)
        kmx2 = sb("kmx2", [1, FH], F32)
        qmx2 = sb("qmx2", [1, FH], F32)
        for nm, dst in (("ident", id_f), ("lt64", lt64_s), ("lt128", lt128_s)):
            p.op("sp", lambda g, nm=nm, dst=dst: g.dma_start(out=dst[:, :], in_=din[nm][:, :]),
                 inc=(("ld", 16),))
        p.op("pool", lambda g: g.dma_start(out=sel2_s[:, :], in_=din["sel2"][:, :]),
             inc=(("lg", 16),), dma=True)
        p.barrier()

        if "logits" in taps:
            p.op("sp", lambda g: g.dma_start(out=tap["logits"][:, :], in_=lgT[:, :]),
                 inc=(("st", 16),))
            p.barrier()

        NQT = T // 128

        lo = sb("lo", [128, 1], F32)
        hi = sb("hi", [128, 1], F32)
        mid = sb("mid", [128, 1], F32)
        sel = sb("sel", [128, 1], F32)
        selm1 = sb("selm1", [128, 1], F32)
        dtmp = sb("dtmp", [128, 1], F32)
        cntc = sb("cntc", [128, 1], F32)
        cnts = sb("cnts", [1, 1], F32)
        junkN = sb("junkN", [128, NQT], F32)
        p.op("dve", lambda v: v.memset(lo[:, :], -256.0), inc=(("dve", 1),))
        p.op("dve", lambda v: v.memset(hi[:, :], 256.0), inc=(("dve", 1),))
        p.op("dve", lambda v: v.memset(mid[:, :], 0.0), inc=(("dve", 1),))
        for it in range(BIS_ITERS):
            p.op("dve", lambda v: v.tensor_scalar(
                out=junkN[:, :], in0=lgT[:, :], scalar1=mid[:, 0:1], scalar2=0.0,
                op0=ALU.is_ge, op1=ALU.add, accum_out=cntc[:, 0:1]), inc=(("dve", 1),))
            p.op("te", lambda t: t.matmul(PS_[0:1, 0:1], cntc[:, 0:1], one_f[:, 0:1],
                                          start=True, stop=True), inc=(("te", 1),),
                 after=(("dve", None),))
            p.op("dve", lambda v: v.tensor_copy(cnts[0:1, 0:1], PS_[0:1, 0:1]),
                 inc=(("dve", 1),), after=(("te", None),))
            p.op("te", lambda t: t.matmul(PS_[:, 2:3], onesrow_f[0:1, :], cnts[0:1, 0:1],
                                          start=True, stop=True), inc=(("te", 1),),
                 after=(("dve", None),))
            p.op("dve", lambda v: v.tensor_scalar(
                out=sel[:, 0:1], in0=PS_[:, 2:3], scalar1=float(CSEL), scalar2=None,
                op0=ALU.is_ge), inc=(("dve", 1),), after=(("te", None),))
            p.op("dve", lambda v: v.tensor_scalar(
                out=selm1[:, 0:1], in0=sel[:, 0:1], scalar1=1.0, scalar2=None,
                op0=ALU.subtract), inc=(("dve", 1),))
            p.op("dve", lambda v: v.tensor_sub(dtmp[:, 0:1], mid[:, 0:1], lo[:, 0:1]),
                 inc=(("dve", 1),))
            p.op("dve", lambda v: v.scalar_tensor_tensor(
                out=lo[:, 0:1], in0=dtmp[:, 0:1], scalar=sel[:, 0:1], in1=lo[:, 0:1],
                op0=ALU.mult, op1=ALU.add), inc=(("dve", 1),))
            p.op("dve", lambda v: v.tensor_sub(dtmp[:, 0:1], hi[:, 0:1], mid[:, 0:1]),
                 inc=(("dve", 1),))
            p.op("dve", lambda v: v.scalar_tensor_tensor(
                out=hi[:, 0:1], in0=dtmp[:, 0:1], scalar=selm1[:, 0:1], in1=hi[:, 0:1],
                op0=ALU.mult, op1=ALU.add), inc=(("dve", 1),))
            p.op("dve", lambda v: v.tensor_add(dtmp[:, 0:1], lo[:, 0:1], hi[:, 0:1]),
                 inc=(("dve", 1),))
            p.op("dve", lambda v: v.tensor_scalar(
                out=mid[:, 0:1], in0=dtmp[:, 0:1], scalar1=0.5, scalar2=None, op0=ALU.mult),
                inc=(("dve", 1),))
        if "theta" in taps:
            p.op("sp", lambda g: g.dma_start(out=tap["theta"][:, :], in_=lo[:, :]),
                 inc=(("st", 16),), after=(("dve", None),))
        p.barrier()
        cut("bisect")

        maskA = sb("maskA", [128, NQT], F32)
        maskT = sb("maskT", [NQT, 128], F32)
        colcnt = sb("colcnt", [NQT, 1], F32)
        colpf = sb("colpf", [NQT, 1], F32)
        slotT = sb("slotT", [NQT, 128], F32)
        t1s = sb("t1s", [NQT, 128], F32)
        wsS = sb("wsS", [1, 1], F32)
        wsI = sb("wsI", [1, 1], I32)
        wsB = sb("wsB", [NQT, 1], F32)
        slotI = sb("slotI", [128, NQT], I32)
        srelW = sb("srelW", [128, MYT // 128], F32)

        p.op("dve", lambda v: v.tensor_scalar(
            out=maskA[:, :], in0=lgT[:, :], scalar1=lo[:, 0:1], scalar2=None,
            op0=ALU.is_ge), inc=(("dve", 1),))
        p.op("te", lambda t: t.transpose(PS_[0:NQT, 64:64 + 128], maskA[:, :], id_f[:, :]),
             inc=(("te", 1),), after=(("dve", None),))
        p.op("dve", lambda v: v.tensor_copy(maskT[:, :], PS_[0:NQT, 64:64 + 128]),
             inc=(("dve", 1),), after=(("te", None),))
        p.op("dve", lambda v: v.tensor_reduce(
            out=colcnt[:, 0:1], in_=maskT[:, :], axis=AX.X, op=ALU.add), inc=(("dve", 1),))
        p.op("te", lambda t: t.matmul(PS_[0:NQT, 0:1], lt64_s[:, :], colcnt[:, 0:1],
                                      start=True, stop=True), inc=(("te", 1),),
             after=(("dve", None),))
        p.op("dve", lambda v: v.tensor_copy(colpf[:, 0:1], PS_[0:NQT, 0:1]),
             inc=(("dve", 1),), after=(("te", None),))
        p.op("te", lambda t: t.matmul(PS_[0:NQT, 192:192 + 128], maskA[:, :], lt128_s[:, :],
                                      start=True, stop=True), inc=(("te", 1),),
             after=(("dve", None),))
        p.op("dve", lambda v: v.tensor_scalar(
            out=slotT[:, :], in0=PS_[0:NQT, 192:192 + 128], scalar1=colpf[:, 0:1],
            scalar2=None, op0=ALU.add), inc=(("dve", 1),), after=(("te", None),))
        p.op("dve", lambda v: v.tensor_scalar(
            out=t1s[:, :], in0=maskT[:, :], scalar1=-1.0e9, scalar2=1.0e9,
            op0=ALU.mult, op1=ALU.add), inc=(("dve", 1),))
        p.op("dve", lambda v: v.tensor_add(slotT[:, :], slotT[:, :], t1s[:, :]),
             inc=(("dve", 1),))
        p.op("sp", lambda g: g.dma_start(out=cpfD[:, :], in_=colpf[:, :]),
             inc=(("st", 16),), after=(("dve", None),))
        stv = p.c("st")

        def rt_read_cpf(g):
            r = g.value_load(myq_s[0:1, 0:1])
            return g.dma_start(out=wsS[0:1, 0:1], in_=cpfD[bass.ts(r, NQT // 4), :][0:1, :])
        p.op("pool", rt_read_cpf, dma=True, inc=(("lg", 16),), after=(("st", stv),))
        p.op("dve", lambda v: v.tensor_scalar(
            out=wsS[0:1, 0:1], in0=wsS[0:1, 0:1], scalar1=float(CSEL - QW), scalar2=None,
            op0=ALU.min), inc=(("dve", 1),), after=(("lg", None),))
        p.op("dve", lambda v: v.tensor_scalar(
            out=wsS[0:1, 0:1], in0=wsS[0:1, 0:1], scalar1=0.0, scalar2=None,
            op0=ALU.max), inc=(("dve", 1),))
        p.op("dve", lambda v: v.tensor_copy(wsI[0:1, 0:1], wsS[0:1, 0:1]), inc=(("dve", 1),))
        p.op("te", lambda t: t.matmul(PS_[0:NQT, 32:33], onesrow_f[0:1, 0:NQT],
                                      wsS[0:1, 0:1], start=True, stop=True),
             inc=(("te", 1),), after=(("dve", None),))
        p.op("dve", lambda v: v.tensor_copy(wsB[:, 0:1], PS_[0:NQT, 32:33]),
             inc=(("dve", 1),), after=(("te", None),))
        p.op("dve", lambda v: v.tensor_scalar(
            out=t1s[:, :], in0=slotT[:, :], scalar1=wsB[:, 0:1], scalar2=None,
            op0=ALU.subtract), inc=(("dve", 1),), after=(("pool", None),))
        p.op("sp", lambda g: g.dma_start(out=srelD[:, :], in_=t1s[:, :]),
             inc=(("st", 16),), after=(("dve", None),))
        p.op("te", lambda t: t.transpose(PS_[:, 320:320 + NQT], slotT[:, :],
                                         id_f[0:NQT, 0:NQT]), inc=(("te", 1),),
             after=(("dve", None),))
        p.op("dve", lambda v: v.tensor_copy(slotI[:, :], PS_[:, 320:320 + NQT]),
             inc=(("dve", 1),), after=(("te", None),))
        if "slot" in taps:
            p.op("sp", lambda g: g.dma_start(out=tap["slot"][:, :], in_=PS_[:, 320:320 + NQT]),
                 inc=(("st", 16),), after=(("te", None),))
        p.barrier()

        rowbuf = sb("rowbuf", [128, 4, 1032], BF16)
        regh = {}
        p.op("pool", lambda g: regh.setdefault("bc", g.to_reg(CSEL - 1)) and None, dma=True)
        scat_hist = []
        for i in range(NQT):
            bi = i % 4
            par = ("lda", "ldb", "ld", "st")[bi]
            spar = "sga" if i % 2 == 0 else "sgb"
            p.op("sp", lambda g, i=i, bi=bi: g.dma_start(
                out=rowbuf[:, bi, :], in_=y1d[128 * i:128 * (i + 1), :]), inc=((par, 16),),
                after=((spar, scat_hist[i - 4]),) if i >= 4 else ())
            ldv = p.c(par)
            sg_self = ((spar, p.c(spar)),) if p.c(spar) > 0 else ()
            p.op("pool", lambda g, i=i, bi=bi: g.indirect_dma_start(
                out=atn[:, :], out_offset=IndirectOffsetOnAxis(slotI[:, i:i + 1], 0),
                in_=rowbuf[:, bi, :], in_offset=None, bounds_check=regh["bc"],
                oob_is_err=False), inc=((spar, 16),), after=((par, ldv),) + sg_self, dma=True)
            scat_hist.append(p.c(spar))
        p.barrier()
        cut("scatter")
        if "attn_in" in taps:
            atnf = sb("atnf", [128, 1032], F32)
            for i in range(CSEL // 128):
                p.op("pool", lambda g, i=i: g.dma_start(
                    out=atnf[:, :], in_=atn[128 * i:128 * (i + 1), :]), inc=(("lg", 16),),
                    after=(("st", None),), dma=True)
                p.op("sp", lambda g, i=i: g.dma_start(
                    out=tap["attn_in"][128 * i:128 * (i + 1), :], in_=atnf[:, :]),
                    inc=(("st", 16),), after=(("lg", None),))
                p.barrier()

        kv_stack = contextlib.ExitStack()
        sbk = lambda name, shape, dt: kv_stack.enter_context(nc.sbuf_tensor(name, shape, dt))
        fwk_s = sbk("fwk_s", [128, NDC, D], BF16)
        fwv_s = sbk("fwv_s", [128, NDC, D], BF16)
        for (dst, src) in ((fwk_s, "fWk"), (fwv_s, "fWv")):
            for dc in range(NDC):
                p.op("pool", lambda g, d=dst, s=src, dc=dc: g.dma_start(
                    out=d[:, dc, :], in_=din[s][128 * dc:128 * (dc + 1), :]),
                    inc=(("lw", 16),), dma=True)

        aT = sbk("aT", [128, NDC, CSEL], BF16)
        for j in range(NDC):
            p.op("sp", lambda g, j=j: g.dma_start(
                out=aT[:, j, :], in_=atn[:, 128 * j:128 * (j + 1)], transpose=True),
                inc=(("ld", 16),))
        p.barrier()
        cut("aT")
        p.wait("act", "lw")
        p.wait("dve", "lw")
        p.wait("te", "lw")

        kvbuf = sbk("kvbuf", [128, 2, FH * 65], BF16)
        k2st_hist = []
        cpv = {}
        sqv = {}
        nrv = {}
        for dc in range(NDC):
            for qt in range(CSEL // 512):
                it = dc * (CSEL // 512) + qt
                reg = PAB[:, 0:512] if qt % 2 == 0 else PAB[:, 512:1024]
                for di in range(NDC):
                    p.op("te", lambda t, dc=dc, qt=qt, di=di, reg=reg: t.matmul(
                        reg, fwk_s[:, di, 128 * dc:128 * (dc + 1)],
                        aT[:, di, 512 * qt:512 * (qt + 1)],
                        start=(di == 0), stop=(di == NDC - 1)),
                        inc=(("te", 1),) if di == NDC - 1 else (),
                        after=((("dve", cpv.get(it - 2, 0)),) if di == 0 else ()))
                grp = p.c("te")
                bi = it % 2
                spar = "sta" if bi == 0 else "stb"
                p.op("dve", lambda v, bi=bi, reg=reg: v.tensor_copy(kvbuf[:, bi, 0:512], reg),
                     inc=(("dve", 1),),
                     after=(("te", grp),) + (((spar, k2st_hist[it - 2]),) if it >= 2 else ()))
                cpv[it] = p.c("dve")
                p.op("sp", lambda g, dc=dc, qt=qt, bi=bi: g.dma_start(
                    out=k2d[dc, :, 512 * qt:512 * (qt + 1)], in_=kvbuf[:, bi, 0:512]),
                    inc=((spar, 16),), after=(("dve", cpv[it]),))
                k2st_hist.append(p.c(spar))
                p.op("act", lambda a, bi=bi: a.activation(
                    out=sqb[:, 0:512], in_=kvbuf[:, bi, 0:512], func=AF.Square),
                    inc=(("act", 1),), after=(("dve", cpv[it]),))
                sqv[it] = p.c("act")
                for hh in range(2):
                    h = 2 * dc + hh
                    p.op("te", lambda t, hh=hh: t.matmul(
                        PS_[0:1, 0:512], one_b[0:64, 0:1] if hh == 0 else one_b[64:128, 0:1],
                        sqb[0:64, 0:512] if hh == 0 else sqb[64:128, 0:512],
                        start=True, stop=True), inc=(("te", 1),),
                        after=(("act", sqv[it]),) +
                        ((("dve", nrv[it - 1]),) if (hh == 0 and it >= 1) else
                         ((("dve", None),) if hh == 1 else ())))
                    p.op("dve", lambda v, h=h, qt=qt: v.tensor_reduce(
                        out=nrm2[0:1, h, qt:qt + 1], in_=PS_[0:1, 0:512], axis=AX.X,
                        op=ALU.max), inc=(("dve", 1),), after=(("te", None),))
                nrv[it] = p.c("dve")
        for h in range(FH):
            p.op("dve", lambda v, h=h: v.tensor_reduce(
                out=kmx2[0:1, h:h + 1], in_=nrm2[0:1, h, :], axis=AX.X, op=ALU.max),
                inc=(("dve", 1),))
        p.barrier()
        p.op("dve", lambda v: v.memset(v2S[:, :, :], 0.0), inc=(("dve", 1),))
        for h in range(FH):
            p.op("dve", lambda v, h=h: v.memset(
                v2S[:, :, 65 * h + 64:65 * h + 65], 1.0), inc=(("dve", 1),))
        vcp = {}
        for kt in range(NKT):
            for half in range(2):
                ih = 2 * kt + half
                reg = PCD[:, 0:512] if half == 0 else PCD[:, 512:1024]
                for di in range(NDC):
                    p.op("te", lambda t, kt=kt, half=half, di=di, reg=reg: t.matmul(
                        reg, aT[:, di, 128 * kt:128 * (kt + 1)],
                        fwv_s[:, di, 512 * half:512 * (half + 1)],
                        start=(di == 0), stop=(di == NDC - 1)),
                        inc=(("te", 1),) if di == NDC - 1 else (),
                        after=((("dve", vcp.get(ih - 2, 0)),) if di == 0 else ()))
                vgrp = p.c("te")
                for hh in range(8):
                    h = 8 * half + hh
                    p.op("dve", lambda v, kt=kt, h=h, hh=hh, reg=reg: v.tensor_copy(
                        v2S[:, kt, 65 * h:65 * h + 64], reg[:, 64 * hh:64 * (hh + 1)]),
                        inc=(("dve", 1),),
                        after=((("te", vgrp),) if hh == 0 else ()))
                vcp[ih] = p.c("dve")
        p.barrier()
        cut("kv")

        kv_stack.close()
        d_stack = contextlib.ExitStack()
        sbd = lambda name, shape, dt: d_stack.enter_context(nc.sbuf_tensor(name, shape, dt))
        q2T = sbd("q2T", [128, NDC, QW], BF16)
        bmix = sbd("bmix", [1, FH], F32)
        bpair = sbd("bpair", [1, NDC], F32)
        bpb = sbd("bpb", [1, NDC], BF16)
        bcols = sbd("bcols", [128, NDC], F32)
        q_stack = contextlib.ExitStack()
        sbq = lambda name, shape, dt: q_stack.enter_context(nc.sbuf_tensor(name, shape, dt))
        fwq_s = sbq("fwq_s", [128, NDC, D], BF16)
        for dc in range(NDC):
            p.op("pool", lambda g, dc=dc: g.dma_start(
                out=fwq_s[:, dc, :], in_=din["fWq"][128 * dc:128 * (dc + 1), :]),
                inc=(("lw", 16),), dma=True)
        myaT = sbq("myaT", [128, NDC, QW], BF16)

        def ldwin(g):
            r = g.value_load(wsI[0:1, 0:1])
            return g.dma_start(out=myrows[:, :], in_=atn[bass.ds(r, QW), :])
        p.op("pool", ldwin, dma=True, inc=(("lg", 16),))
        p.wait("sp", "lg")
        for j in range(NDC):
            p.op("sp", lambda g, j=j: g.dma_start(
                out=myaT[:, j, :], in_=myrows[:, 128 * j:128 * (j + 1)], transpose=True),
                inc=(("ld", 16),))
        p.barrier()

        qt_sizes = [(i * 512, min(512, QW - i * 512)) for i in range((QW + 511) // 512)]
        for dc in range(NDC):
            for (q0, qn) in qt_sizes:
                reg = PAB[:, 0:512]
                for di in range(NDC):
                    p.op("te", lambda t, dc=dc, di=di, q0=q0, qn=qn, reg=reg: t.matmul(
                        reg[:, 0:qn], fwq_s[:, di, 128 * dc:128 * (dc + 1)],
                        myaT[:, di, q0:q0 + qn], start=(di == 0), stop=(di == NDC - 1)),
                        inc=(("te", 1),) if di == NDC - 1 else (),
                        after=(("dve", None), ("act", None)) if di == 0 else ())
                p.op("dve", lambda v, dc=dc, q0=q0, qn=qn, reg=reg: v.tensor_copy(
                    q2T[:, dc, q0:q0 + qn], reg[:, 0:qn]), inc=(("dve", 1),),
                    after=(("te", None),))
                p.op("act", lambda a, qn=qn, reg=reg: a.activation(
                    out=sqb[:, 0:qn], in_=reg[:, 0:qn], func=AF.Square),
                    inc=(("act", 1),), after=(("dve", None),))
                qi = q0 // 512
                for hh in range(2):
                    h = 2 * dc + hh
                    p.op("te", lambda t, hh=hh, qn=qn: t.matmul(
                        PS_[0:1, 0:qn], one_b[0:64, 0:1] if hh == 0 else one_b[64:128, 0:1],
                        sqb[0:64, 0:qn] if hh == 0 else sqb[64:128, 0:qn],
                        start=True, stop=True), inc=(("te", 1),),
                        after=(("act", None), ("dve", None)))
                    p.op("dve", lambda v, h=h, qi=qi, qn=qn: v.tensor_reduce(
                        out=nrm2[0:1, h, qi:qi + 1], in_=PS_[0:1, 0:qn], axis=AX.X,
                        op=ALU.max), inc=(("dve", 1),), after=(("te", None),))
        NQB = len(qt_sizes)
        for h in range(FH):
            p.op("dve", lambda v, h=h: v.tensor_reduce(
                out=qmx2[0:1, h:h + 1], in_=nrm2[0:1, h, 0:NQB], axis=AX.X, op=ALU.max),
                inc=(("dve", 1),))
        p.op("dve", lambda v: v.tensor_mul(bmix[0:1, :], qmx2[0:1, :], kmx2[0:1, :]),
             inc=(("dve", 1),))
        p.op("act", lambda a: a.activation(out=bmix[0:1, :], in_=bmix[0:1, :], func=AF.Sqrt,
                                           scale=1.0 / 64.0), inc=(("act", 1),),
             after=(("dve", None),))
        p.op("dve", lambda v: v.tensor_max(
            bpair[0:1, :], bmix.rearrange("o (dc hh) -> o dc hh", hh=2)[0:1, :, 0],
            bmix.rearrange("o (dc hh) -> o dc hh", hh=2)[0:1, :, 1]),
            inc=(("dve", 1),), after=(("act", None),))
        p.op("dve", lambda v: v.tensor_scalar(
            out=bpair[0:1, :], in0=bpair[0:1, :], scalar1=-1.0, scalar2=None, op0=ALU.mult),
            inc=(("dve", 1),))
        p.op("dve", lambda v: v.tensor_copy(bpb[0:1, :], bpair[0:1, :]), inc=(("dve", 1),))
        p.op("te", lambda t: t.matmul(PS_[:, 0:NDC], onerow_b[0:1, :], bpb[0:1, :],
                                      start=True, stop=True), inc=(("te", 1),),
             after=(("dve", None),))
        p.op("dve", lambda v: v.tensor_copy(bcols[:, :], PS_[:, 0:NDC]), inc=(("dve", 1),),
             after=(("te", None),))
        p.barrier()
        cut("q2")

        q_stack.close()
        o2hT = sbd("o2hT", [128, NDC, QW], BF16)
        sc_stack = contextlib.ExitStack()
        sbs = lambda name, shape, dt: sc_stack.enter_context(nc.sbuf_tensor(name, shape, dt))
        eS = sbs("eS", [128, 2, 1024], BF16)
        k2blk = sbs("k2blk", [128, 2, 128], BF16)
        drow = sbs("drow", [128, 2, 512], F32)
        dr2 = sbs("dr2", [2, 512], BF16)
        rcpbE = sbs("rcpbE", [128, 512], F32)
        rcpbO = sbs("rcpbO", [128, 512], F32)
        oddst = sbs("oddst", [128, 512], BF16)
        schist = {}
        expv = {}
        for hp in range(NDC):
            for (q0, qn) in qt_sizes:
                for kt in range(NKT + 1):
                    if kt < NKT:
                        bi = kt % 2
                        par = "lda" if bi == 0 else "ldb"
                        guard = (("te", schist[kt - 2]),) if kt >= 2 else ()
                        p.op("sp", lambda g, hp=hp, kt=kt, bi=bi: g.dma_start(
                            out=k2blk[:, bi, :], in_=k2d[hp, :, 128 * kt:128 * (kt + 1)]),
                            inc=((par, 16),), after=guard)
                        ldk = p.c(par)
                        pS = PAB if kt % 2 == 0 else PCD
                        sguard = ((("act", expv[kt - 2]),) if kt >= 2 else
                                  ((("act", None),) if (hp, q0) != (0, 0) else ()))
                        p.op("te", lambda t, bi=bi, hp=hp, q0=q0, qn=qn, pS=pS, par=par: t.matmul(
                            pS[:, 0:qn], k2blk[0:64, bi, :], q2T[0:64, hp, q0:q0 + qn],
                            start=True, stop=True, tile_position=(0, 0)),
                            inc=(("te", 1),), after=((par, ldk),) + sguard)
                        p.op("te", lambda t, bi=bi, hp=hp, q0=q0, qn=qn, pS=pS: t.matmul(
                            pS[:, 512:512 + qn], k2blk[64:128, bi, :],
                            q2T[64:128, hp, q0:q0 + qn],
                            start=True, stop=True, tile_position=(64, 0)), inc=(("te", 1),))
                        schist[kt] = p.c("te")
                        p.op("act", lambda a, hp=hp, qn=qn, pS=pS, bi=bi: a.activation(
                            out=eS[:, bi, 0:2 * qn],
                            in_=pS[:, 0:1024].rearrange("pp (a b) -> pp a b", a=2)[:, :, 0:qn],
                            func=AF.Exp, scale=0.125, bias=bcols[:, hp:hp + 1]),
                            inc=(("act", 1),), after=(("te", schist[kt]),))
                        expv[kt] = p.c("act")
                    if kt >= 1:
                        ka = kt - 1
                        for par in range(2):
                            av = PE_ if par == 0 else PF_
                            p.op("te", lambda t, hp=hp, ka=ka, par=par, av=av, qn=qn: t.matmul(
                                av[0:65, 0:qn],
                                v2S[:, ka, 65 * (2 * hp + par):65 * (2 * hp + par) + 65],
                                eS[:, ka % 2, qn * par:qn * (par + 1)],
                                start=(ka == 0), stop=(ka == NKT - 1)),
                                inc=(("te", 1),),
                                after=(("act", expv[ka]),) + ((("dve", None),) if ka == 0 else ()))
                tev = p.c("te")
                for par in range(2):
                    av = PE_ if par == 0 else PF_
                    p.op("dve", lambda v, par=par, av=av, qn=qn: v.tensor_copy(
                        drow[64:65, par, 0:qn], av[64:65, 0:qn]), inc=(("dve", 1),),
                        after=(("te", tev), ("lg", None)))
                    p.op("dve", lambda v, par=par, qn=qn: v.reciprocal(
                        drow[64:65, par, 0:qn], drow[64:65, par, 0:qn]), inc=(("dve", 1),))
                    p.op("pool", lambda g, par=par, qn=qn: g.dma_start(
                        out=dr2[par:par + 1, 0:qn], in_=drow[64:65, par, 0:qn]),
                        inc=(("lg", 16),), after=(("dve", None), ("te", None)), dma=True)
                ldd = p.c("lg")
                p.op("te", lambda t, qn=qn: t.matmul(
                    PS_[:, 0:qn], sel2_s[:, 0:128], dr2[:, 0:qn], start=True, stop=True),
                    inc=(("te", 1),), after=(("lg", ldd), ("dve", None)))
                p.op("dve", lambda v, qn=qn: v.tensor_copy(rcpbE[:, 0:qn], PS_[:, 0:qn]),
                     inc=(("dve", 1),), after=(("te", None),))
                p.op("te", lambda t, qn=qn: t.matmul(
                    PS_[:, 0:qn], sel2_s[:, 128:256], dr2[:, 0:qn], start=True, stop=True),
                    inc=(("te", 1),), after=(("dve", None),))
                p.op("dve", lambda v, qn=qn: v.tensor_copy(rcpbO[:, 0:qn], PS_[:, 0:qn]),
                     inc=(("dve", 1),), after=(("te", None),))
                p.op("dve", lambda v, hp=hp, q0=q0, qn=qn: v.tensor_mul(
                    o2hT[0:64, hp, q0:q0 + qn], PE_[0:64, 0:qn], rcpbE[0:64, 0:qn]),
                    inc=(("dve", 1),))
                p.op("dve", lambda v, qn=qn: v.tensor_mul(
                    oddst[0:64, 0:qn], PF_[0:64, 0:qn], rcpbO[0:64, 0:qn]),
                    inc=(("dve", 1),), after=(("st", None),))
                p.op("sp", lambda g, hp=hp, q0=q0, qn=qn: g.dma_start(
                    out=o2hT[64:128, hp, q0:q0 + qn], in_=oddst[0:64, 0:qn]),
                    inc=(("st", 16),), after=(("dve", None),))
                cut(("sc", hp, q0))
        cut("scores")
        if "o2" in taps:
            o2f = sbs("o2f", [128, QW], F32)
            for dc in range(NDC):
                p.op("dve", lambda v, dc=dc: v.tensor_copy(o2f[:, :], o2hT[:, dc, :]),
                     inc=(("dve", 1),))
                p.op("sp", lambda g, dc=dc: g.dma_start(
                    out=tap["o2"][128 * dc:128 * (dc + 1), :], in_=o2f[:, :]),
                    inc=(("st", 16),), after=(("dve", None),))
                p.barrier()

        sc_stack.close()
        wo_stack = contextlib.ExitStack()
        sbw = lambda name, shape, dt: wo_stack.enter_context(nc.sbuf_tensor(name, shape, dt))
        fwo_s = sbw("fwo_s", [128, NDC, D], BF16)
        for dc in range(NDC):
            p.op("pool", lambda g, dc=dc: g.dma_start(
                out=fwo_s[:, dc, :], in_=din["fWo"][128 * dc:128 * (dc + 1), :]),
                inc=(("lw", 16),), dma=True,
                after=(("lg", None), ("lda", None), ("ldb", None), ("st", None)))
        p.wait("te", "lw")
        ffnT = sbw("ffnT", [128, NDC, QW], BF16)
        fsl = sbw("fsl", [128, 2, D], BF16)
        for dc in range(NDC):
            for (q0, qn) in qt_sizes:
                reg = PAB[:, 0:512]
                for di in range(NDC):
                    p.op("te", lambda t, dc=dc, di=di, q0=q0, qn=qn, reg=reg: t.matmul(
                        reg[:, 0:qn], fwo_s[:, di, 128 * dc:128 * (dc + 1)],
                        o2hT[:, di, q0:q0 + qn], start=(di == 0), stop=(di == NDC - 1)),
                        inc=(("te", 1),) if di == NDC - 1 else (),
                        after=(("dve", None),) if di == 0 else ())
                p.op("dve", lambda v, dc=dc, q0=q0, qn=qn, reg=reg: v.tensor_copy(
                    ffnT[:, dc, q0:q0 + qn], reg[:, 0:qn]), inc=(("dve", 1),),
                    after=(("te", None),))
        p.barrier()
        fst_hist = []
        for tb in range(QW // 128):
            spar = "sta" if tb % 2 == 0 else "stb"
            for dc in range(NDC):
                p.op("te", lambda t, tb=tb, dc=dc: t.transpose(
                    PT_[:, 128 * dc:128 * (dc + 1)], ffnT[:, dc, 128 * tb:128 * (tb + 1)],
                    id_b[:, :]), inc=(("te", 1),),
                    after=(("dve", None),))
                guard = (("sta" if tb % 2 == 0 else "stb", fst_hist[tb - 2]),) if (tb >= 2 and dc == 0) else ()
                p.op("dve", lambda v, tb=tb, dc=dc: v.tensor_copy(
                    fsl[:, tb % 2, 128 * dc:128 * (dc + 1)], PT_[:, 128 * dc:128 * (dc + 1)]),
                    inc=(("dve", 1),), after=(("te", None),) + guard)
            p.op("sp", lambda g, tb=tb: g.dma_start(
                out=o2d[128 * tb:128 * (tb + 1), :], in_=fsl[:, tb % 2, :]),
                inc=((spar, 16),), after=(("dve", None),))
            fst_hist.append(p.c(spar))
        p.barrier()
        p.barrier()
        cut("wo")

        wo_stack.close()
        d_stack.close()
        fin_stack = contextlib.ExitStack()
        sbf = lambda name, shape, dt: fin_stack.enter_context(nc.sbuf_tensor(name, shape, dt))
        y1t = sbf("y1t", [128, MYT // 128, D], F32)
        gbuf = sbf("gbuf", [128, 2, D], BF16)
        idxc = sbf("idxc", [128, 2], I32)
        srl16 = sbf("srl16", [MYT // 128, 128], F32)
        ej = sbf("ej", [128, 2, D], F32)
        fst = sbf("fst", [128, 8], F32)
        ytmp = sbf("ytmp", [128, 2, D], F32)
        yout = sbf("yout", [128, 2, D], F32)
        NJT = MYT // 128

        def rt_load_srel(g):
            r = g.value_load(myq_s[0:1, 0:1])
            return g.dma_start(out=srl16[:, :], in_=srelD[bass.ts(r, NJT), :])
        p.op("pool", rt_load_srel, dma=True, inc=(("lg", 16),), after=(("st", None),))

        def rt_load_y1(g):
            r = g.value_load(myq_s[0:1, 0:1])
            return g.dma_start(
                out=y1t[:, :, :],
                in_=y1f32d[bass.ds(r * MYT, MYT), :].rearrange("(j pp) d -> pp j d", pp=128))
        p.op("pool", rt_load_y1, dma=True, inc=(("lg", 16),), after=(("st", None),))
        p.op("te", lambda t: t.transpose(PS_[:, 64:64 + NJT], srl16[:, :],
                                         id_f[0:NJT, 0:NJT]), inc=(("te", 1),),
             after=(("lg", None),))
        p.op("dve", lambda v: v.tensor_copy(srelW[:, :], PS_[:, 64:64 + NJT]),
             inc=(("dve", 1),), after=(("te", None),))
        if "dbg" in taps:
            p.op("dve", lambda v: v.tensor_copy(junkN[:, 0:MYT // 128], srelW[:, :]),
                 inc=(("dve", 1),))
            p.op("sp", lambda g: g.dma_start(out=tap["dbg"][:, 0:MYT // 128],
                                             in_=junkN[:, 0:MYT // 128]),
                 inc=(("st", 16),), after=(("dve", None),))
        p.barrier()

        ldyv = p.c("lg")
        p.op("pool", lambda g: regh.__setitem__("bc2", g.to_reg(QW - 1)), dma=True)
        jt = {}
        for j in range(NJT):
            jj = j % 2
            sa, sb, sc2, sf = 0 + jj, 2 + jj, 4 + jj, 6 + jj
            pv = jt.get(j - 2, {})
            gguard = (("sg", 16 * (j - 1)),) if j >= 2 else ()
            p.op("dve", lambda v, j=j, jj=jj: v.tensor_copy(
                idxc[:, jj:jj + 1], srelW[:, j:j + 1]),
                inc=(("dve", 1),), after=gguard)
            p.op("dve", lambda v, jj=jj: v.memset(gbuf[:, jj, :], 0.0),
                 inc=(("dve", 1),), after=((("dve", pv["ytmp"]),) if j >= 2 else ()))
            dvg = p.c("dve")
            p.op("pool", lambda g, j=j, jj=jj: g.indirect_dma_start(
                out=gbuf[:, jj, :], out_offset=None, in_=o2d[:, :],
                in_offset=IndirectOffsetOnAxis(idxc[:, jj:jj + 1], 0),
                bounds_check=regh["bc2"],
                oob_is_err=False), inc=(("sg", 16),), after=(("dve", dvg),), dma=True)
            sgv2 = p.c("sg")
            p.op("dve", lambda v, j=j, jj=jj: v.tensor_mul(
                ej[:, jj, :], y1t[:, j, :], wgbc[:, :]),
                inc=(("dve", 1),), after=(("lg", ldyv),))
            p.op("dve", lambda v, jj=jj, sa=sa: v.tensor_reduce(
                out=fst[:, sa:sa + 1], in_=ej[:, jj, :], axis=AX.X, op=ALU.add),
                inc=(("dve", 1),), after=((("act", pv["sgm"]),) if j >= 2 else ()))
            rda = p.c("dve")
            p.op("act", lambda a, jj=jj, sa=sa, sf=sf: a.activation(
                out=fst[:, sf:sf + 1], in_=fst[:, sa:sa + 1], func=AF.Sigmoid),
                inc=(("act", 1),), after=(("dve", rda),))
            sgm = p.c("act")
            p.op("dve", lambda v, j=j, jj=jj, sf=sf: v.scalar_tensor_tensor(
                out=ytmp[:, jj, :], in0=gbuf[:, jj, :], scalar=fst[:, sf:sf + 1],
                in1=y1t[:, j, :], op0=ALU.mult, op1=ALU.add), inc=(("dve", 1),),
                after=(("sg", sgv2), ("act", sgm)))
            p.op("dve", lambda v, jj=jj, sa=sa: v.tensor_reduce(
                out=fst[:, sa:sa + 1], in_=ytmp[:, jj, :], axis=AX.X, op=ALU.add),
                inc=(("dve", 1),))
            rdy = p.c("dve")
            p.op("act", lambda a, jj=jj, sb=sb: a.activation(
                out=junk[:, :], in_=ytmp[:, jj, :], func=AF.Square,
                accum_out=fst[:, sb:sb + 1]),
                inc=(("act", 1),), after=(("dve", rdy),))
            sqv2 = p.c("act")
            p.op("dve", lambda v, sa=sa: v.tensor_scalar(
                out=fst[:, sa:sa + 1], in0=fst[:, sa:sa + 1], scalar1=1.0 / D,
                scalar2=None, op0=ALU.mult), inc=(("dve", 1),), after=(("act", sqv2),))
            p.op("dve", lambda v, sa=sa, sc2=sc2: v.tensor_mul(
                fst[:, sc2:sc2 + 1], fst[:, sa:sa + 1], fst[:, sa:sa + 1]),
                inc=(("dve", 1),), after=((("act", pv["idv"]),) if j >= 2 else ()))
            p.op("dve", lambda v, sb=sb, sc2=sc2: v.scalar_tensor_tensor(
                out=fst[:, sb:sb + 1], in0=fst[:, sb:sb + 1], scalar=1.0 / D,
                in1=fst[:, sc2:sc2 + 1], op0=ALU.mult, op1=ALU.subtract),
                inc=(("dve", 1),))
            vrv = p.c("dve")
            p.op("act", lambda a, sb=sb: a.activation(
                out=fst[:, sb:sb + 1], in_=fst[:, sb:sb + 1], func=AF.Sqrt,
                bias=eps_c[:, 0:1]), inc=(("act", 1),), after=(("dve", vrv),))
            sqr = p.c("act")
            p.op("dve", lambda v, sb=sb: v.reciprocal(fst[:, sb:sb + 1], fst[:, sb:sb + 1]),
                 inc=(("dve", 1),), after=(("act", sqr),))
            p.op("dve", lambda v, sa=sa, sb=sb, sc2=sc2: v.scalar_tensor_tensor(
                out=fst[:, sc2:sc2 + 1], in0=fst[:, sa:sa + 1], scalar=-1.0,
                in1=fst[:, sb:sb + 1], op0=ALU.mult, op1=ALU.mult),
                inc=(("dve", 1),))
            biasv = p.c("dve")
            opar = "outa" if j % 2 == 0 else "outb"
            oguard = ((opar, p.c(opar)),) if j >= 2 else ()
            p.op("act", lambda a, j=j, jj=jj, sb=sb, sc2=sc2: a.activation(
                out=yout[:, jj, :], in_=ytmp[:, jj, :], func=AF.Identity,
                scale=fst[:, sb:sb + 1], bias=fst[:, sc2:sc2 + 1]),
                inc=(("act", 1),), after=(("dve", biasv),) + oguard)
            idv = p.c("act")
            p.op("sp", lambda g, j=j, jj=jj: g.dma_start(
                out=out_ext[128 * j:128 * (j + 1), :], in_=yout[:, jj, :]),
                inc=((opar, 16),), after=(("act", idv),))
            jt[j] = {"ytmp": rdy, "sgm": sgm, "idv": idv}
        if "dbg" in taps:
            p.barrier()
            p.op("sp", lambda g: g.dma_start(out=tap["dbg"][:, 0:8], in_=junkN[:, 0:8]),
                 inc=(("st", 16),), after=(("dve", None),))
        for e in ENGS:
            p.wait(e, "outa")
            p.wait(e, "outb")
            p.wait(e, "st")
            p.wait(e, "sta")
            p.wait(e, "stb")
            p.wait(e, "sg")

    except _Cut:
        p.barrier()
    try:
        fin_stack.close()
    except Exception:
        pass
    with nc.Block() as block:
        @block.tensor
        def _(te):
            for f in p.q["te"]:
                f(te)

        @block.scalar
        def _(a):
            for f in p.q["act"]:
                f(a)

        @block.vector
        def _(v):
            for f in p.q["dve"]:
                f(v)

        @block.gpsimd
        def _(g):
            for f in p.q["pool"]:
                f(g)

        @block.sync
        def _(s):
            for f in p.q["sp"]:
                f(s)

    try:
        stack.close()
    except Exception:
        pass
    return nc, tap


def _host_inputs(inp):
    x = np.ascontiguousarray(np.asarray(inp["L0_attn_hidden_states"], np.float32))
    mk = {}
    mk["wqT"] = np.ascontiguousarray(np.asarray(inp["L0_attn_w_q"], np.float32).T)
    mk["wkT"] = np.ascontiguousarray(np.asarray(inp["L0_attn_w_k"], np.float32).T)
    mk["wvT"] = np.ascontiguousarray(np.asarray(inp["L0_attn_w_v"], np.float32).T)
    mk["wfT"] = np.ascontiguousarray(np.asarray(inp["L0_attn_w_f"], np.float32).T)
    mk["woT"] = np.ascontiguousarray(np.asarray(inp["L0_attn_w_o"], np.float32).T)
    mk["fWq"] = np.ascontiguousarray(np.asarray(inp["L0_ffn_Wq"], np.float32))
    mk["fWk"] = np.ascontiguousarray(np.asarray(inp["L0_ffn_Wk"], np.float32))
    mk["fWv"] = np.ascontiguousarray(np.asarray(inp["L0_ffn_Wv"], np.float32))
    mk["fWo"] = np.ascontiguousarray(np.asarray(inp["L0_ffn_Wo"], np.float32))
    mk["wg"] = np.ascontiguousarray(np.asarray(inp["L0_ffn_Wg"], np.float32).reshape(1, D))
    mk["nw"] = np.ascontiguousarray(
        np.asarray(inp["L0_attn_norm_weight"], np.float32).reshape(NDC, 128).T)
    mk["gn"] = np.ascontiguousarray(
        np.asarray(inp["L0_attn_gnorm_weight"], np.float32).reshape(NDC, 128).T)
    tpi = np.arange(L)[:, None]
    ti = np.arange(L)[None, :]
    mk["mcum"] = np.ascontiguousarray(
        (tpi <= ti).astype(np.float32) - (tpi <= L // 2 - 1).astype(np.float32))
    mk["trim"] = np.ascontiguousarray((tpi <= ti).astype(np.float32))
    mk["ident"] = np.eye(128, dtype=np.float32)
    c64 = np.arange(T // 128)
    mk["lt64"] = np.ascontiguousarray((c64[:, None] < c64[None, :]).astype(np.float32))
    c128 = np.arange(128)
    mk["lt128"] = np.ascontiguousarray((c128[:, None] < c128[None, :]).astype(np.float32))
    s2 = np.zeros((2, 256), np.float32)
    s2[0, 0:64] = 1.0
    s2[1, 128:192] = 1.0
    mk["sel2"] = s2
    in_maps = []
    for core in range(8):
        b, q = core // 4, core % 4
        m = dict(mk)
        m["x_b"] = np.ascontiguousarray(x[b])
        m["myq16"] = np.array([[q]], np.int32)
        in_maps.append(m)
    return in_maps


def kernel(**inputs):
    nc, _ = build_program(TAPS)
    in_maps = _host_inputs(inputs)
    res = run_bass_kernel_spmd(nc, in_maps, core_ids=list(range(8)))
    parts = [res.results[c]["out_mine"] for c in range(8)]
    out = np.concatenate(parts, axis=0).reshape(B, T, D)
    return np.ascontiguousarray(out.astype(np.float32))
